# version 13
# baseline (speedup 1.0000x reference)
import numpy as np
import ml_dtypes
from contextlib import ExitStack

import concourse.bass as bass
import concourse.mybir as mybir
from concourse.bass_utils import run_bass_kernel_spmd

F32 = mybir.dt.float32
BF16 = mybir.dt.bfloat16
AF = mybir.ActivationFunctionType
ALU = mybir.AluOpType
AX = mybir.AxisListType
BF = ml_dtypes.bfloat16

T = 2048
D = 1024
NCORES = 8
ENGINES = ("tensor", "vector", "scalar", "gpsimd", "sync")


class Prog:

    def __init__(self, nc, es):
        self.nc = nc
        self.es = es
        self.stream = {e: [] for e in ENGINES}
        self.cnt = {}
        self.channels = []
        self.last_wait = {e: {} for e in ENGINES}

    def sem(self, name):
        s = self.es.enter_context(self.nc.semaphore(name))
        self.cnt[s] = 0
        return s

    def wait(self, eng, sem, val):
        if val is None or val <= 0:
            return
        lw = self.last_wait[eng]
        if lw.get(sem, -1) >= val:
            return
        lw[sem] = val
        self.stream[eng].append(lambda E, s=sem, v=val: E.wait_ge(s, v))

    def do(self, eng, build, wait=(), inc=None, n=1):
        import sys as _sys
        _fr = []
        for _i in range(1, 5):
            try:
                f = _sys._getframe(_i)
                _fr.append(str(f.f_lineno))
            except ValueError:
                break
        build._src = ",".join(_fr)
        for (s, v) in wait:
            self.wait(eng, s, v)
        if inc is None:
            self.stream[eng].append(build)
            return None
        self.cnt[inc] += n
        tgt = self.cnt[inc]
        self.stream[eng].append(
            lambda E, b=build, s=inc, k=n: b(E).then_inc(s, k))
        return tgt

    def dma(self, eng, out, in_, sem, wait=(), n=16):
        return self.do(eng, lambda E, o=out, i=in_: E.dma_start(out=o, in_=i),
                       wait=wait, inc=sem, n=n)

    def mark(self, sem):
        return self.cnt[sem]

    def channel(self, name):
        s = self.sem(name)
        self.channels.append(s)
        return s

    def mil(self, eng, sem, wait=()):
        return self.do(eng, lambda E: E.drain(), wait=wait, inc=sem)

    def finalize(self):
        def run(E, ee):
            for fn in self.stream[ee]:
                try:
                    fn(E)
                except Exception:
                    print("FAILED op from", getattr(fn, "_src", "?"), "eng", ee)
                    raise
        with self.nc.Block() as blk:
            for e in ENGINES:
                getattr(blk, e)(lambda E, ee=e: run(E, ee))


class G:

    def __init__(self, nc):
        self.nc = nc
        self.es = ExitStack()
        self.p = Prog(nc, self.es)
        p = self.p
        self.sT = p.sem("sT")
        self.sV = p.sem("sV")
        self.sS = p.sem("sS")
        self.sG = p.sem("sG")
        self.dSY = p.sem("dSY")
        self.dGP = p.sem("dGP")
        self.bar = p.sem("bar")
        self.pool = [self.es.enter_context(
            nc.psum_tensor(f"pp{i}", [128, 512], F32)) for i in range(8)]
        self.dram = {}
        self.innames = []
        self.arenas = []

    def din(self, name, shape, dtype=F32):
        ap = self.nc.dram_tensor(name, list(shape), dtype,
                                 kind="ExternalInput").ap()
        self.dram[name] = ap
        self.innames.append(name)
        return ap

    def dout(self, name, shape, dtype=F32):
        ap = self.nc.dram_tensor(name, list(shape), dtype,
                                 kind="ExternalOutput").ap()
        self.dram[name] = ap
        return ap

    def dint(self, name, shape, dtype=F32):
        ap = self.nc.dram_tensor(name, list(shape), dtype,
                                 kind="Internal").ap()
        self.dram[name] = ap
        return ap

    def sb(self, name, shape, dtype=F32):
        es = self.arenas[-1] if self.arenas else self.es
        return es.enter_context(
            self.nc.sbuf_tensor(name, list(shape), dtype))

    def push_arena(self):
        self.arenas.append(ExitStack())

    def pop_arena(self):
        self.arenas.pop().close()

    def barrier(self):
        p = self.p
        p.wait("sync", self.dSY, p.cnt[self.dSY])
        p.wait("gpsimd", self.dGP, p.cnt[self.dGP])
        for ch in p.channels:
            p.wait("sync", ch, p.cnt[ch])
        for e in ("tensor", "vector", "scalar"):
            p.do(e, lambda E: E.drain(), inc=self.bar)
        for e in ("gpsimd", "sync"):
            p.do(e, lambda E: E.nop(), inc=self.bar)
        tgt = p.cnt[self.bar]
        for e in ENGINES:
            p.wait(e, self.bar, tgt)

    def load_consts(self, names_shapes):
        out = {}
        for name, shape, dtype in names_shapes:
            ap = self.din(name, shape, dtype)
            t = self.sb("s_" + name, shape, dtype)
            self.p.dma("sync", t[:], ap, self.dSY)
            out[name] = t
        return out, self.p.mark(self.dSY)


def emit_ln_stats_pre(g, TT, src_sl, sq_buf, c_ones16, pp_s1, pp_s2,
                      src_ready=(), war_rows=()):
    p, sT, sV, sS = g.p, g.sT, g.sV, g.sS
    bv = sq_buf.bitcast(BF16)
    for c in range(8):
        p.do("vector",
             lambda E, c=c: E.tensor_copy(bv[:, c, TT:2 * TT], src_sl(c)),
             wait=(list(src_ready) + list(war_rows)) if c == 0 else ())
    m_xc = p.mil("vector", sV)
    for c in range(8):
        p.do("scalar",
             lambda E, c=c: E.activation(bv[:, c, 0:TT], src_sl(c), AF.Square),
             wait=src_ready if c == 0 else ())
    m_sq = p.mil("scalar", sS)
    for c in range(8):
        p.do("tensor",
             lambda E, c=c: E.matmul(pp_s1[0:1, 0:TT], c_ones16[:],
                                     bv[:, c, TT:2 * TT], start=(c == 0),
                                     stop=(c == 7)),
             wait=([(sV, m_xc)] + list(war_rows)) if c == 0 else ())
    for c in range(8):
        p.do("tensor",
             lambda E, c=c: E.matmul(pp_s2[0:1, 0:TT], c_ones16[:],
                                     bv[:, c, 0:TT], start=(c == 0),
                                     stop=(c == 7)),
             wait=[(sS, m_sq)] if c == 0 else (),
             inc=(g.sT if c == 7 else None))
    return p.mark(g.sT)


def emit_ln_stats_post(g, TT, rows, c_ones_row, c_eps, pp_s1, pp_s2,
                       pp_bcm, pp_bcr, mk_stats, war_bc=()):
    p, sT, sV, sS = g.p, g.sT, g.sV, g.sS
    p.do("vector", lambda E: E.tensor_scalar_mul(rows[:, 0, 0:TT], pp_s1[0:1, 0:TT],
                                                 1.0 / D),
         wait=[(g.sT, mk_stats)])
    m_m = p.mil("vector", sV)
    p.do("tensor", lambda E: E.matmul(pp_bcm[:, 0:TT], c_ones_row[:],
                                      rows[:, 0, 0:TT], start=True, stop=True),
         wait=[(sV, m_m)] + list(war_bc))
    p.do("vector", lambda E: E.tensor_tensor(rows[:, 1, 0:TT], rows[:, 0, 0:TT],
                                             rows[:, 0, 0:TT], ALU.mult))
    p.do("vector", lambda E: E.drain())
    p.do("vector", lambda E: E.scalar_tensor_tensor(
        rows[:, 2, 0:TT], rows[:, 1, 0:TT], -float(D), pp_s2[0:1, 0:TT],
        ALU.mult, ALU.add))
    m_v = p.mil("vector", sV)
    p.do("scalar", lambda E: E.activation(rows[:, 3, 0:TT], rows[:, 2, 0:TT],
                                          AF.Sqrt, scale=1.0 / D,
                                          bias=c_eps[0:1, 0:1]),
         wait=[(sV, m_v)])
    m_s = p.mil("scalar", sS)
    p.do("vector", lambda E: E.reciprocal(rows[:, 4, 0:TT], rows[:, 3, 0:TT]),
         wait=[(sS, m_s)])
    m_r = p.mil("vector", sV)
    mk_bc = p.do("tensor", lambda E: E.matmul(pp_bcr[:, 0:TT], c_ones_row[:],
                                              rows[:, 4, 0:TT], start=True,
                                              stop=True),
                 wait=[(sV, m_r)] + list(war_bc), inc=g.sT)
    return mk_bc


def emit_ln_stats(g, TT, src_sl, sq_buf, rows, c_ones16, c_ones_row, c_eps,
                  pp_s1, pp_s2, pp_bcm, pp_bcr, src_ready=(), war_rows=(),
                  war_bc=()):
    mk_stats = emit_ln_stats_pre(g, TT, src_sl, sq_buf, c_ones16, pp_s1,
                                 pp_s2, src_ready, war_rows)
    return emit_ln_stats_post(g, TT, rows, c_ones_row, c_eps, pp_s1, pp_s2,
                              pp_bcm, pp_bcr, mk_stats, war_bc)


def emit_l1ffn(g, xin, xout, C):
    nc, p = g.nc, g.p
    sT, sV, sS = g.sT, g.sV, g.sS
    TT = 256
    NTILE = T // TT
    H1 = 4096

    w1t = g.din("l1f_w1t", [D, H1])
    w1n = g.din("l1f_w1n", [H1, D])
    w2t = g.din("l1f_w2t", [H1, D])
    w2n = g.din("l1f_w2n", [D, H1])
    b1h = g.din("l1f_b1", [128, 32])
    b2h = g.din("l1f_b2", [128, 8])

    sgn1 = g.sb("sgn1", [128, 8, H1], BF16)
    sgn2 = g.sb("sgn2", [128, 32, D], BF16)
    a1T = g.sb("a1T", [128, 32])
    a2T = g.sb("a2T", [128, 8])
    b1T = g.sb("b1T", [128, 32])
    b2T = g.sb("b2T", [128, 8])

    p.dma("sync", b1T[:], b1h, g.dSY)
    p.dma("sync", b2T[:], b2h, g.dSY)

    ch_st = [p.channel("l1f_st0"), p.channel("l1f_st1")]
    ch_ns = [p.channel("l1f_ns0"), p.channel("l1f_ns1")]
    with nc.sbuf_tensor("l1f_stage", [128, 2, H1], F32) as stage, \
         nc.sbuf_tensor("l1f_nstage", [128, 2, H1], F32) as nstage:
        sgn_marks = [None, None]
        red_marks = [None, None]
        for c in range(8):
            w = [(sS, sgn_marks[c % 2])] if sgn_marks[c % 2] else []
            mk = p.dma("gpsimd", stage[:, c % 2],
                       w1t[c * 128:(c + 1) * 128, :], ch_st[c % 2], wait=w)
            sgn_marks[c % 2] = p.do(
                "scalar",
                lambda E, c=c: E.activation(sgn1[:, c], stage[:, c % 2],
                                            AF.Sign),
                wait=[(ch_st[c % 2], mk)], inc=sS)
        for c in range(32):
            w = [(sS, sgn_marks[c % 2])]
            mk = p.dma("gpsimd", stage[:, c % 2][:, 0:D],
                       w2t[c * 128:(c + 1) * 128, :], ch_st[c % 2], wait=w)
            sgn_marks[c % 2] = p.do(
                "scalar",
                lambda E, c=c: E.activation(sgn2[:, c], stage[:, c % 2][:, 0:D],
                                            AF.Sign),
                wait=[(ch_st[c % 2], mk)], inc=sS)
        for c in range(32):
            w = [(sV, red_marks[c % 2])]
            mk = p.dma("sync", nstage[:, c % 2][:, 0:D],
                       w1n[c * 128:(c + 1) * 128, :], ch_ns[c % 2], wait=w)
            red_marks[c % 2] = p.do(
                "vector",
                lambda E, c=c: E.reduce_sum(a1T[:, c:c + 1],
                                            nstage[:, c % 2][:, 0:D],
                                            axis=AX.X,
                                            apply_absolute_value=True),
                wait=[(ch_ns[c % 2], mk)], inc=sV)
        for c in range(8):
            w = [(sV, red_marks[c % 2])]
            mk = p.dma("sync", nstage[:, c % 2],
                       w2n[c * 128:(c + 1) * 128, :], ch_ns[c % 2], wait=w)
            red_marks[c % 2] = p.do(
                "vector",
                lambda E, c=c: E.reduce_sum(a2T[:, c:c + 1], nstage[:, c % 2],
                                            axis=AX.X,
                                            apply_absolute_value=True),
                wait=[(ch_ns[c % 2], mk)], inc=sV)
        p.do("vector", lambda E: E.drain())
        p.do("vector", lambda E: E.tensor_scalar_mul(a1T[:], a1T[:], 1.0 / D))
        p.do("vector", lambda E: E.tensor_scalar_mul(a2T[:], a2T[:], 1.0 / H1))
        g.barrier()

    sx = g.sb("l1f_sx", [128, 16, TT])
    xsq = g.sb("l1f_xsq", [128, 8, TT])
    xsq1 = g.sb("l1f_xsq1", [128, 8, TT])
    yb = g.sb("l1f_yb", [128, 8, TT], BF16)
    h1b = g.sb("l1f_h1b", [128, 32, TT], BF16)
    fb = g.sb("l1f_f", [128, 8, TT])
    tmp = g.sb("l1f_tmp", [128, 8, TT])
    rows = g.sb("l1f_rows", [1, 5, TT])
    rows2 = g.sb("l1f_rows2", [1, 5, TT])
    xo = xsq

    pp = g.pool
    ch_x = [p.channel("l1f_chx0"), p.channel("l1f_chx1")]
    ch_o = p.channel("l1f_cho")
    mark_apply = None
    mark_apply_hist = [None, None]
    mark_store = None
    bank_evac = [None, None]
    bank_fe = [None, None]
    mark_evac = [None] * 32
    mark_fe = [None] * 8
    mk_bc_next = None

    for t in range(NTILE):
        t0 = t * TT
        sl = t % 2
        xin_sl = xin[:, t0:t0 + TT].rearrange("(c p) t -> p c t", p=128)
        xout_sl = xout[:, t0:t0 + TT].rearrange("(c p) t -> p c t", p=128)

        if t == 0:
            mk_x = p.dma("sync", sx[:, 0:8], xin_sl, ch_x[0])
            mk_bc = emit_ln_stats(
                g, TT, lambda c: sx[:, c], xsq1, rows2, C["c_ones16"],
                C["c_ones_row"], C["c_eps"],
                pp[0], pp[1], pp[6], pp[7],
                src_ready=[(ch_x[0], mk_x)])
        else:
            mk_bc = mk_bc_next
        for c in range(8):
            p.do("vector",
                 lambda E, c=c, sl=sl: E.tensor_tensor(
                     xsq1[:, c], sx[:, sl * 8 + c],
                     pp[6][:, 0:TT], ALU.subtract),
                 wait=[(sT, mk_bc)] if c == 0 else ())
        p.do("vector", lambda E: E.drain())
        for c in range(8):
            p.do("vector",
                 lambda E, c=c: E.tensor_tensor(yb[:, c], xsq1[:, c],
                                                pp[7][:, 0:TT], ALU.mult))
        mark_yb = p.mil("vector", sV)

        for jc in range(32):
            bank = pp[jc % 2]
            for c in range(8):
                w = []
                if c == 0:
                    w = [(sV, mark_yb)]
                    if bank_evac[jc % 2] is not None:
                        w.append((sS, bank_evac[jc % 2]))
                p.do("tensor",
                     lambda E, jc=jc, c=c, bank=bank: E.matmul(
                         bank[:, 0:TT],
                         sgn1[:, c][:, jc * 128:(jc + 1) * 128],
                         yb[:, c], start=(c == 0), stop=(c == 7)),
                     wait=w, inc=(sT if c == 7 else None))
            mm = p.mark(sT)
            mark_evac[jc] = bank_evac[jc % 2] = p.do(
                "scalar",
                lambda E, jc=jc, bank=bank: E.activation(
                    h1b[:, jc], bank[:, 0:TT], AF.Gelu,
                    scale=a1T[:, jc:jc + 1], bias=b1T[:, jc:jc + 1]),
                wait=[(sT, mm)], inc=sS)

        for oc in range(8):
            bank = pp[6 + (oc % 2)]
            for jc in range(32):
                w = []
                if jc == 0:
                    w = [(sS, mark_evac[31])]
                    if bank_fe[oc % 2] is not None:
                        w.append((sS, bank_fe[oc % 2]))
                p.do("tensor",
                     lambda E, oc=oc, jc=jc, bank=bank: E.matmul(
                         bank[:, 0:TT],
                         sgn2[:, jc][:, oc * 128:(oc + 1) * 128],
                         h1b[:, jc], start=(jc == 0), stop=(jc == 31)),
                     wait=w, inc=(sT if jc == 31 else None))
            mm = p.mark(sT)
            mark_fe[oc] = bank_fe[oc % 2] = p.do(
                "scalar",
                lambda E, oc=oc, bank=bank: E.activation(
                    fb[:, oc], bank[:, 0:TT], AF.Identity,
                    scale=a2T[:, oc:oc + 1], bias=b2T[:, oc:oc + 1]),
                wait=[(sT, mm)], inc=sS)

        if t + 1 < NTILE:
            nsl = (t + 1) % 2
            t1 = (t + 1) * TT
            xin_n = xin[:, t1:t1 + TT].rearrange("(c p) t -> p c t", p=128)
            w = ([(sV, mark_apply_hist[nsl])]
                 if mark_apply_hist[nsl] is not None else [])
            mk_xn = p.dma("sync", sx[:, nsl * 8:nsl * 8 + 8], xin_n,
                          ch_x[nsl], wait=w)
            mkst1 = emit_ln_stats_pre(
                g, TT, lambda c, s=nsl: sx[:, s * 8 + c], xsq1,
                C["c_ones16"], pp[0], pp[1],
                src_ready=[(ch_x[nsl], mk_xn), (sV, mark_yb)])

        w = [(sS, mark_fe[7])]
        if mark_store is not None:
            w.append((ch_o, mark_store))
        mk_bc2 = emit_ln_stats(
            g, TT, lambda c: fb[:, c], xsq, rows, C["c_ones16"],
            C["c_ones_row"], C["c_eps"],
            pp[2], pp[3], pp[4], pp[5],
            src_ready=w)
        if t + 1 < NTILE:
            mk_bc_next = emit_ln_stats_post(
                g, TT, rows2, C["c_ones_row"], C["c_eps"], pp[0], pp[1],
                pp[6], pp[7], mkst1, war_bc=[(sS, mark_fe[7])])
        for c in range(8):
            w = [(sT, mk_bc2)] if c == 0 else []
            if c == 0 and mark_store is not None:
                w.append((ch_o, mark_store))
            p.do("vector",
                 lambda E, c=c: E.tensor_tensor(xsq[:, c], fb[:, c],
                                                pp[4][:, 0:TT], ALU.subtract),
                 wait=w)
        p.do("vector", lambda E: E.drain())
        for c in range(8):
            p.do("vector",
                 lambda E, c=c: E.tensor_tensor(tmp[:, c], xsq[:, c],
                                                pp[5][:, 0:TT], ALU.mult))
        p.do("vector", lambda E: E.drain())
        for c in range(8):
            p.do("vector",
                 lambda E, c=c, sl=sl: E.tensor_tensor(
                     xo[:, c], sx[:, sl * 8 + c], tmp[:, c], ALU.add))
        mark_apply = p.mil("vector", sV)
        mark_apply_hist[sl] = mark_apply
        mark_store = p.dma("sync", xout_sl, xo[:], ch_o,
                           wait=[(sV, mark_apply)])
    g.barrier()


def emit_l0ffn(g, xin, xout, C):
    nc, p = g.nc, g.p
    sT, sV, sS = g.sT, g.sV, g.sS
    TT = 256
    NTILE = T // TT

    wnames = [("cn_w1", [D, D], 8), ("cn_w2", [D, 512], 8),
              ("cn_w3", [512, 256], 4), ("cn_w4", [256, 1], 2),
              ("gc_w1", [D, 512], 8), ("gc_w2", [512, 256], 4),
              ("gc_w3", [256, 1], 2)]
    wd = {n: g.din("l0f_" + n, shp) for n, shp, _ in wnames}
    wb = {n: g.sb("l0f_b_" + n, [128, shp[0] // 128, shp[1]], BF16)
          for n, shp, _ in wnames}
    for n, shp, nch in wnames:
        for c in range(nch):
            p.dma("gpsimd", wb[n][:, c], wd[n][c * 128:(c + 1) * 128, :],
                  g.dGP)
    bdecl = [("cn_b1", [128, 8]), ("cn_b2", [128, 4]), ("cn_b3", [128, 2]),
             ("cn_b4", [1, 1]), ("gc_b1", [128, 4]), ("gc_b2", [128, 2]),
             ("gc_b3", [1, 1]), ("arng", [1, 1])]
    bb = {}
    for n, shp in bdecl:
        ap = g.din("l0f_" + n, shp)
        bb[n] = g.sb("l0f_s_" + n, shp)
        p.dma("sync", bb[n][:], ap, g.dSY)

    pp = g.pool
    sx = g.sb("l0f_sx", [128, 16, TT])
    xsq = g.sb("l0f_xsq", [128, 8, TT])
    xsq2 = g.sb("l0f_xsq2", [128, 8, TT])
    yb = g.sb("l0f_yb", [128, 8, TT], BF16)
    rows = g.sb("l0f_rows", [1, 5, TT])
    rows2 = g.sb("l0f_rows2", [1, 5, TT])
    gacc = g.sb("l0f_gacc", [128, 8, NTILE])
    gcv = g.sb("l0f_gcv", [128, 8])
    gcb = g.sb("l0f_gcb", [128, 8], BF16)
    srow = g.sb("l0f_srow", [1, 8])
    g.barrier()

    ch_x = [p.channel("l0f_chx0"), p.channel("l0f_chx1")]
    ch_o = p.channel("l0f_cho")

    mark_red = None
    mark_red_hist = [None, None]
    statsb = [(pp[2], pp[3]), (pp[6], pp[7])]
    rowsAB = [rows, rows2]
    mkst = [None, None]
    xin_0 = xin[:, 0:TT].rearrange("(c p) t -> p c t", p=128)
    mk_x0 = p.dma("sync", sx[:, 0:8], xin_0, ch_x[0])
    mkst[0] = emit_ln_stats_pre(
        g, TT, lambda c: sx[:, c], xsq2, C["c_ones16"],
        statsb[0][0], statsb[0][1], src_ready=[(ch_x[0], mk_x0)])
    for t in range(NTILE):
        t0 = t * TT
        sl = t % 2
        if t + 1 < NTILE:
            nsl = 1 - sl
            t1 = (t + 1) * TT
            xin_n = xin[:, t1:t1 + TT].rearrange("(c p) t -> p c t", p=128)
            w = ([(sV, mark_red_hist[nsl])]
                 if mark_red_hist[nsl] is not None else [])
            mk_xn = p.dma("sync", sx[:, nsl * 8:nsl * 8 + 8], xin_n,
                          ch_x[nsl], wait=w)
            wr = [(sT, mkst[sl])]
            if mark_red_hist[nsl] is not None:
                wr.append((sV, mark_red_hist[nsl]))
            mkst[nsl] = emit_ln_stats_pre(
                g, TT, lambda c, s=nsl: sx[:, s * 8 + c], xsq2,
                C["c_ones16"], statsb[nsl][0], statsb[nsl][1],
                src_ready=[(ch_x[nsl], mk_xn), (sT, mkst[sl])],
                war_rows=wr)
        mk_bc = emit_ln_stats_post(
            g, TT, rowsAB[sl], C["c_ones_row"], C["c_eps"],
            statsb[sl][0], statsb[sl][1], pp[4], pp[5], mkst[sl],
            war_bc=[(sV, mark_red)] if mark_red else ())
        for c in range(8):
            p.do("vector",
                 lambda E, c=c, sl=sl: E.tensor_tensor(
                     xsq[:, c], sx[:, sl * 8 + c],
                     pp[4][:, 0:TT], ALU.subtract),
                 wait=[(sT, mk_bc)] if c == 0 else ())
        p.do("vector", lambda E: E.drain())
        for c in range(8):
            p.do("vector",
                 lambda E, c=c: E.tensor_tensor(yb[:, c], xsq[:, c],
                                                pp[5][:, 0:TT], ALU.mult))
        p.do("vector", lambda E: E.drain())
        for c in range(8):
            p.do("vector",
                 lambda E, c=c, t=t: E.reduce_sum(gacc[:, c, t:t + 1],
                                                  yb[:, c], axis=AX.X))
        mark_red = p.mil("vector", sV)
        mark_red_hist[sl] = mark_red
    for c in range(8):
        p.do("vector",
             lambda E, c=c: E.reduce_sum(gcv[:, c:c + 1], gacc[:, c],
                                         axis=AX.X))
    p.do("vector", lambda E: E.drain())
    p.do("vector", lambda E: E.tensor_scalar_mul(gcb[:], gcv[:], 1.0 / T))
    m_gcb = p.mil("vector", sV)
    for jc in range(4):
        for c in range(8):
            p.do("tensor",
                 lambda E, jc=jc, c=c: E.matmul(
                     pp[0][:, jc:jc + 1], wb["gc_w1"][:, c, jc * 128:(jc + 1) * 128],
                     gcb[:, c:c + 1], start=(c == 0), stop=(c == 7)),
                 wait=[(sV, m_gcb)] if (jc == 0 and c == 0) else ())
    m_mm = p.mil("tensor", sT)
    g1b = g.sb("l0f_g1b", [128, 4], BF16)
    g2b = g.sb("l0f_g2b", [128, 2], BF16)
    for jc in range(4):
        p.do("scalar",
             lambda E, jc=jc: E.activation(g1b[:, jc:jc + 1],
                                           pp[0][:, jc:jc + 1], AF.Gelu,
                                           bias=bb["gc_b1"][:, jc:jc + 1]),
             wait=[(sT, m_mm)])
    m_g1 = p.mil("scalar", sS)
    for jc in range(2):
        for c in range(4):
            p.do("tensor",
                 lambda E, jc=jc, c=c: E.matmul(
                     pp[1][:, jc:jc + 1], wb["gc_w2"][:, c, jc * 128:(jc + 1) * 128],
                     g1b[:, c:c + 1], start=(c == 0), stop=(c == 3)),
                 wait=[(sS, m_g1)] if (jc == 0 and c == 0) else ())
    m_mm = p.mil("tensor", sT)
    for jc in range(2):
        p.do("scalar",
             lambda E, jc=jc: E.activation(g2b[:, jc:jc + 1],
                                           pp[1][:, jc:jc + 1], AF.Gelu,
                                           bias=bb["gc_b2"][:, jc:jc + 1]),
             wait=[(sT, m_mm)])
    m_g2 = p.mil("scalar", sS)
    for c in range(2):
        p.do("tensor",
             lambda E, c=c: E.matmul(pp[2][0:1, 0:1], wb["gc_w3"][:, c],
                                     g2b[:, c:c + 1], start=(c == 0),
                                     stop=(c == 1)),
             wait=[(sS, m_g2)] if c == 0 else ())
    m_mm = p.mil("tensor", sT)
    m_gf = p.do("scalar",
                lambda E: E.activation(srow[0:1, 4:5], pp[2][0:1, 0:1],
                                       AF.Sigmoid, bias=bb["gc_b3"][0:1, 0:1]),
                wait=[(sT, m_mm)], inc=sS)
    p.do("vector", lambda E: E.tensor_scalar_mul(srow[0:1, 0:1],
                                                 bb["arng"][0:1, 0:1], 0.6),
         wait=[(sS, m_gf)])
    p.do("vector", lambda E: E.tensor_tensor(srow[0:1, 2:3], bb["arng"][0:1, 0:1],
                                             srow[0:1, 4:5], ALU.mult))
    p.do("vector", lambda E: E.drain())
    p.do("vector", lambda E: E.tensor_scalar(srow[0:1, 3:4], srow[0:1, 2:3],
                                             1.4, 0.1, ALU.mult, ALU.add))
    p.do("vector", lambda E: E.drain())
    p.do("vector", lambda E: E.tensor_tensor(srow[0:1, 1:2], srow[0:1, 3:4],
                                             bb["arng"][0:1, 0:1],
                                             ALU.subtract))
    m_srow = p.mil("vector", sV)

    ostream = g.sb("l0f_ostream", [128, 8, TT])
    ob = g.sb("l0f_ob", [128, 8, TT], BF16)
    h1c = g.sb("l0f_h1c", [128, 8, TT], BF16)
    h2c = g.sb("l0f_h2c", [128, 4, TT], BF16)
    h3c = g.sb("l0f_h3c", [128, 2, TT], BF16)
    tw = g.sb("l0f_tw", [128, 8, TT])
    xo = g.sb("l0f_xo", [128, 8, TT])
    tfrow = g.sb("l0f_tfrow", [1, 1, TT])
    crow = g.sb("l0f_crow", [1, 1, TT])
    TSIGN = [ALU.add, ALU.add, ALU.subtract, ALU.subtract,
             ALU.subtract, ALU.subtract, ALU.add, ALU.add]

    mark_apply = m_srow
    mark_apply_hist = list(mark_red_hist)
    mark_store = None
    bank_h1 = [None, None]
    bank_h2 = [None, None]
    bank_h3 = None
    mark_tf = None
    mark_crow = None
    mark_tw = None
    mk_bc_next = None
    mkst1 = None
    for t in range(NTILE):
        t0 = t * TT
        sl = t % 2
        xout_sl = xout[:, t0:t0 + TT].rearrange("(c p) t -> p c t", p=128)
        if t == 0:
            xin_sl = xin[:, t0:t0 + TT].rearrange("(c p) t -> p c t", p=128)
            mk_x = p.dma("sync", sx[:, 0:8], xin_sl, ch_x[0],
                         wait=[(sV, mark_apply_hist[0])])
            mkst1 = emit_ln_stats_pre(
                g, TT, lambda c: sx[:, c], xsq2, C["c_ones16"],
                pp[0], pp[1],
                src_ready=[(ch_x[0], mk_x)], war_rows=[(sV, m_srow)])
            mk_bc = emit_ln_stats_post(
                g, TT, rows2, C["c_ones_row"], C["c_eps"], pp[0], pp[1],
                pp[4], pp[5], mkst1,
                war_bc=[(sV, mark_red)])
        else:
            mk_bc = mk_bc_next
        for c in range(8):
            p.do("vector",
                 lambda E, c=c, sl=sl: E.tensor_tensor(
                     xsq[:, c], sx[:, sl * 8 + c],
                     pp[4][:, 0:TT], ALU.subtract),
                 wait=[(sT, mk_bc)] if c == 0 else ())
        p.do("vector", lambda E: E.drain())
        for c in range(8):
            p.do("vector",
                 lambda E, c=c: E.tensor_tensor(ostream[:, c], xsq[:, c],
                                                pp[5][:, 0:TT], ALU.mult))
        p.do("vector", lambda E: E.drain())
        for c in range(8):
            p.do("vector",
                 lambda E, c=c: E.tensor_copy(ob[:, c], ostream[:, c]))
        mark_ob = p.mil("vector", sV)

        for cyc in range(3):
            h1m = None
            for jc in range(8):
                bank = pp[jc % 2]
                for c in range(8):
                    w = []
                    if c == 0:
                        w = [(sV, mark_ob)]
                        if bank_h1[jc % 2] is not None:
                            w.append((sS, bank_h1[jc % 2]))
                    p.do("tensor",
                         lambda E, jc=jc, c=c, bank=bank: E.matmul(
                             bank[:, 0:TT],
                             wb["cn_w1"][:, c, jc * 128:(jc + 1) * 128],
                             ob[:, c], start=(c == 0), stop=(c == 7)),
                         wait=w, inc=(sT if c == 7 else None))
                mm = p.mark(sT)
                h1m = bank_h1[jc % 2] = p.do(
                    "scalar",
                    lambda E, jc=jc, bank=bank: E.activation(
                        h1c[:, jc], bank[:, 0:TT], AF.Gelu,
                        bias=bb["cn_b1"][:, jc:jc + 1]),
                    wait=[(sT, mm)], inc=sS)
            h2m = None
            for jc in range(4):
                bank = pp[2 + jc % 2]
                for c in range(8):
                    w = []
                    if c == 0:
                        w = [(sS, h1m)]
                        if bank_h2[jc % 2] is not None:
                            w.append((sS, bank_h2[jc % 2]))
                    p.do("tensor",
                         lambda E, jc=jc, c=c, bank=bank: E.matmul(
                             bank[:, 0:TT],
                             wb["cn_w2"][:, c, jc * 128:(jc + 1) * 128],
                             h1c[:, c], start=(c == 0), stop=(c == 7)),
                         wait=w, inc=(sT if c == 7 else None))
                mm = p.mark(sT)
                h2m = bank_h2[jc % 2] = p.do(
                    "scalar",
                    lambda E, jc=jc, bank=bank: E.activation(
                        h2c[:, jc], bank[:, 0:TT], AF.Gelu,
                        bias=bb["cn_b2"][:, jc:jc + 1]),
                    wait=[(sT, mm)], inc=sS)
            h3m = None
            for jc in range(2):
                bank = pp[4]
                for c in range(4):
                    w = []
                    if c == 0:
                        w = [(sS, h2m)]
                        if bank_h3 is not None:
                            w.append((sS, bank_h3))
                    p.do("tensor",
                         lambda E, jc=jc, c=c, bank=bank: E.matmul(
                             bank[:, 0:TT],
                             wb["cn_w3"][:, c, jc * 128:(jc + 1) * 128],
                             h2c[:, c], start=(c == 0), stop=(c == 3)),
                         wait=w, inc=(sT if c == 3 else None))
                mm = p.mark(sT)
                h3m = bank_h3 = p.do(
                    "scalar",
                    lambda E, jc=jc, bank=bank: E.activation(
                        h3c[:, jc], bank[:, 0:TT], AF.Gelu,
                        bias=bb["cn_b3"][:, jc:jc + 1]),
                    wait=[(sT, mm)], inc=sS)
            for c in range(2):
                w = [(sS, h3m)] if c == 0 else []
                if c == 0 and mark_tf is not None:
                    w.append((sS, mark_tf))
                p.do("tensor",
                     lambda E, c=c: E.matmul(pp[5][0:1, 0:TT], wb["cn_w4"][:, c],
                                             h3c[:, c], start=(c == 0),
                                             stop=(c == 1)),
                     wait=w, inc=(sT if c == 1 else None))
            mm = p.mark(sT)
            mark_tf = p.do(
                "scalar",
                lambda E: E.activation(tfrow[:, 0], pp[5][0:1, 0:TT],
                                       AF.Sigmoid, bias=bb["cn_b4"][0:1, 0:1]),
                wait=[(sT, mm)], inc=sS)
            mark_crow = p.do(
                "vector",
                lambda E: E.tensor_scalar(crow[:, 0], tfrow[:, 0],
                                          srow[0:1, 0:1], srow[0:1, 1:2],
                                          ALU.mult, ALU.add),
                wait=[(sS, mark_tf), (sV, m_srow)], inc=sV)
            mk_cb = p.do("tensor",
                         lambda E: E.matmul(pp[4][:, 0:TT], C["c_ones_row"][:],
                                            crow[:, 0], start=True, stop=True),
                         wait=[(sV, mark_crow), (sS, h3m)], inc=sT)
            bank_h3 = None
            for c in range(8):
                w = [(sT, mk_cb)] if c == 0 else []
                p.do("vector",
                     lambda E, c=c: E.tensor_tensor(
                         tw[:, c], pp[4][:, 0:TT], ostream[:, (c + 4) % 8],
                         ALU.mult), wait=w)
            p.do("vector", lambda E: E.drain())
            for c in range(8):
                p.do("vector",
                     lambda E, c=c: E.tensor_tensor(ostream[:, c],
                                                    ostream[:, c], tw[:, c],
                                                    TSIGN[c]))
            mark_tw = p.mil("vector", sV)
            if cyc < 2:
                for c in range(8):
                    p.do("vector",
                         lambda E, c=c: E.tensor_copy(ob[:, c], ostream[:, c]))
                mark_ob = p.mil("vector", sV)

        if t + 1 < NTILE:
            nsl = (t + 1) % 2
            t1 = (t + 1) * TT
            xin_n = xin[:, t1:t1 + TT].rearrange("(c p) t -> p c t", p=128)
            w = ([(sV, mark_apply_hist[nsl])]
                 if mark_apply_hist[nsl] is not None else [])
            mk_xn = p.dma("sync", sx[:, nsl * 8:nsl * 8 + 8], xin_n,
                          ch_x[nsl], wait=w)
            wr = [(sT, mkst1), (sV, mark_tw)]
            for mb in bank_h1:
                if mb is not None:
                    wr.append((sS, mb))
            mkst1 = emit_ln_stats_pre(
                g, TT, lambda c, s=nsl: sx[:, s * 8 + c], xsq2,
                C["c_ones16"], pp[0], pp[1],
                src_ready=[(ch_x[nsl], mk_xn), (sT, mkst1), (sV, mark_tw)],
                war_rows=wr)

        mk_bc2 = emit_ln_stats(
            g, TT, lambda c: ostream[:, c], xsq, rows, C["c_ones16"],
            C["c_ones_row"], C["c_eps"], pp[2], pp[3], pp[6], pp[7],
            src_ready=[(sV, mark_tw)],
            war_bc=[(sV, mark_tw)])
        if t + 1 < NTILE:
            mk_bc_next = emit_ln_stats_post(
                g, TT, rows2, C["c_ones_row"], C["c_eps"], pp[0], pp[1],
                pp[4], pp[5], mkst1, war_bc=[(sV, mark_tw)])
        for c in range(8):
            w = [(sT, mk_bc2)] if c == 0 else []
            if c == 0 and mark_store is not None:
                w.append((ch_o, mark_store))
            p.do("vector",
                 lambda E, c=c: E.tensor_tensor(xsq[:, c], ostream[:, c],
                                                pp[6][:, 0:TT], ALU.subtract),
                 wait=w)
        p.do("vector", lambda E: E.drain())
        for c in range(8):
            p.do("vector",
                 lambda E, c=c: E.tensor_tensor(tw[:, c], xsq[:, c],
                                                pp[7][:, 0:TT], ALU.mult))
        p.do("vector", lambda E: E.drain())
        for c in range(8):
            p.do("vector",
                 lambda E, c=c, sl=sl: E.tensor_tensor(
                     xo[:, c], sx[:, sl * 8 + c], tw[:, c], ALU.add))
        mark_apply = p.mil("vector", sV)
        mark_apply_hist[sl] = mark_apply
        mark_store = p.dma("sync", xout_sl, xo[:], ch_o,
                           wait=[(sV, mark_apply)])
    g.barrier()


def _prep_l0ffn(inputs):
    d = {}
    for n in ["cn_w1", "cn_w2", "cn_w3", "cn_w4", "gc_w1", "gc_w2", "gc_w3"]:
        d["l0f_" + n] = _as32(inputs["L0_ffn_" + n])
    for n, nch in [("cn_b1", 8), ("cn_b2", 4), ("cn_b3", 2), ("gc_b1", 4),
                   ("gc_b2", 2)]:
        b = _as32(inputs["L0_ffn_" + n])
        d["l0f_" + n] = np.ascontiguousarray(b.reshape(nch, 128).T)
    d["l0f_cn_b4"] = _as32(inputs["L0_ffn_cn_b4"]).reshape(1, 1)
    d["l0f_gc_b3"] = _as32(inputs["L0_ffn_gc_b3"]).reshape(1, 1)
    d["l0f_arng"] = _as32(inputs["L0_ffn_adaptive_range"]).reshape(1, 1)
    return d


def emit_l0attnA(g, xin, C, hand="internal"):
    nc, p = g.nc, g.p
    sT, sV, sS = g.sT, g.sV, g.sS
    TT = 256
    NTILE = T // TT
    H = 4
    DK = 256

    qpT = g.din("l0a_qpT", [D, D])
    kpT = g.din("l0a_kpT", [D, D])
    vpT = g.din("l0a_vpT", [D, D])
    bpT = g.din("l0a_bpT", [D, 4])
    qcv = g.din("l0a_qcv", [128, 8, 4])
    kcv = g.din("l0a_kcv", [128, 8, 4])
    vcv = g.din("l0a_vcv", [128, 8, 4])
    mku = g.din("l0a_mku", [128, 128], BF16)
    mkl = g.din("l0a_mkl", [128, 128], BF16)
    mki = g.din("l0a_mki", [128, 128], BF16)
    i32q = g.din("l0a_i32q", [128, 32], BF16)
    idf = g.din("l0a_idf", [128, 128], BF16)
    dh = g.dint if hand == "internal" else g.dout
    delta_d = dh("l0d_delta", [128, 8, T], BF16)
    vs_d = dh("l0d_vs", [128, 8, 30 + T], BF16)

    pp = g.pool
    qpw = g.sb("l0a_qpw", [128, 8, D], BF16)
    kpw = g.sb("l0a_kpw", [128, 8, D], BF16)
    vpw = g.sb("l0a_vpw", [128, 8, D], BF16)
    bpw = g.sb("l0a_bpw", [128, 8, 4], BF16)
    cvw = {}
    for nm, ap in [("q", qcv), ("k", kcv), ("v", vcv)]:
        cvw[nm] = g.sb("l0a_cv" + nm, [128, 8, 4])
        p.dma("sync", cvw[nm][:], ap, g.dSY)
    smku = g.sb("l0a_smku", [128, 128], BF16)
    smkl = g.sb("l0a_smkl", [128, 128], BF16)
    smki = g.sb("l0a_smki", [128, 128], BF16)
    si32 = g.sb("l0a_si32", [128, 32], BF16)
    sidf = g.sb("l0a_sidf", [128, 128], BF16)
    for t_, a_ in [(smku, mku), (smkl, mkl), (smki, mki), (si32, i32q),
                   (sidf, idf)]:
        p.dma("sync", t_[:], a_, g.dSY)
    for c in range(8):
        p.dma("gpsimd", qpw[:, c], qpT[c * 128:(c + 1) * 128, :], g.dGP)
        p.dma("gpsimd", kpw[:, c], kpT[c * 128:(c + 1) * 128, :], g.dGP)
        p.dma("gpsimd", vpw[:, c], vpT[c * 128:(c + 1) * 128, :], g.dGP)
        p.dma("gpsimd", bpw[:, c], bpT[c * 128:(c + 1) * 128, :], g.dGP)

    sx = g.sb("l0a_sx", [128, 8, TT])
    xsq = g.sb("l0a_xsq", [128, 8, TT])
    rows = g.sb("l0a_rows", [1, 6, TT])
    y1t = g.sb("l0a_y1t", [128, 8, TT], BF16)
    praw = {nm: g.sb("l0a_praw" + nm, [128, 8, TT + 3], BF16)
            for nm in "qkv"}
    acc = {nm: g.sb("l0a_acc" + nm, [128, 8, TT], BF16) for nm in "qkv"}
    qsb = g.sb("l0a_qsb", [128, 8, TT], BF16)
    ksb = g.sb("l0a_ksb", [128, 8, TT], BF16)
    vsb = g.sb("l0a_vsb", [128, 8, 30 + TT], BF16)
    scr8 = g.sb("l0a_scr8", [128, 8, TT])
    nrows = g.sb("l0a_nrows", [1, 12, TT])
    brows = g.sb("l0a_brows", [1, 4, TT])
    qnT = g.sb("l0a_qnT", [128, 8, TT], BF16)
    knT = g.sb("l0a_knT", [128, 8, TT], BF16)
    kbT = g.sb("l0a_kbT", [128, 8, TT], BF16)
    vbT = g.sb("l0a_vbT", [128, 8, TT], BF16)
    kntm = g.sb("l0a_kntm", [128, 4, 2, 256], BF16)
    vbtm = g.sb("l0a_vbtm", [128, 4, 2, 256], BF16)
    pkU = g.sb("l0a_pkU", [128, 4, 32], BF16)
    pkL = g.sb("l0a_pkL", [128, 4, 32], BF16)
    pkP = g.sb("l0a_pkP", [128, 2, 4, 32], BF16)
    pkAT = g.sb("l0a_pkAT", [128, 2, 4, 32], BF16)
    pkIV = g.sb("l0a_pkIV", [128, 2, 4, 32], BF16)
    upk = g.sb("l0a_upk", [128, 4, TT], BF16)
    uipk = g.sb("l0a_uipk", [128, 4, TT], BF16)
    S_sb = [g.sb("l0a_Ssb%d" % h, [128, 2, DK], BF16) for h in range(H)]
    dtok = g.sb("l0a_dtok", [128, 8, TT], BF16)
    deltaT = g.sb("l0a_deltaT", [128, 8, TT], BF16)
    g.barrier()

    ch_x = p.channel("l0a_chx")
    ch_dd = p.channel("l0a_chdd")
    ch_dv = p.channel("l0a_chdv")

    zrow = g.sb("l0a_zrow", [1, 512], BF16)
    zcol = g.sb("l0a_zcol", [1, 128], BF16)
    p.do("vector", lambda E: E.memset(praw["q"][:, :, 0:3], 0.0))
    p.do("vector", lambda E: E.memset(praw["k"][:, :, 0:3], 0.0))
    p.do("vector", lambda E: E.memset(praw["v"][:, :, 0:3], 0.0))
    p.do("vector", lambda E: E.memset(vsb[:, :, 0:30], 0.0))
    p.do("vector", lambda E: E.memset(zrow[:], 0.0))
    p.do("vector", lambda E: E.memset(zcol[:], 0.0))
    for h in range(H):
        p.do("vector", lambda E, h=h: E.memset(S_sb[h][:], 0.0))
    m_init = p.mil("vector", sV)
    for h in range(H):
        p.do("tensor",
             lambda E, h=h: E.matmul(pp[h][:, 0:512], zcol[:], zrow[:],
                                     start=True, stop=True),
             wait=[(sV, m_init)] if h == 0 else ())
    m_szero = p.mil("tensor", sT)
    for e in ENGINES:
        p.wait(e, sT, m_szero)

    PQ = lambda m: slice(32 * m, 32 * m + 32)
    mark_use = m_init
    m_pevac = [None, None, None, None]
    m_tile_end_v = None
    m_tile_end_s = None
    m_bc_used = None
    m_bc_used_s = None
    m_sq_used = None
    m_scaled = None
    m_tm_used = None
    m_vdump = None
    m_ddump = None
    m_Sref = [m_init] * H
    m_Sref_sem = [sV] * H

    for t in range(NTILE):
        t0 = t * TT
        first = (t == 0)
        xin_sl = xin[:, t0:t0 + TT].rearrange("(c p) t -> p c t", p=128)
        mk_x = p.dma("sync", sx[:], xin_sl, ch_x, wait=[(sV, mark_use)])
        mk_bc = emit_ln_stats(
            g, TT, lambda c: sx[:, c], xsq, rows, C["c_ones16"],
            C["c_ones_row"], C["c_eps"], pp[4][:, 256:512], pp[5][:, 256:512],
            pp[4][:, 0:256], pp[5][:, 0:256],
            src_ready=[(ch_x, mk_x)],
            war_rows=([(sV, m_bc_used), (sS, m_bc_used_s)]
                      if m_bc_used else ()),
            war_bc=([(sV, m_bc_used), (sS, m_bc_used_s)]
                    if m_bc_used else ()))
        for c in range(8):
            p.do("vector",
                 lambda E, c=c: E.tensor_tensor(xsq[:, c], sx[:, c],
                                                pp[4][:, 0:TT], ALU.subtract),
                 wait=[(sT, mk_bc)] if c == 0 else ())
        p.do("vector", lambda E: E.drain())
        for c in range(8):
            p.do("vector",
                 lambda E, c=c: E.tensor_tensor(y1t[:, c], xsq[:, c],
                                                pp[5][:, 0:TT], ALU.mult))
        mark_use = p.mil("vector", sV)

        if not first:
            for nm in "qkv":
                for c in range(8):
                    p.do("vector",
                         lambda E, nm=nm, c=c: E.tensor_copy(
                             praw[nm][:, c, 0:3], praw[nm][:, c, TT:TT + 3]))
            for c in range(8):
                p.do("vector",
                     lambda E, c=c: E.tensor_copy(vsb[:, c, 0:30],
                                                  vsb[:, c, TT:TT + 30]),
                     wait=[(ch_dv, m_vdump)] if c == 0 else ())
        m_tail = p.mil("vector", sV)

        wsl = [pp[6][:, 0:256], pp[7][:, 0:256], pp[6][:, 256:512],
               pp[7][:, 256:512]]
        wi = 0
        m_prawm = {}
        for nm, ww in [("q", qpw), ("k", kpw), ("v", vpw)]:
            for c in range(8):
                bank = wsl[wi % 4]
                for d in range(8):
                    w = []
                    if d == 0:
                        w = [(sV, mark_use)]
                        if m_pevac[wi % 4] is not None:
                            w.append((sS, m_pevac[wi % 4]))
                        if m_pevac[(wi + 2) % 4] is not None:
                            w.append((sS, m_pevac[(wi + 2) % 4]))
                        if wi == 0 and m_tile_end_v is not None:
                            w.append((sV, m_tile_end_v))
                        if wi == 0 and m_tile_end_s is not None:
                            w.append((sS, m_tile_end_s))
                    p.do("tensor",
                         lambda E, ww=ww, c=c, d=d, bank=bank:
                         E.matmul(bank, ww[:, d, c * 128:(c + 1) * 128],
                                  y1t[:, d], start=(d == 0), stop=(d == 7)),
                         wait=w, inc=(sT if d == 7 else None))
                mm = p.mark(sT)
                m_pevac[wi % 4] = p.do(
                    "scalar",
                    lambda E, nm=nm, c=c, bank=bank: E.activation(
                        praw[nm][:, c, 3:3 + TT], bank, AF.Identity),
                    wait=[(sT, mm), (sV, m_tail)], inc=sS)
                wi += 1
            m_prawm[nm] = p.mark(sS)
        for h in range(H):
            bank = wsl[wi % 4]
            for d in range(8):
                w = []
                if d == 0:
                    w = [(sV, mark_use)]
                    if m_pevac[wi % 4] is not None:
                        w.append((sS, m_pevac[wi % 4]))
                    if m_pevac[(wi + 2) % 4] is not None:
                        w.append((sS, m_pevac[(wi + 2) % 4]))
                p.do("tensor",
                     lambda E, h=h, d=d, bank=bank: E.matmul(
                         bank[0:1, :], bpw[:, d, h:h + 1], y1t[:, d],
                         start=(d == 0), stop=(d == 7)),
                     wait=w, inc=(sT if d == 7 else None))
            mm = p.mark(sT)
            m_pevac[wi % 4] = p.do(
                "scalar",
                lambda E, h=h, bank=bank: E.activation(
                    brows[:, h], bank[0:1, :], AF.Sigmoid),
                wait=[(sT, mm)], inc=sS)
            wi += 1

        m_silu_nm = {}
        for nm in "qkv":
            for j in range(4):
                for c in range(8):
                    if j == 0:
                        w = []
                        if c == 0:
                            w = [(sS, m_prawm[nm])]
                            if nm == "q" and m_scaled is not None:
                                w.append((sV, m_scaled))
                        p.do("vector",
                             lambda E, nm=nm, c=c: E.tensor_scalar_mul(
                                 acc[nm][:, c], praw[nm][:, c, 0:TT],
                                 cvw[nm][:, c, 0:1]),
                             wait=w)
                    else:
                        p.do("vector",
                             lambda E, nm=nm, c=c, j=j: E.scalar_tensor_tensor(
                                 acc[nm][:, c], praw[nm][:, c, j:j + TT],
                                 cvw[nm][:, c, j:j + 1], acc[nm][:, c],
                                 ALU.mult, ALU.add))
                p.do("vector", lambda E: E.drain())
            m_conv = p.mil("vector", sV)
            for c in range(8):
                w = [(sV, m_conv)] if c == 0 else []
                if c == 0 and m_scaled is not None:
                    w.append((sV, m_scaled))
                if nm == "q":
                    p.do("scalar",
                         lambda E, c=c: E.activation(qsb[:, c], acc["q"][:, c],
                                                     AF.Silu), wait=w)
                elif nm == "k":
                    p.do("scalar",
                         lambda E, c=c: E.activation(ksb[:, c], acc["k"][:, c],
                                                     AF.Silu), wait=w)
                else:
                    p.do("scalar",
                         lambda E, c=c: E.activation(vsb[:, c, 30:30 + TT],
                                                     acc["v"][:, c], AF.Silu),
                         wait=w)
            m_silu_nm[nm] = p.mil("scalar", sS)
        m_silu = m_silu_nm["v"]

        w = [(sS, m_silu)]
        if m_vdump is not None:
            w.append((ch_dv, m_vdump))
        if first:
            m_vdump = p.dma("sync", vs_d[:, :, 0:30 + TT], vsb[:], ch_dv,
                            wait=w)
        else:
            m_vdump = p.dma("sync", vs_d[:, :, 30 + t0:30 + t0 + TT],
                            vsb[:, :, 30:30 + TT], ch_dv, wait=w)

        for c in range(8):
            w = [(sS, m_silu_nm["q"])] if c == 0 else []
            if c == 0 and m_sq_used is not None:
                w.append((sV, m_sq_used))
            p.do("scalar",
                 lambda E, c=c: E.activation(scr8[:, c], qsb[:, c],
                                             AF.Square), wait=w)
        m_sq = p.mil("scalar", sS)
        rsl = [pp[6][0:1, 0:256], pp[7][0:1, 0:256], pp[6][0:1, 256:512],
               pp[7][0:1, 256:512]]
        m_rev = list(m_pevac)
        for h in range(H):
            bank = rsl[h]
            for i in range(2):
                w = [(sS, m_sq)] if i == 0 else []
                if i == 0:
                    for mx in (m_rev[h], m_rev[(h + 2) % 4]):
                        if mx is not None:
                            w.append((sS, mx))
                p.do("tensor",
                     lambda E, h=h, i=i, bank=bank: E.matmul(
                         bank, C["c_ones_col"][:], scr8[:, 2 * h + i],
                         start=(i == 0), stop=(i == 1)),
                     wait=w, inc=(sT if i == 1 else None))
            mm = p.mark(sT)
            m_rev[h] = p.do(
                "scalar",
                lambda E, h=h, bank=bank: E.activation(nrows[:, h], bank,
                                                       AF.Identity),
                wait=[(sT, mm)], inc=sS)
        for c in range(8):
            p.do("scalar",
                 lambda E, c=c: E.activation(scr8[:, c], ksb[:, c],
                                             AF.Square))
        m_sq2 = p.mil("scalar", sS)
        for h in range(H):
            bank = rsl[h]
            for i in range(2):
                w = []
                if i == 0:
                    w = [(sS, m_sq2), (sS, m_rev[h])]
                    if m_rev[(h + 2) % 4] is not None:
                        w.append((sS, m_rev[(h + 2) % 4]))
                p.do("tensor",
                     lambda E, h=h, i=i, bank=bank: E.matmul(
                         bank, C["c_ones_col"][:], scr8[:, 2 * h + i],
                         start=(i == 0), stop=(i == 1)),
                     wait=w, inc=(sT if i == 1 else None))
            mm = p.mark(sT)
            m_rev[h] = p.do(
                "scalar",
                lambda E, h=h, bank=bank: E.activation(nrows[:, 4 + h], bank,
                                                       AF.Identity),
                wait=[(sT, mm)], inc=sS)
        m_pevac = [None, None, None, None]
        m_rows0 = p.mil("scalar", sS)

        p.do("scalar",
             lambda E: E.activation(
                 nrows[:, 8:12, :].rearrange("o a t -> o (a t)"),
                 nrows[:, 0:4, :].rearrange("o a t -> o (a t)"),
                 AF.Sqrt, bias=C["c_eps12"][0:1, 0:1]),
             wait=[(sS, m_rows0)])
        m_s1 = p.mil("scalar", sS)
        p.do("vector",
             lambda E: E.reciprocal(
                 nrows[:, 0:4, :].rearrange("o a t -> o (a t)"),
                 nrows[:, 8:12, :].rearrange("o a t -> o (a t)")),
             wait=[(sS, m_s1)])
        m_v1 = p.mil("vector", sV)
        p.do("scalar",
             lambda E: E.activation(
                 nrows[:, 8:12, :].rearrange("o a t -> o (a t)"),
                 nrows[:, 4:8, :].rearrange("o a t -> o (a t)"),
                 AF.Sqrt, bias=C["c_eps12"][0:1, 0:1]),
             wait=[(sV, m_v1)])
        m_s2 = p.mil("scalar", sS)
        p.do("vector",
             lambda E: E.reciprocal(
                 nrows[:, 4:8, :].rearrange("o a t -> o (a t)"),
                 nrows[:, 8:12, :].rearrange("o a t -> o (a t)")),
             wait=[(sS, m_s2)])
        p.do("vector", lambda E: E.drain())
        p.do("vector",
             lambda E: E.tensor_tensor(
                 nrows[:, 8:12, :].rearrange("o a t -> o (a t)"),
                 nrows[:, 4:8, :].rearrange("o a t -> o (a t)"),
                 brows[:, :, :].rearrange("o a t -> o (a t)"), ALU.mult))
        m_rows = p.mil("vector", sV)

        bsl = [pp[4][:, 0:256], pp[4][:, 256:512], pp[5][:, 0:256],
               pp[5][:, 256:512]]
        for h in range(H):
            srcs = [nrows[0:1, h], nrows[0:1, 4 + h], nrows[0:1, 8 + h],
                    brows[0:1, h]]
            mkb = None
            for k in range(4):
                w = [(sV, m_rows)]
                if h > 0:
                    w.append((sV, m_scaled))
                elif m_bc_used is not None:
                    w.append((sV, m_bc_used))
                mkb = p.do("tensor",
                           lambda E, k=k, src=srcs[k]: E.matmul(
                               bsl[k], C["c_ones_row"][:], src,
                               start=True, stop=True),
                           wait=w, inc=(sT if k == 3 else None))
            for i in range(2):
                c = 2 * h + i
                p.do("vector",
                     lambda E, c=c: E.tensor_tensor(qnT[:, c], qsb[:, c],
                                                    bsl[0], ALU.mult),
                     wait=[(sT, mkb)] if i == 0 else ())
                p.do("vector",
                     lambda E, c=c: E.tensor_tensor(knT[:, c], ksb[:, c],
                                                    bsl[1], ALU.mult))
                p.do("vector",
                     lambda E, c=c: E.tensor_tensor(kbT[:, c], ksb[:, c],
                                                    bsl[2], ALU.mult))
                p.do("vector",
                     lambda E, c=c: E.tensor_tensor(vbT[:, c],
                                                    vsb[:, c, 30:30 + TT],
                                                    bsl[3], ALU.mult))
            m_scaled = p.mil("vector", sV)
        m_sq_used = m_scaled

        tbv6 = pp[6].bitcast(BF16)
        tbv7 = pp[7].bitcast(BF16)
        tsl = [tbv6[:, 0:128], tbv7[:, 0:128], tbv6[:, 128:256],
               tbv7[:, 128:256]]
        ti = 0
        m_tevac = [None, None, None, None]
        for src, dst in [(knT, kntm), (vbT, vbtm)]:
            for h in range(H):
                for j in range(2):
                    for b2 in range(2):
                        bank = tsl[ti % 4]
                        w = [(sV, m_scaled), (sS, m_rows0)]
                        for mx in (m_tevac[ti % 4], m_tevac[(ti + 2) % 4]):
                            if mx is not None:
                                w.append((sV, mx))
                        if ti == 0 and m_tm_used is not None:
                            w.append((sT, m_tm_used))
                        mm = p.do("tensor",
                                  lambda E, src=src, h=h, j=j, b2=b2,
                                  bank=bank: E.transpose(
                                      bank,
                                      src[:, 2 * h + j,
                                          b2 * 128:(b2 + 1) * 128],
                                      sidf[:]),
                                  wait=w, inc=sT)
                        m_tevac[ti % 4] = p.do(
                            "vector",
                            lambda E, dst=dst, h=h, j=j, b2=b2, bank=bank:
                            E.tensor_copy(
                                dst[:, h, b2, j * 128:(j + 1) * 128], bank),
                            wait=[(sT, mm)], inc=sV)
                        ti += 1
        m_tm_evac = p.mil("vector", sV)

        pA = pp[6][:, 0:128]
        pB = pp[6][:, 128:256]
        pC = pp[6][:, 256:384]
        pD = pp[6][:, 384:512]
        m_prep_v = m_tm_evac
        for wv in range(2):
            for m in range(4):
                ci = 4 * wv + m
                for h in range(H):
                    for i in range(2):
                        w = []
                        if m == 0 and h == 0 and i == 0:
                            w = [(sV, m_prep_v), (sV, m_tm_evac)]
                        p.do("tensor",
                             lambda E, m=m, h=h, i=i, ci=ci: E.matmul(
                                 pA[PQ(m), 32 * h:32 * h + 32],
                                 knT[:, 2 * h + i, 32 * ci:32 * ci + 32],
                                 kbT[:, 2 * h + i, 32 * ci:32 * ci + 32],
                                 start=(i == 0), stop=(i == 1),
                                 tile_position=(0, 32 * m)),
                             wait=w)
                    for i in range(2):
                        p.do("tensor",
                             lambda E, m=m, h=h, i=i, ci=ci: E.matmul(
                                 pB[PQ(m), 32 * h:32 * h + 32],
                                 kbT[:, 2 * h + i, 32 * ci:32 * ci + 32],
                                 knT[:, 2 * h + i, 32 * ci:32 * ci + 32],
                                 start=(i == 0), stop=(i == 1),
                                 tile_position=(0, 32 * m)))
                    for i in range(2):
                        p.do("tensor",
                             lambda E, m=m, h=h, i=i, ci=ci: E.matmul(
                                 pC[PQ(m), 32 * h:32 * h + 32],
                                 knT[:, 2 * h + i, 32 * ci:32 * ci + 32],
                                 qnT[:, 2 * h + i, 32 * ci:32 * ci + 32],
                                 start=(i == 0), stop=(i == 1),
                                 tile_position=(0, 32 * m)))
            m_amm = p.mil("tensor", sT)
            p.do("vector",
                 lambda E: E.tensor_tensor(
                     pkU.rearrange("p a b -> p (a b)"), pA, smku[:],
                     ALU.mult),
                 wait=[(sT, m_amm)])
            p.do("vector",
                 lambda E: E.tensor_tensor(
                     pkL.rearrange("p a b -> p (a b)"), pB, smkl[:],
                     ALU.mult))
            p.do("vector",
                 lambda E, wv=wv: E.tensor_tensor(
                     pkAT[:, wv].rearrange("p a b -> p (a b)"), pC, smki[:],
                     ALU.mult))
            m_UL = p.mil("vector", sV)
            for m in range(4):
                for h in range(H):
                    p.do("tensor",
                         lambda E, m=m, h=h: E.matmul(
                             pD[PQ(m), 32 * h:32 * h + 32],
                             si32[PQ(m), :], pkU[PQ(m), h, :],
                             start=True, stop=False,
                             tile_position=(32 * m, 32 * m)),
                         wait=[(sV, m_UL)] if (m == 0 and h == 0) else ())
                    p.do("tensor",
                         lambda E, m=m, h=h: E.matmul(
                             pD[PQ(m), 32 * h:32 * h + 32],
                             si32[PQ(m), :], si32[PQ(m), :],
                             start=False, stop=True,
                             tile_position=(32 * m, 32 * m)))
            m_pmm = p.mil("tensor", sT)
            p.do("vector",
                 lambda E: E.tensor_copy(
                     pkP[:, 0].rearrange("p a b -> p (a b)"), pD),
                 wait=[(sT, m_pmm)])
            m_P = p.mil("vector", sV)
            cur = 0
            for lvl in range(4):
                last = (lvl == 3)
                for m in range(4):
                    for h in range(H):
                        w = [(sV, m_P)] if (m == 0 and h == 0) else []
                        if not last:
                            p.do("tensor",
                                 lambda E, m=m, h=h: E.matmul(
                                     pA[PQ(m), 32 * h:32 * h + 32],
                                     pkL[PQ(m), h, :], pkU[PQ(m), h, :],
                                     start=True, stop=True,
                                     tile_position=(32 * m, 32 * m)),
                                 wait=w)
                            w = []
                        p.do("tensor",
                             lambda E, m=m, h=h: E.matmul(
                                 pB[PQ(m), 32 * h:32 * h + 32],
                                 pkU[PQ(m), h, :], pkL[PQ(m), h, :],
                                 start=True, stop=True,
                                 tile_position=(32 * m, 32 * m)),
                             wait=w)
                m_sqm = p.mil("tensor", sT)
                if not last:
                    p.do("vector",
                         lambda E: E.tensor_copy(
                             pkU.rearrange("p a b -> p (a b)"), pA),
                         wait=[(sT, m_sqm)])
                p.do("vector",
                     lambda E: E.tensor_copy(
                         pkL.rearrange("p a b -> p (a b)"), pB),
                     wait=[(sT, m_sqm)])
                m_UL = p.mil("vector", sV)
                nxt = 1 - cur
                for m in range(4):
                    for h in range(H):
                        w = [(sV, m_UL)] if (m == 0 and h == 0) else []
                        p.do("tensor",
                             lambda E, m=m, h=h, cur=cur: E.matmul(
                                 pD[PQ(m), 32 * h:32 * h + 32],
                                 pkL[PQ(m), h, :], pkP[PQ(m), cur, h, :],
                                 start=True, stop=False,
                                 tile_position=(32 * m, 32 * m)),
                             wait=w)
                        p.do("tensor",
                             lambda E, m=m, h=h, cur=cur: E.matmul(
                                 pD[PQ(m), 32 * h:32 * h + 32],
                                 si32[PQ(m), :], pkP[PQ(m), cur, h, :],
                                 start=False, stop=True,
                                 tile_position=(32 * m, 32 * m)))
                m_pmm = p.mil("tensor", sT)
                p.do("vector",
                     lambda E, nxt=nxt: E.tensor_copy(
                         pkP[:, nxt].rearrange("p a b -> p (a b)"), pD),
                     wait=[(sT, m_pmm)])
                m_P = p.mil("vector", sV)
                cur = nxt
            p.do("vector",
                 lambda E, wv=wv, cur=cur: E.tensor_copy(
                     pkIV[:, wv].rearrange("p a b -> p (a b)"),
                     pkP[:, cur].rearrange("p a b -> p (a b)")))
            m_prep_v = p.mil("vector", sV)
        m_prep_done = m_prep_v

        zb = [pp[4], pp[5]]
        oisl = [pp[6][:, 0:256], pp[7][:, 0:256]]
        m_bankio = [(sV, m_scaled), (sV, m_scaled)]
        m_oi_done = [(sV, m_prep_done), (sV, m_prep_done)]
        m_upk_used = [m_prep_done] * H
        m_uipk_used = [None] * H
        m_oi_last = None
        for ci in range(8):
            m = ci % 4
            wv = ci // 4
            for h in range(H):
                pi = (ci * H + h) % 2
                for i in range(2):
                    w = []
                    if i == 0:
                        w = [(m_Sref_sem[h], m_Sref[h]), m_bankio[pi],
                             (sV, m_prep_done)]
                    p.do("tensor",
                         lambda E, m=m, h=h, i=i, ci=ci, pi=pi: E.matmul(
                             zb[pi][PQ(m), 0:256],
                             kbT[:, 2 * h + i, 32 * ci:32 * ci + 32],
                             S_sb[h][:, i, :], start=(i == 0), stop=(i == 1),
                             tile_position=(0, 32 * m)),
                         wait=w, inc=(sT if i == 1 else None))
                mm_z = p.mark(sT)
                m_up = p.do(
                    "vector",
                    lambda E, m=m, h=h, wv=wv, pi=pi: E.tensor_tensor(
                        upk[PQ(m), h, :], vbtm[PQ(m), h, wv, :],
                        zb[pi][PQ(m), 0:256], ALU.subtract),
                    wait=[(sT, mm_z), (sT, m_upk_used[h])], inc=sV)
                mm_ui = p.do(
                    "tensor",
                    lambda E, m=m, h=h, wv=wv, pi=pi: E.matmul(
                        zb[pi][PQ(m), 256:512], pkIV[PQ(m), wv, h, :],
                        upk[PQ(m), h, :], start=True, stop=True,
                        tile_position=(32 * m, 32 * m)),
                    wait=[(sV, m_up)], inc=sT)
                wev = [(sT, mm_ui)]
                if m_uipk_used[h] is not None:
                    wev.append((sT, m_uipk_used[h]))
                m_uiev = p.do(
                    "scalar",
                    lambda E, m=m, h=h, pi=pi: E.activation(
                        uipk[PQ(m), h, :], zb[pi][PQ(m), 256:512],
                        AF.Identity),
                    wait=wev, inc=sS)
                m_bankio[pi] = (sS, m_uiev)
                oz = ci % 2
                for i in range(2):
                    w = []
                    if i == 0:
                        w = [(m_Sref_sem[h], m_Sref[h])]
                        if h == 0:
                            w.append(m_oi_done[oz])
                    p.do("tensor",
                         lambda E, m=m, h=h, i=i, ci=ci, oz=oz: E.matmul(
                             oisl[oz][32 * h:32 * h + 32, :],
                             qnT[:, 2 * h + i, 32 * ci:32 * ci + 32],
                             S_sb[h][:, i, :], start=(i == 0), stop=False,
                             tile_position=(0, 32 * h)),
                         wait=w)
                m_oi_last = p.do(
                    "tensor",
                    lambda E, m=m, h=h, wv=wv, oz=oz: E.matmul(
                        oisl[oz][32 * h:32 * h + 32, :],
                        pkAT[PQ(m), wv, h, :], uipk[PQ(m), h, :],
                        start=False, stop=True,
                        tile_position=(32 * m, 32 * h)),
                    wait=[(sS, m_uiev)], inc=sT)
                for i in range(2):
                    p.do("tensor",
                         lambda E, m=m, h=h, wv=wv, i=i, t=t, ci=ci:
                         E.matmul(
                             pp[h][:, i * 256:(i + 1) * 256],
                             kntm[PQ(m), h, wv, i * 128:(i + 1) * 128],
                             uipk[PQ(m), h, :],
                             start=False, stop=True,
                             skip_group_check=True,
                             tile_position=(32 * m, 0)),
                         inc=(sT if i == 1 else None))
                m_supd = p.mark(sT)
                m_uipk_used[h] = m_supd
                if (ci + h) % 2 == 0:
                    for i in range(2):
                        p.do("scalar",
                             lambda E, h=h, i=i: E.activation(
                                 S_sb[h][:, i, :],
                                 pp[h][:, i * 256:(i + 1) * 256],
                                 AF.Identity),
                             wait=[(sT, m_supd)] if i == 0 else ())
                    m_Sref[h] = p.mil("scalar", sS)
                    m_Sref_sem[h] = sS
                else:
                    for i in range(2):
                        p.do("vector",
                             lambda E, h=h, i=i: E.tensor_copy(
                                 S_sb[h][:, i, :],
                                 pp[h][:, i * 256:(i + 1) * 256]),
                             wait=[(sT, m_supd)] if i == 0 else ())
                    m_Sref[h] = p.mil("vector", sV)
                    m_Sref_sem[h] = sV
            w = [(sT, m_oi_last)]
            if t > 0 and ci < 2 and m_tm_used is not None:
                w.append((sT, m_tm_used))
            if ci % 2 == 0:
                mk = p.do("scalar",
                          lambda E, ci=ci, oz=ci % 2: E.activation(
                              dtok[:, ci, :], oisl[oz], AF.Identity),
                          wait=w, inc=sS)
                m_oi_done[ci % 2] = (sS, mk)
            else:
                mk = p.do("vector",
                          lambda E, ci=ci, oz=ci % 2: E.tensor_copy(
                              dtok[:, ci, :], oisl[oz]),
                          wait=w, inc=sV)
                m_oi_done[ci % 2] = (sV, mk)
        m_scan_v = p.mil("vector", sV)
        m_scan_s = p.mil("scalar", sS)
        m_bc_used = m_scan_v
        m_bc_used_s = m_scan_s

        t7sl = [tbv6[:, 0:128], tbv7[:, 0:128], tbv6[:, 128:256],
                tbv7[:, 128:256]]
        dvw = deltaT.rearrange("p (hh o) t -> p hh o t", o=2)
        ti = 0
        m_t7 = [None] * 4
        m_tm_used = None
        for ci in range(8):
            for half in range(2):
                bank = t7sl[ti % 4]
                w = [m_oi_done[0], m_oi_done[1],
                     (sV, m_scan_v), (sS, m_scan_s)] if ti == 0 else []
                for mx in (m_t7[ti % 4], m_t7[(ti + 2) % 4]):
                    if mx is not None:
                        w = list(w) + [(sV, mx)]
                mm = p.do("tensor",
                          lambda E, ci=ci, half=half, bank=bank: E.transpose(
                              bank, dtok[:, ci, half * 128:(half + 1) * 128],
                              sidf[:]),
                          wait=w, inc=sT)
                m_tm_used = mm
                wev = [(sT, mm)]
                if t > 0 and ti == 0:
                    wev.append((ch_dd, m_ddump))
                m_t7[ti % 4] = p.do(
                    "vector",
                    lambda E, ci=ci, half=half, bank=bank: E.tensor_copy(
                        dvw[:, :, half, 32 * ci:32 * ci + 32], bank),
                    wait=wev, inc=sV)
                ti += 1
        m_dT = p.mil("vector", sV)
        m_ddump = p.dma("sync", delta_d[:, :, t0:t0 + TT], deltaT[:], ch_dd,
                        wait=[(sV, m_dT)])
        m_tile_end_v = m_dT
        m_tile_end_s = m_scan_s
    g.barrier()

def emit_l0attnB(g, xin, xout, C, hand="internal"):
    nc, p = g.nc, g.p
    sT, sV, sS, sG = g.sT, g.sV, g.sS, g.sG
    TT = 256
    NTILE = T // TT
    H = 4

    dh = g.dram.get
    if "l0d_delta" in g.dram:
        delta_d = g.dram["l0d_delta"]
        vs_d = g.dram["l0d_vs"]
    else:
        delta_d = g.din("l0d_delta", [128, 8, T], BF16)
        vs_d = g.din("l0d_vs", [128, 8, 30 + T], BF16)
    g1T = g.din("l0a_g1T", [1048, 2 * D])
    g2T = g.din("l0a_g2T", [2 * D, 24])
    opT = g.din("l0a_opT", [D, D])
    fw = {k: g.din("l0a_fw%d" % k, [128, 8, k]) for k in (3, 7, 15, 31)}
    gb1 = g.din("l0a_gb1", [128, 16])
    gb2a = g.din("l0a_gb2", [24, 1])
    alst = g.din("l0a_alst", [24, 1])
    ltmp = g.din("l0a_ltmp", [4, 1])
    texp = g.din("l0a_texp", [4, 24])
    onw = g.din("l0a_onw", [128, 2])
    stpat = g.din("l0a_stpat", [128, 47], BF16)
    orow = g.din("l0a_orow", [1, 128], BF16)
    d32_d = g.din("l0b_d32", [8, 128, 56 * 32], BF16)
    prb_scr = g.dint("l0b_prbscr", [128, 2, 24])

    pp = g.pool
    g1w = g.sb("l0b_g1w", [128, 8, 2 * D], BF16)
    g1s = g.sb("l0b_g1s", [24, 2 * D], BF16)
    g2w = g.sb("l0b_g2w", [128, 16, 24], BF16)
    opw = g.sb("l0b_opw", [128, 8, D], BF16)
    sfw = {k: g.sb("l0b_fw%d" % k, [128, 8, k]) for k in (3, 7, 15, 31)}
    sgb1 = g.sb("l0b_gb1", [128, 16])
    sgb2 = g.sb("l0b_gb2", [24, 1])
    salst = g.sb("l0b_alst", [24, 1])
    sltmp = g.sb("l0b_ltmp", [4, 1])
    stexp = g.sb("l0b_texp", [4, 24])
    sonw = g.sb("l0b_onw", [128, 2])
    sstp = g.sb("l0b_stpat", [128, 47], BF16)
    sorow = g.sb("l0b_orow", [1, 128], BF16)
    tscr = g.sb("l0b_tscr", [4, 8])
    tinv24 = g.sb("l0b_tinv", [24, 1])
    d32b = g.sb("l0b_d32b", [128, 8, 56, 32], BF16)
    zrowB = g.sb("l0b_zrow", [1, 512], BF16)
    zcolB = g.sb("l0b_zcol", [1, 128], BF16)
    for c in range(8):
        p.dma("gpsimd", g1w[:, c], g1T[c * 128:(c + 1) * 128, :], g.dGP)
        p.dma("gpsimd", opw[:, c], opT[c * 128:(c + 1) * 128, :], g.dGP)
    p.dma("gpsimd", g1s[:], g1T[1024:1048, :], g.dGP)
    for jc in range(16):
        p.dma("gpsimd", g2w[:, jc], g2T[jc * 128:(jc + 1) * 128, :], g.dGP)
    for k in (3, 7, 15, 31):
        p.dma("sync", sfw[k][:], fw[k], g.dSY)
    for c in range(8):
        p.dma("sync", d32b[:, c].rearrange("p a b -> p (a b)"),
              d32_d[c], g.dSY)
    for t_, a_ in [(sgb1, gb1), (sgb2, gb2a), (salst, alst), (sltmp, ltmp),
                   (stexp, texp), (sonw, onw), (sstp, stpat), (sorow, orow)]:
        p.dma("sync", t_[:], a_, g.dSY)

    sx = g.sb("l0b_sx", [128, 16, TT])
    xsq = g.sb("l0b_xsq", [128, 8, TT])
    xsq2 = g.sb("l0b_xsq2", [128, 8, TT])
    rows = g.sb("l0b_rows", [1, 5, TT])
    rows2 = g.sb("l0b_rows2", [1, 5, TT])
    y1b = g.sb("l0b_y1b", [128, 8, TT], BF16)
    vst = g.sb("l0b_vst", [128, 8, 32 + TT], BF16)
    dTt = g.sb("l0b_dT", [128, 8, TT], BF16)
    fbuf = {k: g.sb("l0b_f%d" % k, [128, 8, TT], BF16) for k in (3, 7, 15, 31)}
    babs = g.sb("l0b_babs", [128, 8, TT], BF16)
    stats24 = g.sb("l0b_st24", [24, TT], BF16)
    h1b = g.sb("l0b_h1b", [128, 16, TT], BF16)
    lgt = g.sb("l0b_lgt", [24, TT])
    prb = g.sb("l0b_prb", [128, 2, 24])
    pex = g.sb("l0b_pex", [128, 2, 24])
    mxs = g.sb("l0b_mxs", [128, 2, 12])
    probsR = g.sb("l0b_probsR", [1, 2, 128, 24], BF16)
    bcb = g.sb("l0b_bcb", [128, 6, TT], BF16)
    mixt = g.sb("l0b_mixt", [128, 12, TT], BF16)
    o_acc = xsq
    nr2 = g.sb("l0b_nr2", [1, 8, TT])
    ob = babs
    fo = g.sb("l0b_fo", [128, 8, TT])
    g.barrier()

    p.do("vector", lambda E: E.memset(zrowB[:], 0.0))
    p.do("vector", lambda E: E.memset(zcolB[:], 0.0))
    m_zB = p.mil("vector", sV)

    p.do("vector", lambda E: E.scalar_tensor_tensor(
        tscr[:, 0:1], sltmp[:], -1.0, sltmp[:], ALU.mult, ALU.max))
    m_v = p.mil("vector", sV)
    p.do("scalar", lambda E: E.activation(tscr[:, 1:2], tscr[:, 0:1], AF.Exp,
                                          scale=-1.0), wait=[(sV, m_v)])
    m_s = p.mil("scalar", sS)
    p.do("scalar", lambda E: E.activation(tscr[:, 2:3], tscr[:, 1:2], AF.Ln,
                                          bias=C["c_ones_col"][0:4, 0:1]),
         wait=[(sS, m_s)])
    m_s = p.mil("scalar", sS)
    p.do("vector", lambda E: E.tensor_scalar_max(tscr[:, 3:4], sltmp[:], 0.0),
         wait=[(sS, m_s)])
    p.do("vector", lambda E: E.drain())
    p.do("vector", lambda E: E.tensor_tensor(tscr[:, 4:5], tscr[:, 3:4],
                                             tscr[:, 2:3], ALU.add))
    p.do("vector", lambda E: E.drain())
    p.do("vector", lambda E: E.tensor_scalar_add(tscr[:, 5:6], tscr[:, 4:5],
                                                 1e-4))
    m_v = p.mil("vector", sV)
    mm_t = p.do("tensor",
                lambda E: E.matmul(pp[2][0:24, 0:1], stexp[:], tscr[:, 5:6],
                                   start=True, stop=True),
                wait=[(sV, m_v)], inc=sT)
    p.do("vector", lambda E: E.reciprocal(tinv24[:], pp[2][0:24, 0:1]),
         wait=[(sT, mm_t)])
    m_temp = p.mil("vector", sV)

    ch_x = [p.channel("l0b_chx0"), p.channel("l0b_chx1")]
    ch_v = p.channel("l0b_chv")
    ch_d = p.channel("l0b_chd")
    ch_p1 = p.channel("l0b_chp1")
    ch_p2 = p.channel("l0b_chp2")
    ch_o = p.channel("l0b_cho")
    TAPS = []
    for k in (3, 7, 15, 31):
        for j in range(k):
            TAPS.append((k, j, 31 - k + j))

    mark_use = m_temp
    mark_use_hist = [None, None]
    mk_bc_next = None
    mkst1 = None
    m_fir_used = None
    m_fir_used_s = None
    m_fir_pe = None
    m_tile_s = None
    m_st_used = None
    m_bank01 = [None, None]
    m_gbank = [None] * 4
    m_lgt_used = None
    m_prb_out = None
    m_prbR_used = None
    m_bcb_used = [None] * 6
    m_mix_done = None
    m_oacc_used = None
    m_rms_used = None
    m_ob_used = None
    mark_store = None
    mark_ln2 = None
    m_stats_ev = None

    for t in range(NTILE):
        t0 = t * TT
        sl = t % 2
        xout_sl = xout[:, t0:t0 + TT].rearrange("(c p) t -> p c t", p=128)
        w = [(sV, m_fir_used)] if m_fir_used else []
        if m_fir_used_s:
            w.append((sS, m_fir_used_s))
        if m_fir_pe is not None:
            w.append((sT, m_fir_pe))
        mk_v = p.dma("sync", vst[:, :, 0:30 + TT],
                     vs_d[:, :, t0:t0 + 30 + TT], ch_v, wait=w)
        mk_d = p.dma("sync", dTt[:], delta_d[:, :, t0:t0 + TT], ch_d, wait=w)

        if t == 0:
            xin_sl = xin[:, t0:t0 + TT].rearrange("(c p) t -> p c t", p=128)
            mk_x = p.dma("sync", sx[:, 0:8], xin_sl, ch_x[0],
                         wait=[(sV, mark_use)])
            mkst1 = emit_ln_stats_pre(
                g, TT, lambda c: sx[:, c], xsq2, C["c_ones16"],
                pp[2][0:1, 256:512], pp[3][0:1, 256:512],
                src_ready=[(ch_x[0], mk_x)])
            mk_bc = emit_ln_stats_post(
                g, TT, rows2, C["c_ones_row"], C["c_eps"],
                pp[2][0:1, 256:512], pp[3][0:1, 256:512],
                pp[6][:, 0:256], pp[7][:, 0:256], mkst1)
        else:
            mk_bc = mk_bc_next
        for c in range(8):
            p.do("vector",
                 lambda E, c=c, sl=sl: E.tensor_tensor(
                     xsq[:, c], sx[:, sl * 8 + c],
                     pp[6][:, 0:TT], ALU.subtract),
                 wait=[(sT, mk_bc)] if c == 0 else ())
        p.do("vector", lambda E: E.drain())
        for c in range(8):
            p.do("vector",
                 lambda E, c=c: E.tensor_tensor(y1b[:, c], xsq[:, c],
                                                pp[7][:, 0:TT], ALU.mult))
        mark_use = p.mil("vector", sV)

        m_fevac = {0: None, 1: None}
        for c in range(8):
            par = c % 2
            ppA = pp[0] if par == 0 else pp[2]
            ppB = pp[1] if par == 0 else pp[3]
            w = [(ch_v, mk_v), (sV, m_zB)]
            if m_fevac[par] is not None:
                w.extend(m_fevac[par])
            if c < 2:
                w.append((sT, mk_bc))
                if m_fir_used:
                    w.append((sV, m_fir_used))
                if m_fir_used_s:
                    w.append((sS, m_fir_used_s))
                if m_tile_s is not None:
                    w.append((sS, m_tile_s))
                if mark_ln2 is not None:
                    w.append((sV, mark_ln2))
            p.do("tensor",
                 lambda E, ppA=ppA: E.matmul(ppA[:, 0:512], zcolB[:],
                                             zrowB[:], start=True, stop=True),
                 wait=w)
            p.do("tensor",
                 lambda E, ppB=ppB: E.matmul(ppB[:, 0:512], zcolB[:],
                                             zrowB[:], start=True, stop=True))
            for ti, (k, j, off) in enumerate(TAPS):
                if k in (3, 7):
                    reg = ppA[:, 0:256] if k == 3 else ppA[:, 256:512]
                else:
                    reg = ppB[:, 0:256] if k == 15 else ppB[:, 256:512]
                for q in range(4):
                    p.do("tensor",
                         lambda E, c=c, par=par, ti=ti, off=off, q=q, reg=reg:
                         E.matmul(
                             reg[32 * q:32 * q + 32, :],
                             d32b[32 * q:32 * q + 32, c, ti, :],
                             vst[32 * q:32 * q + 32, c, off:off + TT],
                             start=False, stop=True, skip_group_check=True,
                             tile_position=(32 * q, 32 * q)))
                if ti == 9:
                    mA = p.do("tensor", lambda E: E.drain(), inc=sT)
                    p.do("scalar",
                         lambda E, c=c, ppA=ppA: E.activation(
                             fbuf[3][:, c], ppA[:, 0:256], AF.Identity),
                         wait=[(sT, mA)], inc=sS)
                    m_f7 = p.do("scalar",
                                lambda E, c=c, ppA=ppA: E.activation(
                                    fbuf[7][:, c], ppA[:, 256:512],
                                    AF.Identity),
                                inc=sS)
            mB = p.do("tensor", lambda E: E.drain(), inc=sT)
            p.do("vector",
                 lambda E, c=c, ppB=ppB: E.tensor_copy(
                     fbuf[15][:, c], ppB[:, 0:256]),
                 wait=[(sT, mB)], inc=sV)
            m_f31 = p.do("vector",
                         lambda E, c=c, ppB=ppB: E.tensor_copy(
                             fbuf[31][:, c], ppB[:, 256:512]),
                         inc=sV)
            m_fevac[par] = [(sS, m_f7), (sV, m_f31)]
        m_fir_v = p.mil("vector", sV)
        m_fir_sS = p.mil("scalar", sS)

        branches = [fbuf[3][:, :], fbuf[7][:, :], fbuf[15][:, :],
                    fbuf[31][:, :], vst[:, :, 30:30 + TT], dTt[:, :]]
        m_abs = None
        for pi, br in enumerate(branches):
            for c in range(8):
                w = []
                if c == 0:
                    w = [(sV, m_fir_v), (sS, m_fir_sS), (ch_d, mk_d)]
                    if m_abs is not None:
                        w.append((sT, m_abs))
                p.do("scalar",
                     lambda E, pi=pi, br=br, c=c: E.activation(
                         babs[:, c], br[:, c] if br is not vst else
                         vst[:, c, 30:30 + TT], AF.Abs),
                     wait=w)
            m_b = p.mil("scalar", sS)
            for c in range(8):
                hh = c // 2
                kk = 6 * hh + pi
                w = [(sS, m_b)]
                if pi == 0 and c == 0 and m_st_used is not None:
                    w.append((sV, m_st_used))
                if pi == 0 and c == 0 and m_tile_s is not None:
                    w.append((sS, m_tile_s))
                p.do("tensor",
                     lambda E, c=c, kk=kk, pi=pi: E.matmul(
                         pp[3][0:24, 0:256], sstp[:, 23 - kk:47 - kk],
                         babs[:, c], start=(pi == 0 and c == 0),
                         stop=(pi == 5 and c == 7)),
                     wait=w, inc=(sT if c == 7 else None))
            m_abs = p.mark(sT)
        mm_st = p.mark(sT)
        m_stats_ev = p.do(
            "scalar",
            lambda E: E.activation(stats24[:], pp[3][0:24, 0:256],
                                   AF.Identity),
            wait=[(sT, mm_st)], inc=sS)

        for jc in range(16):
            bank = pp[jc % 4][:, 0:256]
            for d in range(9):
                w = []
                if d == 0:
                    w = [(sV, mark_use), (sS, m_stats_ev)]
                    if m_gbank[jc % 4] is not None:
                        w.append((sS, m_gbank[jc % 4]))
                if d < 8:
                    p.do("tensor",
                         lambda E, jc=jc, d=d, bank=bank: E.matmul(
                             bank, g1w[:, d, jc * 128:(jc + 1) * 128],
                             y1b[:, d], start=(d == 0), stop=False),
                         wait=w)
                else:
                    p.do("tensor",
                         lambda E, jc=jc, bank=bank: E.matmul(
                             bank, g1s[:, jc * 128:(jc + 1) * 128],
                             stats24[:], start=False, stop=True),
                         inc=sT)
            mm = p.mark(sT)
            m_gbank[jc % 4] = p.do(
                "scalar",
                lambda E, jc=jc, bank=bank: E.activation(
                    h1b[:, jc], bank, AF.Gelu, bias=sgb1[:, jc:jc + 1]),
                wait=[(sT, mm)], inc=sS)

        for jc in range(16):
            w = [(sS, m_gbank[jc % 4])] if jc >= 12 else []
            if jc == 0 and m_lgt_used is not None:
                w = [(sV, m_lgt_used)]
                if m_tile_s is not None:
                    w.append((sS, m_tile_s))
            p.do("tensor",
                 lambda E, jc=jc: E.matmul(
                     pp[4][0:24, 0:256], g2w[:, jc, :], h1b[:, jc],
                     start=(jc == 0), stop=(jc == 15)),
                 wait=w, inc=(sT if jc == 15 else None))
        mm_lg = p.mark(sT)
        p.do("vector",
             lambda E: E.scalar_tensor_tensor(
                 lgt[:], stats24[:], salst[:], pp[4][0:24, 0:256],
                 ALU.mult, ALU.add),
             wait=[(sT, mm_lg), (sV, m_temp)])
        p.do("vector", lambda E: E.drain())
        p.do("vector",
             lambda E: E.tensor_scalar(lgt[:], lgt[:], sgb2[:], tinv24[:],
                                       ALU.add, ALU.mult))
        m_lgt = p.mil("vector", sV)
        m_lgt_used = m_lgt

        trb = [pp[2], pp[3]]
        for b2 in range(2):
            w = [(sV, m_lgt)]
            if b2 == 0:
                w.append((sS, m_stats_ev))
                if m_prb_out is not None:
                    w.append((ch_p1, m_prb_out))
            mm = p.do("tensor",
                      lambda E, b2=b2: E.transpose(
                          trb[b2][:, 0:24],
                          lgt[:, b2 * 128:(b2 + 1) * 128],
                          C["c_idf32"][0:24, 0:24]),
                      wait=w, inc=sT)
            p.do("scalar",
                 lambda E, b2=b2: E.activation(prb[:, b2], trb[b2][:, 0:24],
                                               AF.Identity),
                 wait=[(sT, mm)], inc=(sS if b2 == 1 else None))
        m_tr = p.mark(sS)
        for b2 in range(2):
            for hh in range(4):
                p.do("vector",
                     lambda E, b2=b2, hh=hh: E.reduce_max(
                         mxs[:, b2, hh:hh + 1],
                         prb[:, b2, 6 * hh:6 * hh + 6], axis=AX.X),
                     wait=[(sS, m_tr)] if (b2 == 0 and hh == 0) else ())
        p.do("vector", lambda E: E.drain())
        for b2 in range(2):
            for hh in range(4):
                p.do("vector",
                     lambda E, b2=b2, hh=hh: E.tensor_scalar_mul(
                         mxs[:, b2, 4 + hh:5 + hh], mxs[:, b2, hh:hh + 1],
                         -1.0))
        m_mx = p.mil("vector", sV)
        for b2 in range(2):
            for hh in range(4):
                p.do("scalar",
                     lambda E, b2=b2, hh=hh: E.activation(
                         pex[:, b2, 6 * hh:6 * hh + 6],
                         prb[:, b2, 6 * hh:6 * hh + 6], AF.Exp,
                         bias=mxs[:, b2, 4 + hh:5 + hh]),
                     wait=[(sV, m_mx)] if (b2 == 0 and hh == 0) else ())
        m_ex = p.mil("scalar", sS)
        for b2 in range(2):
            for hh in range(4):
                p.do("vector",
                     lambda E, b2=b2, hh=hh: E.reduce_sum(
                         mxs[:, b2, hh:hh + 1],
                         pex[:, b2, 6 * hh:6 * hh + 6], axis=AX.X),
                     wait=[(sS, m_ex)] if (b2 == 0 and hh == 0) else ())
        p.do("vector", lambda E: E.drain())
        p.do("vector",
             lambda E: E.reciprocal(mxs[:, :, 4:8], mxs[:, :, 0:4]))
        p.do("vector", lambda E: E.drain())
        p.do("vector",
             lambda E: E.tensor_scalar_mul(
                 mxs[:, :, 8:12], mxs[:, :, 4:8], 0.95))
        p.do("vector", lambda E: E.drain())
        for b2 in range(2):
            for hh in range(4):
                p.do("vector",
                     lambda E, b2=b2, hh=hh: E.tensor_scalar_mul(
                         prb[:, b2, 6 * hh:6 * hh + 6],
                         pex[:, b2, 6 * hh:6 * hh + 6],
                         mxs[:, b2, 8 + hh:9 + hh]))
        p.do("vector", lambda E: E.drain())
        for b2 in range(2):
            for hh in range(4):
                p.do("vector",
                     lambda E, b2=b2, hh=hh: E.tensor_scalar_add(
                         prb[:, b2, 6 * hh + 5:6 * hh + 6],
                         prb[:, b2, 6 * hh + 5:6 * hh + 6], 0.05))
        m_prb = p.mil("vector", sV)
        mk1 = p.dma("sync", prb_scr, prb[:], ch_p1, wait=[(sV, m_prb)])
        w = [(ch_p1, mk1)]
        if m_prbR_used is not None:
            w.append((sV, m_prbR_used))
        m_prbR = p.dma("gpsimd", probsR[:],
                       prb_scr.rearrange("p b h -> b p h").rearrange(
                           "(o b) p h -> o b p h", o=1), ch_p2, wait=w)
        m_prb_out = mk1

        bsl = [pp[4][:, 0:256], pp[5][:, 0:256], pp[2][:, 0:256],
               pp[4][:, 256:512], pp[5][:, 256:512], pp[2][:, 256:512]]
        m_mix_marks = []
        for hh in range(4):
            for pi in range(6):
                kk = 6 * hh + pi
                w = [(ch_p2, m_prbR)]
                if hh == 0 and pi == 0 and m_lgt_used is not None:
                    w.append((sV, m_lgt_used))
                if m_bcb_used[pi] is not None:
                    w.append((sV, m_bcb_used[pi]))
                if pi == 0 and m_rms_used is not None:
                    w.append((sV, m_rms_used))
                if pi == 0 and m_tile_s is not None:
                    w.append((sS, m_tile_s))
                mm = p.do("tensor",
                          lambda E, kk=kk, pi=pi: E.matmul(
                              bsl[pi], sorow[:], probsR[0:1, :, :, kk],
                              start=True, stop=True),
                          wait=w, inc=sT)
                p.do("scalar",
                     lambda E, pi=pi, mmx=mm: E.activation(
                         bcb[:, pi], bsl[pi], AF.Identity),
                     wait=[(sT, mm)],
                     inc=(sS if pi == 5 else None))
            m_bc6 = p.mark(sS)
            for i in range(2):
                c = 2 * hh + i
                for pi, br in enumerate(branches):
                    w = []
                    if pi == 0 and i == 0:
                        w = [(sS, m_bc6), (sV, m_fir_v), (sS, m_fir_sS)]
                        if m_oacc_used is not None:
                            w.append((sV, m_oacc_used))
                    p.do("vector",
                         lambda E, c=c, i=i, pi=pi, br=br: E.tensor_tensor(
                             mixt[:, 6 * i + pi],
                             br[:, c] if br is not vst else
                             vst[:, c, 30:30 + TT],
                             bcb[:, pi], ALU.mult),
                         wait=w)
            p.do("vector", lambda E: E.drain())
            for i in range(2):
                base = 6 * i
                p.do("vector",
                     lambda E, base=base, hh=hh, i=i: E.tensor_tensor(
                         mixt[:, base + 0], mixt[:, base + 0],
                         mixt[:, base + 1], ALU.add))
                p.do("vector",
                     lambda E, base=base: E.tensor_tensor(
                         mixt[:, base + 2], mixt[:, base + 2],
                         mixt[:, base + 3], ALU.add))
                p.do("vector",
                     lambda E, base=base: E.tensor_tensor(
                         mixt[:, base + 4], mixt[:, base + 4],
                         mixt[:, base + 5], ALU.add))
            p.do("vector", lambda E: E.drain())
            for i in range(2):
                base = 6 * i
                p.do("vector",
                     lambda E, base=base: E.tensor_tensor(
                         mixt[:, base + 0], mixt[:, base + 0],
                         mixt[:, base + 2], ALU.add))
            p.do("vector", lambda E: E.drain())
            for i in range(2):
                base = 6 * i
                c = 2 * hh + i
                p.do("vector",
                     lambda E, base=base, c=c: E.tensor_tensor(
                         o_acc[:, c], mixt[:, base + 0], mixt[:, base + 4],
                         ALU.add))
            mk = p.mil("vector", sV)
            for pi in range(6):
                m_bcb_used[pi] = mk
        m_mix = p.mil("vector", sV)
        m_fir_used = m_mix
        m_prbR_used = m_mix

        for c in range(8):
            w = [(sV, m_mix)] if c == 0 else []
            p.do("scalar",
                 lambda E, c=c: E.activation(xsq2[:, c], o_acc[:, c],
                                             AF.Square), wait=w)
        m_osq = p.mil("scalar", sS)
        rsl = [pp[3][0:1, 0:256], pp[4][0:1, 0:256], pp[3][0:1, 256:512],
               pp[4][0:1, 256:512]]
        m_rev = [None] * 4
        for hh in range(4):
            bank = rsl[hh]
            for i in range(2):
                w = []
                if i == 0:
                    w = [(sS, m_osq)]
                    if m_rev[(hh + 2) % 4] is not None:
                        w.append((sS, m_rev[(hh + 2) % 4]))
                p.do("tensor",
                     lambda E, hh=hh, i=i, bank=bank: E.matmul(
                         bank, C["c_ones_col"][:], xsq2[:, 2 * hh + i],
                         start=(i == 0), stop=(i == 1)),
                     wait=w, inc=(sT if i == 1 else None))
            mm = p.mark(sT)
            m_rev[hh] = p.do(
                "scalar",
                lambda E, hh=hh, bank=bank: E.activation(nr2[:, hh], bank,
                                                         AF.Identity),
                wait=[(sT, mm)], inc=sS)
        m_r0 = p.mil("scalar", sS)
        p.do("scalar",
             lambda E: E.activation(
                 nr2[:, 4:8, :].rearrange("o a t -> o (a t)"),
                 nr2[:, 0:4, :].rearrange("o a t -> o (a t)"),
                 AF.Sqrt, scale=1.0 / 256, bias=C["c_eps"][0:1, 0:1]),
             wait=[(sS, m_r0)])
        m_r1 = p.mil("scalar", sS)
        p.do("vector",
             lambda E: E.reciprocal(
                 nr2[:, 0:4, :].rearrange("o a t -> o (a t)"),
                 nr2[:, 4:8, :].rearrange("o a t -> o (a t)")),
             wait=[(sS, m_r1)])
        m_rrow = p.mil("vector", sV)
        for hh in range(4):
            w = [(sV, m_rrow)]
            if hh > 0:
                w.append((sV, m_rms_used))
            mm = p.do("tensor",
                      lambda E, hh=hh: E.matmul(
                          pp[5][:, 0:256], C["c_ones_row"][:],
                          nr2[0:1, hh], start=True, stop=True),
                      wait=w, inc=sT)
            for i in range(2):
                c = 2 * hh + i
                p.do("vector",
                     lambda E, c=c: E.tensor_tensor(o_acc[:, c], o_acc[:, c],
                                                    pp[5][:, 0:256],
                                                    ALU.mult),
                     wait=[(sT, mm)] if i == 0 else ())
            m_rms_used = p.mil("vector", sV)
        p.do("vector", lambda E: E.drain())
        for c in range(8):
            w = []
            if c == 0 and m_ob_used is not None:
                w = [(sT, m_ob_used)]
            p.do("vector",
                 lambda E, c=c: E.tensor_scalar_mul(
                     ob[:, c], o_acc[:, c], sonw[:, c % 2:c % 2 + 1]),
                 wait=w)
        m_obv = p.mil("vector", sV)
        m_oacc_used = m_obv

        for oc in range(8):
            bank = pp[oc % 2][:, 256:512]
            for c in range(8):
                w = []
                if c == 0:
                    w = [(sV, m_obv)]
                    if m_bank01[oc % 2] is not None:
                        w.append((sS, m_bank01[oc % 2]))
                    if m_bank01[(oc + 1) % 2] is not None:
                        w.append((sS, m_bank01[(oc + 1) % 2]))
                    if oc == 0:
                        for mg in m_gbank:
                            if mg is not None:
                                w.append((sS, mg))
                p.do("tensor",
                     lambda E, oc=oc, c=c, bank=bank: E.matmul(
                         bank, opw[:, c, oc * 128:(oc + 1) * 128],
                         ob[:, c], start=(c == 0), stop=(c == 7)),
                     wait=w, inc=(sT if c == 7 else None))
            mm = p.mark(sT)
            wev = [(sT, mm)]
            if oc == 0 and mark_store is not None:
                wev.append((ch_o, mark_store))
            m_bank01[oc % 2] = p.do(
                "scalar",
                lambda E, oc=oc, bank=bank: E.activation(
                    fo[:, oc], bank, AF.Identity),
                wait=wev, inc=sS)

        if t + 1 < NTILE:
            nsl = (t + 1) % 2
            t1 = (t + 1) * TT
            xin_n = xin[:, t1:t1 + TT].rearrange("(c p) t -> p c t", p=128)
            w = ([(sV, mark_use_hist[nsl])]
                 if mark_use_hist[nsl] is not None else [(sV, mark_use)])
            mk_xn = p.dma("sync", sx[:, nsl * 8:nsl * 8 + 8], xin_n,
                          ch_x[nsl], wait=w)
            wr = [(sT, mkst1), (sT, p.mark(sT)), (sS, m_bank01[0]),
                  (sS, m_bank01[1]), (sV, mark_use)]
            mkst1 = emit_ln_stats_pre(
                g, TT, lambda c, s=nsl: sx[:, s * 8 + c], xsq2,
                C["c_ones16"], pp[2][0:1, 256:512], pp[3][0:1, 256:512],
                src_ready=[(ch_x[nsl], mk_xn), (sT, mkst1)],
                war_rows=wr)

        mk_bc2 = emit_ln_stats(
            g, TT, lambda c: fo[:, c], xsq, rows, C["c_ones16"],
            C["c_ones_row"], C["c_eps"], pp[6][:, 256:512], pp[7][:, 256:512],
            pp[6][:, 0:256], pp[7][:, 0:256],
            src_ready=[(sS, m_bank01[0]), (sS, m_bank01[1])],
            war_rows=[(sV, mark_use)],
            war_bc=[(sV, mark_use)])
        for c in range(8):
            w = [(sT, mk_bc2)] if c == 0 else []
            if c == 0 and mark_store is not None:
                w.append((ch_o, mark_store))
            p.do("vector",
                 lambda E, c=c: E.tensor_tensor(xsq[:, c], fo[:, c],
                                                pp[6][:, 0:TT], ALU.subtract),
                 wait=w)
        p.do("vector", lambda E: E.drain())
        for c in range(8):
            p.do("vector",
                 lambda E, c=c: E.tensor_tensor(xsq[:, c], xsq[:, c],
                                                pp[7][:, 0:TT], ALU.mult))
        p.do("vector", lambda E: E.drain())
        for c in range(8):
            p.do("vector",
                 lambda E, c=c, sl=sl: E.tensor_tensor(
                     fo[:, c], sx[:, sl * 8 + c], xsq[:, c], ALU.add))
        mark_ln2 = p.mil("vector", sV)
        mark_store = p.dma("sync", xout_sl, fo[:], ch_o,
                           wait=[(sV, mark_ln2)])
        if t + 1 < NTILE:
            mk_bc_next = emit_ln_stats_post(
                g, TT, rows2, C["c_ones_row"], C["c_eps"],
                pp[2][0:1, 256:512], pp[3][0:1, 256:512],
                pp[6][:, 0:256], pp[7][:, 0:256], mkst1,
                war_bc=[(sV, mark_ln2)])
        mark_use = mark_ln2
        mark_use_hist[sl] = mark_ln2
        m_st_used = m_lgt
        m_fir_used = m_mix
        m_fir_used_s = m_stats_ev
        m_tile_s = p.mil("scalar", sS)
        m_fir_pe = p.mark(sT)
        m_ob_used = p.mark(sT)
    g.barrier()


def emit_l1attn(g, xin, xout, C):
    nc, p = g.nc, g.p
    sT, sV, sS = g.sT, g.sV, g.sS
    NH = 16
    TB = 256
    NBT = T // TB
    TQ = 512
    NQT = T // TQ

    wq = g.din("l1a_wqT", [D, D])
    wk = g.din("l1a_wkT", [D, D])
    wv = g.din("l1a_wvT", [D, D])
    wo = g.din("l1a_woT", [D, D])
    ropeC = g.din("l1a_ropeC", [128, T])
    ropeS = g.din("l1a_ropeS", [128, T])
    masks = g.din("l1a_masks", [128, 4, TQ], BF16)
    qn_pat = g.din("l1a_qnpat", [128, 30])
    mh_scr = g.dint("l1a_mhscr", [16, TQ])
    khat_d = g.dint("l1a_khatd", [65, NH, T], BF16)

    pp = g.pool
    SHUF = [1, 0, 3, 2, 5, 4, 7, 6, 9, 8, 11, 10, 13, 12, 15, 14,
            17, 16, 19, 18, 21, 20, 23, 22, 25, 24, 27, 26, 29, 28, 31, 30]

    vtm = g.sb("l1a_vtm", [128, 16, NH * 65], BF16)
    sqnp = g.sb("l1a_qnp", [128, 30])
    mk2n = g.sb("l1a_mk2n", [16, 4])
    p.dma("sync", sqnp[:], qn_pat, g.dSY)

    es1 = ExitStack()
    sC = es1.enter_context(nc.sbuf_tensor("l1a_C", [128, T], F32))
    sS_ = es1.enter_context(nc.sbuf_tensor("l1a_S", [128, T], F32))
    p.dma("sync", sC[:], ropeC, g.dSY)
    p.dma("sync", sS_[:], ropeS, g.dSY)
    wkb = es1.enter_context(nc.sbuf_tensor("l1a_wk", [128, 8, D], BF16))
    wvb = es1.enter_context(nc.sbuf_tensor("l1a_wv", [128, 8, D], BF16))
    sx = es1.enter_context(nc.sbuf_tensor("l1a_sx", [128, 16, TB], F32))
    xsq = es1.enter_context(nc.sbuf_tensor("l1a_xsq", [128, 8, TB], F32))
    xq1 = es1.enter_context(nc.sbuf_tensor("l1a_xq1", [128, 8, TB], F32))
    rows2 = es1.enter_context(nc.sbuf_tensor("l1a_rowsB", [1, 5, TB], F32))
    y3t = es1.enter_context(nc.sbuf_tensor("l1a_y3t", [128, 8, TB], BF16))
    ktile = es1.enter_context(nc.sbuf_tensor("l1a_ktile", [128, 8, TB], F32))
    ksq = es1.enter_context(nc.sbuf_tensor("l1a_ksq", [128, TB], F32))
    knm = es1.enter_context(nc.sbuf_tensor("l1a_knm", [16, 10], F32))
    k65 = es1.enter_context(nc.sbuf_tensor("l1a_k65", [65, NH, TB], BF16))
    if True:
        for c in range(8):
            p.dma("gpsimd", wkb[:, c], wk[c * 128:(c + 1) * 128, :], g.dGP)
            p.dma("gpsimd", wvb[:, c], wv[c * 128:(c + 1) * 128, :], g.dGP)
        g.barrier()
        p.do("vector", lambda E: E.memset(k65[64:65, :, :], 1.0))
        p.do("vector", lambda E: E.memset(vtm[:], 1.0))
        m_ones = p.mil("vector", sV)

        ch_x = [p.channel("l1a_chx0"), p.channel("l1a_chx1")]
        ch_kd = p.channel("l1a_chkd")
        mark_use = None
        mark_use_hist = [None, None]
        m_kevac = None
        m_vevac = None
        m_knred = None
        m_krot = None
        m_k65c = None
        xin_0 = xin[:, 0:TB].rearrange("(c p) t -> p c t", p=128)
        mk_x0 = p.dma("sync", sx[:, 0:8], xin_0, ch_x[0])
        mkst = emit_ln_stats_pre(
            g, TB, lambda c: sx[:, c], xq1, C["c_ones16"], pp[5], pp[6],
            src_ready=[(ch_x[0], mk_x0)])
        mk_bc_next = emit_ln_stats_post(
            g, TB, rows2, C["c_ones_row"], C["c_eps"], pp[5], pp[6],
            pp[7], pp[4], mkst)
        m_vmm = None
        for tb in range(NBT):
            t0 = tb * TB
            sl = tb % 2
            mk_bc = mk_bc_next
            for c in range(8):
                w = []
                if c == 0:
                    w = [(sT, mk_bc)]
                    if m_vmm is not None:
                        w.append((sT, m_vmm))
                p.do("vector",
                     lambda E, c=c, sl=sl: E.tensor_tensor(
                         xsq[:, c], sx[:, sl * 8 + c],
                         pp[7][:, 0:TB], ALU.subtract),
                     wait=w)
            p.do("vector", lambda E: E.drain())
            for c in range(8):
                p.do("vector",
                     lambda E, c=c: E.tensor_tensor(y3t[:, c], xsq[:, c],
                                                    pp[4][:, 0:TB], ALU.mult))
            mark_use = p.mil("vector", sV)
            mark_use_hist[sl] = mark_use
            if tb + 1 < NBT:
                nsl = (tb + 1) % 2
                t1 = (tb + 1) * TB
                xin_n = xin[:, t1:t1 + TB].rearrange("(c p) t -> p c t",
                                                     p=128)
                w = ([(sV, mark_use_hist[nsl])]
                     if mark_use_hist[nsl] is not None else [])
                mk_xn = p.dma("sync", sx[:, nsl * 8:nsl * 8 + 8], xin_n,
                              ch_x[nsl], wait=w)
                mkst = emit_ln_stats_pre(
                    g, TB, lambda c, s=nsl: sx[:, s * 8 + c], xq1,
                    C["c_ones16"], pp[5], pp[6],
                    src_ready=[(ch_x[nsl], mk_xn), (sT, mkst)],
                    war_rows=[(sT, mkst), (sV, mark_use)])

            for c in range(8):
                bank = pp[c % 2]
                for d in range(8):
                    w = []
                    if d == 0:
                        w = [(sV, mark_use)]
                        if m_kevac is not None and c < 2:
                            w.append((sS, m_kevac))
                        if m_krot is not None and c == 0:
                            w.append((sV, m_krot))
                    p.do("tensor",
                         lambda E, c=c, d=d, bank=bank: E.matmul(
                             bank[:, 0:TB],
                             wkb[:, d, c * 128:(c + 1) * 128],
                             y3t[:, d], start=(d == 0), stop=(d == 7)),
                         wait=w, inc=(sT if d == 7 else None))
                mm = p.mark(sT)
                p.do("scalar",
                     lambda E, c=c, bank=bank: E.activation(
                         ktile[:, c], bank[:, 0:TB], AF.Identity, scale=0.125),
                     wait=[(sT, mm)])
                p.do("scalar",
                     lambda E, bank=bank: E.activation(ksq[:], bank[:, 0:TB],
                                                       AF.Square))
                m_kevac = p.mil("scalar", sS)
                p.do("tensor",
                     lambda E, c=c: E.matmul(
                         pp[4][0:16, 0:TB],
                         sqnp[:, 14 - 2 * c:30 - 2 * c], ksq[:],
                         start=(c == 0), stop=(c == 7)),
                     wait=[(sS, m_kevac)],
                     inc=(sT if c == 7 else None))
            mm_kn = p.mark(sT)
            p.do("vector",
                 lambda E, tb=tb: E.reduce_max(knm[:, tb:tb + 1],
                                               pp[4][0:16, 0:TB], axis=AX.X),
                 wait=[(sT, mm_kn)])
            m_knred = p.mil("vector", sV)
            if tb + 1 < NBT:
                mk_bc_next = emit_ln_stats_post(
                    g, TB, rows2, C["c_ones_row"], C["c_eps"], pp[5], pp[6],
                    pp[7], pp[4], mkst,
                    war_bc=[(sV, m_knred), (sV, mark_use)])

            for c in range(8):
                p.do("vector",
                     lambda E, c=c, t0=t0: E.stream_shuffle(xsq[:, c],
                                                            ktile[:, c], SHUF),
                     wait=[(sS, m_kevac)] if c == 0 else ())
            p.do("vector", lambda E: E.drain())
            for c in range(8):
                p.do("vector",
                     lambda E, c=c, t0=t0: E.tensor_tensor(
                         ktile[:, c], ktile[:, c], sC[:, t0:t0 + TB],
                         ALU.mult))
            for c in range(8):
                p.do("vector",
                     lambda E, c=c, t0=t0: E.tensor_tensor(
                         xsq[:, c], xsq[:, c], sS_[:, t0:t0 + TB], ALU.mult))
            p.do("vector", lambda E: E.drain())
            for c in range(8):
                w = [(ch_kd, m_k65c)] if (c == 0 and m_k65c) else []
                p.do("vector",
                     lambda E, c=c: E.tensor_tensor(ktile[:, c], ktile[:, c],
                                                    xsq[:, c], ALU.add),
                     wait=w)
            m_krot = p.mil("vector", sV)
            for c in range(8):
                p.do("scalar",
                     lambda E, c=c: E.activation(k65[0:64, 2 * c, :],
                                                 ktile[0:64, c], AF.Identity),
                     wait=[(sV, m_krot)] if c == 0 else ())
                p.do("scalar",
                     lambda E, c=c: E.activation(k65[0:64, 2 * c + 1, :],
                                                 ktile[64:128, c], AF.Identity))
            m_k65 = p.mil("scalar", sS)
            m_k65c = p.dma("sync", khat_d[:, :, t0:t0 + TB], k65[:], ch_kd,
                           wait=[(sS, m_k65), (sV, m_ones)])

            for i in range(2):
                for half in range(2):
                    bank = pp[2 + half]
                    for d in range(8):
                        w = []
                        if d == 0:
                            w = [(sV, mark_use)]
                            if m_vevac is not None:
                                w.append((sS, m_vevac))
                        p.do("tensor",
                             lambda E, i=i, half=half, d=d, bank=bank:
                             E.matmul(
                                 bank[:, 0:512],
                                 y3t[:, d, i * 128:(i + 1) * 128],
                                 wvb[:, d, half * 512:(half + 1) * 512],
                                 start=(d == 0), stop=(d == 7)),
                             wait=w, inc=(sT if d == 7 else None))
                    mm = p.mark(sT)
                    m_vmm = mm
                    tc = tb * 2 + i
                    p.do("scalar",
                         lambda E, tc=tc, half=half, bank=bank: E.activation(
                             vtm[:, tc].rearrange("p (h x) -> p h x", x=65)
                             [:, half * 8:(half + 1) * 8, 0:64],
                             bank[:, 0:512], AF.Identity),
                         wait=[(sT, mm), (sV, m_ones)])
                    m_vevac = p.mil("scalar", sS)

        p.do("vector", lambda E: E.reduce_max(mk2n[:, 0:1], knm[:, 0:NBT],
                                              axis=AX.X),
             wait=[(sV, m_knred)])
        p.do("vector", lambda E: E.drain())
        p.do("vector", lambda E: E.tensor_scalar_mul(mk2n[:, 1:2],
                                                     mk2n[:, 0:1], 1.0 / 64))
        m_mk2 = p.mil("vector", sV)
        g.barrier()
        es1.close()

    es2 = ExitStack()
    wqb = es2.enter_context(nc.sbuf_tensor("l1a_wq", [128, 8, D], BF16))
    wob = es2.enter_context(nc.sbuf_tensor("l1a_wo", [128, 8, D], BF16))
    sx2 = es2.enter_context(nc.sbuf_tensor("l1a_sx2", [128, 8, TQ], F32))
    xq2 = es2.enter_context(nc.sbuf_tensor("l1a_xsq2", [128, 8, TQ], F32))
    rw2 = es2.enter_context(nc.sbuf_tensor("l1a_rows2", [1, 6, TQ], F32))
    y3q = es2.enter_context(nc.sbuf_tensor("l1a_y3q", [128, 8, TQ], BF16))
    qt = es2.enter_context(nc.sbuf_tensor("l1a_qt", [128, 8, TQ], F32))
    qaug = es2.enter_context(nc.sbuf_tensor("l1a_qaug", [65, NH, TQ], BF16))
    qsq = es2.enter_context(nc.sbuf_tensor("l1a_qsq", [128, TQ], F32))
    mrow = es2.enter_context(nc.sbuf_tensor("l1a_mrow", [16, 2, TQ], F32))
    khbuf = es2.enter_context(nc.sbuf_tensor("l1a_kh", [65, 2, T], BF16))
    pt0 = es2.enter_context(nc.sbuf_tensor("l1a_pt0", [128, TQ], BF16))
    pt1 = es2.enter_context(nc.sbuf_tensor("l1a_pt1", [128, TQ], BF16))
    smask = es2.enter_context(nc.sbuf_tensor("l1a_smask", [128, 4, TQ], BF16))
    numb = es2.enter_context(nc.sbuf_tensor("l1a_numb", [64, 2, TQ], F32))
    linv = es2.enter_context(nc.sbuf_tensor("l1a_linv", [1, 2, TQ], F32))
    aoutT = es2.enter_context(nc.sbuf_tensor("l1a_aoutT", [128, 8, TQ], BF16))
    fo = es2.enter_context(nc.sbuf_tensor("l1a_fo", [128, 8, 256], F32))
    tmp2 = es2.enter_context(nc.sbuf_tensor("l1a_tmp2", [128, 8, 256], F32))
    tbl = es2.enter_context(nc.sbuf_tensor("l1a_tbl", [128, 2, TQ], F32))
    if True:
        ptb = [pt0, pt1]
        for c in range(8):
            p.dma("gpsimd", wqb[:, c], wq[c * 128:(c + 1) * 128, :], g.dGP)
            p.dma("gpsimd", wob[:, c], wo[c * 128:(c + 1) * 128, :], g.dGP)
        p.dma("sync", smask[:], masks, g.dSY)
        g.barrier()

        ch_x2 = p.channel("l1a_chx2")
        ch_tb = p.channel("l1a_chtb")
        ch_mh1 = p.channel("l1a_chmh1")
        ch_mh2 = p.channel("l1a_chmh2")
        ch_o = p.channel("l1a_cho")
        ch_kh = [p.channel("l1a_chkh0"), p.channel("l1a_chkh1")]
        kh_war = [None, None]
        kh_mk = [None, None]
        mark_store = None
        mark_ln2 = None
        m_qevac = None
        m_qrot = None
        psA_war = [None, None]
        pt_war = [None, None]
        psO_war = {0: None, 1: None}
        epi_war = [None, None]
        m_oevac = [None, None]
        m_mrow_out = None

        for tq in range(NQT):
            q0 = tq * TQ
            kc_max = (tq + 1) * (TQ // 128)
            xin_sl = xin[:, q0:q0 + TQ].rearrange("(c p) t -> p c t", p=128)
            w = [(sV, mark_ln2)] if mark_ln2 else []
            mk_x = p.dma("sync", sx2[:], xin_sl, ch_x2, wait=w)
            mk_bc = emit_ln_stats(
                g, TQ, lambda c: sx2[:, c], xq2, rw2, C["c_ones16"],
                C["c_ones_row"], C["c_eps"], pp[6], pp[7], pp[4], pp[5],
                src_ready=[(ch_x2, mk_x)],
                war_rows=[(sS, m_oevac[1])] if m_oevac[1] else (),
                war_bc=[(sV, mark_ln2)] if mark_ln2 else ())
            for c in range(8):
                p.do("vector",
                     lambda E, c=c: E.tensor_tensor(xq2[:, c], sx2[:, c],
                                                    pp[4][:, 0:TQ],
                                                    ALU.subtract),
                     wait=[(sT, mk_bc)] if c == 0 else ())
            p.do("vector", lambda E: E.drain())
            for c in range(8):
                p.do("vector",
                     lambda E, c=c: E.tensor_tensor(y3q[:, c], xq2[:, c],
                                                    pp[5][:, 0:TQ], ALU.mult))
            m_y3 = p.mil("vector", sV)

            for c in range(8):
                bank = pp[c % 2]
                for d in range(8):
                    w = []
                    if d == 0:
                        w = [(sV, m_y3)]
                        if m_qevac is not None and c < 2:
                            w.append((sS, m_qevac))
                        if psA_war[c % 2] is not None:
                            w.append((sS, psA_war[c % 2]))
                        if c == 0 and m_qrot is not None:
                            w.append((sV, m_qrot))
                    p.do("tensor",
                         lambda E, c=c, d=d, bank=bank: E.matmul(
                             bank[:, 0:TQ],
                             wqb[:, d, c * 128:(c + 1) * 128],
                             y3q[:, d], start=(d == 0), stop=(d == 7)),
                         wait=w, inc=(sT if d == 7 else None))
                mm = p.mark(sT)
                p.do("scalar",
                     lambda E, c=c, bank=bank: E.activation(
                         qt[:, c], bank[:, 0:TQ], AF.Identity),
                     wait=[(sT, mm)])
                p.do("scalar",
                     lambda E, bank=bank: E.activation(qsq[:], bank[:, 0:TQ],
                                                       AF.Square))
                m_qevac = p.mil("scalar", sS)
                p.do("tensor",
                     lambda E, c=c: E.matmul(
                         pp[5][0:16, 0:TQ],
                         sqnp[:, 14 - 2 * c:30 - 2 * c], qsq[:],
                         start=(c == 0), stop=(c == 7)),
                     wait=[(sS, m_qevac)] + (
                         [(sV, mark_ln2)]
                         if (c == 0 and mark_ln2 is not None) else []),
                     inc=(sT if c == 7 else None))
            mm_qn = p.mark(sT)

            p.do("vector",
                 lambda E: E.tensor_scalar_mul(mrow[:, 0], pp[5][0:16, 0:TQ],
                                               mk2n[:, 1:2]),
                 wait=[(sT, mm_qn), (sV, m_mk2)])
            m_t = p.mil("vector", sV)
            p.do("scalar", lambda E: E.activation(mrow[:, 1], mrow[:, 0],
                                                  AF.Sqrt),
                 wait=[(sV, m_t)])
            m_t2 = p.mil("scalar", sS)
            p.do("vector", lambda E: E.tensor_scalar_mul(mrow[:, 0],
                                                         mrow[:, 1], -1.0),
                 wait=[(sS, m_t2)])
            m_t3 = p.mil("vector", sV)
            w = [(sV, m_t3)]
            if m_mrow_out is not None:
                w.append((ch_mh2, m_mrow_out))
            mk1 = p.dma("sync", mh_scr, mrow[:, 0], ch_mh1, wait=w)
            m_mrow_out = p.dma("gpsimd", qaug[64:65, :, :],
                               mh_scr.rearrange("(o h) t -> o h t", o=1), ch_mh2,
                               wait=[(ch_mh1, mk1)])

            w = [(sV, m_qrot)] if m_qrot else []
            p.dma("sync", tbl[:, 0], ropeC[:, q0:q0 + TQ], ch_tb, wait=w)
            mk_tb = p.dma("sync", tbl[:, 1], ropeS[:, q0:q0 + TQ], ch_tb)
            for c in range(8):
                p.do("vector",
                     lambda E, c=c: E.stream_shuffle(xq2[:, c], qt[:, c],
                                                     SHUF),
                     wait=[(sS, m_qevac)] if c == 0 else ())
            p.do("vector", lambda E: E.drain())
            for c in range(8):
                p.do("vector",
                     lambda E, c=c: E.tensor_tensor(qt[:, c], qt[:, c],
                                                    tbl[:, 0], ALU.mult),
                     wait=[(ch_tb, mk_tb)] if c == 0 else ())
            for c in range(8):
                p.do("vector",
                     lambda E, c=c: E.tensor_tensor(xq2[:, c], xq2[:, c],
                                                    tbl[:, 1], ALU.mult))
            p.do("vector", lambda E: E.drain())
            for c in range(8):
                p.do("vector",
                     lambda E, c=c: E.tensor_tensor(qt[:, c], qt[:, c],
                                                    xq2[:, c], ALU.add))
            m_qrot = p.mil("vector", sV)
            for c in range(8):
                p.do("scalar",
                     lambda E, c=c: E.activation(qaug[0:64, 2 * c, :],
                                                 qt[0:64, c], AF.Identity),
                     wait=[(sV, m_qrot)] if c == 0 else ())
                p.do("scalar",
                     lambda E, c=c: E.activation(qaug[0:64, 2 * c + 1, :],
                                                 qt[64:128, c], AF.Identity))
            m_qdist = p.mil("scalar", sS)

            nkb = kc_max * 128
            w = [(sT, kh_war[0])] if kh_war[0] else []
            kh_mk[0] = p.dma("sync", khbuf[:, 0, 0:nkb],
                             khat_d[:, 0, 0:nkb], ch_kh[0], wait=w)

            pending_epi = None
            for h in range(NH):
                po = pp[2] if h % 2 == 0 else pp[4]
                if h + 1 < NH:
                    slot = (h + 1) % 2
                    w = [(sT, kh_war[slot])] if kh_war[slot] else []
                    kh_mk[slot] = p.dma("sync", khbuf[:, slot, 0:nkb],
                                        khat_d[:, h + 1, 0:nkb],
                                        ch_kh[slot], wait=w)
                sl = h % 2
                me_list = [None] * kc_max

                def em_score(kc, h=h, sl=sl):
                    bank = pp[kc % 2]
                    w = [(ch_mh2, m_mrow_out), (sS, m_qdist),
                         (ch_kh[sl], kh_mk[sl])]
                    if psA_war[kc % 2] is not None:
                        w.append((sS, psA_war[kc % 2]))
                    mm_s = p.do("tensor",
                                lambda E, h=h, kc=kc, bank=bank, sl=sl:
                                E.matmul(
                                    bank[:, 0:TQ],
                                    khbuf[:, sl, kc * 128:(kc + 1) * 128],
                                    qaug[:, h, :], start=True, stop=True),
                                wait=w, inc=sT)
                    kh_war[sl] = mm_s
                    w = [(sT, mm_s)]
                    if pt_war[kc % 2] is not None:
                        w.append((sT, pt_war[kc % 2]))
                    m_e = p.do("scalar",
                               lambda E, kc=kc, bank=bank: E.activation(
                                   ptb[kc % 2][:], bank[:, 0:TQ], AF.Exp),
                               wait=w, inc=sS)
                    psA_war[kc % 2] = m_e
                    dep_sem, dep_mark = sS, m_e
                    m = kc - 4 * tq
                    if m >= 0:
                        dep_mark = p.do(
                            "vector",
                            lambda E, kc=kc, m=m: E.tensor_tensor(
                                ptb[kc % 2][:], ptb[kc % 2][:], smask[:, m],
                                ALU.mult),
                            wait=[(sS, m_e)], inc=sV)
                        dep_sem = sV
                    me_list[kc] = (dep_sem, dep_mark)

                em_score(0)
                for kc in range(kc_max):
                    if kc + 1 < kc_max:
                        em_score(kc + 1)
                    if kc == min(2, kc_max - 1) and pending_epi is not None:
                        pending_epi()
                        pending_epi = None
                    wav = [me_list[kc]]
                    if kc == 0 and psO_war[h % 2] is not None:
                        wav.append(psO_war[h % 2])
                    pt_war[kc % 2] = p.do(
                        "tensor",
                        lambda E, h=h, kc=kc, km=kc_max, po=po: E.matmul(
                            po[0:65, 0:TQ],
                            vtm[:, kc, h * 65:(h + 1) * 65],
                            ptb[kc % 2][:], start=(kc == 0),
                            stop=(kc == km - 1)),
                        wait=wav, inc=sT)
                m_av = pt_war[(kc_max - 1) % 2]
                po_b = pp[3] if h % 2 == 0 else pp[5]

                def emit_epi(h=h, po=po, po_b=po_b, m_av=m_av):
                    p.do("vector",
                         lambda E, po=po, h=h: E.reciprocal(linv[:, h % 2],
                                                            po[64:65, 0:TQ]),
                         wait=[(sT, m_av)] + (
                             [epi_war[h % 2]] if epi_war[h % 2] else []))
                    m_li = p.mil("vector", sV)
                    mk_b = p.do("tensor",
                                lambda E, po_b=po_b, h=h: E.matmul(
                                    po_b[0:64, 0:TQ],
                                    C["c_ones_row"][0:1, 0:64],
                                    linv[:, h % 2], start=True, stop=True),
                                wait=[(sV, m_li)], inc=sT)
                    m_nu = p.do("scalar",
                                lambda E, po=po, h=h: E.activation(
                                    numb[:, h % 2], po[0:64, 0:TQ],
                                    AF.Identity),
                                wait=[(sT, m_av)], inc=sS)
                    mm_ao = p.do("vector",
                         lambda E, h=h, po_b=po_b: E.tensor_tensor(
                             aoutT[(h % 2) * 64:(h % 2) * 64 + 64, h // 2, :],
                             numb[:, h % 2], po_b[0:64, 0:TQ], ALU.mult),
                         wait=[(sT, mk_b), (sS, m_nu)], inc=sV)
                    epi_war[h % 2] = (sV, mm_ao)
                    psO_war[h % 2] = (sV, mm_ao)

                pending_epi = emit_epi
            pending_epi()
            pending_epi = None

            for s2 in range(2):
                s0 = q0 + s2 * 256
                xin_s = xin[:, s0:s0 + 256].rearrange("(c p) t -> p c t",
                                                      p=128)
                xout_s = xout[:, s0:s0 + 256].rearrange("(c p) t -> p c t",
                                                        p=128)
                w = [(sV, mark_ln2)] if mark_ln2 else []
                mk_xs = p.dma("sync", sx2[:, :, 0:256] if False else
                              sx2.rearrange("p c (u t) -> p c u t", u=2)
                              [:, :, s2], xin_s, ch_x2, wait=w)
                for oc in range(8):
                    bank = pp[4 + oc % 2]
                    for c in range(8):
                        w = []
                        if c == 0:
                            w = [m for m in (psO_war[0], psO_war[1])
                                 if m is not None]
                            if m_oevac[oc % 2] is not None:
                                w.append((sS, m_oevac[oc % 2]))
                            if oc == 0 and mark_ln2 is not None:
                                w.append((sV, mark_ln2))
                        p.do("tensor",
                             lambda E, oc=oc, c=c, s2=s2, bank=bank: E.matmul(
                                 bank[:, 0:256],
                                 wob[:, c, oc * 128:(oc + 1) * 128],
                                 aoutT[:, c, s2 * 256:(s2 + 1) * 256],
                                 start=(c == 0), stop=(c == 7)),
                             wait=w, inc=(sT if c == 7 else None))
                    mm = p.mark(sT)
                    wev = [(sT, mm)]
                    if oc == 0 and mark_store is not None:
                        wev.append((ch_o, mark_store))
                    m_oevac[oc % 2] = p.do(
                        "scalar",
                        lambda E, oc=oc, bank=bank: E.activation(
                            fo[:, oc], bank[:, 0:256], AF.Identity),
                        wait=wev, inc=sS)
                mk_bc2 = emit_ln_stats(
                    g, 256, lambda c: fo[:, c], tmp2, rw2, C["c_ones16"],
                    C["c_ones_row"], C["c_eps"], pp[6], pp[7], pp[4], pp[5],
                    src_ready=[(sS, m_oevac[1])],
                    war_bc=[(sS, m_oevac[0]), (sS, m_oevac[1])])
                sxs2 = sx2.rearrange("p c (u t) -> p c u t", u=2)
                for c in range(8):
                    w = [(sT, mk_bc2), (ch_x2, mk_xs)] if c == 0 else []
                    if c == 0 and mark_store is not None:
                        w.append((ch_o, mark_store))
                    p.do("vector",
                         lambda E, c=c: E.tensor_tensor(tmp2[:, c], fo[:, c],
                                                        pp[4][:, 0:256],
                                                        ALU.subtract),
                         wait=w)
                p.do("vector", lambda E: E.drain())
                for c in range(8):
                    p.do("vector",
                         lambda E, c=c: E.tensor_tensor(tmp2[:, c], tmp2[:, c],
                                                        pp[5][:, 0:256],
                                                        ALU.mult))
                p.do("vector", lambda E: E.drain())
                for c in range(8):
                    p.do("vector",
                         lambda E, c=c, s2=s2: E.tensor_tensor(
                             fo[:, c], sxs2[:, c, s2], tmp2[:, c], ALU.add))
                mark_ln2 = p.mil("vector", sV)
                mark_store = p.dma("sync", xout_s, fo[:], ch_o,
                                   wait=[(sV, mark_ln2)])
    g.barrier()
    es2.close()


def _prep_l1attn(inputs):
    d = {}
    for n, key in [("wqT", "Wq"), ("wkT", "Wk"), ("wvT", "Wv"), ("woT", "Wo")]:
        d["l1a_" + n] = np.ascontiguousarray(
            _as32(inputs["L1_attn_" + key]).T)
    DK = 64
    half = DK // 2
    inv_freq = (10000.0 ** (-np.arange(half) * 2.0 / DK)).astype(np.float64)
    pos = np.arange(T).astype(np.float64)
    ang = pos[None, :] * inv_freq[:, None]
    C_ = np.repeat(np.cos(ang), 2, axis=0)
    S_ = np.repeat(np.sin(ang), 2, axis=0)
    S_[0::2, :] *= -1.0
    d["l1a_ropeC"] = np.ascontiguousarray(
        np.tile(C_, (2, 1)).astype(np.float32))
    d["l1a_ropeS"] = np.ascontiguousarray(
        np.tile(S_, (2, 1)).astype(np.float32))
    masks = np.zeros((128, 4, 512), np.float32)
    kk = np.arange(128)[:, None]
    qq = np.arange(512)[None, :]
    for m in range(4):
        masks[:, m, :] = (m * 128 + kk <= qq)
    d["l1a_masks"] = masks.astype(BF)
    qn = np.zeros((128, 30), np.float32)
    qn[0:64, 14] = 1.0
    qn[64:128, 15] = 1.0
    d["l1a_qnpat"] = qn
    return d


def _prep_l0attnA(inputs):
    d = {}
    for nm, key in [("qpT", "q_proj_w"), ("kpT", "k_proj_w"),
                    ("vpT", "v_proj_w"), ("bpT", "b_proj_w")]:
        d["l0a_" + nm] = np.ascontiguousarray(
            _as32(inputs["L0_attn_" + key]).T)
    for nm, key in [("qcv", "q_conv_w"), ("kcv", "k_conv_w"),
                    ("vcv", "v_conv_w")]:
        w = _as32(inputs["L0_attn_" + key])
        d["l0a_" + nm] = np.ascontiguousarray(
            w.reshape(8, 128, 4).transpose(1, 0, 2))
    r = np.arange(32)[:, None]
    cc = np.arange(32)[None, :]
    blkU = -(r < cc).astype(np.float32)
    blkL = -(r > cc).astype(np.float32)
    blkI = (r <= cc).astype(np.float32)
    d["l0a_mku"] = np.ascontiguousarray(np.tile(blkU, (4, 4))).astype(BF)
    d["l0a_mkl"] = np.ascontiguousarray(np.tile(blkL, (4, 4))).astype(BF)
    d["l0a_mki"] = np.ascontiguousarray(np.tile(blkI, (4, 4))).astype(BF)
    d["l0a_i32q"] = np.ascontiguousarray(
        np.tile(np.eye(32, dtype=np.float32), (4, 1))).astype(BF)
    d["l0a_idf"] = np.eye(128, dtype=np.float32).astype(BF)
    return d


def _prep_l0attnB(inputs):
    d = {}
    d["l0a_g1T"] = np.ascontiguousarray(_as32(inputs["L0_attn_gate_w1"]).T)
    d["l0a_g2T"] = np.ascontiguousarray(_as32(inputs["L0_attn_gate_w2"]).T)
    d["l0a_opT"] = np.ascontiguousarray(_as32(inputs["L0_attn_o_proj_w"]).T)
    firws = {3: inputs["L0_attn_fir_w3"], 7: inputs["L0_attn_fir_w7"],
             15: inputs["L0_attn_fir_w15"], 31: inputs["L0_attn_fir_w31"]}
    for k in (3, 7, 15, 31):
        w = _as32(firws[k])
        d["l0a_fw%d" % k] = np.ascontiguousarray(
            w.reshape(8, 128, k).transpose(1, 0, 2))
    d["l0a_gb1"] = np.ascontiguousarray(
        _as32(inputs["L0_attn_gate_b1"]).reshape(16, 128).T)
    d["l0a_gb2"] = _as32(inputs["L0_attn_gate_b2"]).reshape(24, 1)
    alst = _as32(inputs["L0_attn_alpha_stat"])
    d["l0a_alst"] = np.ascontiguousarray(np.tile(alst, 4).reshape(24, 1))
    d["l0a_ltmp"] = _as32(inputs["L0_attn_gate_log_temp"]).reshape(4, 1)
    tx = np.zeros((4, 24), np.float32)
    for h in range(4):
        tx[h, 6 * h:6 * h + 6] = 1.0
    d["l0a_texp"] = tx
    d["l0a_onw"] = np.ascontiguousarray(
        _as32(inputs["L0_attn_o_norm_w"]).reshape(2, 128).T)
    stp = np.zeros((128, 47), np.float32)
    stp[:, 23] = 1.0 / 256.0
    d["l0a_stpat"] = stp.astype(BF)
    d["l0a_orow"] = np.ones((1, 128), np.float32).astype(BF)
    taps = []
    for k in (3, 7, 15, 31):
        for j in range(k):
            taps.append((k, j))
    d32 = np.zeros((8, 128, 56, 32), np.float32)
    eye = np.eye(32, dtype=np.float32)
    for c in range(8):
        for ti, (k, j) in enumerate(taps):
            wv = np.asarray(firws[k], dtype=np.float32)
            for q in range(4):
                ch = wv[c * 128 + 32 * q:c * 128 + 32 * q + 32, j]
                d32[c, 32 * q:32 * q + 32, ti, :] = eye * ch[None, :]
    d["l0b_d32"] = np.ascontiguousarray(
        d32.reshape(8, 128, 56 * 32)).astype(BF)
    return d


def _as32(a):
    return np.ascontiguousarray(np.asarray(a, dtype=np.float32))


def _prep_consts():
    c = {}
    c["c_ones_col"] = np.ones((128, 1), np.float32)
    c["c_ones16"] = np.ones((128, 1), BF)
    c["c_ones_row"] = np.ones((1, 128), np.float32)
    c["c_eps"] = np.full((1, 1), 1e-5, np.float32)
    c["c_eps12"] = np.full((1, 1), 1e-12, np.float32)
    c["c_one"] = np.full((1, 1), 1.0, np.float32)
    c["c_idf32"] = np.eye(128, dtype=np.float32)
    return c


CONST_DECL = [
    ("c_ones_col", [128, 1], F32),
    ("c_ones16", [128, 1], BF16),
    ("c_ones_row", [1, 128], F32),
    ("c_eps", [1, 1], F32),
    ("c_eps12", [1, 1], F32),
    ("c_one", [1, 1], F32),
    ("c_idf32", [128, 128], F32),
]


def _prep_l1ffn(inputs):
    w1 = _as32(inputs["L1_ffn_w1"])
    w2 = _as32(inputs["L1_ffn_w2"])
    b1 = _as32(inputs["L1_ffn_b1"])
    b2 = _as32(inputs["L1_ffn_b2"])
    return {
        "l1f_w1t": np.ascontiguousarray(w1.T),
        "l1f_w1n": w1,
        "l1f_w2t": np.ascontiguousarray(w2.T),
        "l1f_w2n": w2,
        "l1f_b1": np.ascontiguousarray(b1.reshape(32, 128).T),
        "l1f_b2": np.ascontiguousarray(b2.reshape(8, 128).T),
    }


def build_single(block):
    nc = bass.Bass("TRN2", target_bir_lowering=False)
    g = G(nc)
    xin = g.din("xin", [D, T])
    consts, mk = g.load_consts(CONST_DECL)
    if block == "l0attnA":
        emit_l0attnA(g, xin, consts, hand="out")
        g.p.finalize()
        g.es.close()
        return g
    xout = g.dout("xout", [D, T])
    if block == "l0attnB":
        emit_l0attnB(g, xin, xout, consts)
        g.p.finalize()
        g.es.close()
        return g
    if block == "l1ffn":
        emit_l1ffn(g, xin, xout, consts)
    elif block == "l0ffn":
        emit_l0ffn(g, xin, xout, consts)
    elif block == "l1attn":
        emit_l1attn(g, xin, xout, consts)
    else:
        raise NotImplementedError(block)
    g.p.finalize()
    g.es.close()
    return g


def build_fused():
    nc = bass.Bass("TRN2", target_bir_lowering=False)
    g = G(nc)
    xin = g.din("xin", [D, T])
    xout = g.dout("xout", [D, T])
    xA = g.dint("x_scrA", [D, T])
    xB = g.dint("x_scrB", [D, T])
    consts, mk = g.load_consts(CONST_DECL)
    g.push_arena()
    emit_l0attnA(g, xin, consts, hand="internal")
    g.pop_arena()
    g.push_arena()
    emit_l0attnB(g, xin, xB, consts)
    g.pop_arena()
    g.push_arena()
    emit_l0ffn(g, xB, xA, consts)
    g.pop_arena()
    g.push_arena()
    emit_l1attn(g, xA, xB, consts)
    g.pop_arena()
    g.push_arena()
    emit_l1ffn(g, xB, xout, consts)
    g.pop_arena()
    g.p.finalize()
    g.es.close()
    return g


def _prep_all(inputs):
    d = dict(_prep_consts())
    d.update(_prep_l0attnA(inputs))
    d.update(_prep_l0attnB(inputs))
    d.update(_prep_l0ffn(inputs))
    d.update(_prep_l1attn(inputs))
    d.update(_prep_l1ffn(inputs))
    return d


_G_CACHE = {}


def _get_fused():
    if "fused" not in _G_CACHE:
        _G_CACHE["fused"] = build_fused()
    return _G_CACHE["fused"]


def kernel(**inputs):
    x = np.asarray(inputs["L0_attn_hidden_states"], dtype=np.float32)
    prep = _prep_all(inputs)
    g = _get_fused()
    in_maps = []
    for b in range(NCORES):
        m = {}
        for name in g.innames:
            if name == "xin":
                m[name] = np.ascontiguousarray(x[b].T)
            else:
                m[name] = prep[name]
        in_maps.append(m)
    res = run_bass_kernel_spmd(g.nc, in_maps, core_ids=list(range(NCORES)))
    out = np.stack([np.asarray(r["xout"]).T for r in res.results])
    return np.ascontiguousarray(out.astype(np.float32))


# version 14
# speedup vs baseline: 1.1910x; 1.1910x over previous
import numpy as np
import ml_dtypes
from contextlib import ExitStack

import concourse.bass as bass
import concourse.mybir as mybir
from concourse.bass_utils import run_bass_kernel_spmd

F32 = mybir.dt.float32
BF16 = mybir.dt.bfloat16
AF = mybir.ActivationFunctionType
ALU = mybir.AluOpType
AX = mybir.AxisListType
BF = ml_dtypes.bfloat16

T = 2048
D = 1024
NCORES = 8
ENGINES = ("tensor", "vector", "scalar", "gpsimd", "sync")


class Prog:

    def __init__(self, nc, es):
        self.nc = nc
        self.es = es
        self.stream = {e: [] for e in ENGINES}
        self.cnt = {}
        self.channels = []
        self.last_wait = {e: {} for e in ENGINES}

    def sem(self, name):
        s = self.es.enter_context(self.nc.semaphore(name))
        self.cnt[s] = 0
        return s

    def wait(self, eng, sem, val):
        if val is None or val <= 0:
            return
        lw = self.last_wait[eng]
        if lw.get(sem, -1) >= val:
            return
        lw[sem] = val
        self.stream[eng].append(lambda E, s=sem, v=val: E.wait_ge(s, v))

    def do(self, eng, build, wait=(), inc=None, n=1):
        import sys as _sys
        _fr = []
        for _i in range(1, 5):
            try:
                f = _sys._getframe(_i)
                _fr.append(str(f.f_lineno))
            except ValueError:
                break
        build._src = ",".join(_fr)
        for (s, v) in wait:
            self.wait(eng, s, v)
        if inc is None:
            self.stream[eng].append(build)
            return None
        self.cnt[inc] += n
        tgt = self.cnt[inc]
        self.stream[eng].append(
            lambda E, b=build, s=inc, k=n: b(E).then_inc(s, k))
        return tgt

    def dma(self, eng, out, in_, sem, wait=(), n=16):
        return self.do(eng, lambda E, o=out, i=in_: E.dma_start(out=o, in_=i),
                       wait=wait, inc=sem, n=n)

    def mark(self, sem):
        return self.cnt[sem]

    def channel(self, name):
        s = self.sem(name)
        self.channels.append(s)
        return s

    def mil(self, eng, sem, wait=()):
        return self.do(eng, lambda E: E.drain(), wait=wait, inc=sem)

    def finalize(self):
        def run(E, ee):
            for fn in self.stream[ee]:
                try:
                    fn(E)
                except Exception:
                    print("FAILED op from", getattr(fn, "_src", "?"), "eng", ee)
                    raise
        with self.nc.Block() as blk:
            for e in ENGINES:
                getattr(blk, e)(lambda E, ee=e: run(E, ee))


class G:

    def __init__(self, nc):
        self.nc = nc
        self.es = ExitStack()
        self.p = Prog(nc, self.es)
        p = self.p
        self.sT = p.sem("sT")
        self.sV = p.sem("sV")
        self.sS = p.sem("sS")
        self.sG = p.sem("sG")
        self.dSY = p.sem("dSY")
        self.dGP = p.sem("dGP")
        self.bar = p.sem("bar")
        self.pool = [self.es.enter_context(
            nc.psum_tensor(f"pp{i}", [128, 512], F32)) for i in range(8)]
        self.dram = {}
        self.innames = []
        self.arenas = []

    def din(self, name, shape, dtype=F32):
        ap = self.nc.dram_tensor(name, list(shape), dtype,
                                 kind="ExternalInput").ap()
        self.dram[name] = ap
        self.innames.append(name)
        return ap

    def dout(self, name, shape, dtype=F32):
        ap = self.nc.dram_tensor(name, list(shape), dtype,
                                 kind="ExternalOutput").ap()
        self.dram[name] = ap
        return ap

    def dint(self, name, shape, dtype=F32):
        ap = self.nc.dram_tensor(name, list(shape), dtype,
                                 kind="Internal").ap()
        self.dram[name] = ap
        return ap

    def sb(self, name, shape, dtype=F32):
        es = self.arenas[-1] if self.arenas else self.es
        return es.enter_context(
            self.nc.sbuf_tensor(name, list(shape), dtype))

    def push_arena(self):
        self.arenas.append(ExitStack())

    def pop_arena(self):
        self.arenas.pop().close()

    def barrier(self):
        p = self.p
        p.wait("sync", self.dSY, p.cnt[self.dSY])
        p.wait("gpsimd", self.dGP, p.cnt[self.dGP])
        for ch in p.channels:
            p.wait("sync", ch, p.cnt[ch])
        for e in ("tensor", "vector", "scalar"):
            p.do(e, lambda E: E.drain(), inc=self.bar)
        for e in ("gpsimd", "sync"):
            p.do(e, lambda E: E.nop(), inc=self.bar)
        tgt = p.cnt[self.bar]
        for e in ENGINES:
            p.wait(e, self.bar, tgt)

    def load_consts(self, names_shapes):
        out = {}
        for name, shape, dtype in names_shapes:
            ap = self.din(name, shape, dtype)
            t = self.sb("s_" + name, shape, dtype)
            self.p.dma("sync", t[:], ap, self.dSY)
            out[name] = t
        return out, self.p.mark(self.dSY)


def emit_ln_stats_pre(g, TT, src_sl, sq_buf, c_ones16, pp_s1, pp_s2,
                      src_ready=(), war_rows=()):
    p, sT, sV, sS = g.p, g.sT, g.sV, g.sS
    bv = sq_buf.bitcast(BF16)
    for c in range(8):
        p.do("vector",
             lambda E, c=c: E.tensor_copy(bv[:, c, TT:2 * TT], src_sl(c)),
             wait=(list(src_ready) + list(war_rows)) if c == 0 else ())
    m_xc = p.mil("vector", sV)
    for c in range(8):
        p.do("scalar",
             lambda E, c=c: E.activation(bv[:, c, 0:TT], src_sl(c), AF.Square),
             wait=src_ready if c == 0 else ())
    m_sq = p.mil("scalar", sS)
    for c in range(8):
        p.do("tensor",
             lambda E, c=c: E.matmul(pp_s1[0:1, 0:TT], c_ones16[:],
                                     bv[:, c, TT:2 * TT], start=(c == 0),
                                     stop=(c == 7)),
             wait=([(sV, m_xc)] + list(war_rows)) if c == 0 else ())
    for c in range(8):
        p.do("tensor",
             lambda E, c=c: E.matmul(pp_s2[0:1, 0:TT], c_ones16[:],
                                     bv[:, c, 0:TT], start=(c == 0),
                                     stop=(c == 7)),
             wait=[(sS, m_sq)] if c == 0 else (),
             inc=(g.sT if c == 7 else None))
    return p.mark(g.sT)


def emit_ln_stats_post(g, TT, rows, c_ones_row, c_eps, pp_s1, pp_s2,
                       pp_bcm, pp_bcr, mk_stats, war_bc=()):
    p, sT, sV, sS = g.p, g.sT, g.sV, g.sS
    p.do("vector", lambda E: E.tensor_scalar_mul(rows[:, 0, 0:TT], pp_s1[0:1, 0:TT],
                                                 1.0 / D),
         wait=[(g.sT, mk_stats)])
    m_m = p.mil("vector", sV)
    p.do("tensor", lambda E: E.matmul(pp_bcm[:, 0:TT], c_ones_row[:],
                                      rows[:, 0, 0:TT], start=True, stop=True),
         wait=[(sV, m_m)] + list(war_bc))
    p.do("vector", lambda E: E.tensor_tensor(rows[:, 1, 0:TT], rows[:, 0, 0:TT],
                                             rows[:, 0, 0:TT], ALU.mult))
    p.do("vector", lambda E: E.drain())
    p.do("vector", lambda E: E.scalar_tensor_tensor(
        rows[:, 2, 0:TT], rows[:, 1, 0:TT], -float(D), pp_s2[0:1, 0:TT],
        ALU.mult, ALU.add))
    m_v = p.mil("vector", sV)
    p.do("scalar", lambda E: E.activation(rows[:, 3, 0:TT], rows[:, 2, 0:TT],
                                          AF.Sqrt, scale=1.0 / D,
                                          bias=c_eps[0:1, 0:1]),
         wait=[(sV, m_v)])
    m_s = p.mil("scalar", sS)
    p.do("vector", lambda E: E.reciprocal(rows[:, 4, 0:TT], rows[:, 3, 0:TT]),
         wait=[(sS, m_s)])
    m_r = p.mil("vector", sV)
    mk_bc = p.do("tensor", lambda E: E.matmul(pp_bcr[:, 0:TT], c_ones_row[:],
                                              rows[:, 4, 0:TT], start=True,
                                              stop=True),
                 wait=[(sV, m_r)] + list(war_bc), inc=g.sT)
    return mk_bc


def emit_ln_stats(g, TT, src_sl, sq_buf, rows, c_ones16, c_ones_row, c_eps,
                  pp_s1, pp_s2, pp_bcm, pp_bcr, src_ready=(), war_rows=(),
                  war_bc=()):
    mk_stats = emit_ln_stats_pre(g, TT, src_sl, sq_buf, c_ones16, pp_s1,
                                 pp_s2, src_ready, war_rows)
    return emit_ln_stats_post(g, TT, rows, c_ones_row, c_eps, pp_s1, pp_s2,
                              pp_bcm, pp_bcr, mk_stats, war_bc)


def emit_l1ffn(g, xin, xout, C):
    nc, p = g.nc, g.p
    sT, sV, sS = g.sT, g.sV, g.sS
    TT = 256
    NTILE = T // TT
    H1 = 4096

    w1t = g.din("l1f_w1t", [D, H1])
    w1n = g.din("l1f_w1n", [H1, D])
    w2t = g.din("l1f_w2t", [H1, D])
    w2n = g.din("l1f_w2n", [D, H1])
    b1h = g.din("l1f_b1", [128, 32])
    b2h = g.din("l1f_b2", [128, 8])

    sgn1 = g.sb("sgn1", [128, 8, H1], BF16)
    sgn2 = g.sb("sgn2", [128, 32, D], BF16)
    a1T = g.sb("a1T", [128, 32])
    a2T = g.sb("a2T", [128, 8])
    b1T = g.sb("b1T", [128, 32])
    b2T = g.sb("b2T", [128, 8])

    p.dma("sync", b1T[:], b1h, g.dSY)
    p.dma("sync", b2T[:], b2h, g.dSY)

    ch_st = [p.channel("l1f_st0"), p.channel("l1f_st1")]
    ch_ns = [p.channel("l1f_ns0"), p.channel("l1f_ns1")]
    with nc.sbuf_tensor("l1f_stage", [128, 2, H1], F32) as stage, \
         nc.sbuf_tensor("l1f_nstage", [128, 2, H1], F32) as nstage:
        sgn_marks = [None, None]
        red_marks = [None, None]
        for c in range(8):
            w = [(sS, sgn_marks[c % 2])] if sgn_marks[c % 2] else []
            mk = p.dma("gpsimd", stage[:, c % 2],
                       w1t[c * 128:(c + 1) * 128, :], ch_st[c % 2], wait=w)
            sgn_marks[c % 2] = p.do(
                "scalar",
                lambda E, c=c: E.activation(sgn1[:, c], stage[:, c % 2],
                                            AF.Sign),
                wait=[(ch_st[c % 2], mk)], inc=sS)
        for c in range(32):
            w = [(sS, sgn_marks[c % 2])]
            mk = p.dma("gpsimd", stage[:, c % 2][:, 0:D],
                       w2t[c * 128:(c + 1) * 128, :], ch_st[c % 2], wait=w)
            sgn_marks[c % 2] = p.do(
                "scalar",
                lambda E, c=c: E.activation(sgn2[:, c], stage[:, c % 2][:, 0:D],
                                            AF.Sign),
                wait=[(ch_st[c % 2], mk)], inc=sS)
        for c in range(32):
            w = [(sV, red_marks[c % 2])]
            mk = p.dma("sync", nstage[:, c % 2][:, 0:D],
                       w1n[c * 128:(c + 1) * 128, :], ch_ns[c % 2], wait=w)
            red_marks[c % 2] = p.do(
                "vector",
                lambda E, c=c: E.reduce_sum(a1T[:, c:c + 1],
                                            nstage[:, c % 2][:, 0:D],
                                            axis=AX.X,
                                            apply_absolute_value=True),
                wait=[(ch_ns[c % 2], mk)], inc=sV)
        for c in range(8):
            w = [(sV, red_marks[c % 2])]
            mk = p.dma("sync", nstage[:, c % 2],
                       w2n[c * 128:(c + 1) * 128, :], ch_ns[c % 2], wait=w)
            red_marks[c % 2] = p.do(
                "vector",
                lambda E, c=c: E.reduce_sum(a2T[:, c:c + 1], nstage[:, c % 2],
                                            axis=AX.X,
                                            apply_absolute_value=True),
                wait=[(ch_ns[c % 2], mk)], inc=sV)
        p.do("vector", lambda E: E.drain())
        p.do("vector", lambda E: E.tensor_scalar_mul(a1T[:], a1T[:], 1.0 / D))
        p.do("vector", lambda E: E.tensor_scalar_mul(a2T[:], a2T[:], 1.0 / H1))
        g.barrier()

    sx = g.sb("l1f_sx", [128, 16, TT])
    xsq = g.sb("l1f_xsq", [128, 8, TT])
    xsq1 = g.sb("l1f_xsq1", [128, 8, TT])
    yb = g.sb("l1f_yb", [128, 8, TT], BF16)
    h1b = g.sb("l1f_h1b", [128, 32, TT], BF16)
    fb = g.sb("l1f_f", [128, 8, TT])
    tmp = g.sb("l1f_tmp", [128, 8, TT])
    rows = g.sb("l1f_rows", [1, 5, TT])
    rows2 = g.sb("l1f_rows2", [1, 5, TT])
    xo = xsq

    pp = g.pool
    ch_x = [p.channel("l1f_chx0"), p.channel("l1f_chx1")]
    ch_o = p.channel("l1f_cho")
    mark_apply = None
    mark_apply_hist = [None, None]
    mark_store = None
    bank_evac = [None, None]
    bank_fe = [None, None]
    mark_evac = [None] * 32
    mark_fe = [None] * 8
    mk_bc_next = None

    for t in range(NTILE):
        t0 = t * TT
        sl = t % 2
        xin_sl = xin[:, t0:t0 + TT].rearrange("(c p) t -> p c t", p=128)
        xout_sl = xout[:, t0:t0 + TT].rearrange("(c p) t -> p c t", p=128)

        if t == 0:
            mk_x = p.dma("sync", sx[:, 0:8], xin_sl, ch_x[0])
            mk_bc = emit_ln_stats(
                g, TT, lambda c: sx[:, c], xsq1, rows2, C["c_ones16"],
                C["c_ones_row"], C["c_eps"],
                pp[0], pp[1], pp[6], pp[7],
                src_ready=[(ch_x[0], mk_x)])
        else:
            mk_bc = mk_bc_next
        for c in range(8):
            p.do("vector",
                 lambda E, c=c, sl=sl: E.tensor_tensor(
                     xsq1[:, c], sx[:, sl * 8 + c],
                     pp[6][:, 0:TT], ALU.subtract),
                 wait=[(sT, mk_bc)] if c == 0 else ())
        p.do("vector", lambda E: E.drain())
        for c in range(8):
            p.do("vector",
                 lambda E, c=c: E.tensor_tensor(yb[:, c], xsq1[:, c],
                                                pp[7][:, 0:TT], ALU.mult))
        mark_yb = p.mil("vector", sV)

        for jc in range(32):
            bank = pp[jc % 2]
            for c in range(8):
                w = []
                if c == 0:
                    w = [(sV, mark_yb)]
                    if bank_evac[jc % 2] is not None:
                        w.append((sS, bank_evac[jc % 2]))
                p.do("tensor",
                     lambda E, jc=jc, c=c, bank=bank: E.matmul(
                         bank[:, 0:TT],
                         sgn1[:, c][:, jc * 128:(jc + 1) * 128],
                         yb[:, c], start=(c == 0), stop=(c == 7)),
                     wait=w, inc=(sT if c == 7 else None))
            mm = p.mark(sT)
            mark_evac[jc] = bank_evac[jc % 2] = p.do(
                "scalar",
                lambda E, jc=jc, bank=bank: E.activation(
                    h1b[:, jc], bank[:, 0:TT], AF.Gelu,
                    scale=a1T[:, jc:jc + 1], bias=b1T[:, jc:jc + 1]),
                wait=[(sT, mm)], inc=sS)

        for oc in range(8):
            bank = pp[6 + (oc % 2)]
            for jc in range(32):
                w = []
                if jc == 0:
                    w = [(sS, mark_evac[31])]
                    if bank_fe[oc % 2] is not None:
                        w.append((sS, bank_fe[oc % 2]))
                p.do("tensor",
                     lambda E, oc=oc, jc=jc, bank=bank: E.matmul(
                         bank[:, 0:TT],
                         sgn2[:, jc][:, oc * 128:(oc + 1) * 128],
                         h1b[:, jc], start=(jc == 0), stop=(jc == 31)),
                     wait=w, inc=(sT if jc == 31 else None))
            mm = p.mark(sT)
            mark_fe[oc] = bank_fe[oc % 2] = p.do(
                "scalar",
                lambda E, oc=oc, bank=bank: E.activation(
                    fb[:, oc], bank[:, 0:TT], AF.Identity,
                    scale=a2T[:, oc:oc + 1], bias=b2T[:, oc:oc + 1]),
                wait=[(sT, mm)], inc=sS)

        if t + 1 < NTILE:
            nsl = (t + 1) % 2
            t1 = (t + 1) * TT
            xin_n = xin[:, t1:t1 + TT].rearrange("(c p) t -> p c t", p=128)
            w = ([(sV, mark_apply_hist[nsl])]
                 if mark_apply_hist[nsl] is not None else [])
            mk_xn = p.dma("sync", sx[:, nsl * 8:nsl * 8 + 8], xin_n,
                          ch_x[nsl], wait=w)
            mkst1 = emit_ln_stats_pre(
                g, TT, lambda c, s=nsl: sx[:, s * 8 + c], xsq1,
                C["c_ones16"], pp[0], pp[1],
                src_ready=[(ch_x[nsl], mk_xn), (sV, mark_yb)])

        w = [(sS, mark_fe[7])]
        if mark_store is not None:
            w.append((ch_o, mark_store))
        mk_bc2 = emit_ln_stats(
            g, TT, lambda c: fb[:, c], xsq, rows, C["c_ones16"],
            C["c_ones_row"], C["c_eps"],
            pp[2], pp[3], pp[4], pp[5],
            src_ready=w)
        if t + 1 < NTILE:
            mk_bc_next = emit_ln_stats_post(
                g, TT, rows2, C["c_ones_row"], C["c_eps"], pp[0], pp[1],
                pp[6], pp[7], mkst1, war_bc=[(sS, mark_fe[7])])
        for c in range(8):
            w = [(sT, mk_bc2)] if c == 0 else []
            if c == 0 and mark_store is not None:
                w.append((ch_o, mark_store))
            p.do("vector",
                 lambda E, c=c: E.tensor_tensor(xsq[:, c], fb[:, c],
                                                pp[4][:, 0:TT], ALU.subtract),
                 wait=w)
        p.do("vector", lambda E: E.drain())
        for c in range(8):
            p.do("vector",
                 lambda E, c=c: E.tensor_tensor(tmp[:, c], xsq[:, c],
                                                pp[5][:, 0:TT], ALU.mult))
        p.do("vector", lambda E: E.drain())
        for c in range(8):
            p.do("vector",
                 lambda E, c=c, sl=sl: E.tensor_tensor(
                     xo[:, c], sx[:, sl * 8 + c], tmp[:, c], ALU.add))
        mark_apply = p.mil("vector", sV)
        mark_apply_hist[sl] = mark_apply
        mark_store = p.dma("sync", xout_sl, xo[:], ch_o,
                           wait=[(sV, mark_apply)])
    g.barrier()


def emit_l0ffn(g, xin, xout, C):
    nc, p = g.nc, g.p
    sT, sV, sS = g.sT, g.sV, g.sS
    TT = 256
    NTILE = T // TT

    wnames = [("cn_w1", [D, D], 8), ("cn_w2", [D, 512], 8),
              ("cn_w3", [512, 256], 4), ("cn_w4", [256, 1], 2),
              ("gc_w1", [D, 512], 8), ("gc_w2", [512, 256], 4),
              ("gc_w3", [256, 1], 2)]
    wd = {n: g.din("l0f_" + n, shp) for n, shp, _ in wnames}
    wb = {n: g.sb("l0f_b_" + n, [128, shp[0] // 128, shp[1]], BF16)
          for n, shp, _ in wnames}
    for n, shp, nch in wnames:
        for c in range(nch):
            p.dma("gpsimd", wb[n][:, c], wd[n][c * 128:(c + 1) * 128, :],
                  g.dGP)
    bdecl = [("cn_b1", [128, 8]), ("cn_b2", [128, 4]), ("cn_b3", [128, 2]),
             ("cn_b4", [1, 1]), ("gc_b1", [128, 4]), ("gc_b2", [128, 2]),
             ("gc_b3", [1, 1]), ("arng", [1, 1])]
    bb = {}
    for n, shp in bdecl:
        ap = g.din("l0f_" + n, shp)
        bb[n] = g.sb("l0f_s_" + n, shp)
        p.dma("sync", bb[n][:], ap, g.dSY)

    pp = g.pool
    sx = g.sb("l0f_sx", [128, 16, TT])
    xsq = g.sb("l0f_xsq", [128, 8, TT])
    xsq2 = g.sb("l0f_xsq2", [128, 8, TT])
    yb = g.sb("l0f_yb", [128, 8, TT], BF16)
    rows = g.sb("l0f_rows", [1, 5, TT])
    rows2 = g.sb("l0f_rows2", [1, 5, TT])
    gacc = g.sb("l0f_gacc", [128, 8, NTILE])
    gcv = g.sb("l0f_gcv", [128, 8])
    gcb = g.sb("l0f_gcb", [128, 8], BF16)
    srow = g.sb("l0f_srow", [1, 8])
    g.barrier()

    ch_x = [p.channel("l0f_chx0"), p.channel("l0f_chx1")]
    ch_o = p.channel("l0f_cho")

    mark_red = None
    mark_red_hist = [None, None]
    statsb = [(pp[2], pp[3]), (pp[6], pp[7])]
    rowsAB = [rows, rows2]
    mkst = [None, None]
    xin_0 = xin[:, 0:TT].rearrange("(c p) t -> p c t", p=128)
    mk_x0 = p.dma("sync", sx[:, 0:8], xin_0, ch_x[0])
    mkst[0] = emit_ln_stats_pre(
        g, TT, lambda c: sx[:, c], xsq2, C["c_ones16"],
        statsb[0][0], statsb[0][1], src_ready=[(ch_x[0], mk_x0)])
    for t in range(NTILE):
        t0 = t * TT
        sl = t % 2
        if t + 1 < NTILE:
            nsl = 1 - sl
            t1 = (t + 1) * TT
            xin_n = xin[:, t1:t1 + TT].rearrange("(c p) t -> p c t", p=128)
            w = ([(sV, mark_red_hist[nsl])]
                 if mark_red_hist[nsl] is not None else [])
            mk_xn = p.dma("sync", sx[:, nsl * 8:nsl * 8 + 8], xin_n,
                          ch_x[nsl], wait=w)
            wr = [(sT, mkst[sl])]
            if mark_red_hist[nsl] is not None:
                wr.append((sV, mark_red_hist[nsl]))
            mkst[nsl] = emit_ln_stats_pre(
                g, TT, lambda c, s=nsl: sx[:, s * 8 + c], xsq2,
                C["c_ones16"], statsb[nsl][0], statsb[nsl][1],
                src_ready=[(ch_x[nsl], mk_xn), (sT, mkst[sl])],
                war_rows=wr)
        mk_bc = emit_ln_stats_post(
            g, TT, rowsAB[sl], C["c_ones_row"], C["c_eps"],
            statsb[sl][0], statsb[sl][1], pp[4], pp[5], mkst[sl],
            war_bc=[(sV, mark_red)] if mark_red else ())
        for c in range(8):
            p.do("vector",
                 lambda E, c=c, sl=sl: E.tensor_tensor(
                     xsq[:, c], sx[:, sl * 8 + c],
                     pp[4][:, 0:TT], ALU.subtract),
                 wait=[(sT, mk_bc)] if c == 0 else ())
        p.do("vector", lambda E: E.drain())
        for c in range(8):
            p.do("vector",
                 lambda E, c=c: E.tensor_tensor(yb[:, c], xsq[:, c],
                                                pp[5][:, 0:TT], ALU.mult))
        p.do("vector", lambda E: E.drain())
        for c in range(8):
            p.do("vector",
                 lambda E, c=c, t=t: E.reduce_sum(gacc[:, c, t:t + 1],
                                                  yb[:, c], axis=AX.X))
        mark_red = p.mil("vector", sV)
        mark_red_hist[sl] = mark_red
    for c in range(8):
        p.do("vector",
             lambda E, c=c: E.reduce_sum(gcv[:, c:c + 1], gacc[:, c],
                                         axis=AX.X))
    p.do("vector", lambda E: E.drain())
    p.do("vector", lambda E: E.tensor_scalar_mul(gcb[:], gcv[:], 1.0 / T))
    m_gcb = p.mil("vector", sV)
    for jc in range(4):
        for c in range(8):
            p.do("tensor",
                 lambda E, jc=jc, c=c: E.matmul(
                     pp[0][:, jc:jc + 1], wb["gc_w1"][:, c, jc * 128:(jc + 1) * 128],
                     gcb[:, c:c + 1], start=(c == 0), stop=(c == 7)),
                 wait=[(sV, m_gcb)] if (jc == 0 and c == 0) else ())
    m_mm = p.mil("tensor", sT)
    g1b = g.sb("l0f_g1b", [128, 4], BF16)
    g2b = g.sb("l0f_g2b", [128, 2], BF16)
    for jc in range(4):
        p.do("scalar",
             lambda E, jc=jc: E.activation(g1b[:, jc:jc + 1],
                                           pp[0][:, jc:jc + 1], AF.Gelu,
                                           bias=bb["gc_b1"][:, jc:jc + 1]),
             wait=[(sT, m_mm)])
    m_g1 = p.mil("scalar", sS)
    for jc in range(2):
        for c in range(4):
            p.do("tensor",
                 lambda E, jc=jc, c=c: E.matmul(
                     pp[1][:, jc:jc + 1], wb["gc_w2"][:, c, jc * 128:(jc + 1) * 128],
                     g1b[:, c:c + 1], start=(c == 0), stop=(c == 3)),
                 wait=[(sS, m_g1)] if (jc == 0 and c == 0) else ())
    m_mm = p.mil("tensor", sT)
    for jc in range(2):
        p.do("scalar",
             lambda E, jc=jc: E.activation(g2b[:, jc:jc + 1],
                                           pp[1][:, jc:jc + 1], AF.Gelu,
                                           bias=bb["gc_b2"][:, jc:jc + 1]),
             wait=[(sT, m_mm)])
    m_g2 = p.mil("scalar", sS)
    for c in range(2):
        p.do("tensor",
             lambda E, c=c: E.matmul(pp[2][0:1, 0:1], wb["gc_w3"][:, c],
                                     g2b[:, c:c + 1], start=(c == 0),
                                     stop=(c == 1)),
             wait=[(sS, m_g2)] if c == 0 else ())
    m_mm = p.mil("tensor", sT)
    m_gf = p.do("scalar",
                lambda E: E.activation(srow[0:1, 4:5], pp[2][0:1, 0:1],
                                       AF.Sigmoid, bias=bb["gc_b3"][0:1, 0:1]),
                wait=[(sT, m_mm)], inc=sS)
    p.do("vector", lambda E: E.tensor_scalar_mul(srow[0:1, 0:1],
                                                 bb["arng"][0:1, 0:1], 0.6),
         wait=[(sS, m_gf)])
    p.do("vector", lambda E: E.tensor_tensor(srow[0:1, 2:3], bb["arng"][0:1, 0:1],
                                             srow[0:1, 4:5], ALU.mult))
    p.do("vector", lambda E: E.drain())
    p.do("vector", lambda E: E.tensor_scalar(srow[0:1, 3:4], srow[0:1, 2:3],
                                             1.4, 0.1, ALU.mult, ALU.add))
    p.do("vector", lambda E: E.drain())
    p.do("vector", lambda E: E.tensor_tensor(srow[0:1, 1:2], srow[0:1, 3:4],
                                             bb["arng"][0:1, 0:1],
                                             ALU.subtract))
    m_srow = p.mil("vector", sV)

    ostream = g.sb("l0f_ostream", [128, 8, TT])
    ob = g.sb("l0f_ob", [128, 8, TT], BF16)
    h1c = g.sb("l0f_h1c", [128, 8, TT], BF16)
    h2c = g.sb("l0f_h2c", [128, 4, TT], BF16)
    h3c = g.sb("l0f_h3c", [128, 2, TT], BF16)
    tw = g.sb("l0f_tw", [128, 8, TT])
    xo = g.sb("l0f_xo", [128, 8, TT])
    tfrow = g.sb("l0f_tfrow", [1, 1, TT])
    crow = g.sb("l0f_crow", [1, 1, TT])
    TSIGN = [ALU.add, ALU.add, ALU.subtract, ALU.subtract,
             ALU.subtract, ALU.subtract, ALU.add, ALU.add]

    mark_apply = m_srow
    mark_apply_hist = list(mark_red_hist)
    mark_store = None
    bank_h1 = [None, None]
    bank_h2 = [None, None]
    bank_h3 = None
    mark_tf = None
    mark_crow = None
    mark_tw = None
    mk_bc_next = None
    mkst1 = None
    for t in range(NTILE):
        t0 = t * TT
        sl = t % 2
        xout_sl = xout[:, t0:t0 + TT].rearrange("(c p) t -> p c t", p=128)
        if t == 0:
            xin_sl = xin[:, t0:t0 + TT].rearrange("(c p) t -> p c t", p=128)
            mk_x = p.dma("sync", sx[:, 0:8], xin_sl, ch_x[0],
                         wait=[(sV, mark_apply_hist[0])])
            mkst1 = emit_ln_stats_pre(
                g, TT, lambda c: sx[:, c], xsq2, C["c_ones16"],
                pp[0], pp[1],
                src_ready=[(ch_x[0], mk_x)], war_rows=[(sV, m_srow)])
            mk_bc = emit_ln_stats_post(
                g, TT, rows2, C["c_ones_row"], C["c_eps"], pp[0], pp[1],
                pp[4], pp[5], mkst1,
                war_bc=[(sV, mark_red)])
        else:
            mk_bc = mk_bc_next
        for c in range(8):
            p.do("vector",
                 lambda E, c=c, sl=sl: E.tensor_tensor(
                     xsq[:, c], sx[:, sl * 8 + c],
                     pp[4][:, 0:TT], ALU.subtract),
                 wait=[(sT, mk_bc)] if c == 0 else ())
        p.do("vector", lambda E: E.drain())
        for c in range(8):
            p.do("vector",
                 lambda E, c=c: E.tensor_tensor(ostream[:, c], xsq[:, c],
                                                pp[5][:, 0:TT], ALU.mult))
        p.do("vector", lambda E: E.drain())
        for c in range(8):
            p.do("vector",
                 lambda E, c=c: E.tensor_copy(ob[:, c], ostream[:, c]))
        mark_ob = p.mil("vector", sV)

        for cyc in range(3):
            h1m = None
            for jc in range(8):
                bank = pp[jc % 2]
                for c in range(8):
                    w = []
                    if c == 0:
                        w = [(sV, mark_ob)]
                        if bank_h1[jc % 2] is not None:
                            w.append((sS, bank_h1[jc % 2]))
                    p.do("tensor",
                         lambda E, jc=jc, c=c, bank=bank: E.matmul(
                             bank[:, 0:TT],
                             wb["cn_w1"][:, c, jc * 128:(jc + 1) * 128],
                             ob[:, c], start=(c == 0), stop=(c == 7)),
                         wait=w, inc=(sT if c == 7 else None))
                mm = p.mark(sT)
                h1m = bank_h1[jc % 2] = p.do(
                    "scalar",
                    lambda E, jc=jc, bank=bank: E.activation(
                        h1c[:, jc], bank[:, 0:TT], AF.Gelu,
                        bias=bb["cn_b1"][:, jc:jc + 1]),
                    wait=[(sT, mm)], inc=sS)
            h2m = None
            for jc in range(4):
                bank = pp[2 + jc % 2]
                for c in range(8):
                    w = []
                    if c == 0:
                        w = [(sS, h1m)]
                        if bank_h2[jc % 2] is not None:
                            w.append((sS, bank_h2[jc % 2]))
                    p.do("tensor",
                         lambda E, jc=jc, c=c, bank=bank: E.matmul(
                             bank[:, 0:TT],
                             wb["cn_w2"][:, c, jc * 128:(jc + 1) * 128],
                             h1c[:, c], start=(c == 0), stop=(c == 7)),
                         wait=w, inc=(sT if c == 7 else None))
                mm = p.mark(sT)
                h2m = bank_h2[jc % 2] = p.do(
                    "scalar",
                    lambda E, jc=jc, bank=bank: E.activation(
                        h2c[:, jc], bank[:, 0:TT], AF.Gelu,
                        bias=bb["cn_b2"][:, jc:jc + 1]),
                    wait=[(sT, mm)], inc=sS)
            h3m = None
            for jc in range(2):
                bank = pp[4]
                for c in range(4):
                    w = []
                    if c == 0:
                        w = [(sS, h2m)]
                        if bank_h3 is not None:
                            w.append((sS, bank_h3))
                    p.do("tensor",
                         lambda E, jc=jc, c=c, bank=bank: E.matmul(
                             bank[:, 0:TT],
                             wb["cn_w3"][:, c, jc * 128:(jc + 1) * 128],
                             h2c[:, c], start=(c == 0), stop=(c == 3)),
                         wait=w, inc=(sT if c == 3 else None))
                mm = p.mark(sT)
                h3m = bank_h3 = p.do(
                    "scalar",
                    lambda E, jc=jc, bank=bank: E.activation(
                        h3c[:, jc], bank[:, 0:TT], AF.Gelu,
                        bias=bb["cn_b3"][:, jc:jc + 1]),
                    wait=[(sT, mm)], inc=sS)
            for c in range(2):
                w = [(sS, h3m)] if c == 0 else []
                if c == 0 and mark_tf is not None:
                    w.append((sS, mark_tf))
                p.do("tensor",
                     lambda E, c=c: E.matmul(pp[5][0:1, 0:TT], wb["cn_w4"][:, c],
                                             h3c[:, c], start=(c == 0),
                                             stop=(c == 1)),
                     wait=w, inc=(sT if c == 1 else None))
            mm = p.mark(sT)
            mark_tf = p.do(
                "scalar",
                lambda E: E.activation(tfrow[:, 0], pp[5][0:1, 0:TT],
                                       AF.Sigmoid, bias=bb["cn_b4"][0:1, 0:1]),
                wait=[(sT, mm)], inc=sS)
            mark_crow = p.do(
                "vector",
                lambda E: E.tensor_scalar(crow[:, 0], tfrow[:, 0],
                                          srow[0:1, 0:1], srow[0:1, 1:2],
                                          ALU.mult, ALU.add),
                wait=[(sS, mark_tf), (sV, m_srow)], inc=sV)
            mk_cb = p.do("tensor",
                         lambda E: E.matmul(pp[4][:, 0:TT], C["c_ones_row"][:],
                                            crow[:, 0], start=True, stop=True),
                         wait=[(sV, mark_crow), (sS, h3m)], inc=sT)
            bank_h3 = None
            for c in range(8):
                w = [(sT, mk_cb)] if c == 0 else []
                p.do("vector",
                     lambda E, c=c: E.tensor_tensor(
                         tw[:, c], pp[4][:, 0:TT], ostream[:, (c + 4) % 8],
                         ALU.mult), wait=w)
            p.do("vector", lambda E: E.drain())
            for c in range(8):
                p.do("vector",
                     lambda E, c=c: E.tensor_tensor(ostream[:, c],
                                                    ostream[:, c], tw[:, c],
                                                    TSIGN[c]))
            mark_tw = p.mil("vector", sV)
            if cyc < 2:
                for c in range(8):
                    p.do("vector",
                         lambda E, c=c: E.tensor_copy(ob[:, c], ostream[:, c]))
                mark_ob = p.mil("vector", sV)

        if t + 1 < NTILE:
            nsl = (t + 1) % 2
            t1 = (t + 1) * TT
            xin_n = xin[:, t1:t1 + TT].rearrange("(c p) t -> p c t", p=128)
            w = ([(sV, mark_apply_hist[nsl])]
                 if mark_apply_hist[nsl] is not None else [])
            mk_xn = p.dma("sync", sx[:, nsl * 8:nsl * 8 + 8], xin_n,
                          ch_x[nsl], wait=w)
            wr = [(sT, mkst1), (sV, mark_tw)]
            for mb in bank_h1:
                if mb is not None:
                    wr.append((sS, mb))
            mkst1 = emit_ln_stats_pre(
                g, TT, lambda c, s=nsl: sx[:, s * 8 + c], xsq2,
                C["c_ones16"], pp[0], pp[1],
                src_ready=[(ch_x[nsl], mk_xn), (sT, mkst1), (sV, mark_tw)],
                war_rows=wr)

        mk_bc2 = emit_ln_stats(
            g, TT, lambda c: ostream[:, c], xsq, rows, C["c_ones16"],
            C["c_ones_row"], C["c_eps"], pp[2], pp[3], pp[6], pp[7],
            src_ready=[(sV, mark_tw)],
            war_bc=[(sV, mark_tw)])
        if t + 1 < NTILE:
            mk_bc_next = emit_ln_stats_post(
                g, TT, rows2, C["c_ones_row"], C["c_eps"], pp[0], pp[1],
                pp[4], pp[5], mkst1, war_bc=[(sV, mark_tw)])
        for c in range(8):
            w = [(sT, mk_bc2)] if c == 0 else []
            if c == 0 and mark_store is not None:
                w.append((ch_o, mark_store))
            p.do("vector",
                 lambda E, c=c: E.tensor_tensor(xsq[:, c], ostream[:, c],
                                                pp[6][:, 0:TT], ALU.subtract),
                 wait=w)
        p.do("vector", lambda E: E.drain())
        for c in range(8):
            p.do("vector",
                 lambda E, c=c: E.tensor_tensor(tw[:, c], xsq[:, c],
                                                pp[7][:, 0:TT], ALU.mult))
        p.do("vector", lambda E: E.drain())
        for c in range(8):
            p.do("vector",
                 lambda E, c=c, sl=sl: E.tensor_tensor(
                     xo[:, c], sx[:, sl * 8 + c], tw[:, c], ALU.add))
        mark_apply = p.mil("vector", sV)
        mark_apply_hist[sl] = mark_apply
        mark_store = p.dma("sync", xout_sl, xo[:], ch_o,
                           wait=[(sV, mark_apply)])
    g.barrier()


def _prep_l0ffn(inputs):
    d = {}
    for n in ["cn_w1", "cn_w2", "cn_w3", "cn_w4", "gc_w1", "gc_w2", "gc_w3"]:
        d["l0f_" + n] = _as32(inputs["L0_ffn_" + n])
    for n, nch in [("cn_b1", 8), ("cn_b2", 4), ("cn_b3", 2), ("gc_b1", 4),
                   ("gc_b2", 2)]:
        b = _as32(inputs["L0_ffn_" + n])
        d["l0f_" + n] = np.ascontiguousarray(b.reshape(nch, 128).T)
    d["l0f_cn_b4"] = _as32(inputs["L0_ffn_cn_b4"]).reshape(1, 1)
    d["l0f_gc_b3"] = _as32(inputs["L0_ffn_gc_b3"]).reshape(1, 1)
    d["l0f_arng"] = _as32(inputs["L0_ffn_adaptive_range"]).reshape(1, 1)
    return d


def emit_l0attnA(g, xin, C, hand="internal"):
    nc, p = g.nc, g.p
    sT, sV, sS = g.sT, g.sV, g.sS
    TT = 256
    NTILE = T // TT
    H = 4
    DK = 256

    qpT = g.din("l0a_qpT", [D, D])
    kpT = g.din("l0a_kpT", [D, D])
    vpT = g.din("l0a_vpT", [D, D])
    bpT = g.din("l0a_bpT", [D, 4])
    qcv = g.din("l0a_qcv", [128, 8, 4])
    kcv = g.din("l0a_kcv", [128, 8, 4])
    vcv = g.din("l0a_vcv", [128, 8, 4])
    mku = g.din("l0a_mku", [128, 128], BF16)
    mkl = g.din("l0a_mkl", [128, 128], BF16)
    mki = g.din("l0a_mki", [128, 128], BF16)
    i32q = g.din("l0a_i32q", [128, 32], BF16)
    idf = g.din("l0a_idf", [128, 128], BF16)
    dh = g.dint if hand == "internal" else g.dout
    delta_d = dh("l0d_delta", [128, 8, T], BF16)
    vs_d = dh("l0d_vs", [128, 8, 30 + T], BF16)

    pp = g.pool
    qpw = g.sb("l0a_qpw", [128, 8, D], BF16)
    kpw = g.sb("l0a_kpw", [128, 8, D], BF16)
    vpw = g.sb("l0a_vpw", [128, 8, D], BF16)
    bpw = g.sb("l0a_bpw", [128, 8, 4], BF16)
    cvw = {}
    for nm, ap in [("q", qcv), ("k", kcv), ("v", vcv)]:
        cvw[nm] = g.sb("l0a_cv" + nm, [128, 8, 4])
        p.dma("sync", cvw[nm][:], ap, g.dSY)
    smku = g.sb("l0a_smku", [128, 128], BF16)
    smkl = g.sb("l0a_smkl", [128, 128], BF16)
    smki = g.sb("l0a_smki", [128, 128], BF16)
    si32 = g.sb("l0a_si32", [128, 32], BF16)
    sidf = g.sb("l0a_sidf", [128, 128], BF16)
    for t_, a_ in [(smku, mku), (smkl, mkl), (smki, mki), (si32, i32q),
                   (sidf, idf)]:
        p.dma("sync", t_[:], a_, g.dSY)
    for c in range(8):
        p.dma("gpsimd", qpw[:, c], qpT[c * 128:(c + 1) * 128, :], g.dGP)
        p.dma("gpsimd", kpw[:, c], kpT[c * 128:(c + 1) * 128, :], g.dGP)
        p.dma("gpsimd", vpw[:, c], vpT[c * 128:(c + 1) * 128, :], g.dGP)
        p.dma("gpsimd", bpw[:, c], bpT[c * 128:(c + 1) * 128, :], g.dGP)

    sx = g.sb("l0a_sx", [128, 8, TT])
    xsq = g.sb("l0a_xsq", [128, 8, TT])
    rows = g.sb("l0a_rows", [1, 6, TT])
    y1t = g.sb("l0a_y1t", [128, 8, TT], BF16)
    praw = {nm: g.sb("l0a_praw" + nm, [128, 8, TT + 3], BF16)
            for nm in "qkv"}
    acc = {nm: g.sb("l0a_acc" + nm, [128, 8, TT], BF16) for nm in "qkv"}
    qsb = g.sb("l0a_qsb", [128, 8, TT], BF16)
    ksb = g.sb("l0a_ksb", [128, 8, TT], BF16)
    vsb = g.sb("l0a_vsb", [128, 8, 30 + TT], BF16)
    scr8 = g.sb("l0a_scr8", [128, 8, TT])
    nrows = g.sb("l0a_nrows", [1, 12, TT])
    brows = g.sb("l0a_brows", [1, 4, TT])
    qnT = g.sb("l0a_qnT", [128, 8, TT], BF16)
    knT = g.sb("l0a_knT", [128, 8, TT], BF16)
    kbT = g.sb("l0a_kbT", [128, 8, TT], BF16)
    vbT = g.sb("l0a_vbT", [128, 8, TT], BF16)
    kntm = g.sb("l0a_kntm", [128, 4, 2, 256], BF16)
    vbtm = g.sb("l0a_vbtm", [128, 4, 2, 256], BF16)
    pkU = g.sb("l0a_pkU", [128, 4, 32], BF16)
    pkL = g.sb("l0a_pkL", [128, 4, 32], BF16)
    pkP = g.sb("l0a_pkP", [128, 2, 4, 32], BF16)
    pkAT = g.sb("l0a_pkAT", [128, 2, 4, 32], BF16)
    pkIV = g.sb("l0a_pkIV", [128, 2, 4, 32], BF16)
    upk = g.sb("l0a_upk", [128, 4, TT], BF16)
    uipk = g.sb("l0a_uipk", [128, 4, TT], BF16)
    S_sb = [g.sb("l0a_Ssb%d" % h, [128, 2, DK], BF16) for h in range(H)]
    dtok = g.sb("l0a_dtok", [128, 8, TT], BF16)
    deltaT = g.sb("l0a_deltaT", [128, 8, TT], BF16)
    g.barrier()

    ch_x = p.channel("l0a_chx")
    ch_dd = p.channel("l0a_chdd")
    ch_dv = p.channel("l0a_chdv")

    zrow = g.sb("l0a_zrow", [1, 512], BF16)
    zcol = g.sb("l0a_zcol", [1, 128], BF16)
    p.do("vector", lambda E: E.memset(praw["q"][:, :, 0:3], 0.0))
    p.do("vector", lambda E: E.memset(praw["k"][:, :, 0:3], 0.0))
    p.do("vector", lambda E: E.memset(praw["v"][:, :, 0:3], 0.0))
    p.do("vector", lambda E: E.memset(vsb[:, :, 0:30], 0.0))
    p.do("vector", lambda E: E.memset(zrow[:], 0.0))
    p.do("vector", lambda E: E.memset(zcol[:], 0.0))
    for h in range(H):
        p.do("vector", lambda E, h=h: E.memset(S_sb[h][:], 0.0))
    m_init = p.mil("vector", sV)
    for h in range(H):
        p.do("tensor",
             lambda E, h=h: E.matmul(pp[h][:, 0:512], zcol[:], zrow[:],
                                     start=True, stop=True),
             wait=[(sV, m_init)] if h == 0 else ())
    m_szero = p.mil("tensor", sT)
    for e in ENGINES:
        p.wait(e, sT, m_szero)

    PQ = lambda m: slice(32 * m, 32 * m + 32)
    mark_use = m_init
    m_pevac = [None, None, None, None]
    m_tile_end_v = None
    m_tile_end_s = None
    m_bc_used = None
    m_bc_used_s = None
    m_sq_used = None
    m_scaled = None
    m_tm_used = None
    m_vdump = None
    m_ddump = None
    m_Sref = [m_init] * H
    m_Sref_sem = [sV] * H

    for t in range(NTILE):
        t0 = t * TT
        first = (t == 0)
        xin_sl = xin[:, t0:t0 + TT].rearrange("(c p) t -> p c t", p=128)
        mk_x = p.dma("sync", sx[:], xin_sl, ch_x, wait=[(sV, mark_use)])
        mk_bc = emit_ln_stats(
            g, TT, lambda c: sx[:, c], xsq, rows, C["c_ones16"],
            C["c_ones_row"], C["c_eps"], pp[4][:, 256:512], pp[5][:, 256:512],
            pp[4][:, 0:256], pp[5][:, 0:256],
            src_ready=[(ch_x, mk_x)],
            war_rows=([(sV, m_bc_used), (sS, m_bc_used_s)]
                      if m_bc_used else ()),
            war_bc=([(sV, m_bc_used), (sS, m_bc_used_s)]
                    if m_bc_used else ()))
        for c in range(8):
            p.do("vector",
                 lambda E, c=c: E.tensor_tensor(xsq[:, c], sx[:, c],
                                                pp[4][:, 0:TT], ALU.subtract),
                 wait=[(sT, mk_bc)] if c == 0 else ())
        p.do("vector", lambda E: E.drain())
        for c in range(8):
            p.do("vector",
                 lambda E, c=c: E.tensor_tensor(y1t[:, c], xsq[:, c],
                                                pp[5][:, 0:TT], ALU.mult))
        mark_use = p.mil("vector", sV)

        if not first:
            for nm in "qkv":
                for c in range(8):
                    p.do("vector",
                         lambda E, nm=nm, c=c: E.tensor_copy(
                             praw[nm][:, c, 0:3], praw[nm][:, c, TT:TT + 3]))
            for c in range(8):
                p.do("vector",
                     lambda E, c=c: E.tensor_copy(vsb[:, c, 0:30],
                                                  vsb[:, c, TT:TT + 30]),
                     wait=[(ch_dv, m_vdump)] if c == 0 else ())
        m_tail = p.mil("vector", sV)

        wsl = [pp[6][:, 0:256], pp[7][:, 0:256], pp[6][:, 256:512],
               pp[7][:, 256:512]]
        wi = 0
        m_prawm = {}
        for nm, ww in [("q", qpw), ("k", kpw), ("v", vpw)]:
            for c in range(8):
                bank = wsl[wi % 4]
                for d in range(8):
                    w = []
                    if d == 0:
                        w = [(sV, mark_use)]
                        if m_pevac[wi % 4] is not None:
                            w.append((sS, m_pevac[wi % 4]))
                        if m_pevac[(wi + 2) % 4] is not None:
                            w.append((sS, m_pevac[(wi + 2) % 4]))
                        if wi == 0 and m_tile_end_v is not None:
                            w.append((sV, m_tile_end_v))
                        if wi == 0 and m_tile_end_s is not None:
                            w.append((sS, m_tile_end_s))
                    p.do("tensor",
                         lambda E, ww=ww, c=c, d=d, bank=bank:
                         E.matmul(bank, ww[:, d, c * 128:(c + 1) * 128],
                                  y1t[:, d], start=(d == 0), stop=(d == 7)),
                         wait=w, inc=(sT if d == 7 else None))
                mm = p.mark(sT)
                m_pevac[wi % 4] = p.do(
                    "scalar",
                    lambda E, nm=nm, c=c, bank=bank: E.activation(
                        praw[nm][:, c, 3:3 + TT], bank, AF.Identity),
                    wait=[(sT, mm), (sV, m_tail)], inc=sS)
                wi += 1
            m_prawm[nm] = p.mark(sS)
        for h in range(H):
            bank = wsl[wi % 4]
            for d in range(8):
                w = []
                if d == 0:
                    w = [(sV, mark_use)]
                    if m_pevac[wi % 4] is not None:
                        w.append((sS, m_pevac[wi % 4]))
                    if m_pevac[(wi + 2) % 4] is not None:
                        w.append((sS, m_pevac[(wi + 2) % 4]))
                p.do("tensor",
                     lambda E, h=h, d=d, bank=bank: E.matmul(
                         bank[0:1, :], bpw[:, d, h:h + 1], y1t[:, d],
                         start=(d == 0), stop=(d == 7)),
                     wait=w, inc=(sT if d == 7 else None))
            mm = p.mark(sT)
            m_pevac[wi % 4] = p.do(
                "scalar",
                lambda E, h=h, bank=bank: E.activation(
                    brows[:, h], bank[0:1, :], AF.Sigmoid),
                wait=[(sT, mm)], inc=sS)
            wi += 1

        m_silu_nm = {}
        for nm in "qkv":
            for j in range(4):
                for c in range(8):
                    if j == 0:
                        w = []
                        if c == 0:
                            w = [(sS, m_prawm[nm])]
                            if nm == "q" and m_scaled is not None:
                                w.append((sV, m_scaled))
                        p.do("vector",
                             lambda E, nm=nm, c=c: E.tensor_scalar_mul(
                                 acc[nm][:, c], praw[nm][:, c, 0:TT],
                                 cvw[nm][:, c, 0:1]),
                             wait=w)
                    else:
                        p.do("vector",
                             lambda E, nm=nm, c=c, j=j: E.scalar_tensor_tensor(
                                 acc[nm][:, c], praw[nm][:, c, j:j + TT],
                                 cvw[nm][:, c, j:j + 1], acc[nm][:, c],
                                 ALU.mult, ALU.add))
                p.do("vector", lambda E: E.drain())
            m_conv = p.mil("vector", sV)
            for c in range(8):
                w = [(sV, m_conv)] if c == 0 else []
                if c == 0 and m_scaled is not None:
                    w.append((sV, m_scaled))
                if nm == "q":
                    p.do("scalar",
                         lambda E, c=c: E.activation(qsb[:, c], acc["q"][:, c],
                                                     AF.Silu), wait=w)
                elif nm == "k":
                    p.do("scalar",
                         lambda E, c=c: E.activation(ksb[:, c], acc["k"][:, c],
                                                     AF.Silu), wait=w)
                else:
                    p.do("scalar",
                         lambda E, c=c: E.activation(vsb[:, c, 30:30 + TT],
                                                     acc["v"][:, c], AF.Silu),
                         wait=w)
            m_silu_nm[nm] = p.mil("scalar", sS)
        m_silu = m_silu_nm["v"]

        w = [(sS, m_silu)]
        if m_vdump is not None:
            w.append((ch_dv, m_vdump))
        if first:
            m_vdump = p.dma("sync", vs_d[:, :, 0:30 + TT], vsb[:], ch_dv,
                            wait=w)
        else:
            m_vdump = p.dma("sync", vs_d[:, :, 30 + t0:30 + t0 + TT],
                            vsb[:, :, 30:30 + TT], ch_dv, wait=w)

        for c in range(8):
            w = [(sS, m_silu_nm["q"])] if c == 0 else []
            if c == 0 and m_sq_used is not None:
                w.append((sV, m_sq_used))
            p.do("scalar",
                 lambda E, c=c: E.activation(scr8[:, c], qsb[:, c],
                                             AF.Square), wait=w)
        m_sq = p.mil("scalar", sS)
        rsl = [pp[6][0:1, 0:256], pp[7][0:1, 0:256], pp[6][0:1, 256:512],
               pp[7][0:1, 256:512]]
        m_rev = list(m_pevac)
        for h in range(H):
            bank = rsl[h]
            for i in range(2):
                w = [(sS, m_sq)] if i == 0 else []
                if i == 0:
                    for mx in (m_rev[h], m_rev[(h + 2) % 4]):
                        if mx is not None:
                            w.append((sS, mx))
                p.do("tensor",
                     lambda E, h=h, i=i, bank=bank: E.matmul(
                         bank, C["c_ones_col"][:], scr8[:, 2 * h + i],
                         start=(i == 0), stop=(i == 1)),
                     wait=w, inc=(sT if i == 1 else None))
            mm = p.mark(sT)
            m_rev[h] = p.do(
                "scalar",
                lambda E, h=h, bank=bank: E.activation(nrows[:, h], bank,
                                                       AF.Identity),
                wait=[(sT, mm)], inc=sS)
        for c in range(8):
            p.do("scalar",
                 lambda E, c=c: E.activation(scr8[:, c], ksb[:, c],
                                             AF.Square))
        m_sq2 = p.mil("scalar", sS)
        for h in range(H):
            bank = rsl[h]
            for i in range(2):
                w = []
                if i == 0:
                    w = [(sS, m_sq2), (sS, m_rev[h])]
                    if m_rev[(h + 2) % 4] is not None:
                        w.append((sS, m_rev[(h + 2) % 4]))
                p.do("tensor",
                     lambda E, h=h, i=i, bank=bank: E.matmul(
                         bank, C["c_ones_col"][:], scr8[:, 2 * h + i],
                         start=(i == 0), stop=(i == 1)),
                     wait=w, inc=(sT if i == 1 else None))
            mm = p.mark(sT)
            m_rev[h] = p.do(
                "scalar",
                lambda E, h=h, bank=bank: E.activation(nrows[:, 4 + h], bank,
                                                       AF.Identity),
                wait=[(sT, mm)], inc=sS)
        m_pevac = [None, None, None, None]
        m_rows0 = p.mil("scalar", sS)

        p.do("scalar",
             lambda E: E.activation(
                 nrows[:, 8:12, :].rearrange("o a t -> o (a t)"),
                 nrows[:, 0:4, :].rearrange("o a t -> o (a t)"),
                 AF.Sqrt, bias=C["c_eps12"][0:1, 0:1]),
             wait=[(sS, m_rows0)])
        m_s1 = p.mil("scalar", sS)
        p.do("vector",
             lambda E: E.reciprocal(
                 nrows[:, 0:4, :].rearrange("o a t -> o (a t)"),
                 nrows[:, 8:12, :].rearrange("o a t -> o (a t)")),
             wait=[(sS, m_s1)])
        m_v1 = p.mil("vector", sV)
        p.do("scalar",
             lambda E: E.activation(
                 nrows[:, 8:12, :].rearrange("o a t -> o (a t)"),
                 nrows[:, 4:8, :].rearrange("o a t -> o (a t)"),
                 AF.Sqrt, bias=C["c_eps12"][0:1, 0:1]),
             wait=[(sV, m_v1)])
        m_s2 = p.mil("scalar", sS)
        p.do("vector",
             lambda E: E.reciprocal(
                 nrows[:, 4:8, :].rearrange("o a t -> o (a t)"),
                 nrows[:, 8:12, :].rearrange("o a t -> o (a t)")),
             wait=[(sS, m_s2)])
        p.do("vector", lambda E: E.drain())
        p.do("vector",
             lambda E: E.tensor_tensor(
                 nrows[:, 8:12, :].rearrange("o a t -> o (a t)"),
                 nrows[:, 4:8, :].rearrange("o a t -> o (a t)"),
                 brows[:, :, :].rearrange("o a t -> o (a t)"), ALU.mult))
        m_rows = p.mil("vector", sV)

        bsl = [pp[4][:, 0:256], pp[4][:, 256:512], pp[5][:, 0:256],
               pp[5][:, 256:512]]
        for h in range(H):
            srcs = [nrows[0:1, h], nrows[0:1, 4 + h], nrows[0:1, 8 + h],
                    brows[0:1, h]]
            mkb = None
            for k in range(4):
                w = [(sV, m_rows)]
                if h > 0:
                    w.append((sV, m_scaled))
                elif m_bc_used is not None:
                    w.append((sV, m_bc_used))
                mkb = p.do("tensor",
                           lambda E, k=k, src=srcs[k]: E.matmul(
                               bsl[k], C["c_ones_row"][:], src,
                               start=True, stop=True),
                           wait=w, inc=(sT if k == 3 else None))
            for i in range(2):
                c = 2 * h + i
                p.do("vector",
                     lambda E, c=c: E.tensor_tensor(qnT[:, c], qsb[:, c],
                                                    bsl[0], ALU.mult),
                     wait=[(sT, mkb)] if i == 0 else ())
                p.do("vector",
                     lambda E, c=c: E.tensor_tensor(knT[:, c], ksb[:, c],
                                                    bsl[1], ALU.mult))
                p.do("vector",
                     lambda E, c=c: E.tensor_tensor(kbT[:, c], ksb[:, c],
                                                    bsl[2], ALU.mult))
                p.do("vector",
                     lambda E, c=c: E.tensor_tensor(vbT[:, c],
                                                    vsb[:, c, 30:30 + TT],
                                                    bsl[3], ALU.mult))
            m_scaled = p.mil("vector", sV)
        m_sq_used = m_scaled

        tbv6 = pp[6].bitcast(BF16)
        tbv7 = pp[7].bitcast(BF16)
        tsl = [tbv6[:, 0:128], tbv7[:, 0:128], tbv6[:, 128:256],
               tbv7[:, 128:256]]
        ti = 0
        m_tevac = [None, None, None, None]
        for src, dst in [(knT, kntm), (vbT, vbtm)]:
            for h in range(H):
                for j in range(2):
                    for b2 in range(2):
                        bank = tsl[ti % 4]
                        w = [(sV, m_scaled), (sS, m_rows0)]
                        for mx in (m_tevac[ti % 4], m_tevac[(ti + 2) % 4]):
                            if mx is not None:
                                w.append((sV, mx))
                        if ti == 0 and m_tm_used is not None:
                            w.append((sT, m_tm_used))
                        mm = p.do("tensor",
                                  lambda E, src=src, h=h, j=j, b2=b2,
                                  bank=bank: E.transpose(
                                      bank,
                                      src[:, 2 * h + j,
                                          b2 * 128:(b2 + 1) * 128],
                                      sidf[:]),
                                  wait=w, inc=sT)
                        m_tevac[ti % 4] = p.do(
                            "vector",
                            lambda E, dst=dst, h=h, j=j, b2=b2, bank=bank:
                            E.tensor_copy(
                                dst[:, h, b2, j * 128:(j + 1) * 128], bank),
                            wait=[(sT, mm)], inc=sV)
                        ti += 1
        m_tm_evac = p.mil("vector", sV)

        pA = pp[6][:, 0:128]
        pB = pp[6][:, 128:256]
        pC = pp[6][:, 256:384]
        pD = pp[6][:, 384:512]
        m_prep_v = m_tm_evac
        for wv in range(2):
            for m in range(4):
                ci = 4 * wv + m
                for h in range(H):
                    for i in range(2):
                        w = []
                        if m == 0 and h == 0 and i == 0:
                            w = [(sV, m_prep_v), (sV, m_tm_evac)]
                        p.do("tensor",
                             lambda E, m=m, h=h, i=i, ci=ci: E.matmul(
                                 pA[PQ(m), 32 * h:32 * h + 32],
                                 knT[:, 2 * h + i, 32 * ci:32 * ci + 32],
                                 kbT[:, 2 * h + i, 32 * ci:32 * ci + 32],
                                 start=(i == 0), stop=(i == 1),
                                 tile_position=(0, 32 * m)),
                             wait=w)
                    for i in range(2):
                        p.do("tensor",
                             lambda E, m=m, h=h, i=i, ci=ci: E.matmul(
                                 pB[PQ(m), 32 * h:32 * h + 32],
                                 kbT[:, 2 * h + i, 32 * ci:32 * ci + 32],
                                 knT[:, 2 * h + i, 32 * ci:32 * ci + 32],
                                 start=(i == 0), stop=(i == 1),
                                 tile_position=(0, 32 * m)))
                    for i in range(2):
                        p.do("tensor",
                             lambda E, m=m, h=h, i=i, ci=ci: E.matmul(
                                 pC[PQ(m), 32 * h:32 * h + 32],
                                 knT[:, 2 * h + i, 32 * ci:32 * ci + 32],
                                 qnT[:, 2 * h + i, 32 * ci:32 * ci + 32],
                                 start=(i == 0), stop=(i == 1),
                                 tile_position=(0, 32 * m)))
            m_amm = p.mil("tensor", sT)
            p.do("vector",
                 lambda E: E.tensor_tensor(
                     pkU.rearrange("p a b -> p (a b)"), pA, smku[:],
                     ALU.mult),
                 wait=[(sT, m_amm)])
            p.do("vector",
                 lambda E: E.tensor_tensor(
                     pkL.rearrange("p a b -> p (a b)"), pB, smkl[:],
                     ALU.mult))
            p.do("vector",
                 lambda E, wv=wv: E.tensor_tensor(
                     pkAT[:, wv].rearrange("p a b -> p (a b)"), pC, smki[:],
                     ALU.mult))
            m_UL = p.mil("vector", sV)
            for m in range(4):
                for h in range(H):
                    p.do("tensor",
                         lambda E, m=m, h=h: E.matmul(
                             pD[PQ(m), 32 * h:32 * h + 32],
                             si32[PQ(m), :], pkU[PQ(m), h, :],
                             start=True, stop=False,
                             tile_position=(32 * m, 32 * m)),
                         wait=[(sV, m_UL)] if (m == 0 and h == 0) else ())
                    p.do("tensor",
                         lambda E, m=m, h=h: E.matmul(
                             pD[PQ(m), 32 * h:32 * h + 32],
                             si32[PQ(m), :], si32[PQ(m), :],
                             start=False, stop=True,
                             tile_position=(32 * m, 32 * m)))
            m_pmm = p.mil("tensor", sT)
            p.do("vector",
                 lambda E: E.tensor_copy(
                     pkP[:, 0].rearrange("p a b -> p (a b)"), pD),
                 wait=[(sT, m_pmm)])
            m_P = p.mil("vector", sV)
            cur = 0
            for lvl in range(4):
                last = (lvl == 3)
                for m in range(4):
                    for h in range(H):
                        w = [(sV, m_P)] if (m == 0 and h == 0) else []
                        if not last:
                            p.do("tensor",
                                 lambda E, m=m, h=h: E.matmul(
                                     pA[PQ(m), 32 * h:32 * h + 32],
                                     pkL[PQ(m), h, :], pkU[PQ(m), h, :],
                                     start=True, stop=True,
                                     tile_position=(32 * m, 32 * m)),
                                 wait=w)
                            w = []
                        p.do("tensor",
                             lambda E, m=m, h=h: E.matmul(
                                 pB[PQ(m), 32 * h:32 * h + 32],
                                 pkU[PQ(m), h, :], pkL[PQ(m), h, :],
                                 start=True, stop=True,
                                 tile_position=(32 * m, 32 * m)),
                             wait=w)
                m_sqm = p.mil("tensor", sT)
                if not last:
                    p.do("vector",
                         lambda E: E.tensor_copy(
                             pkU.rearrange("p a b -> p (a b)"), pA),
                         wait=[(sT, m_sqm)])
                p.do("vector",
                     lambda E: E.tensor_copy(
                         pkL.rearrange("p a b -> p (a b)"), pB),
                     wait=[(sT, m_sqm)])
                m_UL = p.mil("vector", sV)
                nxt = 1 - cur
                for m in range(4):
                    for h in range(H):
                        w = [(sV, m_UL)] if (m == 0 and h == 0) else []
                        p.do("tensor",
                             lambda E, m=m, h=h, cur=cur: E.matmul(
                                 pD[PQ(m), 32 * h:32 * h + 32],
                                 pkL[PQ(m), h, :], pkP[PQ(m), cur, h, :],
                                 start=True, stop=False,
                                 tile_position=(32 * m, 32 * m)),
                             wait=w)
                        p.do("tensor",
                             lambda E, m=m, h=h, cur=cur: E.matmul(
                                 pD[PQ(m), 32 * h:32 * h + 32],
                                 si32[PQ(m), :], pkP[PQ(m), cur, h, :],
                                 start=False, stop=True,
                                 tile_position=(32 * m, 32 * m)))
                m_pmm = p.mil("tensor", sT)
                p.do("vector",
                     lambda E, nxt=nxt: E.tensor_copy(
                         pkP[:, nxt].rearrange("p a b -> p (a b)"), pD),
                     wait=[(sT, m_pmm)])
                m_P = p.mil("vector", sV)
                cur = nxt
            p.do("vector",
                 lambda E, wv=wv, cur=cur: E.tensor_copy(
                     pkIV[:, wv].rearrange("p a b -> p (a b)"),
                     pkP[:, cur].rearrange("p a b -> p (a b)")))
            m_prep_v = p.mil("vector", sV)
        m_prep_done = m_prep_v

        zb = [pp[4], pp[5]]
        oisl = [pp[6][:, 0:256], pp[7][:, 0:256]]
        m_bankio = [(sV, m_scaled), (sV, m_scaled)]
        m_oi_done = [(sV, m_prep_done), (sV, m_prep_done)]
        m_upk_used = [m_prep_done] * H
        m_uipk_used = [None] * H
        m_oi_last = None
        for ci in range(8):
            m = ci % 4
            wv = ci // 4
            for h in range(H):
                pi = (ci * H + h) % 2
                for i in range(2):
                    w = []
                    if i == 0:
                        w = [(m_Sref_sem[h], m_Sref[h]), m_bankio[pi],
                             (sV, m_prep_done)]
                    p.do("tensor",
                         lambda E, m=m, h=h, i=i, ci=ci, pi=pi: E.matmul(
                             zb[pi][PQ(m), 0:256],
                             kbT[:, 2 * h + i, 32 * ci:32 * ci + 32],
                             S_sb[h][:, i, :], start=(i == 0), stop=(i == 1),
                             tile_position=(0, 32 * m)),
                         wait=w, inc=(sT if i == 1 else None))
                mm_z = p.mark(sT)
                m_up = p.do(
                    "vector",
                    lambda E, m=m, h=h, wv=wv, pi=pi: E.tensor_tensor(
                        upk[PQ(m), h, :], vbtm[PQ(m), h, wv, :],
                        zb[pi][PQ(m), 0:256], ALU.subtract),
                    wait=[(sT, mm_z), (sT, m_upk_used[h])], inc=sV)
                mm_ui = p.do(
                    "tensor",
                    lambda E, m=m, h=h, wv=wv, pi=pi: E.matmul(
                        zb[pi][PQ(m), 256:512], pkIV[PQ(m), wv, h, :],
                        upk[PQ(m), h, :], start=True, stop=True,
                        tile_position=(32 * m, 32 * m)),
                    wait=[(sV, m_up)], inc=sT)
                wev = [(sT, mm_ui)]
                if m_uipk_used[h] is not None:
                    wev.append((sT, m_uipk_used[h]))
                m_uiev = p.do(
                    "scalar",
                    lambda E, m=m, h=h, pi=pi: E.activation(
                        uipk[PQ(m), h, :], zb[pi][PQ(m), 256:512],
                        AF.Identity),
                    wait=wev, inc=sS)
                m_bankio[pi] = (sS, m_uiev)
                oz = ci % 2
                for i in range(2):
                    w = []
                    if i == 0:
                        w = [(m_Sref_sem[h], m_Sref[h])]
                        if h == 0:
                            w.append(m_oi_done[oz])
                    p.do("tensor",
                         lambda E, m=m, h=h, i=i, ci=ci, oz=oz: E.matmul(
                             oisl[oz][32 * h:32 * h + 32, :],
                             qnT[:, 2 * h + i, 32 * ci:32 * ci + 32],
                             S_sb[h][:, i, :], start=(i == 0), stop=False,
                             tile_position=(0, 32 * h)),
                         wait=w)
                m_oi_last = p.do(
                    "tensor",
                    lambda E, m=m, h=h, wv=wv, oz=oz: E.matmul(
                        oisl[oz][32 * h:32 * h + 32, :],
                        pkAT[PQ(m), wv, h, :], uipk[PQ(m), h, :],
                        start=False, stop=True,
                        tile_position=(32 * m, 32 * h)),
                    wait=[(sS, m_uiev)], inc=sT)
                for i in range(2):
                    p.do("tensor",
                         lambda E, m=m, h=h, wv=wv, i=i, t=t, ci=ci:
                         E.matmul(
                             pp[h][:, i * 256:(i + 1) * 256],
                             kntm[PQ(m), h, wv, i * 128:(i + 1) * 128],
                             uipk[PQ(m), h, :],
                             start=False, stop=True,
                             skip_group_check=True,
                             tile_position=(32 * m, 0)),
                         inc=(sT if i == 1 else None))
                m_supd = p.mark(sT)
                m_uipk_used[h] = m_supd
                if (ci + h) % 2 == 0:
                    for i in range(2):
                        p.do("scalar",
                             lambda E, h=h, i=i: E.activation(
                                 S_sb[h][:, i, :],
                                 pp[h][:, i * 256:(i + 1) * 256],
                                 AF.Identity),
                             wait=[(sT, m_supd)] if i == 0 else ())
                    m_Sref[h] = p.mil("scalar", sS)
                    m_Sref_sem[h] = sS
                else:
                    for i in range(2):
                        p.do("vector",
                             lambda E, h=h, i=i: E.tensor_copy(
                                 S_sb[h][:, i, :],
                                 pp[h][:, i * 256:(i + 1) * 256]),
                             wait=[(sT, m_supd)] if i == 0 else ())
                    m_Sref[h] = p.mil("vector", sV)
                    m_Sref_sem[h] = sV
            w = [(sT, m_oi_last)]
            if t > 0 and ci < 2 and m_tm_used is not None:
                w.append((sT, m_tm_used))
            if ci % 2 == 0:
                mk = p.do("scalar",
                          lambda E, ci=ci, oz=ci % 2: E.activation(
                              dtok[:, ci, :], oisl[oz], AF.Identity),
                          wait=w, inc=sS)
                m_oi_done[ci % 2] = (sS, mk)
            else:
                mk = p.do("vector",
                          lambda E, ci=ci, oz=ci % 2: E.tensor_copy(
                              dtok[:, ci, :], oisl[oz]),
                          wait=w, inc=sV)
                m_oi_done[ci % 2] = (sV, mk)
        m_scan_v = p.mil("vector", sV)
        m_scan_s = p.mil("scalar", sS)
        m_bc_used = m_scan_v
        m_bc_used_s = m_scan_s

        t7sl = [tbv6[:, 0:128], tbv7[:, 0:128], tbv6[:, 128:256],
                tbv7[:, 128:256]]
        dvw = deltaT.rearrange("p (hh o) t -> p hh o t", o=2)
        ti = 0
        m_t7 = [None] * 4
        m_tm_used = None
        for ci in range(8):
            for half in range(2):
                bank = t7sl[ti % 4]
                w = [m_oi_done[0], m_oi_done[1],
                     (sV, m_scan_v), (sS, m_scan_s)] if ti == 0 else []
                for mx in (m_t7[ti % 4], m_t7[(ti + 2) % 4]):
                    if mx is not None:
                        w = list(w) + [(sV, mx)]
                mm = p.do("tensor",
                          lambda E, ci=ci, half=half, bank=bank: E.transpose(
                              bank, dtok[:, ci, half * 128:(half + 1) * 128],
                              sidf[:]),
                          wait=w, inc=sT)
                m_tm_used = mm
                wev = [(sT, mm)]
                if t > 0 and ti == 0:
                    wev.append((ch_dd, m_ddump))
                m_t7[ti % 4] = p.do(
                    "vector",
                    lambda E, ci=ci, half=half, bank=bank: E.tensor_copy(
                        dvw[:, :, half, 32 * ci:32 * ci + 32], bank),
                    wait=wev, inc=sV)
                ti += 1
        m_dT = p.mil("vector", sV)
        m_ddump = p.dma("sync", delta_d[:, :, t0:t0 + TT], deltaT[:], ch_dd,
                        wait=[(sV, m_dT)])
        m_tile_end_v = m_dT
        m_tile_end_s = m_scan_s
    g.barrier()

def emit_l0attnB(g, xin, xout, C, hand="internal"):
    nc, p = g.nc, g.p
    sT, sV, sS, sG = g.sT, g.sV, g.sS, g.sG
    TT = 256
    NTILE = T // TT
    H = 4

    dh = g.dram.get
    if "l0d_delta" in g.dram:
        delta_d = g.dram["l0d_delta"]
        vs_d = g.dram["l0d_vs"]
    else:
        delta_d = g.din("l0d_delta", [128, 8, T], BF16)
        vs_d = g.din("l0d_vs", [128, 8, 30 + T], BF16)
    g1T = g.din("l0a_g1T", [1048, 2 * D])
    g2T = g.din("l0a_g2T", [2 * D, 24])
    opT = g.din("l0a_opT", [D, D])
    fw = {k: g.din("l0a_fw%d" % k, [128, 8, k]) for k in (3, 7, 15, 31)}
    gb1 = g.din("l0a_gb1", [128, 16])
    gb2a = g.din("l0a_gb2", [24, 1])
    alst = g.din("l0a_alst", [24, 1])
    ltmp = g.din("l0a_ltmp", [4, 1])
    texp = g.din("l0a_texp", [4, 24])
    onw = g.din("l0a_onw", [128, 2])
    stpat = g.din("l0a_stpat", [128, 47], BF16)
    orow = g.din("l0a_orow", [1, 128], BF16)
    d32_d = g.din("l0b_d32", [8, 128, 56 * 32], BF16)
    prb_scr = g.dint("l0b_prbscr", [128, 2, 24])

    pp = g.pool
    g1w = g.sb("l0b_g1w", [128, 8, 2 * D], BF16)
    g1s = g.sb("l0b_g1s", [24, 2 * D], BF16)
    g2w = g.sb("l0b_g2w", [128, 16, 24], BF16)
    opw = g.sb("l0b_opw", [128, 8, D], BF16)
    sfw = {k: g.sb("l0b_fw%d" % k, [128, 8, k]) for k in (3, 7, 15, 31)}
    sgb1 = g.sb("l0b_gb1", [128, 16])
    sgb2 = g.sb("l0b_gb2", [24, 1])
    salst = g.sb("l0b_alst", [24, 1])
    sltmp = g.sb("l0b_ltmp", [4, 1])
    stexp = g.sb("l0b_texp", [4, 24])
    sonw = g.sb("l0b_onw", [128, 2])
    sstp = g.sb("l0b_stpat", [128, 47], BF16)
    sorow = g.sb("l0b_orow", [1, 128], BF16)
    tscr = g.sb("l0b_tscr", [4, 8])
    tinv24 = g.sb("l0b_tinv", [24, 1])
    d32b = g.sb("l0b_d32b", [128, 8, 56, 32], BF16)
    zrowB = g.sb("l0b_zrow", [1, 512], BF16)
    zcolB = g.sb("l0b_zcol", [1, 128], BF16)
    for c in range(8):
        p.dma("gpsimd", g1w[:, c], g1T[c * 128:(c + 1) * 128, :], g.dGP)
        p.dma("gpsimd", opw[:, c], opT[c * 128:(c + 1) * 128, :], g.dGP)
    p.dma("gpsimd", g1s[:], g1T[1024:1048, :], g.dGP)
    for jc in range(16):
        p.dma("gpsimd", g2w[:, jc], g2T[jc * 128:(jc + 1) * 128, :], g.dGP)
    for k in (3, 7, 15, 31):
        p.dma("sync", sfw[k][:], fw[k], g.dSY)
    for c in range(8):
        p.dma("sync", d32b[:, c].rearrange("p a b -> p (a b)"),
              d32_d[c], g.dSY)
    for t_, a_ in [(sgb1, gb1), (sgb2, gb2a), (salst, alst), (sltmp, ltmp),
                   (stexp, texp), (sonw, onw), (sstp, stpat), (sorow, orow)]:
        p.dma("sync", t_[:], a_, g.dSY)

    sx = g.sb("l0b_sx", [128, 16, TT])
    xsq = g.sb("l0b_xsq", [128, 8, TT])
    xsq2 = g.sb("l0b_xsq2", [128, 8, TT])
    rows = g.sb("l0b_rows", [1, 5, TT])
    rows2 = g.sb("l0b_rows2", [1, 5, TT])
    y1b = g.sb("l0b_y1b", [128, 8, TT], BF16)
    vst = g.sb("l0b_vst", [128, 8, 32 + TT], BF16)
    dTt = g.sb("l0b_dT", [128, 8, TT], BF16)
    fbuf = {k: g.sb("l0b_f%d" % k, [128, 8, TT], BF16) for k in (3, 7, 15, 31)}
    babs = g.sb("l0b_babs", [128, 8, TT], BF16)
    stats24 = g.sb("l0b_st24", [24, TT], BF16)
    h1b = g.sb("l0b_h1b", [128, 16, TT], BF16)
    lgt = g.sb("l0b_lgt", [24, TT])
    prb = g.sb("l0b_prb", [128, 2, 24])
    pex = g.sb("l0b_pex", [128, 2, 24])
    mxs = g.sb("l0b_mxs", [128, 2, 12])
    probsR = g.sb("l0b_probsR", [1, 2, 128, 24], BF16)
    bcb = g.sb("l0b_bcb", [128, 6, TT], BF16)
    mixt = g.sb("l0b_mixt", [128, 12, TT], BF16)
    o_acc = xsq
    nr2 = g.sb("l0b_nr2", [1, 8, TT])
    ob = babs
    fo = g.sb("l0b_fo", [128, 8, TT])
    g.barrier()

    p.do("vector", lambda E: E.memset(zrowB[:], 0.0))
    p.do("vector", lambda E: E.memset(zcolB[:], 0.0))
    m_zB = p.mil("vector", sV)

    p.do("vector", lambda E: E.scalar_tensor_tensor(
        tscr[:, 0:1], sltmp[:], -1.0, sltmp[:], ALU.mult, ALU.max))
    m_v = p.mil("vector", sV)
    p.do("scalar", lambda E: E.activation(tscr[:, 1:2], tscr[:, 0:1], AF.Exp,
                                          scale=-1.0), wait=[(sV, m_v)])
    m_s = p.mil("scalar", sS)
    p.do("scalar", lambda E: E.activation(tscr[:, 2:3], tscr[:, 1:2], AF.Ln,
                                          bias=C["c_ones_col"][0:4, 0:1]),
         wait=[(sS, m_s)])
    m_s = p.mil("scalar", sS)
    p.do("vector", lambda E: E.tensor_scalar_max(tscr[:, 3:4], sltmp[:], 0.0),
         wait=[(sS, m_s)])
    p.do("vector", lambda E: E.drain())
    p.do("vector", lambda E: E.tensor_tensor(tscr[:, 4:5], tscr[:, 3:4],
                                             tscr[:, 2:3], ALU.add))
    p.do("vector", lambda E: E.drain())
    p.do("vector", lambda E: E.tensor_scalar_add(tscr[:, 5:6], tscr[:, 4:5],
                                                 1e-4))
    m_v = p.mil("vector", sV)
    mm_t = p.do("tensor",
                lambda E: E.matmul(pp[2][0:24, 0:1], stexp[:], tscr[:, 5:6],
                                   start=True, stop=True),
                wait=[(sV, m_v)], inc=sT)
    p.do("vector", lambda E: E.reciprocal(tinv24[:], pp[2][0:24, 0:1]),
         wait=[(sT, mm_t)])
    m_temp = p.mil("vector", sV)

    ch_x = [p.channel("l0b_chx0"), p.channel("l0b_chx1")]
    ch_v = p.channel("l0b_chv")
    ch_d = p.channel("l0b_chd")
    ch_p1 = p.channel("l0b_chp1")
    ch_p2 = p.channel("l0b_chp2")
    ch_o = p.channel("l0b_cho")
    TAPS = []
    for k in (3, 7, 15, 31):
        for j in range(k):
            TAPS.append((k, j, 31 - k + j))

    mark_use = m_temp
    mark_use_hist = [None, None]
    mk_bc_next = None
    mkst1 = None
    m_fir_used = None
    m_fir_used_s = None
    m_fir_pe = None
    m_tile_s = None
    m_st_used = None
    m_bank01 = [None, None]
    m_gbank = [None] * 4
    m_lgt_used = None
    m_prb_out = None
    m_prbR_used = None
    m_bcb_used = [None] * 6
    m_mix_done = None
    m_oacc_used = None
    m_rms_used = None
    m_ob_used = None
    mark_store = None
    mark_ln2 = None
    m_stats_ev = None

    for t in range(NTILE):
        t0 = t * TT
        sl = t % 2
        xout_sl = xout[:, t0:t0 + TT].rearrange("(c p) t -> p c t", p=128)
        w = [(sV, m_fir_used)] if m_fir_used else []
        if m_fir_used_s:
            w.append((sS, m_fir_used_s))
        if m_fir_pe is not None:
            w.append((sT, m_fir_pe))
        mk_v = p.dma("sync", vst[:, :, 0:30 + TT],
                     vs_d[:, :, t0:t0 + 30 + TT], ch_v, wait=w)
        mk_d = p.dma("sync", dTt[:], delta_d[:, :, t0:t0 + TT], ch_d, wait=w)

        if t == 0:
            xin_sl = xin[:, t0:t0 + TT].rearrange("(c p) t -> p c t", p=128)
            mk_x = p.dma("sync", sx[:, 0:8], xin_sl, ch_x[0],
                         wait=[(sV, mark_use)])
            mkst1 = emit_ln_stats_pre(
                g, TT, lambda c: sx[:, c], xsq2, C["c_ones16"],
                pp[2][0:1, 256:512], pp[3][0:1, 256:512],
                src_ready=[(ch_x[0], mk_x)])
            mk_bc = emit_ln_stats_post(
                g, TT, rows2, C["c_ones_row"], C["c_eps"],
                pp[2][0:1, 256:512], pp[3][0:1, 256:512],
                pp[6][:, 0:256], pp[7][:, 0:256], mkst1)
        else:
            mk_bc = mk_bc_next
        for c in range(8):
            p.do("vector",
                 lambda E, c=c, sl=sl: E.tensor_tensor(
                     xsq[:, c], sx[:, sl * 8 + c],
                     pp[6][:, 0:TT], ALU.subtract),
                 wait=[(sT, mk_bc)] if c == 0 else ())
        p.do("vector", lambda E: E.drain())
        for c in range(8):
            p.do("vector",
                 lambda E, c=c: E.tensor_tensor(y1b[:, c], xsq[:, c],
                                                pp[7][:, 0:TT], ALU.mult))
        mark_use = p.mil("vector", sV)

        m_fevac = {0: None, 1: None}
        for c in range(8):
            par = c % 2
            ppA = pp[0] if par == 0 else pp[2]
            ppB = pp[1] if par == 0 else pp[3]
            w = [(ch_v, mk_v), (sV, m_zB)]
            if m_fevac[par] is not None:
                w.extend(m_fevac[par])
            if c < 2:
                w.append((sT, mk_bc))
                if m_fir_used:
                    w.append((sV, m_fir_used))
                if m_fir_used_s:
                    w.append((sS, m_fir_used_s))
                if m_tile_s is not None:
                    w.append((sS, m_tile_s))
                if mark_ln2 is not None:
                    w.append((sV, mark_ln2))
            p.do("tensor",
                 lambda E, ppA=ppA: E.matmul(ppA[:, 0:512], zcolB[:],
                                             zrowB[:], start=True, stop=True),
                 wait=w)
            p.do("tensor",
                 lambda E, ppB=ppB: E.matmul(ppB[:, 0:512], zcolB[:],
                                             zrowB[:], start=True, stop=True))
            for ti, (k, j, off) in enumerate(TAPS):
                if k in (3, 7):
                    reg = ppA[:, 0:256] if k == 3 else ppA[:, 256:512]
                else:
                    reg = ppB[:, 0:256] if k == 15 else ppB[:, 256:512]
                for q in range(4):
                    p.do("tensor",
                         lambda E, c=c, par=par, ti=ti, off=off, q=q, reg=reg:
                         E.matmul(
                             reg[32 * q:32 * q + 32, :],
                             d32b[32 * q:32 * q + 32, c, ti, :],
                             vst[32 * q:32 * q + 32, c, off:off + TT],
                             start=False, stop=True, skip_group_check=True,
                             tile_position=(32 * q, 32 * q)))
                if ti == 9:
                    mA = p.do("tensor", lambda E: E.drain(), inc=sT)
                    p.do("scalar",
                         lambda E, c=c, ppA=ppA: E.activation(
                             fbuf[3][:, c], ppA[:, 0:256], AF.Identity),
                         wait=[(sT, mA)], inc=sS)
                    m_f7 = p.do("scalar",
                                lambda E, c=c, ppA=ppA: E.activation(
                                    fbuf[7][:, c], ppA[:, 256:512],
                                    AF.Identity),
                                inc=sS)
            mB = p.do("tensor", lambda E: E.drain(), inc=sT)
            p.do("vector",
                 lambda E, c=c, ppB=ppB: E.tensor_copy(
                     fbuf[15][:, c], ppB[:, 0:256]),
                 wait=[(sT, mB)], inc=sV)
            m_f31 = p.do("vector",
                         lambda E, c=c, ppB=ppB: E.tensor_copy(
                             fbuf[31][:, c], ppB[:, 256:512]),
                         inc=sV)
            m_fevac[par] = [(sS, m_f7), (sV, m_f31)]
        m_fir_v = p.mil("vector", sV)
        m_fir_sS = p.mil("scalar", sS)

        branches = [fbuf[3][:, :], fbuf[7][:, :], fbuf[15][:, :],
                    fbuf[31][:, :], vst[:, :, 30:30 + TT], dTt[:, :]]
        m_abs = None
        for pi, br in enumerate(branches):
            for c in range(8):
                w = []
                if c == 0:
                    w = [(sV, m_fir_v), (sS, m_fir_sS), (ch_d, mk_d)]
                    if m_abs is not None:
                        w.append((sT, m_abs))
                p.do("scalar",
                     lambda E, pi=pi, br=br, c=c: E.activation(
                         babs[:, c], br[:, c] if br is not vst else
                         vst[:, c, 30:30 + TT], AF.Abs),
                     wait=w)
            m_b = p.mil("scalar", sS)
            for c in range(8):
                hh = c // 2
                kk = 6 * hh + pi
                w = [(sS, m_b)]
                if pi == 0 and c == 0 and m_st_used is not None:
                    w.append((sV, m_st_used))
                if pi == 0 and c == 0 and m_tile_s is not None:
                    w.append((sS, m_tile_s))
                p.do("tensor",
                     lambda E, c=c, kk=kk, pi=pi: E.matmul(
                         pp[3][0:24, 0:256], sstp[:, 23 - kk:47 - kk],
                         babs[:, c], start=(pi == 0 and c == 0),
                         stop=(pi == 5 and c == 7)),
                     wait=w, inc=(sT if c == 7 else None))
            m_abs = p.mark(sT)
        mm_st = p.mark(sT)
        m_stats_ev = p.do(
            "scalar",
            lambda E: E.activation(stats24[:], pp[3][0:24, 0:256],
                                   AF.Identity),
            wait=[(sT, mm_st)], inc=sS)

        for jc in range(16):
            bank = pp[jc % 4][:, 0:256]
            for d in range(9):
                w = []
                if d == 0:
                    w = [(sV, mark_use)]
                    if jc % 4 == 3:
                        w.append((sS, m_stats_ev))
                    if m_gbank[jc % 4] is not None:
                        w.append((sS, m_gbank[jc % 4]))
                if d == 8:
                    w = [(sS, m_stats_ev)]
                if d < 8:
                    p.do("tensor",
                         lambda E, jc=jc, d=d, bank=bank: E.matmul(
                             bank, g1w[:, d, jc * 128:(jc + 1) * 128],
                             y1b[:, d], start=(d == 0), stop=False),
                         wait=w)
                else:
                    p.do("tensor",
                         lambda E, jc=jc, bank=bank: E.matmul(
                             bank, g1s[:, jc * 128:(jc + 1) * 128],
                             stats24[:], start=False, stop=True),
                         wait=w, inc=sT)
            mm = p.mark(sT)
            m_gbank[jc % 4] = p.do(
                "scalar",
                lambda E, jc=jc, bank=bank: E.activation(
                    h1b[:, jc], bank, AF.Gelu, bias=sgb1[:, jc:jc + 1]),
                wait=[(sT, mm)], inc=sS)

        for jc in range(16):
            w = [(sS, m_gbank[jc % 4])] if jc >= 12 else []
            if jc == 0 and m_lgt_used is not None:
                w = [(sV, m_lgt_used)]
                if m_tile_s is not None:
                    w.append((sS, m_tile_s))
            p.do("tensor",
                 lambda E, jc=jc: E.matmul(
                     pp[4][0:24, 0:256], g2w[:, jc, :], h1b[:, jc],
                     start=(jc == 0), stop=(jc == 15)),
                 wait=w, inc=(sT if jc == 15 else None))
        mm_lg = p.mark(sT)
        p.do("vector",
             lambda E: E.scalar_tensor_tensor(
                 lgt[:], stats24[:], salst[:], pp[4][0:24, 0:256],
                 ALU.mult, ALU.add),
             wait=[(sT, mm_lg), (sV, m_temp)])
        p.do("vector", lambda E: E.drain())
        p.do("vector",
             lambda E: E.tensor_scalar(lgt[:], lgt[:], sgb2[:], tinv24[:],
                                       ALU.add, ALU.mult))
        m_lgt = p.mil("vector", sV)
        m_lgt_used = m_lgt

        trb = [pp[2], pp[3]]
        for b2 in range(2):
            w = [(sV, m_lgt)]
            if b2 == 0:
                w.append((sS, m_stats_ev))
                if m_prb_out is not None:
                    w.append((ch_p1, m_prb_out))
            mm = p.do("tensor",
                      lambda E, b2=b2: E.transpose(
                          trb[b2][:, 0:24],
                          lgt[:, b2 * 128:(b2 + 1) * 128],
                          C["c_idf32"][0:24, 0:24]),
                      wait=w, inc=sT)
            p.do("scalar",
                 lambda E, b2=b2: E.activation(prb[:, b2], trb[b2][:, 0:24],
                                               AF.Identity),
                 wait=[(sT, mm)], inc=(sS if b2 == 1 else None))
        m_tr = p.mark(sS)
        for b2 in range(2):
            for hh in range(4):
                p.do("vector",
                     lambda E, b2=b2, hh=hh: E.reduce_max(
                         mxs[:, b2, hh:hh + 1],
                         prb[:, b2, 6 * hh:6 * hh + 6], axis=AX.X),
                     wait=[(sS, m_tr)] if (b2 == 0 and hh == 0) else ())
        p.do("vector", lambda E: E.drain())
        for b2 in range(2):
            for hh in range(4):
                p.do("vector",
                     lambda E, b2=b2, hh=hh: E.tensor_scalar_mul(
                         mxs[:, b2, 4 + hh:5 + hh], mxs[:, b2, hh:hh + 1],
                         -1.0))
        m_mx = p.mil("vector", sV)
        for b2 in range(2):
            for hh in range(4):
                p.do("scalar",
                     lambda E, b2=b2, hh=hh: E.activation(
                         pex[:, b2, 6 * hh:6 * hh + 6],
                         prb[:, b2, 6 * hh:6 * hh + 6], AF.Exp,
                         bias=mxs[:, b2, 4 + hh:5 + hh]),
                     wait=[(sV, m_mx)] if (b2 == 0 and hh == 0) else ())
        m_ex = p.mil("scalar", sS)
        for b2 in range(2):
            for hh in range(4):
                p.do("vector",
                     lambda E, b2=b2, hh=hh: E.reduce_sum(
                         mxs[:, b2, hh:hh + 1],
                         pex[:, b2, 6 * hh:6 * hh + 6], axis=AX.X),
                     wait=[(sS, m_ex)] if (b2 == 0 and hh == 0) else ())
        p.do("vector", lambda E: E.drain())
        p.do("vector",
             lambda E: E.reciprocal(mxs[:, :, 4:8], mxs[:, :, 0:4]))
        p.do("vector", lambda E: E.drain())
        p.do("vector",
             lambda E: E.tensor_scalar_mul(
                 mxs[:, :, 8:12], mxs[:, :, 4:8], 0.95))
        p.do("vector", lambda E: E.drain())
        for b2 in range(2):
            for hh in range(4):
                p.do("vector",
                     lambda E, b2=b2, hh=hh: E.tensor_scalar_mul(
                         prb[:, b2, 6 * hh:6 * hh + 6],
                         pex[:, b2, 6 * hh:6 * hh + 6],
                         mxs[:, b2, 8 + hh:9 + hh]))
        p.do("vector", lambda E: E.drain())
        for b2 in range(2):
            for hh in range(4):
                p.do("vector",
                     lambda E, b2=b2, hh=hh: E.tensor_scalar_add(
                         prb[:, b2, 6 * hh + 5:6 * hh + 6],
                         prb[:, b2, 6 * hh + 5:6 * hh + 6], 0.05))
        m_prb = p.mil("vector", sV)
        mk1 = p.dma("sync", prb_scr, prb[:], ch_p1, wait=[(sV, m_prb)])
        w = [(ch_p1, mk1)]
        if m_prbR_used is not None:
            w.append((sV, m_prbR_used))
        m_prbR = p.dma("gpsimd", probsR[:],
                       prb_scr.rearrange("p b h -> b p h").rearrange(
                           "(o b) p h -> o b p h", o=1), ch_p2, wait=w)
        m_prb_out = mk1

        bsl = [pp[4][:, 0:256], pp[5][:, 0:256], pp[2][:, 0:256],
               pp[4][:, 256:512], pp[5][:, 256:512], pp[2][:, 256:512]]
        m_mix_marks = []
        for hh in range(4):
            for pi in range(6):
                kk = 6 * hh + pi
                w = [(ch_p2, m_prbR)]
                if hh == 0 and pi == 0 and m_lgt_used is not None:
                    w.append((sV, m_lgt_used))
                if m_bcb_used[pi] is not None:
                    w.append((sV, m_bcb_used[pi]))
                if pi == 0 and m_rms_used is not None:
                    w.append((sV, m_rms_used))
                if pi == 0 and m_tile_s is not None:
                    w.append((sS, m_tile_s))
                mm = p.do("tensor",
                          lambda E, kk=kk, pi=pi: E.matmul(
                              bsl[pi], sorow[:], probsR[0:1, :, :, kk],
                              start=True, stop=True),
                          wait=w, inc=sT)
                p.do("scalar",
                     lambda E, pi=pi, mmx=mm: E.activation(
                         bcb[:, pi], bsl[pi], AF.Identity),
                     wait=[(sT, mm)],
                     inc=(sS if pi == 5 else None))
            m_bc6 = p.mark(sS)
            for i in range(2):
                c = 2 * hh + i
                for pi, br in enumerate(branches):
                    w = []
                    if pi == 0 and i == 0:
                        w = [(sS, m_bc6), (sV, m_fir_v), (sS, m_fir_sS)]
                        if m_oacc_used is not None:
                            w.append((sV, m_oacc_used))
                    p.do("vector",
                         lambda E, c=c, i=i, pi=pi, br=br: E.tensor_tensor(
                             mixt[:, 6 * i + pi],
                             br[:, c] if br is not vst else
                             vst[:, c, 30:30 + TT],
                             bcb[:, pi], ALU.mult),
                         wait=w)
            p.do("vector", lambda E: E.drain())
            for i in range(2):
                base = 6 * i
                p.do("vector",
                     lambda E, base=base, hh=hh, i=i: E.tensor_tensor(
                         mixt[:, base + 0], mixt[:, base + 0],
                         mixt[:, base + 1], ALU.add))
                p.do("vector",
                     lambda E, base=base: E.tensor_tensor(
                         mixt[:, base + 2], mixt[:, base + 2],
                         mixt[:, base + 3], ALU.add))
                p.do("vector",
                     lambda E, base=base: E.tensor_tensor(
                         mixt[:, base + 4], mixt[:, base + 4],
                         mixt[:, base + 5], ALU.add))
            p.do("vector", lambda E: E.drain())
            for i in range(2):
                base = 6 * i
                p.do("vector",
                     lambda E, base=base: E.tensor_tensor(
                         mixt[:, base + 0], mixt[:, base + 0],
                         mixt[:, base + 2], ALU.add))
            p.do("vector", lambda E: E.drain())
            for i in range(2):
                base = 6 * i
                c = 2 * hh + i
                p.do("vector",
                     lambda E, base=base, c=c: E.tensor_tensor(
                         o_acc[:, c], mixt[:, base + 0], mixt[:, base + 4],
                         ALU.add))
            mk = p.mil("vector", sV)
            for pi in range(6):
                m_bcb_used[pi] = mk
        m_mix = p.mil("vector", sV)
        m_fir_used = m_mix
        m_prbR_used = m_mix

        for c in range(8):
            w = [(sV, m_mix)] if c == 0 else []
            p.do("scalar",
                 lambda E, c=c: E.activation(xsq2[:, c], o_acc[:, c],
                                             AF.Square), wait=w)
        m_osq = p.mil("scalar", sS)
        rsl = [pp[3][0:1, 0:256], pp[4][0:1, 0:256], pp[3][0:1, 256:512],
               pp[4][0:1, 256:512]]
        m_rev = [None] * 4
        for hh in range(4):
            bank = rsl[hh]
            for i in range(2):
                w = []
                if i == 0:
                    w = [(sS, m_osq)]
                    if m_rev[(hh + 2) % 4] is not None:
                        w.append((sS, m_rev[(hh + 2) % 4]))
                p.do("tensor",
                     lambda E, hh=hh, i=i, bank=bank: E.matmul(
                         bank, C["c_ones_col"][:], xsq2[:, 2 * hh + i],
                         start=(i == 0), stop=(i == 1)),
                     wait=w, inc=(sT if i == 1 else None))
            mm = p.mark(sT)
            m_rev[hh] = p.do(
                "scalar",
                lambda E, hh=hh, bank=bank: E.activation(nr2[:, hh], bank,
                                                         AF.Identity),
                wait=[(sT, mm)], inc=sS)
        m_r0 = p.mil("scalar", sS)
        p.do("scalar",
             lambda E: E.activation(
                 nr2[:, 4:8, :].rearrange("o a t -> o (a t)"),
                 nr2[:, 0:4, :].rearrange("o a t -> o (a t)"),
                 AF.Sqrt, scale=1.0 / 256, bias=C["c_eps"][0:1, 0:1]),
             wait=[(sS, m_r0)])
        m_r1 = p.mil("scalar", sS)
        p.do("vector",
             lambda E: E.reciprocal(
                 nr2[:, 0:4, :].rearrange("o a t -> o (a t)"),
                 nr2[:, 4:8, :].rearrange("o a t -> o (a t)")),
             wait=[(sS, m_r1)])
        m_rrow = p.mil("vector", sV)
        for hh in range(4):
            w = [(sV, m_rrow)]
            if hh > 0:
                w.append((sV, m_rms_used))
            mm = p.do("tensor",
                      lambda E, hh=hh: E.matmul(
                          pp[5][:, 0:256], C["c_ones_row"][:],
                          nr2[0:1, hh], start=True, stop=True),
                      wait=w, inc=sT)
            for i in range(2):
                c = 2 * hh + i
                p.do("vector",
                     lambda E, c=c: E.tensor_tensor(o_acc[:, c], o_acc[:, c],
                                                    pp[5][:, 0:256],
                                                    ALU.mult),
                     wait=[(sT, mm)] if i == 0 else ())
            m_rms_used = p.mil("vector", sV)
        p.do("vector", lambda E: E.drain())
        for c in range(8):
            w = []
            if c == 0 and m_ob_used is not None:
                w = [(sT, m_ob_used)]
            p.do("vector",
                 lambda E, c=c: E.tensor_scalar_mul(
                     ob[:, c], o_acc[:, c], sonw[:, c % 2:c % 2 + 1]),
                 wait=w)
        m_obv = p.mil("vector", sV)
        m_oacc_used = m_obv

        for oc in range(8):
            bank = pp[oc % 2][:, 256:512]
            for c in range(8):
                w = []
                if c == 0:
                    w = [(sV, m_obv)]
                    if m_bank01[oc % 2] is not None:
                        w.append((sS, m_bank01[oc % 2]))
                    if m_bank01[(oc + 1) % 2] is not None:
                        w.append((sS, m_bank01[(oc + 1) % 2]))
                    if oc == 0:
                        for mg in m_gbank:
                            if mg is not None:
                                w.append((sS, mg))
                p.do("tensor",
                     lambda E, oc=oc, c=c, bank=bank: E.matmul(
                         bank, opw[:, c, oc * 128:(oc + 1) * 128],
                         ob[:, c], start=(c == 0), stop=(c == 7)),
                     wait=w, inc=(sT if c == 7 else None))
            mm = p.mark(sT)
            wev = [(sT, mm)]
            if oc == 0 and mark_store is not None:
                wev.append((ch_o, mark_store))
            m_bank01[oc % 2] = p.do(
                "scalar",
                lambda E, oc=oc, bank=bank: E.activation(
                    fo[:, oc], bank, AF.Identity),
                wait=wev, inc=sS)

        if t + 1 < NTILE:
            nsl = (t + 1) % 2
            t1 = (t + 1) * TT
            xin_n = xin[:, t1:t1 + TT].rearrange("(c p) t -> p c t", p=128)
            w = ([(sV, mark_use_hist[nsl])]
                 if mark_use_hist[nsl] is not None else [(sV, mark_use)])
            mk_xn = p.dma("sync", sx[:, nsl * 8:nsl * 8 + 8], xin_n,
                          ch_x[nsl], wait=w)
            wr = [(sT, mkst1), (sT, p.mark(sT)), (sS, m_bank01[0]),
                  (sS, m_bank01[1]), (sV, mark_use)]
            mkst1 = emit_ln_stats_pre(
                g, TT, lambda c, s=nsl: sx[:, s * 8 + c], xsq2,
                C["c_ones16"], pp[2][0:1, 256:512], pp[3][0:1, 256:512],
                src_ready=[(ch_x[nsl], mk_xn), (sT, mkst1)],
                war_rows=wr)

        mk_bc2 = emit_ln_stats(
            g, TT, lambda c: fo[:, c], xsq, rows, C["c_ones16"],
            C["c_ones_row"], C["c_eps"], pp[6][:, 256:512], pp[7][:, 256:512],
            pp[6][:, 0:256], pp[7][:, 0:256],
            src_ready=[(sS, m_bank01[0]), (sS, m_bank01[1])],
            war_rows=[(sV, mark_use)],
            war_bc=[(sV, mark_use)])
        for c in range(8):
            w = [(sT, mk_bc2)] if c == 0 else []
            if c == 0 and mark_store is not None:
                w.append((ch_o, mark_store))
            p.do("vector",
                 lambda E, c=c: E.tensor_tensor(xsq[:, c], fo[:, c],
                                                pp[6][:, 0:TT], ALU.subtract),
                 wait=w)
        p.do("vector", lambda E: E.drain())
        for c in range(8):
            p.do("vector",
                 lambda E, c=c: E.tensor_tensor(xsq[:, c], xsq[:, c],
                                                pp[7][:, 0:TT], ALU.mult))
        p.do("vector", lambda E: E.drain())
        for c in range(8):
            p.do("vector",
                 lambda E, c=c, sl=sl: E.tensor_tensor(
                     fo[:, c], sx[:, sl * 8 + c], xsq[:, c], ALU.add))
        mark_ln2 = p.mil("vector", sV)
        mark_store = p.dma("sync", xout_sl, fo[:], ch_o,
                           wait=[(sV, mark_ln2)])
        if t + 1 < NTILE:
            mk_bc_next = emit_ln_stats_post(
                g, TT, rows2, C["c_ones_row"], C["c_eps"],
                pp[2][0:1, 256:512], pp[3][0:1, 256:512],
                pp[6][:, 0:256], pp[7][:, 0:256], mkst1,
                war_bc=[(sV, mark_ln2)])
        mark_use = mark_ln2
        mark_use_hist[sl] = mark_ln2
        m_st_used = m_lgt
        m_fir_used = m_mix
        m_fir_used_s = m_stats_ev
        m_tile_s = p.mil("scalar", sS)
        m_fir_pe = p.mark(sT)
        m_ob_used = p.mark(sT)
    g.barrier()


def emit_l1attn(g, xin, xout, C):
    nc, p = g.nc, g.p
    sT, sV, sS = g.sT, g.sV, g.sS
    NH = 16
    TB = 256
    NBT = T // TB
    TQ = 512
    NQT = T // TQ

    wq = g.din("l1a_wqT", [D, D])
    wk = g.din("l1a_wkT", [D, D])
    wv = g.din("l1a_wvT", [D, D])
    wo = g.din("l1a_woT", [D, D])
    ropeC = g.din("l1a_ropeC", [128, T])
    ropeS = g.din("l1a_ropeS", [128, T])
    masks = g.din("l1a_masks", [128, 4, TQ], BF16)
    qn_pat = g.din("l1a_qnpat", [128, 30])
    mh_scr = g.dint("l1a_mhscr", [16, TQ])
    khat_d = g.dint("l1a_khatd", [65, NH, T], BF16)

    pp = g.pool
    SHUF = [1, 0, 3, 2, 5, 4, 7, 6, 9, 8, 11, 10, 13, 12, 15, 14,
            17, 16, 19, 18, 21, 20, 23, 22, 25, 24, 27, 26, 29, 28, 31, 30]

    vtm = g.sb("l1a_vtm", [128, 16, NH * 65], BF16)
    sqnp = g.sb("l1a_qnp", [128, 30])
    mk2n = g.sb("l1a_mk2n", [16, 4])
    p.dma("sync", sqnp[:], qn_pat, g.dSY)

    es1 = ExitStack()
    sC = es1.enter_context(nc.sbuf_tensor("l1a_C", [128, T], F32))
    sS_ = es1.enter_context(nc.sbuf_tensor("l1a_S", [128, T], F32))
    p.dma("sync", sC[:], ropeC, g.dSY)
    p.dma("sync", sS_[:], ropeS, g.dSY)
    wkb = es1.enter_context(nc.sbuf_tensor("l1a_wk", [128, 8, D], BF16))
    wvb = es1.enter_context(nc.sbuf_tensor("l1a_wv", [128, 8, D], BF16))
    sx = es1.enter_context(nc.sbuf_tensor("l1a_sx", [128, 16, TB], F32))
    xsq = es1.enter_context(nc.sbuf_tensor("l1a_xsq", [128, 8, TB], F32))
    xq1 = es1.enter_context(nc.sbuf_tensor("l1a_xq1", [128, 8, TB], F32))
    rows2 = es1.enter_context(nc.sbuf_tensor("l1a_rowsB", [1, 5, TB], F32))
    y3t = es1.enter_context(nc.sbuf_tensor("l1a_y3t", [128, 8, TB], BF16))
    ktile = es1.enter_context(nc.sbuf_tensor("l1a_ktile", [128, 8, TB], F32))
    ksq = es1.enter_context(nc.sbuf_tensor("l1a_ksq", [128, TB], F32))
    knm = es1.enter_context(nc.sbuf_tensor("l1a_knm", [16, 10], F32))
    k65 = es1.enter_context(nc.sbuf_tensor("l1a_k65", [65, NH, TB], BF16))
    if True:
        for c in range(8):
            p.dma("gpsimd", wkb[:, c], wk[c * 128:(c + 1) * 128, :], g.dGP)
            p.dma("gpsimd", wvb[:, c], wv[c * 128:(c + 1) * 128, :], g.dGP)
        g.barrier()
        p.do("vector", lambda E: E.memset(k65[64:65, :, :], 1.0))
        p.do("vector", lambda E: E.memset(vtm[:], 1.0))
        m_ones = p.mil("vector", sV)

        ch_x = [p.channel("l1a_chx0"), p.channel("l1a_chx1")]
        ch_kd = p.channel("l1a_chkd")
        mark_use = None
        mark_use_hist = [None, None]
        m_kevac = None
        m_vevac = None
        m_knred = None
        m_krot = None
        m_k65c = None
        xin_0 = xin[:, 0:TB].rearrange("(c p) t -> p c t", p=128)
        mk_x0 = p.dma("sync", sx[:, 0:8], xin_0, ch_x[0])
        mkst = emit_ln_stats_pre(
            g, TB, lambda c: sx[:, c], xq1, C["c_ones16"], pp[5], pp[6],
            src_ready=[(ch_x[0], mk_x0)])
        mk_bc_next = emit_ln_stats_post(
            g, TB, rows2, C["c_ones_row"], C["c_eps"], pp[5], pp[6],
            pp[7], pp[4], mkst)
        m_vmm = None
        for tb in range(NBT):
            t0 = tb * TB
            sl = tb % 2
            mk_bc = mk_bc_next
            for c in range(8):
                w = []
                if c == 0:
                    w = [(sT, mk_bc)]
                    if m_vmm is not None:
                        w.append((sT, m_vmm))
                p.do("vector",
                     lambda E, c=c, sl=sl: E.tensor_tensor(
                         xsq[:, c], sx[:, sl * 8 + c],
                         pp[7][:, 0:TB], ALU.subtract),
                     wait=w)
            p.do("vector", lambda E: E.drain())
            for c in range(8):
                p.do("vector",
                     lambda E, c=c: E.tensor_tensor(y3t[:, c], xsq[:, c],
                                                    pp[4][:, 0:TB], ALU.mult))
            mark_use = p.mil("vector", sV)
            mark_use_hist[sl] = mark_use
            if tb + 1 < NBT:
                nsl = (tb + 1) % 2
                t1 = (tb + 1) * TB
                xin_n = xin[:, t1:t1 + TB].rearrange("(c p) t -> p c t",
                                                     p=128)
                w = ([(sV, mark_use_hist[nsl])]
                     if mark_use_hist[nsl] is not None else [])
                mk_xn = p.dma("sync", sx[:, nsl * 8:nsl * 8 + 8], xin_n,
                              ch_x[nsl], wait=w)
                mkst = emit_ln_stats_pre(
                    g, TB, lambda c, s=nsl: sx[:, s * 8 + c], xq1,
                    C["c_ones16"], pp[5], pp[6],
                    src_ready=[(ch_x[nsl], mk_xn), (sT, mkst)],
                    war_rows=[(sT, mkst), (sV, mark_use)])

            for c in range(8):
                bank = pp[c % 2]
                for d in range(8):
                    w = []
                    if d == 0:
                        w = [(sV, mark_use)]
                        if m_kevac is not None and c < 2:
                            w.append((sS, m_kevac))
                        if c < 2 and getattr(g, "_l1a_vev4", None) \
                                and g._l1a_vev4[c] is not None:
                            w.append((sS, g._l1a_vev4[c]))
                        if m_krot is not None and c == 0:
                            w.append((sV, m_krot))
                    p.do("tensor",
                         lambda E, c=c, d=d, bank=bank: E.matmul(
                             bank[:, 0:TB],
                             wkb[:, d, c * 128:(c + 1) * 128],
                             y3t[:, d], start=(d == 0), stop=(d == 7)),
                         wait=w, inc=(sT if d == 7 else None))
                mm = p.mark(sT)
                p.do("scalar",
                     lambda E, c=c, bank=bank: E.activation(
                         ktile[:, c], bank[:, 0:TB], AF.Identity, scale=0.125),
                     wait=[(sT, mm)])
                p.do("scalar",
                     lambda E, bank=bank: E.activation(ksq[:], bank[:, 0:TB],
                                                       AF.Square))
                m_kevac = p.mil("scalar", sS)
                p.do("tensor",
                     lambda E, c=c: E.matmul(
                         pp[4][0:16, 0:TB],
                         sqnp[:, 14 - 2 * c:30 - 2 * c], ksq[:],
                         start=(c == 0), stop=(c == 7)),
                     wait=[(sS, m_kevac)],
                     inc=(sT if c == 7 else None))
            mm_kn = p.mark(sT)
            p.do("vector",
                 lambda E, tb=tb: E.reduce_max(knm[:, tb:tb + 1],
                                               pp[4][0:16, 0:TB], axis=AX.X),
                 wait=[(sT, mm_kn)])
            m_knred = p.mil("vector", sV)
            if tb + 1 < NBT:
                mk_bc_next = emit_ln_stats_post(
                    g, TB, rows2, C["c_ones_row"], C["c_eps"], pp[5], pp[6],
                    pp[7], pp[4], mkst,
                    war_bc=[(sV, m_knred), (sV, mark_use)])

            for c in range(8):
                p.do("vector",
                     lambda E, c=c, t0=t0: E.stream_shuffle(xsq[:, c],
                                                            ktile[:, c], SHUF),
                     wait=[(sS, m_kevac)] if c == 0 else ())
            p.do("vector", lambda E: E.drain())
            for c in range(8):
                p.do("vector",
                     lambda E, c=c, t0=t0: E.tensor_tensor(
                         ktile[:, c], ktile[:, c], sC[:, t0:t0 + TB],
                         ALU.mult))
            for c in range(8):
                p.do("vector",
                     lambda E, c=c, t0=t0: E.tensor_tensor(
                         xsq[:, c], xsq[:, c], sS_[:, t0:t0 + TB], ALU.mult))
            p.do("vector", lambda E: E.drain())
            for c in range(8):
                w = [(ch_kd, m_k65c)] if (c == 0 and m_k65c) else []
                p.do("vector",
                     lambda E, c=c: E.tensor_tensor(ktile[:, c], ktile[:, c],
                                                    xsq[:, c], ALU.add),
                     wait=w)
            m_krot = p.mil("vector", sV)
            for c in range(8):
                p.do("scalar",
                     lambda E, c=c: E.activation(k65[0:64, 2 * c, :],
                                                 ktile[0:64, c], AF.Identity),
                     wait=[(sV, m_krot)] if c == 0 else ())
                p.do("scalar",
                     lambda E, c=c: E.activation(k65[0:64, 2 * c + 1, :],
                                                 ktile[64:128, c], AF.Identity))
            m_k65 = p.mil("scalar", sS)
            m_k65c = p.dma("sync", khat_d[:, :, t0:t0 + TB], k65[:], ch_kd,
                           wait=[(sS, m_k65), (sV, m_ones)])

            m_vev4 = getattr(g, "_l1a_vev4", [None] * 4)
            for i in range(2):
                for half in range(2):
                    vb = 2 * i + half
                    bank = pp[vb]
                    for d in range(8):
                        w = []
                        if d == 0:
                            w = [(sV, mark_use), (sS, m_kevac)]
                            if m_vev4[vb] is not None:
                                w.append((sS, m_vev4[vb]))
                        p.do("tensor",
                             lambda E, i=i, half=half, d=d, bank=bank:
                             E.matmul(
                                 bank[:, 0:512],
                                 y3t[:, d, i * 128:(i + 1) * 128],
                                 wvb[:, d, half * 512:(half + 1) * 512],
                                 start=(d == 0), stop=(d == 7)),
                             wait=w, inc=(sT if d == 7 else None))
                    mm = p.mark(sT)
                    m_vmm = mm
                    tc = tb * 2 + i
                    m_vev4[vb] = p.do(
                        "scalar",
                        lambda E, tc=tc, half=half, bank=bank: E.activation(
                            vtm[:, tc].rearrange("p (h x) -> p h x", x=65)
                            [:, half * 8:(half + 1) * 8, 0:64],
                            bank[:, 0:512], AF.Identity),
                        wait=[(sT, mm), (sV, m_ones)], inc=sS)
                    g._l1a_vev4 = m_vev4
                    m_vevac = m_vev4[vb]

        p.do("vector", lambda E: E.reduce_max(mk2n[:, 0:1], knm[:, 0:NBT],
                                              axis=AX.X),
             wait=[(sV, m_knred)])
        p.do("vector", lambda E: E.drain())
        p.do("vector", lambda E: E.tensor_scalar_mul(mk2n[:, 1:2],
                                                     mk2n[:, 0:1], 1.0 / 64))
        m_mk2 = p.mil("vector", sV)
        g.barrier()
        es1.close()

    es2 = ExitStack()
    wqb = es2.enter_context(nc.sbuf_tensor("l1a_wq", [128, 8, D], BF16))
    wob = es2.enter_context(nc.sbuf_tensor("l1a_wo", [128, 8, D], BF16))
    sx2 = es2.enter_context(nc.sbuf_tensor("l1a_sx2", [128, 8, TQ], F32))
    xq2 = es2.enter_context(nc.sbuf_tensor("l1a_xsq2", [128, 8, TQ], F32))
    rw2 = es2.enter_context(nc.sbuf_tensor("l1a_rows2", [1, 6, TQ], F32))
    y3q = es2.enter_context(nc.sbuf_tensor("l1a_y3q", [128, 8, TQ], BF16))
    qt = es2.enter_context(nc.sbuf_tensor("l1a_qt", [128, 8, TQ], F32))
    qaug = es2.enter_context(nc.sbuf_tensor("l1a_qaug", [65, NH, TQ], BF16))
    qsq = es2.enter_context(nc.sbuf_tensor("l1a_qsq", [128, TQ], F32))
    mrow = es2.enter_context(nc.sbuf_tensor("l1a_mrow", [16, 2, TQ], F32))
    khbuf = es2.enter_context(nc.sbuf_tensor("l1a_kh", [65, 2, T], BF16))
    pt0 = es2.enter_context(nc.sbuf_tensor("l1a_pt0", [128, TQ], BF16))
    pt1 = es2.enter_context(nc.sbuf_tensor("l1a_pt1", [128, TQ], BF16))
    smask = es2.enter_context(nc.sbuf_tensor("l1a_smask", [128, 4, TQ], BF16))
    numb = es2.enter_context(nc.sbuf_tensor("l1a_numb", [64, 2, TQ], F32))
    linv = es2.enter_context(nc.sbuf_tensor("l1a_linv", [1, 2, TQ], F32))
    aoutT = es2.enter_context(nc.sbuf_tensor("l1a_aoutT", [128, 8, TQ], BF16))
    fo = es2.enter_context(nc.sbuf_tensor("l1a_fo", [128, 8, 256], F32))
    tmp2 = es2.enter_context(nc.sbuf_tensor("l1a_tmp2", [128, 8, 256], F32))
    tbl = es2.enter_context(nc.sbuf_tensor("l1a_tbl", [128, 2, TQ], F32))
    if True:
        ptb = [pt0, pt1]
        for c in range(8):
            p.dma("gpsimd", wqb[:, c], wq[c * 128:(c + 1) * 128, :], g.dGP)
            p.dma("gpsimd", wob[:, c], wo[c * 128:(c + 1) * 128, :], g.dGP)
        p.dma("sync", smask[:], masks, g.dSY)
        g.barrier()

        ch_x2 = p.channel("l1a_chx2")
        ch_tb = p.channel("l1a_chtb")
        ch_mh1 = p.channel("l1a_chmh1")
        ch_mh2 = p.channel("l1a_chmh2")
        ch_o = p.channel("l1a_cho")
        ch_kh = [p.channel("l1a_chkh0"), p.channel("l1a_chkh1")]
        kh_war = [None, None]
        kh_mk = [None, None]
        mark_store = None
        mark_ln2 = None
        m_qevac = None
        m_qrot = None
        psA_war = [None, None]
        pt_war = [None, None]
        psO_war = {0: None, 1: None}
        epi_war = [None, None]
        m_oevac = [None, None]
        m_mrow_out = None

        for tq in range(NQT):
            q0 = tq * TQ
            kc_max = (tq + 1) * (TQ // 128)
            xin_sl = xin[:, q0:q0 + TQ].rearrange("(c p) t -> p c t", p=128)
            w = [(sV, mark_ln2)] if mark_ln2 else []
            mk_x = p.dma("sync", sx2[:], xin_sl, ch_x2, wait=w)
            mk_bc = emit_ln_stats(
                g, TQ, lambda c: sx2[:, c], xq2, rw2, C["c_ones16"],
                C["c_ones_row"], C["c_eps"], pp[6], pp[7], pp[4], pp[5],
                src_ready=[(ch_x2, mk_x)],
                war_rows=[(sS, m_oevac[1])] if m_oevac[1] else (),
                war_bc=[(sV, mark_ln2)] if mark_ln2 else ())
            for c in range(8):
                p.do("vector",
                     lambda E, c=c: E.tensor_tensor(xq2[:, c], sx2[:, c],
                                                    pp[4][:, 0:TQ],
                                                    ALU.subtract),
                     wait=[(sT, mk_bc)] if c == 0 else ())
            p.do("vector", lambda E: E.drain())
            for c in range(8):
                p.do("vector",
                     lambda E, c=c: E.tensor_tensor(y3q[:, c], xq2[:, c],
                                                    pp[5][:, 0:TQ], ALU.mult))
            m_y3 = p.mil("vector", sV)

            for c in range(8):
                bank = pp[c % 2]
                for d in range(8):
                    w = []
                    if d == 0:
                        w = [(sV, m_y3)]
                        if m_qevac is not None and c < 2:
                            w.append((sS, m_qevac))
                        if psA_war[c % 2] is not None:
                            w.append((sS, psA_war[c % 2]))
                        if c == 0 and m_qrot is not None:
                            w.append((sV, m_qrot))
                    p.do("tensor",
                         lambda E, c=c, d=d, bank=bank: E.matmul(
                             bank[:, 0:TQ],
                             wqb[:, d, c * 128:(c + 1) * 128],
                             y3q[:, d], start=(d == 0), stop=(d == 7)),
                         wait=w, inc=(sT if d == 7 else None))
                mm = p.mark(sT)
                p.do("scalar",
                     lambda E, c=c, bank=bank: E.activation(
                         qt[:, c], bank[:, 0:TQ], AF.Identity),
                     wait=[(sT, mm)])
                p.do("scalar",
                     lambda E, bank=bank: E.activation(qsq[:], bank[:, 0:TQ],
                                                       AF.Square))
                m_qevac = p.mil("scalar", sS)
                p.do("tensor",
                     lambda E, c=c: E.matmul(
                         pp[5][0:16, 0:TQ],
                         sqnp[:, 14 - 2 * c:30 - 2 * c], qsq[:],
                         start=(c == 0), stop=(c == 7)),
                     wait=[(sS, m_qevac)] + (
                         [(sV, mark_ln2)]
                         if (c == 0 and mark_ln2 is not None) else []),
                     inc=(sT if c == 7 else None))
            mm_qn = p.mark(sT)

            p.do("vector",
                 lambda E: E.tensor_scalar_mul(mrow[:, 0], pp[5][0:16, 0:TQ],
                                               mk2n[:, 1:2]),
                 wait=[(sT, mm_qn), (sV, m_mk2)])
            m_t = p.mil("vector", sV)
            p.do("scalar", lambda E: E.activation(mrow[:, 1], mrow[:, 0],
                                                  AF.Sqrt),
                 wait=[(sV, m_t)])
            m_t2 = p.mil("scalar", sS)
            p.do("vector", lambda E: E.tensor_scalar_mul(mrow[:, 0],
                                                         mrow[:, 1], -1.0),
                 wait=[(sS, m_t2)])
            m_t3 = p.mil("vector", sV)
            w = [(sV, m_t3)]
            if m_mrow_out is not None:
                w.append((ch_mh2, m_mrow_out))
            mk1 = p.dma("sync", mh_scr, mrow[:, 0], ch_mh1, wait=w)
            m_mrow_out = p.dma("gpsimd", qaug[64:65, :, :],
                               mh_scr.rearrange("(o h) t -> o h t", o=1), ch_mh2,
                               wait=[(ch_mh1, mk1)])

            w = [(sV, m_qrot)] if m_qrot else []
            p.dma("sync", tbl[:, 0], ropeC[:, q0:q0 + TQ], ch_tb, wait=w)
            mk_tb = p.dma("sync", tbl[:, 1], ropeS[:, q0:q0 + TQ], ch_tb)
            for c in range(8):
                p.do("vector",
                     lambda E, c=c: E.stream_shuffle(xq2[:, c], qt[:, c],
                                                     SHUF),
                     wait=[(sS, m_qevac)] if c == 0 else ())
            p.do("vector", lambda E: E.drain())
            for c in range(8):
                p.do("vector",
                     lambda E, c=c: E.tensor_tensor(qt[:, c], qt[:, c],
                                                    tbl[:, 0], ALU.mult),
                     wait=[(ch_tb, mk_tb)] if c == 0 else ())
            for c in range(8):
                p.do("vector",
                     lambda E, c=c: E.tensor_tensor(xq2[:, c], xq2[:, c],
                                                    tbl[:, 1], ALU.mult))
            p.do("vector", lambda E: E.drain())
            for c in range(8):
                p.do("vector",
                     lambda E, c=c: E.tensor_tensor(qt[:, c], qt[:, c],
                                                    xq2[:, c], ALU.add))
            m_qrot = p.mil("vector", sV)
            for c in range(8):
                p.do("scalar",
                     lambda E, c=c: E.activation(qaug[0:64, 2 * c, :],
                                                 qt[0:64, c], AF.Identity),
                     wait=[(sV, m_qrot)] if c == 0 else ())
                p.do("scalar",
                     lambda E, c=c: E.activation(qaug[0:64, 2 * c + 1, :],
                                                 qt[64:128, c], AF.Identity))
            m_qdist = p.mil("scalar", sS)

            nkb = kc_max * 128
            w = [(sT, kh_war[0])] if kh_war[0] else []
            kh_mk[0] = p.dma("sync", khbuf[:, 0, 0:nkb],
                             khat_d[:, 0, 0:nkb], ch_kh[0], wait=w)

            pending_epi = None
            for h in range(NH):
                po = pp[2] if h % 2 == 0 else pp[4]
                if h + 1 < NH:
                    slot = (h + 1) % 2
                    w = [(sT, kh_war[slot])] if kh_war[slot] else []
                    kh_mk[slot] = p.dma("sync", khbuf[:, slot, 0:nkb],
                                        khat_d[:, h + 1, 0:nkb],
                                        ch_kh[slot], wait=w)
                sl = h % 2
                me_list = [None] * kc_max

                def em_score(kc, h=h, sl=sl):
                    bank = pp[kc % 2]
                    w = [(ch_mh2, m_mrow_out), (sS, m_qdist),
                         (ch_kh[sl], kh_mk[sl])]
                    if psA_war[kc % 2] is not None:
                        w.append((sS, psA_war[kc % 2]))
                    mm_s = p.do("tensor",
                                lambda E, h=h, kc=kc, bank=bank, sl=sl:
                                E.matmul(
                                    bank[:, 0:TQ],
                                    khbuf[:, sl, kc * 128:(kc + 1) * 128],
                                    qaug[:, h, :], start=True, stop=True),
                                wait=w, inc=sT)
                    kh_war[sl] = mm_s
                    w = [(sT, mm_s)]
                    if pt_war[kc % 2] is not None:
                        w.append((sT, pt_war[kc % 2]))
                    m_e = p.do("scalar",
                               lambda E, kc=kc, bank=bank: E.activation(
                                   ptb[kc % 2][:], bank[:, 0:TQ], AF.Exp),
                               wait=w, inc=sS)
                    psA_war[kc % 2] = m_e
                    dep_sem, dep_mark = sS, m_e
                    m = kc - 4 * tq
                    if m >= 0:
                        dep_mark = p.do(
                            "vector",
                            lambda E, kc=kc, m=m: E.tensor_tensor(
                                ptb[kc % 2][:], ptb[kc % 2][:], smask[:, m],
                                ALU.mult),
                            wait=[(sS, m_e)], inc=sV)
                        dep_sem = sV
                    me_list[kc] = (dep_sem, dep_mark)

                em_score(0)
                for kc in range(kc_max):
                    if kc + 1 < kc_max:
                        em_score(kc + 1)
                    if kc == min(2, kc_max - 1) and pending_epi is not None:
                        pending_epi()
                        pending_epi = None
                    wav = [me_list[kc]]
                    if kc == 0 and psO_war[h % 2] is not None:
                        wav.append(psO_war[h % 2])
                    pt_war[kc % 2] = p.do(
                        "tensor",
                        lambda E, h=h, kc=kc, km=kc_max, po=po: E.matmul(
                            po[0:65, 0:TQ],
                            vtm[:, kc, h * 65:(h + 1) * 65],
                            ptb[kc % 2][:], start=(kc == 0),
                            stop=(kc == km - 1)),
                        wait=wav, inc=sT)
                m_av = pt_war[(kc_max - 1) % 2]
                po_b = pp[3] if h % 2 == 0 else pp[5]

                def emit_epi(h=h, po=po, po_b=po_b, m_av=m_av):
                    p.do("vector",
                         lambda E, po=po, h=h: E.reciprocal(linv[:, h % 2],
                                                            po[64:65, 0:TQ]),
                         wait=[(sT, m_av)] + (
                             [epi_war[h % 2]] if epi_war[h % 2] else []))
                    m_li = p.mil("vector", sV)
                    mk_b = p.do("tensor",
                                lambda E, po_b=po_b, h=h: E.matmul(
                                    po_b[0:64, 0:TQ],
                                    C["c_ones_row"][0:1, 0:64],
                                    linv[:, h % 2], start=True, stop=True),
                                wait=[(sV, m_li)], inc=sT)
                    m_nu = p.do("scalar",
                                lambda E, po=po, h=h: E.activation(
                                    numb[:, h % 2], po[0:64, 0:TQ],
                                    AF.Identity),
                                wait=[(sT, m_av)], inc=sS)
                    mm_ao = p.do("vector",
                         lambda E, h=h, po_b=po_b: E.tensor_tensor(
                             aoutT[(h % 2) * 64:(h % 2) * 64 + 64, h // 2, :],
                             numb[:, h % 2], po_b[0:64, 0:TQ], ALU.mult),
                         wait=[(sT, mk_b), (sS, m_nu)], inc=sV)
                    epi_war[h % 2] = (sV, mm_ao)
                    psO_war[h % 2] = (sV, mm_ao)

                pending_epi = emit_epi
            pending_epi()
            pending_epi = None

            for s2 in range(2):
                s0 = q0 + s2 * 256
                xin_s = xin[:, s0:s0 + 256].rearrange("(c p) t -> p c t",
                                                      p=128)
                xout_s = xout[:, s0:s0 + 256].rearrange("(c p) t -> p c t",
                                                        p=128)
                w = [(sV, mark_ln2)] if mark_ln2 else []
                mk_xs = p.dma("sync", sx2[:, :, 0:256] if False else
                              sx2.rearrange("p c (u t) -> p c u t", u=2)
                              [:, :, s2], xin_s, ch_x2, wait=w)
                for oc in range(8):
                    bank = pp[4 + oc % 2]
                    for c in range(8):
                        w = []
                        if c == 0:
                            w = [m for m in (psO_war[0], psO_war[1])
                                 if m is not None]
                            if m_oevac[oc % 2] is not None:
                                w.append((sS, m_oevac[oc % 2]))
                            if oc == 0 and mark_ln2 is not None:
                                w.append((sV, mark_ln2))
                        p.do("tensor",
                             lambda E, oc=oc, c=c, s2=s2, bank=bank: E.matmul(
                                 bank[:, 0:256],
                                 wob[:, c, oc * 128:(oc + 1) * 128],
                                 aoutT[:, c, s2 * 256:(s2 + 1) * 256],
                                 start=(c == 0), stop=(c == 7)),
                             wait=w, inc=(sT if c == 7 else None))
                    mm = p.mark(sT)
                    wev = [(sT, mm)]
                    if oc == 0 and mark_store is not None:
                        wev.append((ch_o, mark_store))
                    m_oevac[oc % 2] = p.do(
                        "scalar",
                        lambda E, oc=oc, bank=bank: E.activation(
                            fo[:, oc], bank[:, 0:256], AF.Identity),
                        wait=wev, inc=sS)
                mk_bc2 = emit_ln_stats(
                    g, 256, lambda c: fo[:, c], tmp2, rw2, C["c_ones16"],
                    C["c_ones_row"], C["c_eps"], pp[6], pp[7], pp[4], pp[5],
                    src_ready=[(sS, m_oevac[1])],
                    war_bc=[(sS, m_oevac[0]), (sS, m_oevac[1])])
                sxs2 = sx2.rearrange("p c (u t) -> p c u t", u=2)
                for c in range(8):
                    w = [(sT, mk_bc2), (ch_x2, mk_xs)] if c == 0 else []
                    if c == 0 and mark_store is not None:
                        w.append((ch_o, mark_store))
                    p.do("vector",
                         lambda E, c=c: E.tensor_tensor(tmp2[:, c], fo[:, c],
                                                        pp[4][:, 0:256],
                                                        ALU.subtract),
                         wait=w)
                p.do("vector", lambda E: E.drain())
                for c in range(8):
                    p.do("vector",
                         lambda E, c=c: E.tensor_tensor(tmp2[:, c], tmp2[:, c],
                                                        pp[5][:, 0:256],
                                                        ALU.mult))
                p.do("vector", lambda E: E.drain())
                for c in range(8):
                    p.do("vector",
                         lambda E, c=c, s2=s2: E.tensor_tensor(
                             fo[:, c], sxs2[:, c, s2], tmp2[:, c], ALU.add))
                mark_ln2 = p.mil("vector", sV)
                mark_store = p.dma("sync", xout_s, fo[:], ch_o,
                                   wait=[(sV, mark_ln2)])
    g.barrier()
    es2.close()


def _prep_l1attn(inputs):
    d = {}
    for n, key in [("wqT", "Wq"), ("wkT", "Wk"), ("wvT", "Wv"), ("woT", "Wo")]:
        d["l1a_" + n] = np.ascontiguousarray(
            _as32(inputs["L1_attn_" + key]).T)
    DK = 64
    half = DK // 2
    inv_freq = (10000.0 ** (-np.arange(half) * 2.0 / DK)).astype(np.float64)
    pos = np.arange(T).astype(np.float64)
    ang = pos[None, :] * inv_freq[:, None]
    C_ = np.repeat(np.cos(ang), 2, axis=0)
    S_ = np.repeat(np.sin(ang), 2, axis=0)
    S_[0::2, :] *= -1.0
    d["l1a_ropeC"] = np.ascontiguousarray(
        np.tile(C_, (2, 1)).astype(np.float32))
    d["l1a_ropeS"] = np.ascontiguousarray(
        np.tile(S_, (2, 1)).astype(np.float32))
    masks = np.zeros((128, 4, 512), np.float32)
    kk = np.arange(128)[:, None]
    qq = np.arange(512)[None, :]
    for m in range(4):
        masks[:, m, :] = (m * 128 + kk <= qq)
    d["l1a_masks"] = masks.astype(BF)
    qn = np.zeros((128, 30), np.float32)
    qn[0:64, 14] = 1.0
    qn[64:128, 15] = 1.0
    d["l1a_qnpat"] = qn
    return d


def _prep_l0attnA(inputs):
    d = {}
    for nm, key in [("qpT", "q_proj_w"), ("kpT", "k_proj_w"),
                    ("vpT", "v_proj_w"), ("bpT", "b_proj_w")]:
        d["l0a_" + nm] = np.ascontiguousarray(
            _as32(inputs["L0_attn_" + key]).T)
    for nm, key in [("qcv", "q_conv_w"), ("kcv", "k_conv_w"),
                    ("vcv", "v_conv_w")]:
        w = _as32(inputs["L0_attn_" + key])
        d["l0a_" + nm] = np.ascontiguousarray(
            w.reshape(8, 128, 4).transpose(1, 0, 2))
    r = np.arange(32)[:, None]
    cc = np.arange(32)[None, :]
    blkU = -(r < cc).astype(np.float32)
    blkL = -(r > cc).astype(np.float32)
    blkI = (r <= cc).astype(np.float32)
    d["l0a_mku"] = np.ascontiguousarray(np.tile(blkU, (4, 4))).astype(BF)
    d["l0a_mkl"] = np.ascontiguousarray(np.tile(blkL, (4, 4))).astype(BF)
    d["l0a_mki"] = np.ascontiguousarray(np.tile(blkI, (4, 4))).astype(BF)
    d["l0a_i32q"] = np.ascontiguousarray(
        np.tile(np.eye(32, dtype=np.float32), (4, 1))).astype(BF)
    d["l0a_idf"] = np.eye(128, dtype=np.float32).astype(BF)
    return d


def _prep_l0attnB(inputs):
    d = {}
    d["l0a_g1T"] = np.ascontiguousarray(_as32(inputs["L0_attn_gate_w1"]).T)
    d["l0a_g2T"] = np.ascontiguousarray(_as32(inputs["L0_attn_gate_w2"]).T)
    d["l0a_opT"] = np.ascontiguousarray(_as32(inputs["L0_attn_o_proj_w"]).T)
    firws = {3: inputs["L0_attn_fir_w3"], 7: inputs["L0_attn_fir_w7"],
             15: inputs["L0_attn_fir_w15"], 31: inputs["L0_attn_fir_w31"]}
    for k in (3, 7, 15, 31):
        w = _as32(firws[k])
        d["l0a_fw%d" % k] = np.ascontiguousarray(
            w.reshape(8, 128, k).transpose(1, 0, 2))
    d["l0a_gb1"] = np.ascontiguousarray(
        _as32(inputs["L0_attn_gate_b1"]).reshape(16, 128).T)
    d["l0a_gb2"] = _as32(inputs["L0_attn_gate_b2"]).reshape(24, 1)
    alst = _as32(inputs["L0_attn_alpha_stat"])
    d["l0a_alst"] = np.ascontiguousarray(np.tile(alst, 4).reshape(24, 1))
    d["l0a_ltmp"] = _as32(inputs["L0_attn_gate_log_temp"]).reshape(4, 1)
    tx = np.zeros((4, 24), np.float32)
    for h in range(4):
        tx[h, 6 * h:6 * h + 6] = 1.0
    d["l0a_texp"] = tx
    d["l0a_onw"] = np.ascontiguousarray(
        _as32(inputs["L0_attn_o_norm_w"]).reshape(2, 128).T)
    stp = np.zeros((128, 47), np.float32)
    stp[:, 23] = 1.0 / 256.0
    d["l0a_stpat"] = stp.astype(BF)
    d["l0a_orow"] = np.ones((1, 128), np.float32).astype(BF)
    taps = []
    for k in (3, 7, 15, 31):
        for j in range(k):
            taps.append((k, j))
    d32 = np.zeros((8, 128, 56, 32), np.float32)
    eye = np.eye(32, dtype=np.float32)
    for c in range(8):
        for ti, (k, j) in enumerate(taps):
            wv = np.asarray(firws[k], dtype=np.float32)
            for q in range(4):
                ch = wv[c * 128 + 32 * q:c * 128 + 32 * q + 32, j]
                d32[c, 32 * q:32 * q + 32, ti, :] = eye * ch[None, :]
    d["l0b_d32"] = np.ascontiguousarray(
        d32.reshape(8, 128, 56 * 32)).astype(BF)
    return d


def _as32(a):
    return np.ascontiguousarray(np.asarray(a, dtype=np.float32))


def _prep_consts():
    c = {}
    c["c_ones_col"] = np.ones((128, 1), np.float32)
    c["c_ones16"] = np.ones((128, 1), BF)
    c["c_ones_row"] = np.ones((1, 128), np.float32)
    c["c_eps"] = np.full((1, 1), 1e-5, np.float32)
    c["c_eps12"] = np.full((1, 1), 1e-12, np.float32)
    c["c_one"] = np.full((1, 1), 1.0, np.float32)
    c["c_idf32"] = np.eye(128, dtype=np.float32)
    return c


CONST_DECL = [
    ("c_ones_col", [128, 1], F32),
    ("c_ones16", [128, 1], BF16),
    ("c_ones_row", [1, 128], F32),
    ("c_eps", [1, 1], F32),
    ("c_eps12", [1, 1], F32),
    ("c_one", [1, 1], F32),
    ("c_idf32", [128, 128], F32),
]


def _prep_l1ffn(inputs):
    w1 = _as32(inputs["L1_ffn_w1"])
    w2 = _as32(inputs["L1_ffn_w2"])
    b1 = _as32(inputs["L1_ffn_b1"])
    b2 = _as32(inputs["L1_ffn_b2"])
    return {
        "l1f_w1t": np.ascontiguousarray(w1.T),
        "l1f_w1n": w1,
        "l1f_w2t": np.ascontiguousarray(w2.T),
        "l1f_w2n": w2,
        "l1f_b1": np.ascontiguousarray(b1.reshape(32, 128).T),
        "l1f_b2": np.ascontiguousarray(b2.reshape(8, 128).T),
    }


def build_single(block):
    nc = bass.Bass("TRN2", target_bir_lowering=False)
    g = G(nc)
    xin = g.din("xin", [D, T])
    consts, mk = g.load_consts(CONST_DECL)
    if block == "l0attnA":
        emit_l0attnA(g, xin, consts, hand="out")
        g.p.finalize()
        g.es.close()
        return g
    xout = g.dout("xout", [D, T])
    if block == "l0attnB":
        emit_l0attnB(g, xin, xout, consts)
        g.p.finalize()
        g.es.close()
        return g
    if block == "l1ffn":
        emit_l1ffn(g, xin, xout, consts)
    elif block == "l0ffn":
        emit_l0ffn(g, xin, xout, consts)
    elif block == "l1attn":
        emit_l1attn(g, xin, xout, consts)
    else:
        raise NotImplementedError(block)
    g.p.finalize()
    g.es.close()
    return g


def build_fused():
    nc = bass.Bass("TRN2", target_bir_lowering=False)
    g = G(nc)
    xin = g.din("xin", [D, T])
    xout = g.dout("xout", [D, T])
    xA = g.dint("x_scrA", [D, T])
    xB = g.dint("x_scrB", [D, T])
    consts, mk = g.load_consts(CONST_DECL)
    g.push_arena()
    emit_l0attnA(g, xin, consts, hand="internal")
    g.pop_arena()
    g.push_arena()
    emit_l0attnB(g, xin, xB, consts)
    g.pop_arena()
    g.push_arena()
    emit_l0ffn(g, xB, xA, consts)
    g.pop_arena()
    g.push_arena()
    emit_l1attn(g, xA, xB, consts)
    g.pop_arena()
    g.push_arena()
    emit_l1ffn(g, xB, xout, consts)
    g.pop_arena()
    g.p.finalize()
    g.es.close()
    return g


def _prep_all(inputs):
    d = dict(_prep_consts())
    d.update(_prep_l0attnA(inputs))
    d.update(_prep_l0attnB(inputs))
    d.update(_prep_l0ffn(inputs))
    d.update(_prep_l1attn(inputs))
    d.update(_prep_l1ffn(inputs))
    return d


_G_CACHE = {}


def _get_fused():
    if "fused" not in _G_CACHE:
        _G_CACHE["fused"] = build_fused()
    return _G_CACHE["fused"]


def kernel(**inputs):
    x = np.asarray(inputs["L0_attn_hidden_states"], dtype=np.float32)
    prep = _prep_all(inputs)
    g = _get_fused()
    in_maps = []
    for b in range(NCORES):
        m = {}
        for name in g.innames:
            if name == "xin":
                m[name] = np.ascontiguousarray(x[b].T)
            else:
                m[name] = prep[name]
        in_maps.append(m)
    res = run_bass_kernel_spmd(g.nc, in_maps, core_ids=list(range(NCORES)))
    out = np.stack([np.asarray(r["xout"]).T for r in res.results])
    return np.ascontiguousarray(out.astype(np.float32))


# version 15 (final)
# speedup vs baseline: 1.1956x; 1.0039x over previous
import numpy as np
import ml_dtypes
from contextlib import ExitStack

import concourse.bass as bass
import concourse.mybir as mybir
from concourse.bass_utils import run_bass_kernel_spmd

F32 = mybir.dt.float32
BF16 = mybir.dt.bfloat16
AF = mybir.ActivationFunctionType
ALU = mybir.AluOpType
AX = mybir.AxisListType
BF = ml_dtypes.bfloat16

T = 2048
D = 1024
NCORES = 8
ENGINES = ("tensor", "vector", "scalar", "gpsimd", "sync")


class Prog:

    def __init__(self, nc, es):
        self.nc = nc
        self.es = es
        self.stream = {e: [] for e in ENGINES}
        self.cnt = {}
        self.channels = []
        self.last_wait = {e: {} for e in ENGINES}

    def sem(self, name):
        s = self.es.enter_context(self.nc.semaphore(name))
        self.cnt[s] = 0
        return s

    def wait(self, eng, sem, val):
        if val is None or val <= 0:
            return
        lw = self.last_wait[eng]
        if lw.get(sem, -1) >= val:
            return
        lw[sem] = val
        self.stream[eng].append(lambda E, s=sem, v=val: E.wait_ge(s, v))

    def do(self, eng, build, wait=(), inc=None, n=1):
        import sys as _sys
        _fr = []
        for _i in range(1, 5):
            try:
                f = _sys._getframe(_i)
                _fr.append(str(f.f_lineno))
            except ValueError:
                break
        build._src = ",".join(_fr)
        for (s, v) in wait:
            self.wait(eng, s, v)
        if inc is None:
            self.stream[eng].append(build)
            return None
        self.cnt[inc] += n
        tgt = self.cnt[inc]
        self.stream[eng].append(
            lambda E, b=build, s=inc, k=n: b(E).then_inc(s, k))
        return tgt

    def dma(self, eng, out, in_, sem, wait=(), n=16):
        return self.do(eng, lambda E, o=out, i=in_: E.dma_start(out=o, in_=i),
                       wait=wait, inc=sem, n=n)

    def mark(self, sem):
        return self.cnt[sem]

    def channel(self, name):
        s = self.sem(name)
        self.channels.append(s)
        return s

    def mil(self, eng, sem, wait=()):
        return self.do(eng, lambda E: E.drain(), wait=wait, inc=sem)

    def finalize(self):
        def run(E, ee):
            for fn in self.stream[ee]:
                try:
                    fn(E)
                except Exception:
                    print("FAILED op from", getattr(fn, "_src", "?"), "eng", ee)
                    raise
        with self.nc.Block() as blk:
            for e in ENGINES:
                getattr(blk, e)(lambda E, ee=e: run(E, ee))


class G:

    def __init__(self, nc):
        self.nc = nc
        self.es = ExitStack()
        self.p = Prog(nc, self.es)
        p = self.p
        self.sT = p.sem("sT")
        self.sV = p.sem("sV")
        self.sS = p.sem("sS")
        self.sG = p.sem("sG")
        self.dSY = p.sem("dSY")
        self.dGP = p.sem("dGP")
        self.bar = p.sem("bar")
        self.pool = [self.es.enter_context(
            nc.psum_tensor(f"pp{i}", [128, 512], F32)) for i in range(8)]
        self.dram = {}
        self.innames = []
        self.arenas = []

    def din(self, name, shape, dtype=F32):
        ap = self.nc.dram_tensor(name, list(shape), dtype,
                                 kind="ExternalInput").ap()
        self.dram[name] = ap
        self.innames.append(name)
        return ap

    def dout(self, name, shape, dtype=F32):
        ap = self.nc.dram_tensor(name, list(shape), dtype,
                                 kind="ExternalOutput").ap()
        self.dram[name] = ap
        return ap

    def dint(self, name, shape, dtype=F32):
        ap = self.nc.dram_tensor(name, list(shape), dtype,
                                 kind="Internal").ap()
        self.dram[name] = ap
        return ap

    def sb(self, name, shape, dtype=F32):
        es = self.arenas[-1] if self.arenas else self.es
        return es.enter_context(
            self.nc.sbuf_tensor(name, list(shape), dtype))

    def push_arena(self):
        self.arenas.append(ExitStack())

    def pop_arena(self):
        self.arenas.pop().close()

    def barrier(self):
        p = self.p
        p.wait("sync", self.dSY, p.cnt[self.dSY])
        p.wait("gpsimd", self.dGP, p.cnt[self.dGP])
        for ch in p.channels:
            p.wait("sync", ch, p.cnt[ch])
        for e in ("tensor", "vector", "scalar"):
            p.do(e, lambda E: E.drain(), inc=self.bar)
        for e in ("gpsimd", "sync"):
            p.do(e, lambda E: E.nop(), inc=self.bar)
        tgt = p.cnt[self.bar]
        for e in ENGINES:
            p.wait(e, self.bar, tgt)

    def load_consts(self, names_shapes):
        out = {}
        for name, shape, dtype in names_shapes:
            ap = self.din(name, shape, dtype)
            t = self.sb("s_" + name, shape, dtype)
            self.p.dma("sync", t[:], ap, self.dSY)
            out[name] = t
        return out, self.p.mark(self.dSY)


def emit_ln_stats_pre(g, TT, src_sl, sq_buf, c_ones16, pp_s1, pp_s2,
                      src_ready=(), war_rows=()):
    p, sT, sV, sS = g.p, g.sT, g.sV, g.sS
    bv = sq_buf.bitcast(BF16)
    for c in range(8):
        p.do("vector",
             lambda E, c=c: E.tensor_copy(bv[:, c, TT:2 * TT], src_sl(c)),
             wait=(list(src_ready) + list(war_rows)) if c == 0 else ())
    m_xc = p.mil("vector", sV)
    for c in range(8):
        p.do("scalar",
             lambda E, c=c: E.activation(bv[:, c, 0:TT], src_sl(c), AF.Square),
             wait=src_ready if c == 0 else ())
    m_sq = p.mil("scalar", sS)
    for c in range(8):
        p.do("tensor",
             lambda E, c=c: E.matmul(pp_s1[0:1, 0:TT], c_ones16[:],
                                     bv[:, c, TT:2 * TT], start=(c == 0),
                                     stop=(c == 7)),
             wait=([(sV, m_xc)] + list(war_rows)) if c == 0 else ())
    for c in range(8):
        p.do("tensor",
             lambda E, c=c: E.matmul(pp_s2[0:1, 0:TT], c_ones16[:],
                                     bv[:, c, 0:TT], start=(c == 0),
                                     stop=(c == 7)),
             wait=[(sS, m_sq)] if c == 0 else (),
             inc=(g.sT if c == 7 else None))
    return p.mark(g.sT)


def emit_ln_stats_post(g, TT, rows, c_ones_row, c_eps, pp_s1, pp_s2,
                       pp_bcm, pp_bcr, mk_stats, war_bc=()):
    p, sT, sV, sS = g.p, g.sT, g.sV, g.sS
    p.do("vector", lambda E: E.tensor_scalar_mul(rows[:, 0, 0:TT], pp_s1[0:1, 0:TT],
                                                 1.0 / D),
         wait=[(g.sT, mk_stats)])
    m_m = p.mil("vector", sV)
    p.do("tensor", lambda E: E.matmul(pp_bcm[:, 0:TT], c_ones_row[:],
                                      rows[:, 0, 0:TT], start=True, stop=True),
         wait=[(sV, m_m)] + list(war_bc))
    p.do("vector", lambda E: E.tensor_tensor(rows[:, 1, 0:TT], rows[:, 0, 0:TT],
                                             rows[:, 0, 0:TT], ALU.mult))
    p.do("vector", lambda E: E.drain())
    p.do("vector", lambda E: E.scalar_tensor_tensor(
        rows[:, 2, 0:TT], rows[:, 1, 0:TT], -float(D), pp_s2[0:1, 0:TT],
        ALU.mult, ALU.add))
    m_v = p.mil("vector", sV)
    p.do("scalar", lambda E: E.activation(rows[:, 3, 0:TT], rows[:, 2, 0:TT],
                                          AF.Sqrt, scale=1.0 / D,
                                          bias=c_eps[0:1, 0:1]),
         wait=[(sV, m_v)])
    m_s = p.mil("scalar", sS)
    p.do("vector", lambda E: E.reciprocal(rows[:, 4, 0:TT], rows[:, 3, 0:TT]),
         wait=[(sS, m_s)])
    m_r = p.mil("vector", sV)
    mk_bc = p.do("tensor", lambda E: E.matmul(pp_bcr[:, 0:TT], c_ones_row[:],
                                              rows[:, 4, 0:TT], start=True,
                                              stop=True),
                 wait=[(sV, m_r)] + list(war_bc), inc=g.sT)
    return mk_bc


def emit_ln_stats(g, TT, src_sl, sq_buf, rows, c_ones16, c_ones_row, c_eps,
                  pp_s1, pp_s2, pp_bcm, pp_bcr, src_ready=(), war_rows=(),
                  war_bc=()):
    mk_stats = emit_ln_stats_pre(g, TT, src_sl, sq_buf, c_ones16, pp_s1,
                                 pp_s2, src_ready, war_rows)
    return emit_ln_stats_post(g, TT, rows, c_ones_row, c_eps, pp_s1, pp_s2,
                              pp_bcm, pp_bcr, mk_stats, war_bc)


def emit_l1ffn(g, xin, xout, C):
    nc, p = g.nc, g.p
    sT, sV, sS = g.sT, g.sV, g.sS
    TT = 256
    NTILE = T // TT
    H1 = 4096

    w1t = g.din("l1f_w1t", [D, H1])
    w1n = g.din("l1f_w1n", [H1, D])
    w2t = g.din("l1f_w2t", [H1, D])
    w2n = g.din("l1f_w2n", [D, H1])
    b1h = g.din("l1f_b1", [128, 32])
    b2h = g.din("l1f_b2", [128, 8])

    sgn1 = g.sb("sgn1", [128, 8, H1], BF16)
    sgn2 = g.sb("sgn2", [128, 32, D], BF16)
    a1T = g.sb("a1T", [128, 32])
    a2T = g.sb("a2T", [128, 8])
    b1T = g.sb("b1T", [128, 32])
    b2T = g.sb("b2T", [128, 8])

    p.dma("sync", b1T[:], b1h, g.dSY)
    p.dma("sync", b2T[:], b2h, g.dSY)

    ch_st = [p.channel("l1f_st0"), p.channel("l1f_st1")]
    ch_ns = [p.channel("l1f_ns0"), p.channel("l1f_ns1")]
    with nc.sbuf_tensor("l1f_stage", [128, 2, H1], F32) as stage, \
         nc.sbuf_tensor("l1f_nstage", [128, 2, H1], F32) as nstage:
        sgn_marks = [None, None]
        red_marks = [None, None]
        for c in range(8):
            w = [(sS, sgn_marks[c % 2])] if sgn_marks[c % 2] else []
            mk = p.dma("gpsimd", stage[:, c % 2],
                       w1t[c * 128:(c + 1) * 128, :], ch_st[c % 2], wait=w)
            sgn_marks[c % 2] = p.do(
                "scalar",
                lambda E, c=c: E.activation(sgn1[:, c], stage[:, c % 2],
                                            AF.Sign),
                wait=[(ch_st[c % 2], mk)], inc=sS)
        for c in range(32):
            w = [(sS, sgn_marks[c % 2])]
            mk = p.dma("gpsimd", stage[:, c % 2][:, 0:D],
                       w2t[c * 128:(c + 1) * 128, :], ch_st[c % 2], wait=w)
            sgn_marks[c % 2] = p.do(
                "scalar",
                lambda E, c=c: E.activation(sgn2[:, c], stage[:, c % 2][:, 0:D],
                                            AF.Sign),
                wait=[(ch_st[c % 2], mk)], inc=sS)
        for c in range(32):
            w = [(sV, red_marks[c % 2])]
            mk = p.dma("sync", nstage[:, c % 2][:, 0:D],
                       w1n[c * 128:(c + 1) * 128, :], ch_ns[c % 2], wait=w)
            red_marks[c % 2] = p.do(
                "vector",
                lambda E, c=c: E.reduce_sum(a1T[:, c:c + 1],
                                            nstage[:, c % 2][:, 0:D],
                                            axis=AX.X,
                                            apply_absolute_value=True),
                wait=[(ch_ns[c % 2], mk)], inc=sV)
        for c in range(8):
            w = [(sV, red_marks[c % 2])]
            mk = p.dma("sync", nstage[:, c % 2],
                       w2n[c * 128:(c + 1) * 128, :], ch_ns[c % 2], wait=w)
            red_marks[c % 2] = p.do(
                "vector",
                lambda E, c=c: E.reduce_sum(a2T[:, c:c + 1], nstage[:, c % 2],
                                            axis=AX.X,
                                            apply_absolute_value=True),
                wait=[(ch_ns[c % 2], mk)], inc=sV)
        p.do("vector", lambda E: E.drain())
        p.do("vector", lambda E: E.tensor_scalar_mul(a1T[:], a1T[:], 1.0 / D))
        p.do("vector", lambda E: E.tensor_scalar_mul(a2T[:], a2T[:], 1.0 / H1))
        g.barrier()

    sx = g.sb("l1f_sx", [128, 16, TT])
    xsq = g.sb("l1f_xsq", [128, 8, TT])
    xsq1 = g.sb("l1f_xsq1", [128, 8, TT])
    yb = g.sb("l1f_yb", [128, 8, TT], BF16)
    h1b = g.sb("l1f_h1b", [128, 32, TT], BF16)
    fb = g.sb("l1f_f", [128, 8, TT])
    tmp = g.sb("l1f_tmp", [128, 8, TT])
    rows = g.sb("l1f_rows", [1, 5, TT])
    rows2 = g.sb("l1f_rows2", [1, 5, TT])
    xo = xsq

    pp = g.pool
    ch_x = [p.channel("l1f_chx0"), p.channel("l1f_chx1")]
    ch_o = p.channel("l1f_cho")
    mark_apply = None
    mark_apply_hist = [None, None]
    mark_store = None
    bank_evac = [None, None]
    bank_fe = [None, None]
    mark_evac = [None] * 32
    mark_fe = [None] * 8
    mk_bc_next = None

    for t in range(NTILE):
        t0 = t * TT
        sl = t % 2
        xin_sl = xin[:, t0:t0 + TT].rearrange("(c p) t -> p c t", p=128)
        xout_sl = xout[:, t0:t0 + TT].rearrange("(c p) t -> p c t", p=128)

        if t == 0:
            mk_x = p.dma("sync", sx[:, 0:8], xin_sl, ch_x[0])
            mk_bc = emit_ln_stats(
                g, TT, lambda c: sx[:, c], xsq1, rows2, C["c_ones16"],
                C["c_ones_row"], C["c_eps"],
                pp[0], pp[1], pp[6], pp[7],
                src_ready=[(ch_x[0], mk_x)])
        else:
            mk_bc = mk_bc_next
        for c in range(8):
            p.do("vector",
                 lambda E, c=c, sl=sl: E.tensor_tensor(
                     xsq1[:, c], sx[:, sl * 8 + c],
                     pp[6][:, 0:TT], ALU.subtract),
                 wait=[(sT, mk_bc)] if c == 0 else ())
        p.do("vector", lambda E: E.drain())
        for c in range(8):
            p.do("vector",
                 lambda E, c=c: E.tensor_tensor(yb[:, c], xsq1[:, c],
                                                pp[7][:, 0:TT], ALU.mult))
        mark_yb = p.mil("vector", sV)

        for jc in range(32):
            bank = pp[jc % 2]
            for c in range(8):
                w = []
                if c == 0:
                    w = [(sV, mark_yb)]
                    if bank_evac[jc % 2] is not None:
                        w.append((sS, bank_evac[jc % 2]))
                p.do("tensor",
                     lambda E, jc=jc, c=c, bank=bank: E.matmul(
                         bank[:, 0:TT],
                         sgn1[:, c][:, jc * 128:(jc + 1) * 128],
                         yb[:, c], start=(c == 0), stop=(c == 7)),
                     wait=w, inc=(sT if c == 7 else None))
            mm = p.mark(sT)
            mark_evac[jc] = bank_evac[jc % 2] = p.do(
                "scalar",
                lambda E, jc=jc, bank=bank: E.activation(
                    h1b[:, jc], bank[:, 0:TT], AF.Gelu,
                    scale=a1T[:, jc:jc + 1], bias=b1T[:, jc:jc + 1]),
                wait=[(sT, mm)], inc=sS)

        for oc in range(8):
            bank = pp[6 + (oc % 2)]
            for jc in range(32):
                w = []
                if jc == 0:
                    w = [(sS, mark_evac[31])]
                    if bank_fe[oc % 2] is not None:
                        w.append((sS, bank_fe[oc % 2]))
                p.do("tensor",
                     lambda E, oc=oc, jc=jc, bank=bank: E.matmul(
                         bank[:, 0:TT],
                         sgn2[:, jc][:, oc * 128:(oc + 1) * 128],
                         h1b[:, jc], start=(jc == 0), stop=(jc == 31)),
                     wait=w, inc=(sT if jc == 31 else None))
            mm = p.mark(sT)
            mark_fe[oc] = bank_fe[oc % 2] = p.do(
                "scalar",
                lambda E, oc=oc, bank=bank: E.activation(
                    fb[:, oc], bank[:, 0:TT], AF.Identity,
                    scale=a2T[:, oc:oc + 1], bias=b2T[:, oc:oc + 1]),
                wait=[(sT, mm)], inc=sS)

        if t + 1 < NTILE:
            nsl = (t + 1) % 2
            t1 = (t + 1) * TT
            xin_n = xin[:, t1:t1 + TT].rearrange("(c p) t -> p c t", p=128)
            w = ([(sV, mark_apply_hist[nsl])]
                 if mark_apply_hist[nsl] is not None else [])
            mk_xn = p.dma("sync", sx[:, nsl * 8:nsl * 8 + 8], xin_n,
                          ch_x[nsl], wait=w)
            mkst1 = emit_ln_stats_pre(
                g, TT, lambda c, s=nsl: sx[:, s * 8 + c], xsq1,
                C["c_ones16"], pp[0], pp[1],
                src_ready=[(ch_x[nsl], mk_xn), (sV, mark_yb)])

        w = [(sS, mark_fe[7])]
        if mark_store is not None:
            w.append((ch_o, mark_store))
        mk_bc2 = emit_ln_stats(
            g, TT, lambda c: fb[:, c], xsq, rows, C["c_ones16"],
            C["c_ones_row"], C["c_eps"],
            pp[2], pp[3], pp[4], pp[5],
            src_ready=w)
        if t + 1 < NTILE:
            mk_bc_next = emit_ln_stats_post(
                g, TT, rows2, C["c_ones_row"], C["c_eps"], pp[0], pp[1],
                pp[6], pp[7], mkst1, war_bc=[(sS, mark_fe[7])])
        for c in range(8):
            w = [(sT, mk_bc2)] if c == 0 else []
            if c == 0 and mark_store is not None:
                w.append((ch_o, mark_store))
            p.do("vector",
                 lambda E, c=c: E.tensor_tensor(xsq[:, c], fb[:, c],
                                                pp[4][:, 0:TT], ALU.subtract),
                 wait=w)
        p.do("vector", lambda E: E.drain())
        for c in range(8):
            p.do("vector",
                 lambda E, c=c: E.tensor_tensor(tmp[:, c], xsq[:, c],
                                                pp[5][:, 0:TT], ALU.mult))
        p.do("vector", lambda E: E.drain())
        for c in range(8):
            p.do("vector",
                 lambda E, c=c, sl=sl: E.tensor_tensor(
                     xo[:, c], sx[:, sl * 8 + c], tmp[:, c], ALU.add))
        mark_apply = p.mil("vector", sV)
        mark_apply_hist[sl] = mark_apply
        mark_store = p.dma("sync", xout_sl, xo[:], ch_o,
                           wait=[(sV, mark_apply)])
    g.barrier()


def emit_l0ffn(g, xin, xout, C):
    nc, p = g.nc, g.p
    sT, sV, sS = g.sT, g.sV, g.sS
    TT = 256
    NTILE = T // TT

    wnames = [("cn_w1", [D, D], 8), ("cn_w2", [D, 512], 8),
              ("cn_w3", [512, 256], 4), ("cn_w4", [256, 1], 2),
              ("gc_w1", [D, 512], 8), ("gc_w2", [512, 256], 4),
              ("gc_w3", [256, 1], 2)]
    wd = {n: g.din("l0f_" + n, shp) for n, shp, _ in wnames}
    wb = {n: g.sb("l0f_b_" + n, [128, shp[0] // 128, shp[1]], BF16)
          for n, shp, _ in wnames}
    for n, shp, nch in wnames:
        for c in range(nch):
            p.dma("gpsimd", wb[n][:, c], wd[n][c * 128:(c + 1) * 128, :],
                  g.dGP)
    bdecl = [("cn_b1", [128, 8]), ("cn_b2", [128, 4]), ("cn_b3", [128, 2]),
             ("cn_b4", [1, 1]), ("gc_b1", [128, 4]), ("gc_b2", [128, 2]),
             ("gc_b3", [1, 1]), ("arng", [1, 1])]
    bb = {}
    for n, shp in bdecl:
        ap = g.din("l0f_" + n, shp)
        bb[n] = g.sb("l0f_s_" + n, shp)
        p.dma("sync", bb[n][:], ap, g.dSY)

    pp = g.pool
    sx = g.sb("l0f_sx", [128, 16, TT])
    xsq = g.sb("l0f_xsq", [128, 8, TT])
    xsq2 = g.sb("l0f_xsq2", [128, 8, TT])
    yb = g.sb("l0f_yb", [128, 8, TT], BF16)
    rows = g.sb("l0f_rows", [1, 5, TT])
    rows2 = g.sb("l0f_rows2", [1, 5, TT])
    gacc = g.sb("l0f_gacc", [128, 8, NTILE])
    gcv = g.sb("l0f_gcv", [128, 8])
    gcb = g.sb("l0f_gcb", [128, 8], BF16)
    srow = g.sb("l0f_srow", [1, 8])
    g.barrier()

    ch_x = [p.channel("l0f_chx0"), p.channel("l0f_chx1")]
    ch_o = p.channel("l0f_cho")

    mark_red = None
    mark_red_hist = [None, None]
    statsb = [(pp[2], pp[3]), (pp[6], pp[7])]
    rowsAB = [rows, rows2]
    mkst = [None, None]
    xin_0 = xin[:, 0:TT].rearrange("(c p) t -> p c t", p=128)
    mk_x0 = p.dma("sync", sx[:, 0:8], xin_0, ch_x[0])
    mkst[0] = emit_ln_stats_pre(
        g, TT, lambda c: sx[:, c], xsq2, C["c_ones16"],
        statsb[0][0], statsb[0][1], src_ready=[(ch_x[0], mk_x0)])
    for t in range(NTILE):
        t0 = t * TT
        sl = t % 2
        if t + 1 < NTILE:
            nsl = 1 - sl
            t1 = (t + 1) * TT
            xin_n = xin[:, t1:t1 + TT].rearrange("(c p) t -> p c t", p=128)
            w = ([(sV, mark_red_hist[nsl])]
                 if mark_red_hist[nsl] is not None else [])
            mk_xn = p.dma("sync", sx[:, nsl * 8:nsl * 8 + 8], xin_n,
                          ch_x[nsl], wait=w)
            wr = [(sT, mkst[sl])]
            if mark_red_hist[nsl] is not None:
                wr.append((sV, mark_red_hist[nsl]))
            mkst[nsl] = emit_ln_stats_pre(
                g, TT, lambda c, s=nsl: sx[:, s * 8 + c], xsq2,
                C["c_ones16"], statsb[nsl][0], statsb[nsl][1],
                src_ready=[(ch_x[nsl], mk_xn), (sT, mkst[sl])],
                war_rows=wr)
        mk_bc = emit_ln_stats_post(
            g, TT, rowsAB[sl], C["c_ones_row"], C["c_eps"],
            statsb[sl][0], statsb[sl][1], pp[4], pp[5], mkst[sl],
            war_bc=[(sV, mark_red)] if mark_red else ())
        for c in range(8):
            p.do("vector",
                 lambda E, c=c, sl=sl: E.tensor_tensor(
                     xsq[:, c], sx[:, sl * 8 + c],
                     pp[4][:, 0:TT], ALU.subtract),
                 wait=[(sT, mk_bc)] if c == 0 else ())
        p.do("vector", lambda E: E.drain())
        for c in range(8):
            p.do("vector",
                 lambda E, c=c: E.tensor_tensor(yb[:, c], xsq[:, c],
                                                pp[5][:, 0:TT], ALU.mult))
        p.do("vector", lambda E: E.drain())
        for c in range(8):
            p.do("vector",
                 lambda E, c=c, t=t: E.reduce_sum(gacc[:, c, t:t + 1],
                                                  yb[:, c], axis=AX.X))
        mark_red = p.mil("vector", sV)
        mark_red_hist[sl] = mark_red
    for c in range(8):
        p.do("vector",
             lambda E, c=c: E.reduce_sum(gcv[:, c:c + 1], gacc[:, c],
                                         axis=AX.X))
    p.do("vector", lambda E: E.drain())
    p.do("vector", lambda E: E.tensor_scalar_mul(gcb[:], gcv[:], 1.0 / T))
    m_gcb = p.mil("vector", sV)
    for jc in range(4):
        for c in range(8):
            p.do("tensor",
                 lambda E, jc=jc, c=c: E.matmul(
                     pp[0][:, jc:jc + 1], wb["gc_w1"][:, c, jc * 128:(jc + 1) * 128],
                     gcb[:, c:c + 1], start=(c == 0), stop=(c == 7)),
                 wait=[(sV, m_gcb)] if (jc == 0 and c == 0) else ())
    m_mm = p.mil("tensor", sT)
    g1b = g.sb("l0f_g1b", [128, 4], BF16)
    g2b = g.sb("l0f_g2b", [128, 2], BF16)
    for jc in range(4):
        p.do("scalar",
             lambda E, jc=jc: E.activation(g1b[:, jc:jc + 1],
                                           pp[0][:, jc:jc + 1], AF.Gelu,
                                           bias=bb["gc_b1"][:, jc:jc + 1]),
             wait=[(sT, m_mm)])
    m_g1 = p.mil("scalar", sS)
    for jc in range(2):
        for c in range(4):
            p.do("tensor",
                 lambda E, jc=jc, c=c: E.matmul(
                     pp[1][:, jc:jc + 1], wb["gc_w2"][:, c, jc * 128:(jc + 1) * 128],
                     g1b[:, c:c + 1], start=(c == 0), stop=(c == 3)),
                 wait=[(sS, m_g1)] if (jc == 0 and c == 0) else ())
    m_mm = p.mil("tensor", sT)
    for jc in range(2):
        p.do("scalar",
             lambda E, jc=jc: E.activation(g2b[:, jc:jc + 1],
                                           pp[1][:, jc:jc + 1], AF.Gelu,
                                           bias=bb["gc_b2"][:, jc:jc + 1]),
             wait=[(sT, m_mm)])
    m_g2 = p.mil("scalar", sS)
    for c in range(2):
        p.do("tensor",
             lambda E, c=c: E.matmul(pp[2][0:1, 0:1], wb["gc_w3"][:, c],
                                     g2b[:, c:c + 1], start=(c == 0),
                                     stop=(c == 1)),
             wait=[(sS, m_g2)] if c == 0 else ())
    m_mm = p.mil("tensor", sT)
    m_gf = p.do("scalar",
                lambda E: E.activation(srow[0:1, 4:5], pp[2][0:1, 0:1],
                                       AF.Sigmoid, bias=bb["gc_b3"][0:1, 0:1]),
                wait=[(sT, m_mm)], inc=sS)
    p.do("vector", lambda E: E.tensor_scalar_mul(srow[0:1, 0:1],
                                                 bb["arng"][0:1, 0:1], 0.6),
         wait=[(sS, m_gf)])
    p.do("vector", lambda E: E.tensor_tensor(srow[0:1, 2:3], bb["arng"][0:1, 0:1],
                                             srow[0:1, 4:5], ALU.mult))
    p.do("vector", lambda E: E.drain())
    p.do("vector", lambda E: E.tensor_scalar(srow[0:1, 3:4], srow[0:1, 2:3],
                                             1.4, 0.1, ALU.mult, ALU.add))
    p.do("vector", lambda E: E.drain())
    p.do("vector", lambda E: E.tensor_tensor(srow[0:1, 1:2], srow[0:1, 3:4],
                                             bb["arng"][0:1, 0:1],
                                             ALU.subtract))
    m_srow = p.mil("vector", sV)

    ostream = g.sb("l0f_ostream", [128, 8, TT])
    ob = g.sb("l0f_ob", [128, 8, TT], BF16)
    h1c = g.sb("l0f_h1c", [128, 8, TT], BF16)
    h2c = g.sb("l0f_h2c", [128, 4, TT], BF16)
    h3c = g.sb("l0f_h3c", [128, 2, TT], BF16)
    tw = g.sb("l0f_tw", [128, 8, TT])
    xo = g.sb("l0f_xo", [128, 8, TT])
    tfrow = g.sb("l0f_tfrow", [1, 1, TT])
    crow = g.sb("l0f_crow", [1, 1, TT])
    TSIGN = [ALU.add, ALU.add, ALU.subtract, ALU.subtract,
             ALU.subtract, ALU.subtract, ALU.add, ALU.add]

    mark_apply = m_srow
    mark_apply_hist = list(mark_red_hist)
    mark_store = None
    bank_h1 = [None] * 4
    bank_h2 = [None, None]
    bank_h3 = None
    mark_tf = None
    mark_crow = None
    mark_tw = None
    mk_bc_next = None
    mkst1 = None
    for t in range(NTILE):
        t0 = t * TT
        sl = t % 2
        xout_sl = xout[:, t0:t0 + TT].rearrange("(c p) t -> p c t", p=128)
        if t == 0:
            xin_sl = xin[:, t0:t0 + TT].rearrange("(c p) t -> p c t", p=128)
            mk_x = p.dma("sync", sx[:, 0:8], xin_sl, ch_x[0],
                         wait=[(sV, mark_apply_hist[0])])
            mkst1 = emit_ln_stats_pre(
                g, TT, lambda c: sx[:, c], xsq2, C["c_ones16"],
                pp[0], pp[1],
                src_ready=[(ch_x[0], mk_x)], war_rows=[(sV, m_srow)])
            mk_bc = emit_ln_stats_post(
                g, TT, rows2, C["c_ones_row"], C["c_eps"], pp[0], pp[1],
                pp[4], pp[5], mkst1,
                war_bc=[(sV, mark_red)])
        else:
            mk_bc = mk_bc_next
        for c in range(8):
            p.do("vector",
                 lambda E, c=c, sl=sl: E.tensor_tensor(
                     xsq[:, c], sx[:, sl * 8 + c],
                     pp[4][:, 0:TT], ALU.subtract),
                 wait=[(sT, mk_bc)] if c == 0 else ())
        p.do("vector", lambda E: E.drain())
        for c in range(8):
            p.do("vector",
                 lambda E, c=c: E.tensor_tensor(ostream[:, c], xsq[:, c],
                                                pp[5][:, 0:TT], ALU.mult))
        p.do("vector", lambda E: E.drain())
        for c in range(8):
            p.do("vector",
                 lambda E, c=c: E.tensor_copy(ob[:, c], ostream[:, c]))
        mark_ob = p.mil("vector", sV)

        for cyc in range(3):
            h1m = None
            for jc in range(8):
                bank = pp[jc % 4]
                for c in range(8):
                    w = []
                    if c == 0:
                        w = [(sV, mark_ob)]
                        if bank_h1[jc % 4] is not None:
                            w.append((sS, bank_h1[jc % 4]))
                        if jc % 4 >= 2 and bank_h2[jc % 2] is not None:
                            w.append((sS, bank_h2[jc % 2]))
                    p.do("tensor",
                         lambda E, jc=jc, c=c, bank=bank: E.matmul(
                             bank[:, 0:TT],
                             wb["cn_w1"][:, c, jc * 128:(jc + 1) * 128],
                             ob[:, c], start=(c == 0), stop=(c == 7)),
                         wait=w, inc=(sT if c == 7 else None))
                mm = p.mark(sT)
                h1m = bank_h1[jc % 4] = p.do(
                    "scalar",
                    lambda E, jc=jc, bank=bank: E.activation(
                        h1c[:, jc], bank[:, 0:TT], AF.Gelu,
                        bias=bb["cn_b1"][:, jc:jc + 1]),
                    wait=[(sT, mm)], inc=sS)
            h2m = None
            for jc in range(4):
                bank = pp[2 + jc % 2]
                for c in range(8):
                    w = []
                    if c == 0:
                        w = [(sS, h1m)]
                        if bank_h1[2 + jc % 2] is not None:
                            w.append((sS, bank_h1[2 + jc % 2]))
                        if bank_h2[jc % 2] is not None:
                            w.append((sS, bank_h2[jc % 2]))
                    p.do("tensor",
                         lambda E, jc=jc, c=c, bank=bank: E.matmul(
                             bank[:, 0:TT],
                             wb["cn_w2"][:, c, jc * 128:(jc + 1) * 128],
                             h1c[:, c], start=(c == 0), stop=(c == 7)),
                         wait=w, inc=(sT if c == 7 else None))
                mm = p.mark(sT)
                h2m = bank_h2[jc % 2] = p.do(
                    "scalar",
                    lambda E, jc=jc, bank=bank: E.activation(
                        h2c[:, jc], bank[:, 0:TT], AF.Gelu,
                        bias=bb["cn_b2"][:, jc:jc + 1]),
                    wait=[(sT, mm)], inc=sS)
            h3m = None
            for jc in range(2):
                bank = pp[4]
                for c in range(4):
                    w = []
                    if c == 0:
                        w = [(sS, h2m)]
                        if bank_h3 is not None:
                            w.append((sS, bank_h3))
                    p.do("tensor",
                         lambda E, jc=jc, c=c, bank=bank: E.matmul(
                             bank[:, 0:TT],
                             wb["cn_w3"][:, c, jc * 128:(jc + 1) * 128],
                             h2c[:, c], start=(c == 0), stop=(c == 3)),
                         wait=w, inc=(sT if c == 3 else None))
                mm = p.mark(sT)
                h3m = bank_h3 = p.do(
                    "scalar",
                    lambda E, jc=jc, bank=bank: E.activation(
                        h3c[:, jc], bank[:, 0:TT], AF.Gelu,
                        bias=bb["cn_b3"][:, jc:jc + 1]),
                    wait=[(sT, mm)], inc=sS)
            for c in range(2):
                w = [(sS, h3m)] if c == 0 else []
                if c == 0 and mark_tf is not None:
                    w.append((sS, mark_tf))
                p.do("tensor",
                     lambda E, c=c: E.matmul(pp[5][0:1, 0:TT], wb["cn_w4"][:, c],
                                             h3c[:, c], start=(c == 0),
                                             stop=(c == 1)),
                     wait=w, inc=(sT if c == 1 else None))
            mm = p.mark(sT)
            mark_tf = p.do(
                "scalar",
                lambda E: E.activation(tfrow[:, 0], pp[5][0:1, 0:TT],
                                       AF.Sigmoid, bias=bb["cn_b4"][0:1, 0:1]),
                wait=[(sT, mm)], inc=sS)
            mark_crow = p.do(
                "vector",
                lambda E: E.tensor_scalar(crow[:, 0], tfrow[:, 0],
                                          srow[0:1, 0:1], srow[0:1, 1:2],
                                          ALU.mult, ALU.add),
                wait=[(sS, mark_tf), (sV, m_srow)], inc=sV)
            mk_cb = p.do("tensor",
                         lambda E: E.matmul(pp[4][:, 0:TT], C["c_ones_row"][:],
                                            crow[:, 0], start=True, stop=True),
                         wait=[(sV, mark_crow), (sS, h3m)], inc=sT)
            bank_h3 = None
            for c in range(8):
                w = [(sT, mk_cb)] if c == 0 else []
                p.do("vector",
                     lambda E, c=c: E.tensor_tensor(
                         tw[:, c], pp[4][:, 0:TT], ostream[:, (c + 4) % 8],
                         ALU.mult), wait=w)
            p.do("vector", lambda E: E.drain())
            for c in range(8):
                p.do("vector",
                     lambda E, c=c: E.tensor_tensor(ostream[:, c],
                                                    ostream[:, c], tw[:, c],
                                                    TSIGN[c]))
            mark_tw = p.mil("vector", sV)
            if cyc < 2:
                for c in range(8):
                    p.do("vector",
                         lambda E, c=c: E.tensor_copy(ob[:, c], ostream[:, c]))
                mark_ob = p.mil("vector", sV)

        if t + 1 < NTILE:
            nsl = (t + 1) % 2
            t1 = (t + 1) * TT
            xin_n = xin[:, t1:t1 + TT].rearrange("(c p) t -> p c t", p=128)
            w = ([(sV, mark_apply_hist[nsl])]
                 if mark_apply_hist[nsl] is not None else [])
            mk_xn = p.dma("sync", sx[:, nsl * 8:nsl * 8 + 8], xin_n,
                          ch_x[nsl], wait=w)
            wr = [(sT, mkst1), (sV, mark_tw)]
            for mb in bank_h1:
                if mb is not None:
                    wr.append((sS, mb))
            mkst1 = emit_ln_stats_pre(
                g, TT, lambda c, s=nsl: sx[:, s * 8 + c], xsq2,
                C["c_ones16"], pp[0], pp[1],
                src_ready=[(ch_x[nsl], mk_xn), (sT, mkst1), (sV, mark_tw)],
                war_rows=wr)

        mk_bc2 = emit_ln_stats(
            g, TT, lambda c: ostream[:, c], xsq, rows, C["c_ones16"],
            C["c_ones_row"], C["c_eps"], pp[2], pp[3], pp[6], pp[7],
            src_ready=[(sV, mark_tw)],
            war_bc=[(sV, mark_tw)])
        if t + 1 < NTILE:
            mk_bc_next = emit_ln_stats_post(
                g, TT, rows2, C["c_ones_row"], C["c_eps"], pp[0], pp[1],
                pp[4], pp[5], mkst1, war_bc=[(sV, mark_tw)])
        for c in range(8):
            w = [(sT, mk_bc2)] if c == 0 else []
            if c == 0 and mark_store is not None:
                w.append((ch_o, mark_store))
            p.do("vector",
                 lambda E, c=c: E.tensor_tensor(xsq[:, c], ostream[:, c],
                                                pp[6][:, 0:TT], ALU.subtract),
                 wait=w)
        p.do("vector", lambda E: E.drain())
        for c in range(8):
            p.do("vector",
                 lambda E, c=c: E.tensor_tensor(tw[:, c], xsq[:, c],
                                                pp[7][:, 0:TT], ALU.mult))
        p.do("vector", lambda E: E.drain())
        for c in range(8):
            p.do("vector",
                 lambda E, c=c, sl=sl: E.tensor_tensor(
                     xo[:, c], sx[:, sl * 8 + c], tw[:, c], ALU.add))
        mark_apply = p.mil("vector", sV)
        mark_apply_hist[sl] = mark_apply
        mark_store = p.dma("sync", xout_sl, xo[:], ch_o,
                           wait=[(sV, mark_apply)])
    g.barrier()


def _prep_l0ffn(inputs):
    d = {}
    for n in ["cn_w1", "cn_w2", "cn_w3", "cn_w4", "gc_w1", "gc_w2", "gc_w3"]:
        d["l0f_" + n] = _as32(inputs["L0_ffn_" + n])
    for n, nch in [("cn_b1", 8), ("cn_b2", 4), ("cn_b3", 2), ("gc_b1", 4),
                   ("gc_b2", 2)]:
        b = _as32(inputs["L0_ffn_" + n])
        d["l0f_" + n] = np.ascontiguousarray(b.reshape(nch, 128).T)
    d["l0f_cn_b4"] = _as32(inputs["L0_ffn_cn_b4"]).reshape(1, 1)
    d["l0f_gc_b3"] = _as32(inputs["L0_ffn_gc_b3"]).reshape(1, 1)
    d["l0f_arng"] = _as32(inputs["L0_ffn_adaptive_range"]).reshape(1, 1)
    return d


def emit_l0attnA(g, xin, C, hand="internal"):
    nc, p = g.nc, g.p
    sT, sV, sS = g.sT, g.sV, g.sS
    TT = 256
    NTILE = T // TT
    H = 4
    DK = 256

    qpT = g.din("l0a_qpT", [D, D])
    kpT = g.din("l0a_kpT", [D, D])
    vpT = g.din("l0a_vpT", [D, D])
    bpT = g.din("l0a_bpT", [D, 4])
    qcv = g.din("l0a_qcv", [128, 8, 4])
    kcv = g.din("l0a_kcv", [128, 8, 4])
    vcv = g.din("l0a_vcv", [128, 8, 4])
    mku = g.din("l0a_mku", [128, 128], BF16)
    mkl = g.din("l0a_mkl", [128, 128], BF16)
    mki = g.din("l0a_mki", [128, 128], BF16)
    i32q = g.din("l0a_i32q", [128, 32], BF16)
    idf = g.din("l0a_idf", [128, 128], BF16)
    dh = g.dint if hand == "internal" else g.dout
    delta_d = dh("l0d_delta", [128, 8, T], BF16)
    vs_d = dh("l0d_vs", [128, 8, 30 + T], BF16)

    pp = g.pool
    qpw = g.sb("l0a_qpw", [128, 8, D], BF16)
    kpw = g.sb("l0a_kpw", [128, 8, D], BF16)
    vpw = g.sb("l0a_vpw", [128, 8, D], BF16)
    bpw = g.sb("l0a_bpw", [128, 8, 4], BF16)
    cvw = {}
    for nm, ap in [("q", qcv), ("k", kcv), ("v", vcv)]:
        cvw[nm] = g.sb("l0a_cv" + nm, [128, 8, 4])
        p.dma("sync", cvw[nm][:], ap, g.dSY)
    smku = g.sb("l0a_smku", [128, 128], BF16)
    smkl = g.sb("l0a_smkl", [128, 128], BF16)
    smki = g.sb("l0a_smki", [128, 128], BF16)
    si32 = g.sb("l0a_si32", [128, 32], BF16)
    sidf = g.sb("l0a_sidf", [128, 128], BF16)
    for t_, a_ in [(smku, mku), (smkl, mkl), (smki, mki), (si32, i32q),
                   (sidf, idf)]:
        p.dma("sync", t_[:], a_, g.dSY)
    for c in range(8):
        p.dma("gpsimd", qpw[:, c], qpT[c * 128:(c + 1) * 128, :], g.dGP)
        p.dma("gpsimd", kpw[:, c], kpT[c * 128:(c + 1) * 128, :], g.dGP)
        p.dma("gpsimd", vpw[:, c], vpT[c * 128:(c + 1) * 128, :], g.dGP)
        p.dma("gpsimd", bpw[:, c], bpT[c * 128:(c + 1) * 128, :], g.dGP)

    sx = g.sb("l0a_sx", [128, 8, TT])
    xsq = g.sb("l0a_xsq", [128, 8, TT])
    rows = g.sb("l0a_rows", [1, 6, TT])
    y1t = g.sb("l0a_y1t", [128, 8, TT], BF16)
    praw = {nm: g.sb("l0a_praw" + nm, [128, 8, TT + 3], BF16)
            for nm in "qkv"}
    acc = {nm: g.sb("l0a_acc" + nm, [128, 8, TT], BF16) for nm in "qkv"}
    qsb = g.sb("l0a_qsb", [128, 8, TT], BF16)
    ksb = g.sb("l0a_ksb", [128, 8, TT], BF16)
    vsb = g.sb("l0a_vsb", [128, 8, 30 + TT], BF16)
    scr8 = g.sb("l0a_scr8", [128, 8, TT])
    nrows = g.sb("l0a_nrows", [1, 12, TT])
    brows = g.sb("l0a_brows", [1, 4, TT])
    qnT = g.sb("l0a_qnT", [128, 8, TT], BF16)
    knT = g.sb("l0a_knT", [128, 8, TT], BF16)
    kbT = g.sb("l0a_kbT", [128, 8, TT], BF16)
    vbT = g.sb("l0a_vbT", [128, 8, TT], BF16)
    kntm = g.sb("l0a_kntm", [128, 4, 2, 256], BF16)
    vbtm = g.sb("l0a_vbtm", [128, 4, 2, 256], BF16)
    pkU = g.sb("l0a_pkU", [128, 4, 32], BF16)
    pkL = g.sb("l0a_pkL", [128, 4, 32], BF16)
    pkP = g.sb("l0a_pkP", [128, 2, 4, 32], BF16)
    pkAT = g.sb("l0a_pkAT", [128, 2, 4, 32], BF16)
    pkIV = g.sb("l0a_pkIV", [128, 2, 4, 32], BF16)
    upk = g.sb("l0a_upk", [128, 4, TT], BF16)
    uipk = g.sb("l0a_uipk", [128, 4, TT], BF16)
    S_sb = [g.sb("l0a_Ssb%d" % h, [128, 2, DK], BF16) for h in range(H)]
    dtok = g.sb("l0a_dtok", [128, 8, TT], BF16)
    deltaT = g.sb("l0a_deltaT", [128, 8, TT], BF16)
    g.barrier()

    ch_x = p.channel("l0a_chx")
    ch_dd = p.channel("l0a_chdd")
    ch_dv = p.channel("l0a_chdv")

    zrow = g.sb("l0a_zrow", [1, 512], BF16)
    zcol = g.sb("l0a_zcol", [1, 128], BF16)
    p.do("vector", lambda E: E.memset(praw["q"][:, :, 0:3], 0.0))
    p.do("vector", lambda E: E.memset(praw["k"][:, :, 0:3], 0.0))
    p.do("vector", lambda E: E.memset(praw["v"][:, :, 0:3], 0.0))
    p.do("vector", lambda E: E.memset(vsb[:, :, 0:30], 0.0))
    p.do("vector", lambda E: E.memset(zrow[:], 0.0))
    p.do("vector", lambda E: E.memset(zcol[:], 0.0))
    for h in range(H):
        p.do("vector", lambda E, h=h: E.memset(S_sb[h][:], 0.0))
    m_init = p.mil("vector", sV)
    for h in range(H):
        p.do("tensor",
             lambda E, h=h: E.matmul(pp[h][:, 0:512], zcol[:], zrow[:],
                                     start=True, stop=True),
             wait=[(sV, m_init)] if h == 0 else ())
    m_szero = p.mil("tensor", sT)
    for e in ENGINES:
        p.wait(e, sT, m_szero)

    PQ = lambda m: slice(32 * m, 32 * m + 32)
    mark_use = m_init
    m_pevac = [None, None, None, None]
    m_tile_end_v = None
    m_tile_end_s = None
    m_bc_used = None
    m_bc_used_s = None
    m_sq_used = None
    m_scaled = None
    m_tm_used = None
    m_vdump = None
    m_ddump = None
    m_Sref = [m_init] * H
    m_Sref_sem = [sV] * H

    for t in range(NTILE):
        t0 = t * TT
        first = (t == 0)
        xin_sl = xin[:, t0:t0 + TT].rearrange("(c p) t -> p c t", p=128)
        mk_x = p.dma("sync", sx[:], xin_sl, ch_x, wait=[(sV, mark_use)])
        mk_bc = emit_ln_stats(
            g, TT, lambda c: sx[:, c], xsq, rows, C["c_ones16"],
            C["c_ones_row"], C["c_eps"], pp[4][:, 256:512], pp[5][:, 256:512],
            pp[4][:, 0:256], pp[5][:, 0:256],
            src_ready=[(ch_x, mk_x)],
            war_rows=([(sV, m_bc_used), (sS, m_bc_used_s)]
                      if m_bc_used else ()),
            war_bc=([(sV, m_bc_used), (sS, m_bc_used_s)]
                    if m_bc_used else ()))
        for c in range(8):
            p.do("vector",
                 lambda E, c=c: E.tensor_tensor(xsq[:, c], sx[:, c],
                                                pp[4][:, 0:TT], ALU.subtract),
                 wait=[(sT, mk_bc)] if c == 0 else ())
        p.do("vector", lambda E: E.drain())
        for c in range(8):
            p.do("vector",
                 lambda E, c=c: E.tensor_tensor(y1t[:, c], xsq[:, c],
                                                pp[5][:, 0:TT], ALU.mult))
        mark_use = p.mil("vector", sV)

        if not first:
            for nm in "qkv":
                for c in range(8):
                    p.do("vector",
                         lambda E, nm=nm, c=c: E.tensor_copy(
                             praw[nm][:, c, 0:3], praw[nm][:, c, TT:TT + 3]))
            for c in range(8):
                p.do("vector",
                     lambda E, c=c: E.tensor_copy(vsb[:, c, 0:30],
                                                  vsb[:, c, TT:TT + 30]),
                     wait=[(ch_dv, m_vdump)] if c == 0 else ())
        m_tail = p.mil("vector", sV)

        wsl = [pp[6][:, 0:256], pp[7][:, 0:256], pp[6][:, 256:512],
               pp[7][:, 256:512]]
        wi = 0
        m_prawm = {}
        for nm, ww in [("q", qpw), ("k", kpw), ("v", vpw)]:
            for c in range(8):
                bank = wsl[wi % 4]
                for d in range(8):
                    w = []
                    if d == 0:
                        w = [(sV, mark_use)]
                        if m_pevac[wi % 4] is not None:
                            w.append((sS, m_pevac[wi % 4]))
                        if m_pevac[(wi + 2) % 4] is not None:
                            w.append((sS, m_pevac[(wi + 2) % 4]))
                        if wi == 0 and m_tile_end_v is not None:
                            w.append((sV, m_tile_end_v))
                        if wi == 0 and m_tile_end_s is not None:
                            w.append((sS, m_tile_end_s))
                    p.do("tensor",
                         lambda E, ww=ww, c=c, d=d, bank=bank:
                         E.matmul(bank, ww[:, d, c * 128:(c + 1) * 128],
                                  y1t[:, d], start=(d == 0), stop=(d == 7)),
                         wait=w, inc=(sT if d == 7 else None))
                mm = p.mark(sT)
                m_pevac[wi % 4] = p.do(
                    "scalar",
                    lambda E, nm=nm, c=c, bank=bank: E.activation(
                        praw[nm][:, c, 3:3 + TT], bank, AF.Identity),
                    wait=[(sT, mm), (sV, m_tail)], inc=sS)
                wi += 1
            m_prawm[nm] = p.mark(sS)
        for h in range(H):
            bank = wsl[wi % 4]
            for d in range(8):
                w = []
                if d == 0:
                    w = [(sV, mark_use)]
                    if m_pevac[wi % 4] is not None:
                        w.append((sS, m_pevac[wi % 4]))
                    if m_pevac[(wi + 2) % 4] is not None:
                        w.append((sS, m_pevac[(wi + 2) % 4]))
                p.do("tensor",
                     lambda E, h=h, d=d, bank=bank: E.matmul(
                         bank[0:1, :], bpw[:, d, h:h + 1], y1t[:, d],
                         start=(d == 0), stop=(d == 7)),
                     wait=w, inc=(sT if d == 7 else None))
            mm = p.mark(sT)
            m_pevac[wi % 4] = p.do(
                "scalar",
                lambda E, h=h, bank=bank: E.activation(
                    brows[:, h], bank[0:1, :], AF.Sigmoid),
                wait=[(sT, mm)], inc=sS)
            wi += 1

        m_silu_nm = {}
        for nm in "qkv":
            for j in range(4):
                for c in range(8):
                    if j == 0:
                        w = []
                        if c == 0:
                            w = [(sS, m_prawm[nm])]
                            if nm == "q" and m_scaled is not None:
                                w.append((sV, m_scaled))
                        p.do("vector",
                             lambda E, nm=nm, c=c: E.tensor_scalar_mul(
                                 acc[nm][:, c], praw[nm][:, c, 0:TT],
                                 cvw[nm][:, c, 0:1]),
                             wait=w)
                    else:
                        p.do("vector",
                             lambda E, nm=nm, c=c, j=j: E.scalar_tensor_tensor(
                                 acc[nm][:, c], praw[nm][:, c, j:j + TT],
                                 cvw[nm][:, c, j:j + 1], acc[nm][:, c],
                                 ALU.mult, ALU.add))
                p.do("vector", lambda E: E.drain())
            m_conv = p.mil("vector", sV)
            for c in range(8):
                w = [(sV, m_conv)] if c == 0 else []
                if c == 0 and m_scaled is not None:
                    w.append((sV, m_scaled))
                if nm == "q":
                    p.do("scalar",
                         lambda E, c=c: E.activation(qsb[:, c], acc["q"][:, c],
                                                     AF.Silu), wait=w)
                elif nm == "k":
                    p.do("scalar",
                         lambda E, c=c: E.activation(ksb[:, c], acc["k"][:, c],
                                                     AF.Silu), wait=w)
                else:
                    p.do("scalar",
                         lambda E, c=c: E.activation(vsb[:, c, 30:30 + TT],
                                                     acc["v"][:, c], AF.Silu),
                         wait=w)
            m_silu_nm[nm] = p.mil("scalar", sS)
        m_silu = m_silu_nm["v"]

        w = [(sS, m_silu)]
        if m_vdump is not None:
            w.append((ch_dv, m_vdump))
        if first:
            m_vdump = p.dma("sync", vs_d[:, :, 0:30 + TT], vsb[:], ch_dv,
                            wait=w)
        else:
            m_vdump = p.dma("sync", vs_d[:, :, 30 + t0:30 + t0 + TT],
                            vsb[:, :, 30:30 + TT], ch_dv, wait=w)

        for c in range(8):
            w = [(sS, m_silu_nm["q"])] if c == 0 else []
            if c == 0 and m_sq_used is not None:
                w.append((sV, m_sq_used))
            p.do("scalar",
                 lambda E, c=c: E.activation(scr8[:, c], qsb[:, c],
                                             AF.Square), wait=w)
        m_sq = p.mil("scalar", sS)
        rsl = [pp[6][0:1, 0:256], pp[7][0:1, 0:256], pp[6][0:1, 256:512],
               pp[7][0:1, 256:512]]
        m_rev = list(m_pevac)
        for h in range(H):
            bank = rsl[h]
            for i in range(2):
                w = [(sS, m_sq)] if i == 0 else []
                if i == 0:
                    for mx in (m_rev[h], m_rev[(h + 2) % 4]):
                        if mx is not None:
                            w.append((sS, mx))
                p.do("tensor",
                     lambda E, h=h, i=i, bank=bank: E.matmul(
                         bank, C["c_ones_col"][:], scr8[:, 2 * h + i],
                         start=(i == 0), stop=(i == 1)),
                     wait=w, inc=(sT if i == 1 else None))
            mm = p.mark(sT)
            m_rev[h] = p.do(
                "scalar",
                lambda E, h=h, bank=bank: E.activation(nrows[:, h], bank,
                                                       AF.Identity),
                wait=[(sT, mm)], inc=sS)
        for c in range(8):
            p.do("scalar",
                 lambda E, c=c: E.activation(scr8[:, c], ksb[:, c],
                                             AF.Square))
        m_sq2 = p.mil("scalar", sS)
        for h in range(H):
            bank = rsl[h]
            for i in range(2):
                w = []
                if i == 0:
                    w = [(sS, m_sq2), (sS, m_rev[h])]
                    if m_rev[(h + 2) % 4] is not None:
                        w.append((sS, m_rev[(h + 2) % 4]))
                p.do("tensor",
                     lambda E, h=h, i=i, bank=bank: E.matmul(
                         bank, C["c_ones_col"][:], scr8[:, 2 * h + i],
                         start=(i == 0), stop=(i == 1)),
                     wait=w, inc=(sT if i == 1 else None))
            mm = p.mark(sT)
            m_rev[h] = p.do(
                "scalar",
                lambda E, h=h, bank=bank: E.activation(nrows[:, 4 + h], bank,
                                                       AF.Identity),
                wait=[(sT, mm)], inc=sS)
        m_pevac = [None, None, None, None]
        m_rows0 = p.mil("scalar", sS)

        p.do("scalar",
             lambda E: E.activation(
                 nrows[:, 8:12, :].rearrange("o a t -> o (a t)"),
                 nrows[:, 0:4, :].rearrange("o a t -> o (a t)"),
                 AF.Sqrt, bias=C["c_eps12"][0:1, 0:1]),
             wait=[(sS, m_rows0)])
        m_s1 = p.mil("scalar", sS)
        p.do("vector",
             lambda E: E.reciprocal(
                 nrows[:, 0:4, :].rearrange("o a t -> o (a t)"),
                 nrows[:, 8:12, :].rearrange("o a t -> o (a t)")),
             wait=[(sS, m_s1)])
        m_v1 = p.mil("vector", sV)
        p.do("scalar",
             lambda E: E.activation(
                 nrows[:, 8:12, :].rearrange("o a t -> o (a t)"),
                 nrows[:, 4:8, :].rearrange("o a t -> o (a t)"),
                 AF.Sqrt, bias=C["c_eps12"][0:1, 0:1]),
             wait=[(sV, m_v1)])
        m_s2 = p.mil("scalar", sS)
        p.do("vector",
             lambda E: E.reciprocal(
                 nrows[:, 4:8, :].rearrange("o a t -> o (a t)"),
                 nrows[:, 8:12, :].rearrange("o a t -> o (a t)")),
             wait=[(sS, m_s2)])
        p.do("vector", lambda E: E.drain())
        p.do("vector",
             lambda E: E.tensor_tensor(
                 nrows[:, 8:12, :].rearrange("o a t -> o (a t)"),
                 nrows[:, 4:8, :].rearrange("o a t -> o (a t)"),
                 brows[:, :, :].rearrange("o a t -> o (a t)"), ALU.mult))
        m_rows = p.mil("vector", sV)

        bsl = [pp[4][:, 0:256], pp[4][:, 256:512], pp[5][:, 0:256],
               pp[5][:, 256:512]]
        for h in range(H):
            srcs = [nrows[0:1, h], nrows[0:1, 4 + h], nrows[0:1, 8 + h],
                    brows[0:1, h]]
            mkb = None
            for k in range(4):
                w = [(sV, m_rows)]
                if h > 0:
                    w.append((sV, m_scaled))
                elif m_bc_used is not None:
                    w.append((sV, m_bc_used))
                mkb = p.do("tensor",
                           lambda E, k=k, src=srcs[k]: E.matmul(
                               bsl[k], C["c_ones_row"][:], src,
                               start=True, stop=True),
                           wait=w, inc=(sT if k == 3 else None))
            for i in range(2):
                c = 2 * h + i
                p.do("vector",
                     lambda E, c=c: E.tensor_tensor(qnT[:, c], qsb[:, c],
                                                    bsl[0], ALU.mult),
                     wait=[(sT, mkb)] if i == 0 else ())
                p.do("vector",
                     lambda E, c=c: E.tensor_tensor(knT[:, c], ksb[:, c],
                                                    bsl[1], ALU.mult))
                p.do("vector",
                     lambda E, c=c: E.tensor_tensor(kbT[:, c], ksb[:, c],
                                                    bsl[2], ALU.mult))
                p.do("vector",
                     lambda E, c=c: E.tensor_tensor(vbT[:, c],
                                                    vsb[:, c, 30:30 + TT],
                                                    bsl[3], ALU.mult))
            m_scaled = p.mil("vector", sV)
        m_sq_used = m_scaled

        tbv6 = pp[6].bitcast(BF16)
        tbv7 = pp[7].bitcast(BF16)
        tsl = [tbv6[:, 0:128], tbv7[:, 0:128], tbv6[:, 128:256],
               tbv7[:, 128:256]]
        ti = 0
        m_tevac = [None, None, None, None]
        for src, dst in [(knT, kntm), (vbT, vbtm)]:
            for h in range(H):
                for j in range(2):
                    for b2 in range(2):
                        bank = tsl[ti % 4]
                        w = [(sV, m_scaled), (sS, m_rows0)]
                        for mx in (m_tevac[ti % 4], m_tevac[(ti + 2) % 4]):
                            if mx is not None:
                                w.append((sV, mx))
                        if ti == 0 and m_tm_used is not None:
                            w.append((sT, m_tm_used))
                        mm = p.do("tensor",
                                  lambda E, src=src, h=h, j=j, b2=b2,
                                  bank=bank: E.transpose(
                                      bank,
                                      src[:, 2 * h + j,
                                          b2 * 128:(b2 + 1) * 128],
                                      sidf[:]),
                                  wait=w, inc=sT)
                        m_tevac[ti % 4] = p.do(
                            "vector",
                            lambda E, dst=dst, h=h, j=j, b2=b2, bank=bank:
                            E.tensor_copy(
                                dst[:, h, b2, j * 128:(j + 1) * 128], bank),
                            wait=[(sT, mm)], inc=sV)
                        ti += 1
        m_tm_evac = p.mil("vector", sV)

        pA = pp[6][:, 0:128]
        pB = pp[6][:, 128:256]
        pC = pp[6][:, 256:384]
        pD = pp[6][:, 384:512]
        m_prep_v = m_tm_evac
        for wv in range(2):
            for m in range(4):
                ci = 4 * wv + m
                for h in range(H):
                    for i in range(2):
                        w = []
                        if m == 0 and h == 0 and i == 0:
                            w = [(sV, m_prep_v), (sV, m_tm_evac)]
                        p.do("tensor",
                             lambda E, m=m, h=h, i=i, ci=ci: E.matmul(
                                 pA[PQ(m), 32 * h:32 * h + 32],
                                 knT[:, 2 * h + i, 32 * ci:32 * ci + 32],
                                 kbT[:, 2 * h + i, 32 * ci:32 * ci + 32],
                                 start=(i == 0), stop=(i == 1),
                                 tile_position=(0, 32 * m)),
                             wait=w)
                    for i in range(2):
                        p.do("tensor",
                             lambda E, m=m, h=h, i=i, ci=ci: E.matmul(
                                 pB[PQ(m), 32 * h:32 * h + 32],
                                 kbT[:, 2 * h + i, 32 * ci:32 * ci + 32],
                                 knT[:, 2 * h + i, 32 * ci:32 * ci + 32],
                                 start=(i == 0), stop=(i == 1),
                                 tile_position=(0, 32 * m)))
                    for i in range(2):
                        p.do("tensor",
                             lambda E, m=m, h=h, i=i, ci=ci: E.matmul(
                                 pC[PQ(m), 32 * h:32 * h + 32],
                                 knT[:, 2 * h + i, 32 * ci:32 * ci + 32],
                                 qnT[:, 2 * h + i, 32 * ci:32 * ci + 32],
                                 start=(i == 0), stop=(i == 1),
                                 tile_position=(0, 32 * m)))
            m_amm = p.mil("tensor", sT)
            p.do("vector",
                 lambda E: E.tensor_tensor(
                     pkU.rearrange("p a b -> p (a b)"), pA, smku[:],
                     ALU.mult),
                 wait=[(sT, m_amm)])
            p.do("vector",
                 lambda E: E.tensor_tensor(
                     pkL.rearrange("p a b -> p (a b)"), pB, smkl[:],
                     ALU.mult))
            p.do("vector",
                 lambda E, wv=wv: E.tensor_tensor(
                     pkAT[:, wv].rearrange("p a b -> p (a b)"), pC, smki[:],
                     ALU.mult))
            m_UL = p.mil("vector", sV)
            for m in range(4):
                for h in range(H):
                    p.do("tensor",
                         lambda E, m=m, h=h: E.matmul(
                             pD[PQ(m), 32 * h:32 * h + 32],
                             si32[PQ(m), :], pkU[PQ(m), h, :],
                             start=True, stop=False,
                             tile_position=(32 * m, 32 * m)),
                         wait=[(sV, m_UL)] if (m == 0 and h == 0) else ())
                    p.do("tensor",
                         lambda E, m=m, h=h: E.matmul(
                             pD[PQ(m), 32 * h:32 * h + 32],
                             si32[PQ(m), :], si32[PQ(m), :],
                             start=False, stop=True,
                             tile_position=(32 * m, 32 * m)))
            m_pmm = p.mil("tensor", sT)
            p.do("vector",
                 lambda E: E.tensor_copy(
                     pkP[:, 0].rearrange("p a b -> p (a b)"), pD),
                 wait=[(sT, m_pmm)])
            m_P = p.mil("vector", sV)
            cur = 0
            for lvl in range(4):
                last = (lvl == 3)
                for m in range(4):
                    for h in range(H):
                        w = [(sV, m_P)] if (m == 0 and h == 0) else []
                        if not last:
                            p.do("tensor",
                                 lambda E, m=m, h=h: E.matmul(
                                     pA[PQ(m), 32 * h:32 * h + 32],
                                     pkL[PQ(m), h, :], pkU[PQ(m), h, :],
                                     start=True, stop=True,
                                     tile_position=(32 * m, 32 * m)),
                                 wait=w)
                            w = []
                        p.do("tensor",
                             lambda E, m=m, h=h: E.matmul(
                                 pB[PQ(m), 32 * h:32 * h + 32],
                                 pkU[PQ(m), h, :], pkL[PQ(m), h, :],
                                 start=True, stop=True,
                                 tile_position=(32 * m, 32 * m)),
                             wait=w)
                m_sqm = p.mil("tensor", sT)
                if not last:
                    p.do("vector",
                         lambda E: E.tensor_copy(
                             pkU.rearrange("p a b -> p (a b)"), pA),
                         wait=[(sT, m_sqm)])
                p.do("vector",
                     lambda E: E.tensor_copy(
                         pkL.rearrange("p a b -> p (a b)"), pB),
                     wait=[(sT, m_sqm)])
                m_UL = p.mil("vector", sV)
                nxt = 1 - cur
                for m in range(4):
                    for h in range(H):
                        w = [(sV, m_UL)] if (m == 0 and h == 0) else []
                        p.do("tensor",
                             lambda E, m=m, h=h, cur=cur: E.matmul(
                                 pD[PQ(m), 32 * h:32 * h + 32],
                                 pkL[PQ(m), h, :], pkP[PQ(m), cur, h, :],
                                 start=True, stop=False,
                                 tile_position=(32 * m, 32 * m)),
                             wait=w)
                        p.do("tensor",
                             lambda E, m=m, h=h, cur=cur: E.matmul(
                                 pD[PQ(m), 32 * h:32 * h + 32],
                                 si32[PQ(m), :], pkP[PQ(m), cur, h, :],
                                 start=False, stop=True,
                                 tile_position=(32 * m, 32 * m)))
                m_pmm = p.mil("tensor", sT)
                p.do("vector",
                     lambda E, nxt=nxt: E.tensor_copy(
                         pkP[:, nxt].rearrange("p a b -> p (a b)"), pD),
                     wait=[(sT, m_pmm)])
                m_P = p.mil("vector", sV)
                cur = nxt
            p.do("vector",
                 lambda E, wv=wv, cur=cur: E.tensor_copy(
                     pkIV[:, wv].rearrange("p a b -> p (a b)"),
                     pkP[:, cur].rearrange("p a b -> p (a b)")))
            m_prep_v = p.mil("vector", sV)
        m_prep_done = m_prep_v

        zb = [pp[4], pp[5]]
        oisl = [pp[6][:, 0:256], pp[7][:, 0:256]]
        m_bankio = [(sV, m_scaled), (sV, m_scaled)]
        m_oi_done = [(sV, m_prep_done), (sV, m_prep_done)]
        m_upk_used = [m_prep_done] * H
        m_uipk_used = [None] * H
        m_oi_last = None
        for ci in range(8):
            m = ci % 4
            wv = ci // 4
            for h in range(H):
                pi = (ci * H + h) % 2
                for i in range(2):
                    w = []
                    if i == 0:
                        w = [(m_Sref_sem[h], m_Sref[h]), m_bankio[pi],
                             (sV, m_prep_done)]
                    p.do("tensor",
                         lambda E, m=m, h=h, i=i, ci=ci, pi=pi: E.matmul(
                             zb[pi][PQ(m), 0:256],
                             kbT[:, 2 * h + i, 32 * ci:32 * ci + 32],
                             S_sb[h][:, i, :], start=(i == 0), stop=(i == 1),
                             tile_position=(0, 32 * m)),
                         wait=w, inc=(sT if i == 1 else None))
                mm_z = p.mark(sT)
                m_up = p.do(
                    "vector",
                    lambda E, m=m, h=h, wv=wv, pi=pi: E.tensor_tensor(
                        upk[PQ(m), h, :], vbtm[PQ(m), h, wv, :],
                        zb[pi][PQ(m), 0:256], ALU.subtract),
                    wait=[(sT, mm_z), (sT, m_upk_used[h])], inc=sV)
                mm_ui = p.do(
                    "tensor",
                    lambda E, m=m, h=h, wv=wv, pi=pi: E.matmul(
                        zb[pi][PQ(m), 256:512], pkIV[PQ(m), wv, h, :],
                        upk[PQ(m), h, :], start=True, stop=True,
                        tile_position=(32 * m, 32 * m)),
                    wait=[(sV, m_up)], inc=sT)
                wev = [(sT, mm_ui)]
                if m_uipk_used[h] is not None:
                    wev.append((sT, m_uipk_used[h]))
                m_uiev = p.do(
                    "scalar",
                    lambda E, m=m, h=h, pi=pi: E.activation(
                        uipk[PQ(m), h, :], zb[pi][PQ(m), 256:512],
                        AF.Identity),
                    wait=wev, inc=sS)
                m_bankio[pi] = (sS, m_uiev)
                oz = ci % 2
                for i in range(2):
                    w = []
                    if i == 0:
                        w = [(m_Sref_sem[h], m_Sref[h])]
                        if h == 0:
                            w.append(m_oi_done[oz])
                    p.do("tensor",
                         lambda E, m=m, h=h, i=i, ci=ci, oz=oz: E.matmul(
                             oisl[oz][32 * h:32 * h + 32, :],
                             qnT[:, 2 * h + i, 32 * ci:32 * ci + 32],
                             S_sb[h][:, i, :], start=(i == 0), stop=False,
                             tile_position=(0, 32 * h)),
                         wait=w)
                m_oi_last = p.do(
                    "tensor",
                    lambda E, m=m, h=h, wv=wv, oz=oz: E.matmul(
                        oisl[oz][32 * h:32 * h + 32, :],
                        pkAT[PQ(m), wv, h, :], uipk[PQ(m), h, :],
                        start=False, stop=True,
                        tile_position=(32 * m, 32 * h)),
                    wait=[(sS, m_uiev)], inc=sT)
                for i in range(2):
                    p.do("tensor",
                         lambda E, m=m, h=h, wv=wv, i=i, t=t, ci=ci:
                         E.matmul(
                             pp[h][:, i * 256:(i + 1) * 256],
                             kntm[PQ(m), h, wv, i * 128:(i + 1) * 128],
                             uipk[PQ(m), h, :],
                             start=False, stop=True,
                             skip_group_check=True,
                             tile_position=(32 * m, 0)),
                         inc=(sT if i == 1 else None))
                m_supd = p.mark(sT)
                m_uipk_used[h] = m_supd
                if (ci + h) % 2 == 0:
                    for i in range(2):
                        p.do("scalar",
                             lambda E, h=h, i=i: E.activation(
                                 S_sb[h][:, i, :],
                                 pp[h][:, i * 256:(i + 1) * 256],
                                 AF.Identity),
                             wait=[(sT, m_supd)] if i == 0 else ())
                    m_Sref[h] = p.mil("scalar", sS)
                    m_Sref_sem[h] = sS
                else:
                    for i in range(2):
                        p.do("vector",
                             lambda E, h=h, i=i: E.tensor_copy(
                                 S_sb[h][:, i, :],
                                 pp[h][:, i * 256:(i + 1) * 256]),
                             wait=[(sT, m_supd)] if i == 0 else ())
                    m_Sref[h] = p.mil("vector", sV)
                    m_Sref_sem[h] = sV
            w = [(sT, m_oi_last)]
            if t > 0 and ci < 2 and m_tm_used is not None:
                w.append((sT, m_tm_used))
            if ci % 2 == 0:
                mk = p.do("scalar",
                          lambda E, ci=ci, oz=ci % 2: E.activation(
                              dtok[:, ci, :], oisl[oz], AF.Identity),
                          wait=w, inc=sS)
                m_oi_done[ci % 2] = (sS, mk)
            else:
                mk = p.do("vector",
                          lambda E, ci=ci, oz=ci % 2: E.tensor_copy(
                              dtok[:, ci, :], oisl[oz]),
                          wait=w, inc=sV)
                m_oi_done[ci % 2] = (sV, mk)
        m_scan_v = p.mil("vector", sV)
        m_scan_s = p.mil("scalar", sS)
        m_bc_used = m_scan_v
        m_bc_used_s = m_scan_s

        t7sl = [tbv6[:, 0:128], tbv7[:, 0:128], tbv6[:, 128:256],
                tbv7[:, 128:256]]
        dvw = deltaT.rearrange("p (hh o) t -> p hh o t", o=2)
        ti = 0
        m_t7 = [None] * 4
        m_tm_used = None
        for ci in range(8):
            for half in range(2):
                bank = t7sl[ti % 4]
                w = [m_oi_done[0], m_oi_done[1],
                     (sV, m_scan_v), (sS, m_scan_s)] if ti == 0 else []
                for mx in (m_t7[ti % 4], m_t7[(ti + 2) % 4]):
                    if mx is not None:
                        w = list(w) + [(sV, mx)]
                mm = p.do("tensor",
                          lambda E, ci=ci, half=half, bank=bank: E.transpose(
                              bank, dtok[:, ci, half * 128:(half + 1) * 128],
                              sidf[:]),
                          wait=w, inc=sT)
                m_tm_used = mm
                wev = [(sT, mm)]
                if t > 0 and ti == 0:
                    wev.append((ch_dd, m_ddump))
                m_t7[ti % 4] = p.do(
                    "vector",
                    lambda E, ci=ci, half=half, bank=bank: E.tensor_copy(
                        dvw[:, :, half, 32 * ci:32 * ci + 32], bank),
                    wait=wev, inc=sV)
                ti += 1
        m_dT = p.mil("vector", sV)
        m_ddump = p.dma("sync", delta_d[:, :, t0:t0 + TT], deltaT[:], ch_dd,
                        wait=[(sV, m_dT)])
        m_tile_end_v = m_dT
        m_tile_end_s = m_scan_s
    g.barrier()

def emit_l0attnB(g, xin, xout, C, hand="internal"):
    nc, p = g.nc, g.p
    sT, sV, sS, sG = g.sT, g.sV, g.sS, g.sG
    TT = 256
    NTILE = T // TT
    H = 4

    dh = g.dram.get
    if "l0d_delta" in g.dram:
        delta_d = g.dram["l0d_delta"]
        vs_d = g.dram["l0d_vs"]
    else:
        delta_d = g.din("l0d_delta", [128, 8, T], BF16)
        vs_d = g.din("l0d_vs", [128, 8, 30 + T], BF16)
    g1T = g.din("l0a_g1T", [1048, 2 * D])
    g2T = g.din("l0a_g2T", [2 * D, 24])
    opT = g.din("l0a_opT", [D, D])
    fw = {k: g.din("l0a_fw%d" % k, [128, 8, k]) for k in (3, 7, 15, 31)}
    gb1 = g.din("l0a_gb1", [128, 16])
    gb2a = g.din("l0a_gb2", [24, 1])
    alst = g.din("l0a_alst", [24, 1])
    ltmp = g.din("l0a_ltmp", [4, 1])
    texp = g.din("l0a_texp", [4, 24])
    onw = g.din("l0a_onw", [128, 2])
    stpat = g.din("l0a_stpat", [128, 47], BF16)
    orow = g.din("l0a_orow", [1, 128], BF16)
    d32_d = g.din("l0b_d32", [8, 128, 56 * 32], BF16)
    prb_scr = g.dint("l0b_prbscr", [128, 2, 24])

    pp = g.pool
    g1w = g.sb("l0b_g1w", [128, 8, 2 * D], BF16)
    g1s = g.sb("l0b_g1s", [24, 2 * D], BF16)
    g2w = g.sb("l0b_g2w", [128, 16, 24], BF16)
    opw = g.sb("l0b_opw", [128, 8, D], BF16)
    sfw = {k: g.sb("l0b_fw%d" % k, [128, 8, k]) for k in (3, 7, 15, 31)}
    sgb1 = g.sb("l0b_gb1", [128, 16])
    sgb2 = g.sb("l0b_gb2", [24, 1])
    salst = g.sb("l0b_alst", [24, 1])
    sltmp = g.sb("l0b_ltmp", [4, 1])
    stexp = g.sb("l0b_texp", [4, 24])
    sonw = g.sb("l0b_onw", [128, 2])
    sstp = g.sb("l0b_stpat", [128, 47], BF16)
    sorow = g.sb("l0b_orow", [1, 128], BF16)
    tscr = g.sb("l0b_tscr", [4, 8])
    tinv24 = g.sb("l0b_tinv", [24, 1])
    d32b = g.sb("l0b_d32b", [128, 8, 56, 32], BF16)
    zrowB = g.sb("l0b_zrow", [1, 512], BF16)
    zcolB = g.sb("l0b_zcol", [1, 128], BF16)
    for c in range(8):
        p.dma("gpsimd", g1w[:, c], g1T[c * 128:(c + 1) * 128, :], g.dGP)
        p.dma("gpsimd", opw[:, c], opT[c * 128:(c + 1) * 128, :], g.dGP)
    p.dma("gpsimd", g1s[:], g1T[1024:1048, :], g.dGP)
    for jc in range(16):
        p.dma("gpsimd", g2w[:, jc], g2T[jc * 128:(jc + 1) * 128, :], g.dGP)
    for k in (3, 7, 15, 31):
        p.dma("sync", sfw[k][:], fw[k], g.dSY)
    for c in range(8):
        p.dma("sync", d32b[:, c].rearrange("p a b -> p (a b)"),
              d32_d[c], g.dSY)
    for t_, a_ in [(sgb1, gb1), (sgb2, gb2a), (salst, alst), (sltmp, ltmp),
                   (stexp, texp), (sonw, onw), (sstp, stpat), (sorow, orow)]:
        p.dma("sync", t_[:], a_, g.dSY)

    sx = g.sb("l0b_sx", [128, 16, TT])
    xsq = g.sb("l0b_xsq", [128, 8, TT])
    xsq2 = g.sb("l0b_xsq2", [128, 8, TT])
    rows = g.sb("l0b_rows", [1, 5, TT])
    rows2 = g.sb("l0b_rows2", [1, 5, TT])
    y1b = g.sb("l0b_y1b", [128, 8, TT], BF16)
    vst = g.sb("l0b_vst", [128, 8, 32 + TT], BF16)
    dTt = g.sb("l0b_dT", [128, 8, TT], BF16)
    fbuf = {k: g.sb("l0b_f%d" % k, [128, 8, TT], BF16) for k in (3, 7, 15, 31)}
    babs = g.sb("l0b_babs", [128, 8, TT], BF16)
    stats24 = g.sb("l0b_st24", [24, TT], BF16)
    h1b = g.sb("l0b_h1b", [128, 16, TT], BF16)
    lgt = g.sb("l0b_lgt", [24, TT])
    prb = g.sb("l0b_prb", [128, 2, 24])
    pex = g.sb("l0b_pex", [128, 2, 24])
    mxs = g.sb("l0b_mxs", [128, 2, 12])
    probsR = g.sb("l0b_probsR", [1, 2, 128, 24], BF16)
    bcb = g.sb("l0b_bcb", [128, 6, TT], BF16)
    mixt = g.sb("l0b_mixt", [128, 12, TT], BF16)
    o_acc = xsq
    nr2 = g.sb("l0b_nr2", [1, 8, TT])
    ob = babs
    fo = g.sb("l0b_fo", [128, 8, TT])
    g.barrier()

    p.do("vector", lambda E: E.memset(zrowB[:], 0.0))
    p.do("vector", lambda E: E.memset(zcolB[:], 0.0))
    m_zB = p.mil("vector", sV)

    p.do("vector", lambda E: E.scalar_tensor_tensor(
        tscr[:, 0:1], sltmp[:], -1.0, sltmp[:], ALU.mult, ALU.max))
    m_v = p.mil("vector", sV)
    p.do("scalar", lambda E: E.activation(tscr[:, 1:2], tscr[:, 0:1], AF.Exp,
                                          scale=-1.0), wait=[(sV, m_v)])
    m_s = p.mil("scalar", sS)
    p.do("scalar", lambda E: E.activation(tscr[:, 2:3], tscr[:, 1:2], AF.Ln,
                                          bias=C["c_ones_col"][0:4, 0:1]),
         wait=[(sS, m_s)])
    m_s = p.mil("scalar", sS)
    p.do("vector", lambda E: E.tensor_scalar_max(tscr[:, 3:4], sltmp[:], 0.0),
         wait=[(sS, m_s)])
    p.do("vector", lambda E: E.drain())
    p.do("vector", lambda E: E.tensor_tensor(tscr[:, 4:5], tscr[:, 3:4],
                                             tscr[:, 2:3], ALU.add))
    p.do("vector", lambda E: E.drain())
    p.do("vector", lambda E: E.tensor_scalar_add(tscr[:, 5:6], tscr[:, 4:5],
                                                 1e-4))
    m_v = p.mil("vector", sV)
    mm_t = p.do("tensor",
                lambda E: E.matmul(pp[2][0:24, 0:1], stexp[:], tscr[:, 5:6],
                                   start=True, stop=True),
                wait=[(sV, m_v)], inc=sT)
    p.do("vector", lambda E: E.reciprocal(tinv24[:], pp[2][0:24, 0:1]),
         wait=[(sT, mm_t)])
    m_temp = p.mil("vector", sV)

    ch_x = [p.channel("l0b_chx0"), p.channel("l0b_chx1")]
    ch_v = p.channel("l0b_chv")
    ch_d = p.channel("l0b_chd")
    ch_p1 = p.channel("l0b_chp1")
    ch_p2 = p.channel("l0b_chp2")
    ch_o = p.channel("l0b_cho")
    TAPS = []
    for k in (3, 7, 15, 31):
        for j in range(k):
            TAPS.append((k, j, 31 - k + j))

    mark_use = m_temp
    mark_use_hist = [None, None]
    mk_bc_next = None
    mkst1 = None
    m_fir_used = None
    m_fir_used_s = None
    m_fir_pe = None
    m_tile_s = None
    m_st_used = None
    m_bank01 = [None, None]
    m_gbank = [None] * 4
    m_lgt_used = None
    m_prb_out = None
    m_prbR_used = None
    m_bcb_used = [None] * 6
    m_mix_done = None
    m_oacc_used = None
    m_rms_used = None
    m_ob_used = None
    mark_store = None
    mark_ln2 = None
    m_stats_ev = None

    for t in range(NTILE):
        t0 = t * TT
        sl = t % 2
        xout_sl = xout[:, t0:t0 + TT].rearrange("(c p) t -> p c t", p=128)
        w = [(sV, m_fir_used)] if m_fir_used else []
        if m_fir_used_s:
            w.append((sS, m_fir_used_s))
        if m_fir_pe is not None:
            w.append((sT, m_fir_pe))
        mk_v = p.dma("sync", vst[:, :, 0:30 + TT],
                     vs_d[:, :, t0:t0 + 30 + TT], ch_v, wait=w)
        mk_d = p.dma("sync", dTt[:], delta_d[:, :, t0:t0 + TT], ch_d, wait=w)

        if t == 0:
            xin_sl = xin[:, t0:t0 + TT].rearrange("(c p) t -> p c t", p=128)
            mk_x = p.dma("sync", sx[:, 0:8], xin_sl, ch_x[0],
                         wait=[(sV, mark_use)])
            mkst1 = emit_ln_stats_pre(
                g, TT, lambda c: sx[:, c], xsq2, C["c_ones16"],
                pp[2][0:1, 256:512], pp[3][0:1, 256:512],
                src_ready=[(ch_x[0], mk_x)])
            mk_bc = emit_ln_stats_post(
                g, TT, rows2, C["c_ones_row"], C["c_eps"],
                pp[2][0:1, 256:512], pp[3][0:1, 256:512],
                pp[6][:, 0:256], pp[7][:, 0:256], mkst1)
        else:
            mk_bc = mk_bc_next
        for c in range(8):
            p.do("vector",
                 lambda E, c=c, sl=sl: E.tensor_tensor(
                     xsq[:, c], sx[:, sl * 8 + c],
                     pp[6][:, 0:TT], ALU.subtract),
                 wait=[(sT, mk_bc)] if c == 0 else ())
        p.do("vector", lambda E: E.drain())
        for c in range(8):
            p.do("vector",
                 lambda E, c=c: E.tensor_tensor(y1b[:, c], xsq[:, c],
                                                pp[7][:, 0:TT], ALU.mult))
        mark_use = p.mil("vector", sV)

        m_fevac = {0: None, 1: None}
        for c in range(8):
            par = c % 2
            ppA = pp[0] if par == 0 else pp[2]
            ppB = pp[1] if par == 0 else pp[3]
            w = [(ch_v, mk_v), (sV, m_zB)]
            if m_fevac[par] is not None:
                w.extend(m_fevac[par])
            if c < 2:
                w.append((sT, mk_bc))
                if m_fir_used:
                    w.append((sV, m_fir_used))
                if m_fir_used_s:
                    w.append((sS, m_fir_used_s))
                if m_tile_s is not None:
                    w.append((sS, m_tile_s))
                if mark_ln2 is not None:
                    w.append((sV, mark_ln2))
            p.do("tensor",
                 lambda E, ppA=ppA: E.matmul(ppA[:, 0:512], zcolB[:],
                                             zrowB[:], start=True, stop=True),
                 wait=w)
            p.do("tensor",
                 lambda E, ppB=ppB: E.matmul(ppB[:, 0:512], zcolB[:],
                                             zrowB[:], start=True, stop=True))
            for ti, (k, j, off) in enumerate(TAPS):
                if k in (3, 7):
                    reg = ppA[:, 0:256] if k == 3 else ppA[:, 256:512]
                else:
                    reg = ppB[:, 0:256] if k == 15 else ppB[:, 256:512]
                for q in range(4):
                    p.do("tensor",
                         lambda E, c=c, par=par, ti=ti, off=off, q=q, reg=reg:
                         E.matmul(
                             reg[32 * q:32 * q + 32, :],
                             d32b[32 * q:32 * q + 32, c, ti, :],
                             vst[32 * q:32 * q + 32, c, off:off + TT],
                             start=False, stop=True, skip_group_check=True,
                             tile_position=(32 * q, 32 * q)))
                if ti == 9:
                    mA = p.do("tensor", lambda E: E.drain(), inc=sT)
                    p.do("scalar",
                         lambda E, c=c, ppA=ppA: E.activation(
                             fbuf[3][:, c], ppA[:, 0:256], AF.Identity),
                         wait=[(sT, mA)], inc=sS)
                    m_f7 = p.do("scalar",
                                lambda E, c=c, ppA=ppA: E.activation(
                                    fbuf[7][:, c], ppA[:, 256:512],
                                    AF.Identity),
                                inc=sS)
            mB = p.do("tensor", lambda E: E.drain(), inc=sT)
            p.do("vector",
                 lambda E, c=c, ppB=ppB: E.tensor_copy(
                     fbuf[15][:, c], ppB[:, 0:256]),
                 wait=[(sT, mB)], inc=sV)
            m_f31 = p.do("vector",
                         lambda E, c=c, ppB=ppB: E.tensor_copy(
                             fbuf[31][:, c], ppB[:, 256:512]),
                         inc=sV)
            m_fevac[par] = [(sS, m_f7), (sV, m_f31)]
        m_fir_v = p.mil("vector", sV)
        m_fir_sS = p.mil("scalar", sS)

        branches = [fbuf[3][:, :], fbuf[7][:, :], fbuf[15][:, :],
                    fbuf[31][:, :], vst[:, :, 30:30 + TT], dTt[:, :]]
        m_abs = None
        for pi, br in enumerate(branches):
            for c in range(8):
                w = []
                if c == 0:
                    w = [(sV, m_fir_v), (sS, m_fir_sS), (ch_d, mk_d)]
                    if m_abs is not None:
                        w.append((sT, m_abs))
                p.do("scalar",
                     lambda E, pi=pi, br=br, c=c: E.activation(
                         babs[:, c], br[:, c] if br is not vst else
                         vst[:, c, 30:30 + TT], AF.Abs),
                     wait=w)
            m_b = p.mil("scalar", sS)
            for c in range(8):
                hh = c // 2
                kk = 6 * hh + pi
                w = [(sS, m_b)]
                if pi == 0 and c == 0 and m_st_used is not None:
                    w.append((sV, m_st_used))
                if pi == 0 and c == 0 and m_tile_s is not None:
                    w.append((sS, m_tile_s))
                p.do("tensor",
                     lambda E, c=c, kk=kk, pi=pi: E.matmul(
                         pp[3][0:24, 0:256], sstp[:, 23 - kk:47 - kk],
                         babs[:, c], start=(pi == 0 and c == 0),
                         stop=(pi == 5 and c == 7)),
                     wait=w, inc=(sT if c == 7 else None))
            m_abs = p.mark(sT)
        mm_st = p.mark(sT)
        m_stats_ev = p.do(
            "scalar",
            lambda E: E.activation(stats24[:], pp[3][0:24, 0:256],
                                   AF.Identity),
            wait=[(sT, mm_st)], inc=sS)

        for jc in range(16):
            bank = pp[jc % 4][:, 0:256]
            for d in range(9):
                w = []
                if d == 0:
                    w = [(sV, mark_use)]
                    if jc % 4 == 3:
                        w.append((sS, m_stats_ev))
                    if m_gbank[jc % 4] is not None:
                        w.append((sS, m_gbank[jc % 4]))
                if d == 8:
                    w = [(sS, m_stats_ev)]
                if d < 8:
                    p.do("tensor",
                         lambda E, jc=jc, d=d, bank=bank: E.matmul(
                             bank, g1w[:, d, jc * 128:(jc + 1) * 128],
                             y1b[:, d], start=(d == 0), stop=False),
                         wait=w)
                else:
                    p.do("tensor",
                         lambda E, jc=jc, bank=bank: E.matmul(
                             bank, g1s[:, jc * 128:(jc + 1) * 128],
                             stats24[:], start=False, stop=True),
                         wait=w, inc=sT)
            mm = p.mark(sT)
            m_gbank[jc % 4] = p.do(
                "scalar",
                lambda E, jc=jc, bank=bank: E.activation(
                    h1b[:, jc], bank, AF.Gelu, bias=sgb1[:, jc:jc + 1]),
                wait=[(sT, mm)], inc=sS)

        for jc in range(16):
            w = [(sS, m_gbank[jc % 4])] if jc >= 12 else []
            if jc == 0 and m_lgt_used is not None:
                w = [(sV, m_lgt_used)]
                if m_tile_s is not None:
                    w.append((sS, m_tile_s))
            p.do("tensor",
                 lambda E, jc=jc: E.matmul(
                     pp[4][0:24, 0:256], g2w[:, jc, :], h1b[:, jc],
                     start=(jc == 0), stop=(jc == 15)),
                 wait=w, inc=(sT if jc == 15 else None))
        mm_lg = p.mark(sT)
        p.do("vector",
             lambda E: E.scalar_tensor_tensor(
                 lgt[:], stats24[:], salst[:], pp[4][0:24, 0:256],
                 ALU.mult, ALU.add),
             wait=[(sT, mm_lg), (sV, m_temp)])
        p.do("vector", lambda E: E.drain())
        p.do("vector",
             lambda E: E.tensor_scalar(lgt[:], lgt[:], sgb2[:], tinv24[:],
                                       ALU.add, ALU.mult))
        m_lgt = p.mil("vector", sV)
        m_lgt_used = m_lgt

        trb = [pp[2], pp[3]]
        for b2 in range(2):
            w = [(sV, m_lgt)]
            if b2 == 0:
                w.append((sS, m_stats_ev))
                if m_prb_out is not None:
                    w.append((ch_p1, m_prb_out))
            mm = p.do("tensor",
                      lambda E, b2=b2: E.transpose(
                          trb[b2][:, 0:24],
                          lgt[:, b2 * 128:(b2 + 1) * 128],
                          C["c_idf32"][0:24, 0:24]),
                      wait=w, inc=sT)
            p.do("scalar",
                 lambda E, b2=b2: E.activation(prb[:, b2], trb[b2][:, 0:24],
                                               AF.Identity),
                 wait=[(sT, mm)], inc=(sS if b2 == 1 else None))
        m_tr = p.mark(sS)
        for b2 in range(2):
            for hh in range(4):
                p.do("vector",
                     lambda E, b2=b2, hh=hh: E.reduce_max(
                         mxs[:, b2, hh:hh + 1],
                         prb[:, b2, 6 * hh:6 * hh + 6], axis=AX.X),
                     wait=[(sS, m_tr)] if (b2 == 0 and hh == 0) else ())
        p.do("vector", lambda E: E.drain())
        for b2 in range(2):
            for hh in range(4):
                p.do("vector",
                     lambda E, b2=b2, hh=hh: E.tensor_scalar_mul(
                         mxs[:, b2, 4 + hh:5 + hh], mxs[:, b2, hh:hh + 1],
                         -1.0))
        m_mx = p.mil("vector", sV)
        for b2 in range(2):
            for hh in range(4):
                p.do("scalar",
                     lambda E, b2=b2, hh=hh: E.activation(
                         pex[:, b2, 6 * hh:6 * hh + 6],
                         prb[:, b2, 6 * hh:6 * hh + 6], AF.Exp,
                         bias=mxs[:, b2, 4 + hh:5 + hh]),
                     wait=[(sV, m_mx)] if (b2 == 0 and hh == 0) else ())
        m_ex = p.mil("scalar", sS)
        for b2 in range(2):
            for hh in range(4):
                p.do("vector",
                     lambda E, b2=b2, hh=hh: E.reduce_sum(
                         mxs[:, b2, hh:hh + 1],
                         pex[:, b2, 6 * hh:6 * hh + 6], axis=AX.X),
                     wait=[(sS, m_ex)] if (b2 == 0 and hh == 0) else ())
        p.do("vector", lambda E: E.drain())
        p.do("vector",
             lambda E: E.reciprocal(mxs[:, :, 4:8], mxs[:, :, 0:4]))
        p.do("vector", lambda E: E.drain())
        p.do("vector",
             lambda E: E.tensor_scalar_mul(
                 mxs[:, :, 8:12], mxs[:, :, 4:8], 0.95))
        p.do("vector", lambda E: E.drain())
        for b2 in range(2):
            for hh in range(4):
                p.do("vector",
                     lambda E, b2=b2, hh=hh: E.tensor_scalar_mul(
                         prb[:, b2, 6 * hh:6 * hh + 6],
                         pex[:, b2, 6 * hh:6 * hh + 6],
                         mxs[:, b2, 8 + hh:9 + hh]))
        p.do("vector", lambda E: E.drain())
        for b2 in range(2):
            for hh in range(4):
                p.do("vector",
                     lambda E, b2=b2, hh=hh: E.tensor_scalar_add(
                         prb[:, b2, 6 * hh + 5:6 * hh + 6],
                         prb[:, b2, 6 * hh + 5:6 * hh + 6], 0.05))
        m_prb = p.mil("vector", sV)
        mk1 = p.dma("sync", prb_scr, prb[:], ch_p1, wait=[(sV, m_prb)])
        w = [(ch_p1, mk1)]
        if m_prbR_used is not None:
            w.append((sV, m_prbR_used))
        m_prbR = p.dma("gpsimd", probsR[:],
                       prb_scr.rearrange("p b h -> b p h").rearrange(
                           "(o b) p h -> o b p h", o=1), ch_p2, wait=w)
        m_prb_out = mk1

        bsl = [pp[4][:, 0:256], pp[5][:, 0:256], pp[2][:, 0:256],
               pp[4][:, 256:512], pp[5][:, 256:512], pp[2][:, 256:512]]
        m_mix_marks = []
        for hh in range(4):
            for pi in range(6):
                kk = 6 * hh + pi
                w = [(ch_p2, m_prbR)]
                if hh == 0 and pi == 0 and m_lgt_used is not None:
                    w.append((sV, m_lgt_used))
                if m_bcb_used[pi] is not None:
                    w.append((sV, m_bcb_used[pi]))
                if pi == 0 and m_rms_used is not None:
                    w.append((sV, m_rms_used))
                if pi == 0 and m_tile_s is not None:
                    w.append((sS, m_tile_s))
                mm = p.do("tensor",
                          lambda E, kk=kk, pi=pi: E.matmul(
                              bsl[pi], sorow[:], probsR[0:1, :, :, kk],
                              start=True, stop=True),
                          wait=w, inc=sT)
                p.do("scalar",
                     lambda E, pi=pi, mmx=mm: E.activation(
                         bcb[:, pi], bsl[pi], AF.Identity),
                     wait=[(sT, mm)],
                     inc=(sS if pi == 5 else None))
            m_bc6 = p.mark(sS)
            for i in range(2):
                c = 2 * hh + i
                for pi, br in enumerate(branches):
                    w = []
                    if pi == 0 and i == 0:
                        w = [(sS, m_bc6), (sV, m_fir_v), (sS, m_fir_sS)]
                        if m_oacc_used is not None:
                            w.append((sV, m_oacc_used))
                    p.do("vector",
                         lambda E, c=c, i=i, pi=pi, br=br: E.tensor_tensor(
                             mixt[:, 6 * i + pi],
                             br[:, c] if br is not vst else
                             vst[:, c, 30:30 + TT],
                             bcb[:, pi], ALU.mult),
                         wait=w)
            p.do("vector", lambda E: E.drain())
            for i in range(2):
                base = 6 * i
                p.do("vector",
                     lambda E, base=base, hh=hh, i=i: E.tensor_tensor(
                         mixt[:, base + 0], mixt[:, base + 0],
                         mixt[:, base + 1], ALU.add))
                p.do("vector",
                     lambda E, base=base: E.tensor_tensor(
                         mixt[:, base + 2], mixt[:, base + 2],
                         mixt[:, base + 3], ALU.add))
                p.do("vector",
                     lambda E, base=base: E.tensor_tensor(
                         mixt[:, base + 4], mixt[:, base + 4],
                         mixt[:, base + 5], ALU.add))
            p.do("vector", lambda E: E.drain())
            for i in range(2):
                base = 6 * i
                p.do("vector",
                     lambda E, base=base: E.tensor_tensor(
                         mixt[:, base + 0], mixt[:, base + 0],
                         mixt[:, base + 2], ALU.add))
            p.do("vector", lambda E: E.drain())
            for i in range(2):
                base = 6 * i
                c = 2 * hh + i
                p.do("vector",
                     lambda E, base=base, c=c: E.tensor_tensor(
                         o_acc[:, c], mixt[:, base + 0], mixt[:, base + 4],
                         ALU.add))
            mk = p.mil("vector", sV)
            for pi in range(6):
                m_bcb_used[pi] = mk
        m_mix = p.mil("vector", sV)
        m_fir_used = m_mix
        m_prbR_used = m_mix

        for c in range(8):
            w = [(sV, m_mix)] if c == 0 else []
            p.do("scalar",
                 lambda E, c=c: E.activation(xsq2[:, c], o_acc[:, c],
                                             AF.Square), wait=w)
        m_osq = p.mil("scalar", sS)
        rsl = [pp[3][0:1, 0:256], pp[4][0:1, 0:256], pp[3][0:1, 256:512],
               pp[4][0:1, 256:512]]
        m_rev = [None] * 4
        for hh in range(4):
            bank = rsl[hh]
            for i in range(2):
                w = []
                if i == 0:
                    w = [(sS, m_osq)]
                    if m_rev[(hh + 2) % 4] is not None:
                        w.append((sS, m_rev[(hh + 2) % 4]))
                p.do("tensor",
                     lambda E, hh=hh, i=i, bank=bank: E.matmul(
                         bank, C["c_ones_col"][:], xsq2[:, 2 * hh + i],
                         start=(i == 0), stop=(i == 1)),
                     wait=w, inc=(sT if i == 1 else None))
            mm = p.mark(sT)
            m_rev[hh] = p.do(
                "scalar",
                lambda E, hh=hh, bank=bank: E.activation(nr2[:, hh], bank,
                                                         AF.Identity),
                wait=[(sT, mm)], inc=sS)
        m_r0 = p.mil("scalar", sS)
        p.do("scalar",
             lambda E: E.activation(
                 nr2[:, 4:8, :].rearrange("o a t -> o (a t)"),
                 nr2[:, 0:4, :].rearrange("o a t -> o (a t)"),
                 AF.Sqrt, scale=1.0 / 256, bias=C["c_eps"][0:1, 0:1]),
             wait=[(sS, m_r0)])
        m_r1 = p.mil("scalar", sS)
        p.do("vector",
             lambda E: E.reciprocal(
                 nr2[:, 0:4, :].rearrange("o a t -> o (a t)"),
                 nr2[:, 4:8, :].rearrange("o a t -> o (a t)")),
             wait=[(sS, m_r1)])
        m_rrow = p.mil("vector", sV)
        for hh in range(4):
            w = [(sV, m_rrow)]
            if hh > 0:
                w.append((sV, m_rms_used))
            mm = p.do("tensor",
                      lambda E, hh=hh: E.matmul(
                          pp[5][:, 0:256], C["c_ones_row"][:],
                          nr2[0:1, hh], start=True, stop=True),
                      wait=w, inc=sT)
            for i in range(2):
                c = 2 * hh + i
                p.do("vector",
                     lambda E, c=c: E.tensor_tensor(o_acc[:, c], o_acc[:, c],
                                                    pp[5][:, 0:256],
                                                    ALU.mult),
                     wait=[(sT, mm)] if i == 0 else ())
            m_rms_used = p.mil("vector", sV)
        p.do("vector", lambda E: E.drain())
        for c in range(8):
            w = []
            if c == 0 and m_ob_used is not None:
                w = [(sT, m_ob_used)]
            p.do("vector",
                 lambda E, c=c: E.tensor_scalar_mul(
                     ob[:, c], o_acc[:, c], sonw[:, c % 2:c % 2 + 1]),
                 wait=w)
        m_obv = p.mil("vector", sV)
        m_oacc_used = m_obv

        for oc in range(8):
            bank = pp[oc % 2][:, 256:512]
            for c in range(8):
                w = []
                if c == 0:
                    w = [(sV, m_obv)]
                    if m_bank01[oc % 2] is not None:
                        w.append((sS, m_bank01[oc % 2]))
                    if m_bank01[(oc + 1) % 2] is not None:
                        w.append((sS, m_bank01[(oc + 1) % 2]))
                    if oc == 0:
                        for mg in m_gbank:
                            if mg is not None:
                                w.append((sS, mg))
                p.do("tensor",
                     lambda E, oc=oc, c=c, bank=bank: E.matmul(
                         bank, opw[:, c, oc * 128:(oc + 1) * 128],
                         ob[:, c], start=(c == 0), stop=(c == 7)),
                     wait=w, inc=(sT if c == 7 else None))
            mm = p.mark(sT)
            wev = [(sT, mm)]
            if oc == 0 and mark_store is not None:
                wev.append((ch_o, mark_store))
            m_bank01[oc % 2] = p.do(
                "scalar",
                lambda E, oc=oc, bank=bank: E.activation(
                    fo[:, oc], bank, AF.Identity),
                wait=wev, inc=sS)

        if t + 1 < NTILE:
            nsl = (t + 1) % 2
            t1 = (t + 1) * TT
            xin_n = xin[:, t1:t1 + TT].rearrange("(c p) t -> p c t", p=128)
            w = ([(sV, mark_use_hist[nsl])]
                 if mark_use_hist[nsl] is not None else [(sV, mark_use)])
            mk_xn = p.dma("sync", sx[:, nsl * 8:nsl * 8 + 8], xin_n,
                          ch_x[nsl], wait=w)
            wr = [(sT, mkst1), (sT, p.mark(sT)), (sS, m_bank01[0]),
                  (sS, m_bank01[1]), (sV, mark_use)]
            mkst1 = emit_ln_stats_pre(
                g, TT, lambda c, s=nsl: sx[:, s * 8 + c], xsq2,
                C["c_ones16"], pp[2][0:1, 256:512], pp[3][0:1, 256:512],
                src_ready=[(ch_x[nsl], mk_xn), (sT, mkst1)],
                war_rows=wr)

        mk_bc2 = emit_ln_stats(
            g, TT, lambda c: fo[:, c], xsq, rows, C["c_ones16"],
            C["c_ones_row"], C["c_eps"], pp[6][:, 256:512], pp[7][:, 256:512],
            pp[6][:, 0:256], pp[7][:, 0:256],
            src_ready=[(sS, m_bank01[0]), (sS, m_bank01[1])],
            war_rows=[(sV, mark_use)],
            war_bc=[(sV, mark_use)])
        for c in range(8):
            w = [(sT, mk_bc2)] if c == 0 else []
            if c == 0 and mark_store is not None:
                w.append((ch_o, mark_store))
            p.do("vector",
                 lambda E, c=c: E.tensor_tensor(xsq[:, c], fo[:, c],
                                                pp[6][:, 0:TT], ALU.subtract),
                 wait=w)
        p.do("vector", lambda E: E.drain())
        for c in range(8):
            p.do("vector",
                 lambda E, c=c: E.tensor_tensor(xsq[:, c], xsq[:, c],
                                                pp[7][:, 0:TT], ALU.mult))
        p.do("vector", lambda E: E.drain())
        for c in range(8):
            p.do("vector",
                 lambda E, c=c, sl=sl: E.tensor_tensor(
                     fo[:, c], sx[:, sl * 8 + c], xsq[:, c], ALU.add))
        mark_ln2 = p.mil("vector", sV)
        mark_store = p.dma("sync", xout_sl, fo[:], ch_o,
                           wait=[(sV, mark_ln2)])
        if t + 1 < NTILE:
            mk_bc_next = emit_ln_stats_post(
                g, TT, rows2, C["c_ones_row"], C["c_eps"],
                pp[2][0:1, 256:512], pp[3][0:1, 256:512],
                pp[6][:, 0:256], pp[7][:, 0:256], mkst1,
                war_bc=[(sV, mark_ln2)])
        mark_use = mark_ln2
        mark_use_hist[sl] = mark_ln2
        m_st_used = m_lgt
        m_fir_used = m_mix
        m_fir_used_s = m_stats_ev
        m_tile_s = p.mil("scalar", sS)
        m_fir_pe = p.mark(sT)
        m_ob_used = p.mark(sT)
    g.barrier()


def emit_l1attn(g, xin, xout, C):
    nc, p = g.nc, g.p
    sT, sV, sS = g.sT, g.sV, g.sS
    NH = 16
    TB = 256
    NBT = T // TB
    TQ = 512
    NQT = T // TQ

    wq = g.din("l1a_wqT", [D, D])
    wk = g.din("l1a_wkT", [D, D])
    wv = g.din("l1a_wvT", [D, D])
    wo = g.din("l1a_woT", [D, D])
    ropeC = g.din("l1a_ropeC", [128, T])
    ropeS = g.din("l1a_ropeS", [128, T])
    masks = g.din("l1a_masks", [128, 4, TQ], BF16)
    qn_pat = g.din("l1a_qnpat", [128, 30])
    mh_scr = g.dint("l1a_mhscr", [16, TQ])
    khat_d = g.dint("l1a_khatd", [65, NH, T], BF16)

    pp = g.pool
    SHUF = [1, 0, 3, 2, 5, 4, 7, 6, 9, 8, 11, 10, 13, 12, 15, 14,
            17, 16, 19, 18, 21, 20, 23, 22, 25, 24, 27, 26, 29, 28, 31, 30]

    vtm = g.sb("l1a_vtm", [128, 16, NH * 65], BF16)
    sqnp = g.sb("l1a_qnp", [128, 30])
    mk2n = g.sb("l1a_mk2n", [16, 4])
    p.dma("sync", sqnp[:], qn_pat, g.dSY)

    es1 = ExitStack()
    sC = es1.enter_context(nc.sbuf_tensor("l1a_C", [128, T], F32))
    sS_ = es1.enter_context(nc.sbuf_tensor("l1a_S", [128, T], F32))
    p.dma("sync", sC[:], ropeC, g.dSY)
    p.dma("sync", sS_[:], ropeS, g.dSY)
    wkb = es1.enter_context(nc.sbuf_tensor("l1a_wk", [128, 8, D], BF16))
    wvb = es1.enter_context(nc.sbuf_tensor("l1a_wv", [128, 8, D], BF16))
    sx = es1.enter_context(nc.sbuf_tensor("l1a_sx", [128, 16, TB], F32))
    xsq = es1.enter_context(nc.sbuf_tensor("l1a_xsq", [128, 8, TB], F32))
    xq1 = es1.enter_context(nc.sbuf_tensor("l1a_xq1", [128, 8, TB], F32))
    rows2 = es1.enter_context(nc.sbuf_tensor("l1a_rowsB", [1, 5, TB], F32))
    y3t = es1.enter_context(nc.sbuf_tensor("l1a_y3t", [128, 8, TB], BF16))
    ktile = es1.enter_context(nc.sbuf_tensor("l1a_ktile", [128, 8, TB], F32))
    ksq = es1.enter_context(nc.sbuf_tensor("l1a_ksq", [128, TB], F32))
    knm = es1.enter_context(nc.sbuf_tensor("l1a_knm", [16, 10], F32))
    k65 = es1.enter_context(nc.sbuf_tensor("l1a_k65", [65, NH, TB], BF16))
    if True:
        for c in range(8):
            p.dma("gpsimd", wkb[:, c], wk[c * 128:(c + 1) * 128, :], g.dGP)
            p.dma("gpsimd", wvb[:, c], wv[c * 128:(c + 1) * 128, :], g.dGP)
        g.barrier()
        p.do("vector", lambda E: E.memset(k65[64:65, :, :], 1.0))
        p.do("vector", lambda E: E.memset(vtm[:], 1.0))
        m_ones = p.mil("vector", sV)

        ch_x = [p.channel("l1a_chx0"), p.channel("l1a_chx1")]
        ch_kd = p.channel("l1a_chkd")
        mark_use = None
        mark_use_hist = [None, None]
        m_kevac = None
        m_vevac = None
        m_knred = None
        m_krot = None
        m_k65c = None
        xin_0 = xin[:, 0:TB].rearrange("(c p) t -> p c t", p=128)
        mk_x0 = p.dma("sync", sx[:, 0:8], xin_0, ch_x[0])
        mkst = emit_ln_stats_pre(
            g, TB, lambda c: sx[:, c], xq1, C["c_ones16"], pp[5], pp[6],
            src_ready=[(ch_x[0], mk_x0)])
        mk_bc_next = emit_ln_stats_post(
            g, TB, rows2, C["c_ones_row"], C["c_eps"], pp[5], pp[6],
            pp[7], pp[4], mkst)
        m_vmm = None
        for tb in range(NBT):
            t0 = tb * TB
            sl = tb % 2
            mk_bc = mk_bc_next
            for c in range(8):
                w = []
                if c == 0:
                    w = [(sT, mk_bc)]
                    if m_vmm is not None:
                        w.append((sT, m_vmm))
                p.do("vector",
                     lambda E, c=c, sl=sl: E.tensor_tensor(
                         xsq[:, c], sx[:, sl * 8 + c],
                         pp[7][:, 0:TB], ALU.subtract),
                     wait=w)
            p.do("vector", lambda E: E.drain())
            for c in range(8):
                p.do("vector",
                     lambda E, c=c: E.tensor_tensor(y3t[:, c], xsq[:, c],
                                                    pp[4][:, 0:TB], ALU.mult))
            mark_use = p.mil("vector", sV)
            mark_use_hist[sl] = mark_use
            if tb + 1 < NBT:
                nsl = (tb + 1) % 2
                t1 = (tb + 1) * TB
                xin_n = xin[:, t1:t1 + TB].rearrange("(c p) t -> p c t",
                                                     p=128)
                w = ([(sV, mark_use_hist[nsl])]
                     if mark_use_hist[nsl] is not None else [])
                mk_xn = p.dma("sync", sx[:, nsl * 8:nsl * 8 + 8], xin_n,
                              ch_x[nsl], wait=w)
                mkst = emit_ln_stats_pre(
                    g, TB, lambda c, s=nsl: sx[:, s * 8 + c], xq1,
                    C["c_ones16"], pp[5], pp[6],
                    src_ready=[(ch_x[nsl], mk_xn), (sT, mkst)],
                    war_rows=[(sT, mkst), (sV, mark_use)])

            for c in range(8):
                bank = pp[c % 2]
                for d in range(8):
                    w = []
                    if d == 0:
                        w = [(sV, mark_use)]
                        if m_kevac is not None and c < 2:
                            w.append((sS, m_kevac))
                        if c < 2 and getattr(g, "_l1a_vev4", None) \
                                and g._l1a_vev4[c] is not None:
                            w.append((sS, g._l1a_vev4[c]))
                        if m_krot is not None and c == 0:
                            w.append((sV, m_krot))
                    p.do("tensor",
                         lambda E, c=c, d=d, bank=bank: E.matmul(
                             bank[:, 0:TB],
                             wkb[:, d, c * 128:(c + 1) * 128],
                             y3t[:, d], start=(d == 0), stop=(d == 7)),
                         wait=w, inc=(sT if d == 7 else None))
                mm = p.mark(sT)
                p.do("scalar",
                     lambda E, c=c, bank=bank: E.activation(
                         ktile[:, c], bank[:, 0:TB], AF.Identity, scale=0.125),
                     wait=[(sT, mm)])
                p.do("scalar",
                     lambda E, bank=bank: E.activation(ksq[:], bank[:, 0:TB],
                                                       AF.Square))
                m_kevac = p.mil("scalar", sS)
                p.do("tensor",
                     lambda E, c=c: E.matmul(
                         pp[4][0:16, 0:TB],
                         sqnp[:, 14 - 2 * c:30 - 2 * c], ksq[:],
                         start=(c == 0), stop=(c == 7)),
                     wait=[(sS, m_kevac)],
                     inc=(sT if c == 7 else None))
            mm_kn = p.mark(sT)
            p.do("vector",
                 lambda E, tb=tb: E.reduce_max(knm[:, tb:tb + 1],
                                               pp[4][0:16, 0:TB], axis=AX.X),
                 wait=[(sT, mm_kn)])
            m_knred = p.mil("vector", sV)
            if tb + 1 < NBT:
                mk_bc_next = emit_ln_stats_post(
                    g, TB, rows2, C["c_ones_row"], C["c_eps"], pp[5], pp[6],
                    pp[7], pp[4], mkst,
                    war_bc=[(sV, m_knred), (sV, mark_use)])

            for c in range(8):
                p.do("vector",
                     lambda E, c=c, t0=t0: E.stream_shuffle(xsq[:, c],
                                                            ktile[:, c], SHUF),
                     wait=[(sS, m_kevac)] if c == 0 else ())
            p.do("vector", lambda E: E.drain())
            for c in range(8):
                p.do("vector",
                     lambda E, c=c, t0=t0: E.tensor_tensor(
                         ktile[:, c], ktile[:, c], sC[:, t0:t0 + TB],
                         ALU.mult))
            for c in range(8):
                p.do("vector",
                     lambda E, c=c, t0=t0: E.tensor_tensor(
                         xsq[:, c], xsq[:, c], sS_[:, t0:t0 + TB], ALU.mult))
            p.do("vector", lambda E: E.drain())
            for c in range(8):
                w = [(ch_kd, m_k65c)] if (c == 0 and m_k65c) else []
                p.do("vector",
                     lambda E, c=c: E.tensor_tensor(ktile[:, c], ktile[:, c],
                                                    xsq[:, c], ALU.add),
                     wait=w)
            m_krot = p.mil("vector", sV)
            for c in range(8):
                p.do("scalar",
                     lambda E, c=c: E.activation(k65[0:64, 2 * c, :],
                                                 ktile[0:64, c], AF.Identity),
                     wait=[(sV, m_krot)] if c == 0 else ())
                p.do("scalar",
                     lambda E, c=c: E.activation(k65[0:64, 2 * c + 1, :],
                                                 ktile[64:128, c], AF.Identity))
            m_k65 = p.mil("scalar", sS)
            m_k65c = p.dma("sync", khat_d[:, :, t0:t0 + TB], k65[:], ch_kd,
                           wait=[(sS, m_k65), (sV, m_ones)])

            m_vev4 = getattr(g, "_l1a_vev4", [None] * 4)
            for i in range(2):
                for half in range(2):
                    vb = 2 * i + half
                    bank = pp[vb]
                    for d in range(8):
                        w = []
                        if d == 0:
                            w = [(sV, mark_use), (sS, m_kevac)]
                            if m_vev4[vb] is not None:
                                w.append((sS, m_vev4[vb]))
                        p.do("tensor",
                             lambda E, i=i, half=half, d=d, bank=bank:
                             E.matmul(
                                 bank[:, 0:512],
                                 y3t[:, d, i * 128:(i + 1) * 128],
                                 wvb[:, d, half * 512:(half + 1) * 512],
                                 start=(d == 0), stop=(d == 7)),
                             wait=w, inc=(sT if d == 7 else None))
                    mm = p.mark(sT)
                    m_vmm = mm
                    tc = tb * 2 + i
                    m_vev4[vb] = p.do(
                        "scalar",
                        lambda E, tc=tc, half=half, bank=bank: E.activation(
                            vtm[:, tc].rearrange("p (h x) -> p h x", x=65)
                            [:, half * 8:(half + 1) * 8, 0:64],
                            bank[:, 0:512], AF.Identity),
                        wait=[(sT, mm), (sV, m_ones)], inc=sS)
                    g._l1a_vev4 = m_vev4
                    m_vevac = m_vev4[vb]

        p.do("vector", lambda E: E.reduce_max(mk2n[:, 0:1], knm[:, 0:NBT],
                                              axis=AX.X),
             wait=[(sV, m_knred)])
        p.do("vector", lambda E: E.drain())
        p.do("vector", lambda E: E.tensor_scalar_mul(mk2n[:, 1:2],
                                                     mk2n[:, 0:1], 1.0 / 64))
        m_mk2 = p.mil("vector", sV)
        g.barrier()
        es1.close()

    es2 = ExitStack()
    wqb = es2.enter_context(nc.sbuf_tensor("l1a_wq", [128, 8, D], BF16))
    wob = es2.enter_context(nc.sbuf_tensor("l1a_wo", [128, 8, D], BF16))
    sx2 = es2.enter_context(nc.sbuf_tensor("l1a_sx2", [128, 8, TQ], F32))
    xq2 = es2.enter_context(nc.sbuf_tensor("l1a_xsq2", [128, 8, TQ], F32))
    rw2 = es2.enter_context(nc.sbuf_tensor("l1a_rows2", [1, 6, TQ], F32))
    y3q = es2.enter_context(nc.sbuf_tensor("l1a_y3q", [128, 8, TQ], BF16))
    qt = es2.enter_context(nc.sbuf_tensor("l1a_qt", [128, 8, TQ], F32))
    qaug = es2.enter_context(nc.sbuf_tensor("l1a_qaug", [65, NH, TQ], BF16))
    qsq = es2.enter_context(nc.sbuf_tensor("l1a_qsq", [128, TQ], F32))
    mrow = es2.enter_context(nc.sbuf_tensor("l1a_mrow", [16, 2, TQ], F32))
    khbuf = es2.enter_context(nc.sbuf_tensor("l1a_kh", [65, 2, T], BF16))
    pt0 = es2.enter_context(nc.sbuf_tensor("l1a_pt0", [128, TQ], BF16))
    pt1 = es2.enter_context(nc.sbuf_tensor("l1a_pt1", [128, TQ], BF16))
    smask = es2.enter_context(nc.sbuf_tensor("l1a_smask", [128, 4, TQ], BF16))
    numb = es2.enter_context(nc.sbuf_tensor("l1a_numb", [64, 2, TQ], F32))
    linv = es2.enter_context(nc.sbuf_tensor("l1a_linv", [1, 2, TQ], F32))
    aoutT = es2.enter_context(nc.sbuf_tensor("l1a_aoutT", [128, 8, TQ], BF16))
    fo = es2.enter_context(nc.sbuf_tensor("l1a_fo", [128, 8, 256], F32))
    tmp2 = es2.enter_context(nc.sbuf_tensor("l1a_tmp2", [128, 8, 256], F32))
    tbl = es2.enter_context(nc.sbuf_tensor("l1a_tbl", [128, 2, TQ], F32))
    if True:
        ptb = [pt0, pt1]
        for c in range(8):
            p.dma("gpsimd", wqb[:, c], wq[c * 128:(c + 1) * 128, :], g.dGP)
            p.dma("gpsimd", wob[:, c], wo[c * 128:(c + 1) * 128, :], g.dGP)
        p.dma("sync", smask[:], masks, g.dSY)
        g.barrier()

        ch_x2 = p.channel("l1a_chx2")
        ch_tb = p.channel("l1a_chtb")
        ch_mh1 = p.channel("l1a_chmh1")
        ch_mh2 = p.channel("l1a_chmh2")
        ch_o = p.channel("l1a_cho")
        ch_kh = [p.channel("l1a_chkh0"), p.channel("l1a_chkh1")]
        kh_war = [None, None]
        kh_mk = [None, None]
        mark_store = None
        mark_ln2 = None
        m_qevac = None
        m_qrot = None
        psA_war = [None, None]
        pt_war = [None, None]
        psO_war = {0: None, 1: None}
        epi_war = [None, None]
        m_oevac = [None, None]
        m_mrow_out = None

        for tq in range(NQT):
            q0 = tq * TQ
            kc_max = (tq + 1) * (TQ // 128)
            xin_sl = xin[:, q0:q0 + TQ].rearrange("(c p) t -> p c t", p=128)
            w = [(sV, mark_ln2)] if mark_ln2 else []
            mk_x = p.dma("sync", sx2[:], xin_sl, ch_x2, wait=w)
            mk_bc = emit_ln_stats(
                g, TQ, lambda c: sx2[:, c], xq2, rw2, C["c_ones16"],
                C["c_ones_row"], C["c_eps"], pp[6], pp[7], pp[4], pp[5],
                src_ready=[(ch_x2, mk_x)],
                war_rows=[(sS, m_oevac[1])] if m_oevac[1] else (),
                war_bc=[(sV, mark_ln2)] if mark_ln2 else ())
            for c in range(8):
                p.do("vector",
                     lambda E, c=c: E.tensor_tensor(xq2[:, c], sx2[:, c],
                                                    pp[4][:, 0:TQ],
                                                    ALU.subtract),
                     wait=[(sT, mk_bc)] if c == 0 else ())
            p.do("vector", lambda E: E.drain())
            for c in range(8):
                p.do("vector",
                     lambda E, c=c: E.tensor_tensor(y3q[:, c], xq2[:, c],
                                                    pp[5][:, 0:TQ], ALU.mult))
            m_y3 = p.mil("vector", sV)

            for c in range(8):
                bank = pp[c % 2]
                for d in range(8):
                    w = []
                    if d == 0:
                        w = [(sV, m_y3)]
                        if m_qevac is not None and c < 2:
                            w.append((sS, m_qevac))
                        if psA_war[c % 2] is not None:
                            w.append((sS, psA_war[c % 2]))
                        if c == 0 and m_qrot is not None:
                            w.append((sV, m_qrot))
                    p.do("tensor",
                         lambda E, c=c, d=d, bank=bank: E.matmul(
                             bank[:, 0:TQ],
                             wqb[:, d, c * 128:(c + 1) * 128],
                             y3q[:, d], start=(d == 0), stop=(d == 7)),
                         wait=w, inc=(sT if d == 7 else None))
                mm = p.mark(sT)
                p.do("scalar",
                     lambda E, c=c, bank=bank: E.activation(
                         qt[:, c], bank[:, 0:TQ], AF.Identity),
                     wait=[(sT, mm)])
                p.do("scalar",
                     lambda E, bank=bank: E.activation(qsq[:], bank[:, 0:TQ],
                                                       AF.Square))
                m_qevac = p.mil("scalar", sS)
                p.do("tensor",
                     lambda E, c=c: E.matmul(
                         pp[5][0:16, 0:TQ],
                         sqnp[:, 14 - 2 * c:30 - 2 * c], qsq[:],
                         start=(c == 0), stop=(c == 7)),
                     wait=[(sS, m_qevac)] + (
                         [(sV, mark_ln2)]
                         if (c == 0 and mark_ln2 is not None) else []),
                     inc=(sT if c == 7 else None))
            mm_qn = p.mark(sT)

            p.do("vector",
                 lambda E: E.tensor_scalar_mul(mrow[:, 0], pp[5][0:16, 0:TQ],
                                               mk2n[:, 1:2]),
                 wait=[(sT, mm_qn), (sV, m_mk2)])
            m_t = p.mil("vector", sV)
            p.do("scalar", lambda E: E.activation(mrow[:, 1], mrow[:, 0],
                                                  AF.Sqrt),
                 wait=[(sV, m_t)])
            m_t2 = p.mil("scalar", sS)
            p.do("vector", lambda E: E.tensor_scalar_mul(mrow[:, 0],
                                                         mrow[:, 1], -1.0),
                 wait=[(sS, m_t2)])
            m_t3 = p.mil("vector", sV)
            w = [(sV, m_t3)]
            if m_mrow_out is not None:
                w.append((ch_mh2, m_mrow_out))
            mk1 = p.dma("sync", mh_scr, mrow[:, 0], ch_mh1, wait=w)
            m_mrow_out = p.dma("gpsimd", qaug[64:65, :, :],
                               mh_scr.rearrange("(o h) t -> o h t", o=1), ch_mh2,
                               wait=[(ch_mh1, mk1)])

            w = [(sV, m_qrot)] if m_qrot else []
            p.dma("sync", tbl[:, 0], ropeC[:, q0:q0 + TQ], ch_tb, wait=w)
            mk_tb = p.dma("sync", tbl[:, 1], ropeS[:, q0:q0 + TQ], ch_tb)
            for c in range(8):
                p.do("vector",
                     lambda E, c=c: E.stream_shuffle(xq2[:, c], qt[:, c],
                                                     SHUF),
                     wait=[(sS, m_qevac)] if c == 0 else ())
            p.do("vector", lambda E: E.drain())
            for c in range(8):
                p.do("vector",
                     lambda E, c=c: E.tensor_tensor(qt[:, c], qt[:, c],
                                                    tbl[:, 0], ALU.mult),
                     wait=[(ch_tb, mk_tb)] if c == 0 else ())
            for c in range(8):
                p.do("vector",
                     lambda E, c=c: E.tensor_tensor(xq2[:, c], xq2[:, c],
                                                    tbl[:, 1], ALU.mult))
            p.do("vector", lambda E: E.drain())
            for c in range(8):
                p.do("vector",
                     lambda E, c=c: E.tensor_tensor(qt[:, c], qt[:, c],
                                                    xq2[:, c], ALU.add))
            m_qrot = p.mil("vector", sV)
            for c in range(8):
                p.do("scalar",
                     lambda E, c=c: E.activation(qaug[0:64, 2 * c, :],
                                                 qt[0:64, c], AF.Identity),
                     wait=[(sV, m_qrot)] if c == 0 else ())
                p.do("scalar",
                     lambda E, c=c: E.activation(qaug[0:64, 2 * c + 1, :],
                                                 qt[64:128, c], AF.Identity))
            m_qdist = p.mil("scalar", sS)

            nkb = kc_max * 128
            w = [(sT, kh_war[0])] if kh_war[0] else []
            kh_mk[0] = p.dma("sync", khbuf[:, 0, 0:nkb],
                             khat_d[:, 0, 0:nkb], ch_kh[0], wait=w)

            pending_epi = None
            for h in range(NH):
                po = pp[2] if h % 2 == 0 else pp[4]
                if h + 1 < NH:
                    slot = (h + 1) % 2
                    w = [(sT, kh_war[slot])] if kh_war[slot] else []
                    kh_mk[slot] = p.dma("sync", khbuf[:, slot, 0:nkb],
                                        khat_d[:, h + 1, 0:nkb],
                                        ch_kh[slot], wait=w)
                sl = h % 2
                me_list = [None] * kc_max

                def em_score(kc, h=h, sl=sl):
                    bank = pp[kc % 2]
                    w = [(ch_mh2, m_mrow_out), (sS, m_qdist),
                         (ch_kh[sl], kh_mk[sl])]
                    if psA_war[kc % 2] is not None:
                        w.append((sS, psA_war[kc % 2]))
                    mm_s = p.do("tensor",
                                lambda E, h=h, kc=kc, bank=bank, sl=sl:
                                E.matmul(
                                    bank[:, 0:TQ],
                                    khbuf[:, sl, kc * 128:(kc + 1) * 128],
                                    qaug[:, h, :], start=True, stop=True),
                                wait=w, inc=sT)
                    kh_war[sl] = mm_s
                    w = [(sT, mm_s)]
                    if pt_war[kc % 2] is not None:
                        w.append((sT, pt_war[kc % 2]))
                    m_e = p.do("scalar",
                               lambda E, kc=kc, bank=bank: E.activation(
                                   ptb[kc % 2][:], bank[:, 0:TQ], AF.Exp),
                               wait=w, inc=sS)
                    psA_war[kc % 2] = m_e
                    dep_sem, dep_mark = sS, m_e
                    m = kc - 4 * tq
                    if m >= 0:
                        dep_mark = p.do(
                            "vector",
                            lambda E, kc=kc, m=m: E.tensor_tensor(
                                ptb[kc % 2][:], ptb[kc % 2][:], smask[:, m],
                                ALU.mult),
                            wait=[(sS, m_e)], inc=sV)
                        dep_sem = sV
                    me_list[kc] = (dep_sem, dep_mark)

                em_score(0)
                for kc in range(kc_max):
                    if kc + 1 < kc_max:
                        em_score(kc + 1)
                    if kc == min(2, kc_max - 1) and pending_epi is not None:
                        pending_epi()
                        pending_epi = None
                    wav = [me_list[kc]]
                    if kc == 0 and psO_war[h % 2] is not None:
                        wav.append(psO_war[h % 2])
                    pt_war[kc % 2] = p.do(
                        "tensor",
                        lambda E, h=h, kc=kc, km=kc_max, po=po: E.matmul(
                            po[0:65, 0:TQ],
                            vtm[:, kc, h * 65:(h + 1) * 65],
                            ptb[kc % 2][:], start=(kc == 0),
                            stop=(kc == km - 1)),
                        wait=wav, inc=sT)
                m_av = pt_war[(kc_max - 1) % 2]
                po_b = pp[3] if h % 2 == 0 else pp[5]

                def emit_epi(h=h, po=po, po_b=po_b, m_av=m_av):
                    p.do("vector",
                         lambda E, po=po, h=h: E.reciprocal(linv[:, h % 2],
                                                            po[64:65, 0:TQ]),
                         wait=[(sT, m_av)] + (
                             [epi_war[h % 2]] if epi_war[h % 2] else []))
                    m_li = p.mil("vector", sV)
                    mk_b = p.do("tensor",
                                lambda E, po_b=po_b, h=h: E.matmul(
                                    po_b[0:64, 0:TQ],
                                    C["c_ones_row"][0:1, 0:64],
                                    linv[:, h % 2], start=True, stop=True),
                                wait=[(sV, m_li)], inc=sT)
                    m_nu = p.do("scalar",
                                lambda E, po=po, h=h: E.activation(
                                    numb[:, h % 2], po[0:64, 0:TQ],
                                    AF.Identity),
                                wait=[(sT, m_av)], inc=sS)
                    mm_ao = p.do("vector",
                         lambda E, h=h, po_b=po_b: E.tensor_tensor(
                             aoutT[(h % 2) * 64:(h % 2) * 64 + 64, h // 2, :],
                             numb[:, h % 2], po_b[0:64, 0:TQ], ALU.mult),
                         wait=[(sT, mk_b), (sS, m_nu)], inc=sV)
                    epi_war[h % 2] = (sV, mm_ao)
                    psO_war[h % 2] = (sV, mm_ao)

                pending_epi = emit_epi
            pending_epi()
            pending_epi = None

            for s2 in range(2):
                s0 = q0 + s2 * 256
                xin_s = xin[:, s0:s0 + 256].rearrange("(c p) t -> p c t",
                                                      p=128)
                xout_s = xout[:, s0:s0 + 256].rearrange("(c p) t -> p c t",
                                                        p=128)
                w = [(sV, mark_ln2)] if mark_ln2 else []
                mk_xs = p.dma("sync", sx2[:, :, 0:256] if False else
                              sx2.rearrange("p c (u t) -> p c u t", u=2)
                              [:, :, s2], xin_s, ch_x2, wait=w)
                for oc in range(8):
                    bank = pp[4 + oc % 2]
                    for c in range(8):
                        w = []
                        if c == 0:
                            w = [m for m in (psO_war[0], psO_war[1])
                                 if m is not None]
                            if m_oevac[oc % 2] is not None:
                                w.append((sS, m_oevac[oc % 2]))
                            if oc == 0 and mark_ln2 is not None:
                                w.append((sV, mark_ln2))
                        p.do("tensor",
                             lambda E, oc=oc, c=c, s2=s2, bank=bank: E.matmul(
                                 bank[:, 0:256],
                                 wob[:, c, oc * 128:(oc + 1) * 128],
                                 aoutT[:, c, s2 * 256:(s2 + 1) * 256],
                                 start=(c == 0), stop=(c == 7)),
                             wait=w, inc=(sT if c == 7 else None))
                    mm = p.mark(sT)
                    wev = [(sT, mm)]
                    if oc == 0 and mark_store is not None:
                        wev.append((ch_o, mark_store))
                    m_oevac[oc % 2] = p.do(
                        "scalar",
                        lambda E, oc=oc, bank=bank: E.activation(
                            fo[:, oc], bank[:, 0:256], AF.Identity),
                        wait=wev, inc=sS)
                mk_bc2 = emit_ln_stats(
                    g, 256, lambda c: fo[:, c], tmp2, rw2, C["c_ones16"],
                    C["c_ones_row"], C["c_eps"], pp[6], pp[7], pp[4], pp[5],
                    src_ready=[(sS, m_oevac[1])],
                    war_bc=[(sS, m_oevac[0]), (sS, m_oevac[1])])
                sxs2 = sx2.rearrange("p c (u t) -> p c u t", u=2)
                for c in range(8):
                    w = [(sT, mk_bc2), (ch_x2, mk_xs)] if c == 0 else []
                    if c == 0 and mark_store is not None:
                        w.append((ch_o, mark_store))
                    p.do("vector",
                         lambda E, c=c: E.tensor_tensor(tmp2[:, c], fo[:, c],
                                                        pp[4][:, 0:256],
                                                        ALU.subtract),
                         wait=w)
                p.do("vector", lambda E: E.drain())
                for c in range(8):
                    p.do("vector",
                         lambda E, c=c: E.tensor_tensor(tmp2[:, c], tmp2[:, c],
                                                        pp[5][:, 0:256],
                                                        ALU.mult))
                p.do("vector", lambda E: E.drain())
                for c in range(8):
                    p.do("vector",
                         lambda E, c=c, s2=s2: E.tensor_tensor(
                             fo[:, c], sxs2[:, c, s2], tmp2[:, c], ALU.add))
                mark_ln2 = p.mil("vector", sV)
                mark_store = p.dma("sync", xout_s, fo[:], ch_o,
                                   wait=[(sV, mark_ln2)])
    g.barrier()
    es2.close()


def _prep_l1attn(inputs):
    d = {}
    for n, key in [("wqT", "Wq"), ("wkT", "Wk"), ("wvT", "Wv"), ("woT", "Wo")]:
        d["l1a_" + n] = np.ascontiguousarray(
            _as32(inputs["L1_attn_" + key]).T)
    DK = 64
    half = DK // 2
    inv_freq = (10000.0 ** (-np.arange(half) * 2.0 / DK)).astype(np.float64)
    pos = np.arange(T).astype(np.float64)
    ang = pos[None, :] * inv_freq[:, None]
    C_ = np.repeat(np.cos(ang), 2, axis=0)
    S_ = np.repeat(np.sin(ang), 2, axis=0)
    S_[0::2, :] *= -1.0
    d["l1a_ropeC"] = np.ascontiguousarray(
        np.tile(C_, (2, 1)).astype(np.float32))
    d["l1a_ropeS"] = np.ascontiguousarray(
        np.tile(S_, (2, 1)).astype(np.float32))
    masks = np.zeros((128, 4, 512), np.float32)
    kk = np.arange(128)[:, None]
    qq = np.arange(512)[None, :]
    for m in range(4):
        masks[:, m, :] = (m * 128 + kk <= qq)
    d["l1a_masks"] = masks.astype(BF)
    qn = np.zeros((128, 30), np.float32)
    qn[0:64, 14] = 1.0
    qn[64:128, 15] = 1.0
    d["l1a_qnpat"] = qn
    return d


def _prep_l0attnA(inputs):
    d = {}
    for nm, key in [("qpT", "q_proj_w"), ("kpT", "k_proj_w"),
                    ("vpT", "v_proj_w"), ("bpT", "b_proj_w")]:
        d["l0a_" + nm] = np.ascontiguousarray(
            _as32(inputs["L0_attn_" + key]).T)
    for nm, key in [("qcv", "q_conv_w"), ("kcv", "k_conv_w"),
                    ("vcv", "v_conv_w")]:
        w = _as32(inputs["L0_attn_" + key])
        d["l0a_" + nm] = np.ascontiguousarray(
            w.reshape(8, 128, 4).transpose(1, 0, 2))
    r = np.arange(32)[:, None]
    cc = np.arange(32)[None, :]
    blkU = -(r < cc).astype(np.float32)
    blkL = -(r > cc).astype(np.float32)
    blkI = (r <= cc).astype(np.float32)
    d["l0a_mku"] = np.ascontiguousarray(np.tile(blkU, (4, 4))).astype(BF)
    d["l0a_mkl"] = np.ascontiguousarray(np.tile(blkL, (4, 4))).astype(BF)
    d["l0a_mki"] = np.ascontiguousarray(np.tile(blkI, (4, 4))).astype(BF)
    d["l0a_i32q"] = np.ascontiguousarray(
        np.tile(np.eye(32, dtype=np.float32), (4, 1))).astype(BF)
    d["l0a_idf"] = np.eye(128, dtype=np.float32).astype(BF)
    return d


def _prep_l0attnB(inputs):
    d = {}
    d["l0a_g1T"] = np.ascontiguousarray(_as32(inputs["L0_attn_gate_w1"]).T)
    d["l0a_g2T"] = np.ascontiguousarray(_as32(inputs["L0_attn_gate_w2"]).T)
    d["l0a_opT"] = np.ascontiguousarray(_as32(inputs["L0_attn_o_proj_w"]).T)
    firws = {3: inputs["L0_attn_fir_w3"], 7: inputs["L0_attn_fir_w7"],
             15: inputs["L0_attn_fir_w15"], 31: inputs["L0_attn_fir_w31"]}
    for k in (3, 7, 15, 31):
        w = _as32(firws[k])
        d["l0a_fw%d" % k] = np.ascontiguousarray(
            w.reshape(8, 128, k).transpose(1, 0, 2))
    d["l0a_gb1"] = np.ascontiguousarray(
        _as32(inputs["L0_attn_gate_b1"]).reshape(16, 128).T)
    d["l0a_gb2"] = _as32(inputs["L0_attn_gate_b2"]).reshape(24, 1)
    alst = _as32(inputs["L0_attn_alpha_stat"])
    d["l0a_alst"] = np.ascontiguousarray(np.tile(alst, 4).reshape(24, 1))
    d["l0a_ltmp"] = _as32(inputs["L0_attn_gate_log_temp"]).reshape(4, 1)
    tx = np.zeros((4, 24), np.float32)
    for h in range(4):
        tx[h, 6 * h:6 * h + 6] = 1.0
    d["l0a_texp"] = tx
    d["l0a_onw"] = np.ascontiguousarray(
        _as32(inputs["L0_attn_o_norm_w"]).reshape(2, 128).T)
    stp = np.zeros((128, 47), np.float32)
    stp[:, 23] = 1.0 / 256.0
    d["l0a_stpat"] = stp.astype(BF)
    d["l0a_orow"] = np.ones((1, 128), np.float32).astype(BF)
    taps = []
    for k in (3, 7, 15, 31):
        for j in range(k):
            taps.append((k, j))
    d32 = np.zeros((8, 128, 56, 32), np.float32)
    eye = np.eye(32, dtype=np.float32)
    for c in range(8):
        for ti, (k, j) in enumerate(taps):
            wv = np.asarray(firws[k], dtype=np.float32)
            for q in range(4):
                ch = wv[c * 128 + 32 * q:c * 128 + 32 * q + 32, j]
                d32[c, 32 * q:32 * q + 32, ti, :] = eye * ch[None, :]
    d["l0b_d32"] = np.ascontiguousarray(
        d32.reshape(8, 128, 56 * 32)).astype(BF)
    return d


def _as32(a):
    return np.ascontiguousarray(np.asarray(a, dtype=np.float32))


def _prep_consts():
    c = {}
    c["c_ones_col"] = np.ones((128, 1), np.float32)
    c["c_ones16"] = np.ones((128, 1), BF)
    c["c_ones_row"] = np.ones((1, 128), np.float32)
    c["c_eps"] = np.full((1, 1), 1e-5, np.float32)
    c["c_eps12"] = np.full((1, 1), 1e-12, np.float32)
    c["c_one"] = np.full((1, 1), 1.0, np.float32)
    c["c_idf32"] = np.eye(128, dtype=np.float32)
    return c


CONST_DECL = [
    ("c_ones_col", [128, 1], F32),
    ("c_ones16", [128, 1], BF16),
    ("c_ones_row", [1, 128], F32),
    ("c_eps", [1, 1], F32),
    ("c_eps12", [1, 1], F32),
    ("c_one", [1, 1], F32),
    ("c_idf32", [128, 128], F32),
]


def _prep_l1ffn(inputs):
    w1 = _as32(inputs["L1_ffn_w1"])
    w2 = _as32(inputs["L1_ffn_w2"])
    b1 = _as32(inputs["L1_ffn_b1"])
    b2 = _as32(inputs["L1_ffn_b2"])
    return {
        "l1f_w1t": np.ascontiguousarray(w1.T),
        "l1f_w1n": w1,
        "l1f_w2t": np.ascontiguousarray(w2.T),
        "l1f_w2n": w2,
        "l1f_b1": np.ascontiguousarray(b1.reshape(32, 128).T),
        "l1f_b2": np.ascontiguousarray(b2.reshape(8, 128).T),
    }


def build_single(block):
    nc = bass.Bass("TRN2", target_bir_lowering=False)
    g = G(nc)
    xin = g.din("xin", [D, T])
    consts, mk = g.load_consts(CONST_DECL)
    if block == "l0attnA":
        emit_l0attnA(g, xin, consts, hand="out")
        g.p.finalize()
        g.es.close()
        return g
    xout = g.dout("xout", [D, T])
    if block == "l0attnB":
        emit_l0attnB(g, xin, xout, consts)
        g.p.finalize()
        g.es.close()
        return g
    if block == "l1ffn":
        emit_l1ffn(g, xin, xout, consts)
    elif block == "l0ffn":
        emit_l0ffn(g, xin, xout, consts)
    elif block == "l1attn":
        emit_l1attn(g, xin, xout, consts)
    else:
        raise NotImplementedError(block)
    g.p.finalize()
    g.es.close()
    return g


def build_fused():
    nc = bass.Bass("TRN2", target_bir_lowering=False)
    g = G(nc)
    xin = g.din("xin", [D, T])
    xout = g.dout("xout", [D, T])
    xA = g.dint("x_scrA", [D, T])
    xB = g.dint("x_scrB", [D, T])
    consts, mk = g.load_consts(CONST_DECL)
    g.push_arena()
    emit_l0attnA(g, xin, consts, hand="internal")
    g.pop_arena()
    g.push_arena()
    emit_l0attnB(g, xin, xB, consts)
    g.pop_arena()
    g.push_arena()
    emit_l0ffn(g, xB, xA, consts)
    g.pop_arena()
    g.push_arena()
    emit_l1attn(g, xA, xB, consts)
    g.pop_arena()
    g.push_arena()
    emit_l1ffn(g, xB, xout, consts)
    g.pop_arena()
    g.p.finalize()
    g.es.close()
    return g


def _prep_all(inputs):
    d = dict(_prep_consts())
    d.update(_prep_l0attnA(inputs))
    d.update(_prep_l0attnB(inputs))
    d.update(_prep_l0ffn(inputs))
    d.update(_prep_l1attn(inputs))
    d.update(_prep_l1ffn(inputs))
    return d


_G_CACHE = {}


def _get_fused():
    if "fused" not in _G_CACHE:
        _G_CACHE["fused"] = build_fused()
    return _G_CACHE["fused"]


def kernel(**inputs):
    x = np.asarray(inputs["L0_attn_hidden_states"], dtype=np.float32)
    prep = _prep_all(inputs)
    g = _get_fused()
    in_maps = []
    for b in range(NCORES):
        m = {}
        for name in g.innames:
            if name == "xin":
                m[name] = np.ascontiguousarray(x[b].T)
            else:
                m[name] = prep[name]
        in_maps.append(m)
    res = run_bass_kernel_spmd(g.nc, in_maps, core_ids=list(range(NCORES)))
    out = np.stack([np.asarray(r["xout"]).T for r in res.results])
    return np.ascontiguousarray(out.astype(np.float32))
